# version 03
# baseline (speedup 1.0000x reference)
import numpy as np
import concourse.bass as bass
import concourse.mybir as mybir
from concourse.bass_utils import run_bass_kernel_spmd

F32 = mybir.dt.float32
BF16 = mybir.dt.bfloat16
I32 = mybir.dt.int32
AF = mybir.ActivationFunctionType
ALU = mybir.AluOpType
AX = mybir.AxisListType

P = 128
T = 2048
D = 1024
NT = T // P
I0 = 4096
IF = 3072
NC0 = I0 // P
NCF = IF // P
GRP = 256
NG = T // GRP
HB = 16
INV_D = 1.0 / D
INV_PI = 0.3183098861837907
CLIP_LO = -1.0 + 1e-6
CLIP_HI = 1.0 - 1e-6

ENGINES = ("y", "v", "s", "t", "g")


class Ct:
    def __init__(self, handle, name):
        self.h = handle
        self.name = name
        self.n = 0


class Sched:
    def __init__(self):
        self.recs = {e: [] for e in ENGINES}

    def op(self, eng, fn, waits=(), inc=None, inc16=None, serial=True):
        ws = []
        for c, v in waits:
            if v <= 0:
                continue
            assert v <= c.n, f"wait on future value of {c.name}: {v} > {c.n}"
            ws.append((c, v))
        incs = []
        if inc is not None:
            cs = inc if isinstance(inc, list) else [inc]
            for c, k in cs:
                c.n += k
                incs.append((c, k))
        if inc16 is not None:
            inc16.n += 16
            incs.append((inc16, 16))
        self.recs[eng].append((fn, ws, incs))
        if inc16 is not None and serial:
            self.recs[eng].append(
                (lambda e: e.nop(), [(inc16, inc16.n)], []))

    def replay(self, eng_name, eng):
        for fn, ws, incs in self.recs[eng_name]:
            for c, v in ws:
                eng.wait_ge(c.h, v)
            inst = fn(eng)
            for c, k in incs:
                inst.then_inc(c.h, k)


class B:
    def __init__(self, nc):
        self.nc = nc
        self.sched = Sched()
        self._open = {}
        self._seq = 0

    def sb(self, name, shape, dt=F32):
        cm = self.nc.sbuf_tensor(name, shape, dt)
        h = cm.__enter__()
        self._seq += 1
        self._open[name] = (self._seq, cm)
        return h

    def ps(self, name, shape, dt=F32):
        cm = self.nc.psum_tensor(name, shape, dt)
        h = cm.__enter__()
        self._seq += 1
        self._open[name] = (self._seq, cm)
        return h

    def free(self, *names):
        for name in sorted(names, key=lambda n: -self._open[n][0]):
            self._open.pop(name)[1].__exit__(None, None, None)

    def sem(self, name):
        cm = self.nc.semaphore(name)
        h = cm.__enter__()
        self._seq += 1
        self._open["sem_" + name] = (self._seq, cm)
        return Ct(h, name)




def _finish(nc, b):
    with nc.Block() as block:
        @block.sync
        def _(y):
            b.sched.replay("y", y)

        @block.vector
        def _(v):
            b.sched.replay("v", v)

        @block.scalar
        def _(s):
            b.sched.replay("s", s)

        @block.tensor
        def _(t):
            b.sched.replay("t", t)

        @block.gpsimd
        def _(g):
            b.sched.replay("g", g)


def build_nc(upto="all"):
    nc = bass.Bass("TRN2", target_bir_lowering=False)
    b = B(nc)
    E = b.sched.op

    dxin = nc.dram_tensor("xin", [T, D], F32, kind="ExternalInput").ap()
    dkw = nc.dram_tensor("kw", [I0, D], F32, kind="ExternalInput").ap()
    dqw = nc.dram_tensor("qw", [I0, D], F32, kind="ExternalInput").ap()
    dlnw = nc.dram_tensor("lnw", [D], F32, kind="ExternalInput").ap()
    dlnb = nc.dram_tensor("lnb", [D], F32, kind="ExternalInput").ap()
    dbias1 = nc.dram_tensor("bias1", [D], F32, kind="ExternalInput").ap()
    dwrT = nc.dram_tensor("wrT", [D, D], F32, kind="ExternalInput").ap()
    dwkT = nc.dram_tensor("wkT", [D, D], F32, kind="ExternalInput").ap()
    dwvT = nc.dram_tensor("wvT", [D, D], F32, kind="ExternalInput").ap()
    dmur = nc.dram_tensor("mur", [D], F32, kind="ExternalInput").ap()
    dmuk = nc.dram_tensor("muk", [D], F32, kind="ExternalInput").ap()
    dl1q = nc.dram_tensor("l1qT", [D, D], F32, kind="ExternalInput").ap()
    dl1k = nc.dram_tensor("l1kT", [D, D], F32, kind="ExternalInput").ap()
    dl1v = nc.dram_tensor("l1vT", [D, D], F32, kind="ExternalInput").ap()
    dl1o = nc.dram_tensor("l1oT", [D, D], F32, kind="ExternalInput").ap()
    dbo = nc.dram_tensor("bo", [D], F32, kind="ExternalInput").ap()
    dwg = nc.dram_tensor("wg", [IF, D], F32, kind="ExternalInput").ap()
    dwu = nc.dram_tensor("wu", [IF, D], F32, kind="ExternalInput").ap()
    dwd = nc.dram_tensor("wd", [D, IF], F32, kind="ExternalInput").ap()
    dout = nc.dram_tensor("out", [T, D], F32, kind="ExternalOutput").ap()

    dattn0 = nc.dram_tensor("attn0", [T, D], F32).ap()
    dx1 = nc.dram_tensor("x1", [T, D], F32).ap()
    dxn2 = nc.dram_tensor("xn2s", [T, D], F32).ap()
    dattnP = nc.dram_tensor("attnP", [T, D], F32).ap()
    dqwbf = nc.dram_tensor("qwbf", [I0, D], BF16).ap()
    dup = nc.dram_tensor("ups", [NCF, P, T], F32).ap()
    dgate = nc.dram_tensor("gates", [NCF, P, T], F32).ap()
    dgate2 = nc.dram_tensor("gate2s", [NCF, P, T], F32).ap()
    dos = nc.dram_tensor("osp", [NCF, P, T], F32).ap()
    cc1i = nc.dram_tensor("cc1i", [P, 1], F32)
    cc1o = nc.dram_tensor("cc1o", [P, 1], F32)
    cc2i = nc.dram_tensor("cc2i", [P, 1], F32)
    cc2o = nc.dram_tensor("cc2o", [P, 1], F32)
    cc5i = nc.dram_tensor("cc5i", [P, 1], F32)
    cc5o = nc.dram_tensor("cc5o", [P, 1], F32)
    cc3i = nc.dram_tensor("cc3i", [P, 1], F32)
    cc3o = nc.dram_tensor("cc3o", [P, 1], F32)
    cc4i = nc.dram_tensor("cc4i", [P, 1], F32)
    cc4o = nc.dram_tensor("cc4o", [P, 1], F32)

    sLD = b.sem("sLD")
    sSP = b.sem("sSP")
    sGD = b.sem("sGD")
    sV = b.sem("sV")
    sS = b.sem("sS")
    sT = b.sem("sT")
    sG = b.sem("sG")
    sCC = b.sem("sCC")
    sOUT = b.sem("sOUT")
    sA0 = b.sem("sA0")
    sACT = b.sem("sACT")
    sQW = b.sem("sQW")
    sGT = b.sem("sGT")
    sUP = b.sem("sUP")
    sG2c = b.sem("sG2c")
    sOc = b.sem("sOc")

    ident_f = b.sb("ident_f", [P, P], F32)
    ident_b = b.sb("ident_b", [P, P], BF16)
    ones_f = b.sb("ones_f", [P, P], F32)
    tri = b.sb("tri", [P, P], BF16)
    lnw_b = b.sb("lnw_b", [P, D], F32)
    lnb_b = b.sb("lnb_b", [P, D], F32)
    bias1_b = b.sb("bias1_b", [P, D], F32)
    bo_b = b.sb("bo_b", [P, D], F32)
    mur_sb = b.sb("mur_sb", [P, 8], F32)
    muk_sb = b.sb("muk_sb", [P, 8], F32)
    m_all = b.sb("m_all", [P, NT], F32)
    rstd_all = b.sb("rstd_all", [P, NT], F32)
    m2_all = b.sb("m2_all", [P, NT], F32)
    ss2_all = b.sb("ss2_all", [P, NT], F32)
    rstd2_all = b.sb("rstd2_all", [P, NT], F32)
    runmax = b.sb("runmax", [P, 4], F32)
    stat = b.sb("stat", [P, 8], F32)
    kc = b.sb("kc", [P, 4], F32)
    xn2T = b.sb("xn2T", [P, 8 * T], BF16)
    vrow = b.sb("vrow", [1, D], F32)
    sq_scr = b.sb("sq_scr", [P, D], F32)
    E("g", lambda g: g.memset(ident_f[:, :], 0.0))
    E("g", lambda g: g.affine_select(
        out=ident_f[:, :], in_=ident_f[:, :], compare_op=ALU.not_equal,
        fill=1.0, base=0, pattern=[[-1, P]], channel_multiplier=1))
    E("g", lambda g: g.memset(ident_b[:, :], 0.0))
    E("g", lambda g: g.affine_select(
        out=ident_b[:, :], in_=ident_b[:, :], compare_op=ALU.not_equal,
        fill=1.0, base=0, pattern=[[-1, P]], channel_multiplier=1))
    E("g", lambda g: g.memset(ones_f[:, :], 1.0))
    E("g", lambda g: g.memset(tri[:, :], 1.0))
    E("g", lambda g: g.affine_select(
        out=tri[:, :], in_=tri[:, :], compare_op=ALU.is_ge,
        fill=0.0, base=0, pattern=[[1, P]], channel_multiplier=-1))
    E("g", lambda g: g.memset(runmax[:, :], 0.0))
    E("g", lambda g: g.memset(kc[:, 0:1], 1e-5))
    E("g", lambda g: g.memset(kc[:, 1:2], 1e-12))
    E("g", lambda g: g.memset(kc[:, 2:3], 0.5), inc=(sG, 1))
    g_const = sG.n

    ps_bc = b.ps("ps_bc", [P, D], F32)
    for src, dst in [(dlnw, lnw_b), (dlnb, lnb_b), (dbias1, bias1_b), (dbo, bo_b)]:
        vprev = sV.n
        E("y", lambda y, s=src: y.dma_start(out=vrow[0:1, :], in_=s.rearrange("(o d) -> o d", o=1)),
          waits=[(sV, vprev)], inc16=sLD)
        ld = sLD.n
        for half in range(2):
            E("t", lambda t, h=half: t.matmul(
                out=ps_bc[:, h * 512:(h + 1) * 512], lhsT=ones_f[0:1, :],
                rhs=vrow[0:1, h * 512:(h + 1) * 512], start=True, stop=True),
              waits=[(sLD, ld), (sG, g_const), (sV, vprev)],
              inc=(sT, 1) if half == 1 else None)
        E("v", lambda v, d=dst: v.tensor_copy(out=d[:, :], in_=ps_bc[:, :]),
          waits=[(sT, sT.n)], inc=(sV, 1))

    def _mu_load(y, dst, srcap):
        with nc.allow_non_contiguous_dma(reason="tiny one-time mu load"):
            return y.dma_start(out=dst[:, :], in_=srcap.rearrange("(c p) -> p c", p=P))
    E("y", lambda y: _mu_load(y, mur_sb, dmur), inc16=sLD)
    E("y", lambda y: _mu_load(y, muk_sb, dmuk), inc16=sLD)
    mu_ld = sLD.n

    KT = b.sb("KT", [P, 8 * I0], BF16)
    kwb = [b.sb(f"kwb{i}", [P, D], F32) for i in range(2)]
    knb = [b.sb(f"knb{i}", [P, D], BF16) for i in range(2)]
    qcb = [b.sb(f"qcb{i}", [P, D], BF16) for i in range(2)]
    ps_kt = [b.ps(f"ps_kt{i}", [P, D], BF16) for i in range(2)]

    kw_free = [0, 0]
    kt_busy = [0, 0]
    kn_free = [0, 0]
    qc_free = [0, 0]
    for k in range(NC0):
        i2 = k % 2
        E("y", lambda y, k=k, i2=i2: y.dma_start(
            out=kwb[i2][:, :], in_=dkw[k * P:(k + 1) * P, :]),
          waits=[(sV, kw_free[i2])], inc16=sLD)
        ld = sLD.n
        E("s", lambda s, i2=i2: s.activation(
            out=sq_scr[:, :], in_=kwb[i2][:, :], func=AF.Square,
            bias=0.0, scale=1.0),
          waits=[(sLD, ld), (sV, sV.n)], inc=(sS, 1))
        ssd = sS.n
        E("v", lambda v: v.tensor_reduce(
            out=stat[:, 2:3], in_=sq_scr[:, :], axis=AX.X, op=ALU.add),
          waits=[(sS, ssd)], inc=(sV, 1))
        E("s", lambda s: s.activation(
            out=stat[:, 3:4], in_=stat[:, 2:3], func=AF.Sqrt, bias=0.0, scale=1.0),
          waits=[(sV, sV.n)], inc=(sS, 1))
        E("v", lambda v: v.tensor_scalar(
            out=stat[:, 4:5], in0=stat[:, 3:4], scalar1=1e-12, scalar2=0.0,
            op0=ALU.max, op1=ALU.bypass), waits=[(sS, sS.n)])
        E("v", lambda v: v.drain())
        E("v", lambda v: v.reciprocal(out=stat[:, 5:6], in_=stat[:, 4:5]))
        E("v", lambda v: v.drain())
        E("v", lambda v, i2=i2: v.tensor_scalar(
            out=knb[i2][:, :], in0=kwb[i2][:, :], scalar1=stat[:, 5:6], scalar2=0.0,
            op0=ALU.mult, op1=ALU.bypass),
          waits=[(sT, kn_free[i2])], inc=(sV, 1))
        E("v", lambda v: v.drain())
        kw_free[i2] = sV.n
        vn = sV.n
        for c in range(8):
            E("t", lambda t, i2=i2, c=c: t.transpose(
                out=ps_kt[i2][:, c * P:(c + 1) * P], in_=knb[i2][:, c * P:(c + 1) * P],
                identity=ident_b[:, :]),
              waits=[(sV, vn), (sV, kt_busy[i2]), (sG, g_const)] if c == 0 else [],
              inc=(sT, 1) if c == 7 else None)
        kn_free[i2] = sT.n
        tn = sT.n
        E("v", lambda v, i2=i2, k=k: v.tensor_copy(
            out=KT[:, :].rearrange("p (h c) -> p h c", h=8)[:, :, k * P:(k + 1) * P],
            in_=ps_kt[i2][:, :].rearrange("p (h c) -> p h c", h=8)),
          waits=[(sT, tn)], inc=(sV, 1))
        kt_busy[i2] = sV.n
        E("g", lambda g, k=k, i2=i2: g.dma_start(
            out=qcb[i2][:, :], in_=dqw[k * P:(k + 1) * P, :]),
          waits=[(sQW, qc_free[i2])], inc16=sGD)
        gld = sGD.n
        E("s", lambda s, k=k, i2=i2: s.dma_start(
            out=dqwbf[k * P:(k + 1) * P, :], in_=qcb[i2][:, :]),
          waits=[(sGD, gld)], inc16=sQW)
        qc_free[i2] = sQW.n
    qw_cached = sQW.n

    b.free("kwb0", "kwb1", "knb0", "knb1", "qcb0", "qcb1",
           "ps_kt0", "ps_kt1", "ps_bc")
    p0_v, p0_s, p0_t = sV.n, sS.n, sT.n

    xg = [b.sb(f"xg{i}", [P, 2 * D], F32) for i in range(2)]
    QT = [b.sb(f"QT{i}", [P, 8 * GRP], BF16) for i in range(2)]
    qwk = [b.sb(f"qwk{i}", [P, D], BF16) for i in range(2)]
    sstage = b.sb("sstage", [P, HB * GRP], F32)
    aux = b.sb("aux", [P, HB * GRP], F32)
    Bst = [b.sb("Bst0", [P, HB * GRP], BF16)] * 2
    xnt = b.sb("xnt", [P, D], F32)
    yt = b.sb("yt", [P, D], F32)
    qtile = b.sb("qtile", [P, D], BF16)
    Xout = [b.sb("Xout0", [P, D], F32)] * 2
    ps_sT = [b.ps(f"ps_sT{i}", [P, GRP], F32) for i in range(3)]
    ps_X = [b.ps(f"ps_X{i}", [P, 512], F32) for i in range(4)]
    ps_qt = b.ps("ps_qt", [P, D], BF16)

    st = {
        "xg_free": [0, 0],
        "qt_evac": 0,
        "qt_built": [0, 0],
        "qt_free": [0, 0],
        "qwk_free": [0, 0],
        "sT_free": [0, 0, 0],
        "bst_free": [0, 0],
        "xfin_v": 0,
        "xout_free": [0, 0],
        "aux_free": 0,
        "attn0_sp": [0] * NT,
    }

    def ln_stats(xt_ap, m_dst, r_dst, eps, ld_waits):
        E("v", lambda v: v.tensor_reduce(
            out=stat[:, 0:1], in_=xt_ap, axis=AX.X, op=ALU.add),
          waits=ld_waits + [(sS, sS.n)], inc=(sV, 1))
        E("s", lambda s: s.activation(
            out=sq_scr[:, :], in_=xt_ap, func=AF.Square, bias=0.0, scale=1.0),
          waits=ld_waits + [(sV, sV.n)], inc=(sS, 1))
        ssd = sS.n
        E("v", lambda v: v.drain())
        E("v", lambda v: v.tensor_scalar(
            out=m_dst, in0=stat[:, 0:1], scalar1=INV_D, scalar2=0.0,
            op0=ALU.mult, op1=ALU.bypass))
        E("v", lambda v: v.tensor_reduce(
            out=stat[:, 3:4], in_=sq_scr[:, :], axis=AX.X, op=ALU.add),
          waits=[(sS, ssd)], inc=(sV, 1))
        E("v", lambda v: v.drain())
        E("v", lambda v: v.tensor_scalar(
            out=stat[:, 3:4], in0=stat[:, 3:4], scalar1=INV_D, scalar2=0.0,
            op0=ALU.mult, op1=ALU.bypass))
        E("v", lambda v: v.drain())
        E("v", lambda v: v.scalar_tensor_tensor(
            out=stat[:, 4:5], in0=m_dst, scalar=m_dst, in1=stat[:, 3:4],
            op0=ALU.mult, op1=ALU.subtract))
        E("v", lambda v: v.drain())
        E("v", lambda v: v.tensor_scalar(
            out=stat[:, 4:5], in0=stat[:, 4:5], scalar1=-1.0, scalar2=eps,
            op0=ALU.mult, op1=ALU.add), inc=(sV, 1))
        E("s", lambda s: s.activation(
            out=stat[:, 5:6], in_=stat[:, 4:5], func=AF.Ln, bias=0.0, scale=1.0),
          waits=[(sV, sV.n)])
        E("s", lambda s: s.activation(
            out=r_dst, in_=stat[:, 5:6], func=AF.Exp, bias=0.0, scale=-0.5),
          inc=(sS, 1))
        return sS.n

    def sq_accum(src_ap, dst_idx, waits):
        E("s", lambda s: s.activation(
            out=sq_scr[:, :], in_=src_ap, func=AF.Square, bias=0.0, scale=1.0),
          waits=list(waits) + [(sV, sV.n)], inc=(sS, 1))
        ssd = sS.n
        E("v", lambda v: v.tensor_reduce(
            out=stat[:, dst_idx:dst_idx + 1], in_=sq_scr[:, :], axis=AX.X,
            op=ALU.add), waits=[(sS, ssd)], inc=(sV, 1))
        E("v", lambda v: v.drain())
        return sV.n

    def rsqrt_nl(src_idx, dst_ap, waits):
        E("s", lambda s: s.activation(
            out=stat[:, 5:6], in_=stat[:, src_idx:src_idx + 1], func=AF.Ln,
            bias=0.0, scale=1.0), waits=waits)
        E("s", lambda s: s.activation(
            out=dst_ap, in_=stat[:, 5:6], func=AF.Exp, bias=0.0, scale=-0.5),
          inc=(sS, 1))
        return sS.n

    def vrecip(out_ap, in_ap):
        E("v", lambda v: v.drain())
        E("v", lambda v: v.reciprocal(out=out_ap, in_=in_ap))
        E("v", lambda v: v.drain())

    def q_build(gq):
        ig = gq % 2
        for tt in range(2):
            tile = gq * 2 + tt
            fence = ([(sV, p0_v), (sS, p0_s), (sT, p0_t)] if gq == 0 else [])
            E("y", lambda y, tile=tile, ig=ig, tt=tt: y.dma_start(
                out=xg[ig][:, tt * D:(tt + 1) * D], in_=dxin[tile * P:(tile + 1) * P, :]),
              waits=([(sV, st["xg_free"][ig])] if tt == 0 else []) + fence, inc16=sLD)
        ld = sLD.n
        for tt in range(2):
            tile = gq * 2 + tt
            xt_ap = xg[ig][:, tt * D:(tt + 1) * D]
            srs = ln_stats(xt_ap, m_all[:, tile:tile + 1], rstd_all[:, tile:tile + 1],
                           1e-5, [(sLD, ld)])
            E("v", lambda v, xt_ap=xt_ap, tile=tile: v.tensor_scalar(
                out=xnt[:, :], in0=xt_ap, scalar1=m_all[:, tile:tile + 1],
                scalar2=rstd_all[:, tile:tile + 1], op0=ALU.subtract, op1=ALU.mult),
              waits=[(sS, srs)], inc=(sV, 1))
            vxn = sV.n
            E("v", lambda v: v.tensor_reduce(
                out=stat[:, 0:1], in_=xnt[:, :], axis=AX.X, op=ALU.add))
            E("v", lambda v: v.drain())
            sq_accum(xnt[:, :], 3, [(sV, vxn)])
            E("v", lambda v: v.tensor_scalar(
                out=stat[:, 6:7], in0=stat[:, 0:1], scalar1=INV_D, scalar2=0.0,
                op0=ALU.mult, op1=ALU.bypass))
            E("v", lambda v: v.drain())
            E("v", lambda v: v.tensor_scalar(
                out=stat[:, 3:4], in0=stat[:, 3:4], scalar1=INV_D, scalar2=0.0,
                op0=ALU.mult, op1=ALU.bypass))
            E("v", lambda v: v.drain())
            E("v", lambda v: v.scalar_tensor_tensor(
                out=stat[:, 4:5], in0=stat[:, 6:7], scalar=stat[:, 6:7],
                in1=stat[:, 3:4], op0=ALU.mult, op1=ALU.subtract))
            E("v", lambda v: v.drain())
            E("v", lambda v: v.tensor_scalar(
                out=stat[:, 4:5], in0=stat[:, 4:5], scalar1=-1.0, scalar2=1e-12,
                op0=ALU.mult, op1=ALU.add), inc=(sV, 1))
            sr2 = rsqrt_nl(4, stat[:, 7:8], [(sV, sV.n)])
            E("v", lambda v: v.tensor_scalar(
                out=yt[:, :], in0=xnt[:, :], scalar1=stat[:, 6:7],
                scalar2=stat[:, 7:8], op0=ALU.subtract, op1=ALU.mult),
              waits=[(sS, sr2)])
            E("v", lambda v: v.tensor_tensor(
                out=yt[:, :], in0=yt[:, :], in1=lnw_b[:, :], op=ALU.mult))
            E("v", lambda v: v.tensor_tensor(
                out=yt[:, :], in0=yt[:, :], in1=lnb_b[:, :], op=ALU.add),
              inc=(sV, 1))
            vy = sV.n
            sq_accum(yt[:, :], 3, [(sV, vy)])
            E("s", lambda s: s.activation(
                out=stat[:, 5:6], in_=stat[:, 3:4], func=AF.Ln, bias=0.0, scale=1.0),
              waits=[(sV, sV.n)])
            E("s", lambda s: s.activation(
                out=stat[:, 4:5], in_=stat[:, 5:6], func=AF.Exp, bias=0.0, scale=0.5),
              inc=(sS, 1))
            E("v", lambda v: v.tensor_scalar(
                out=stat[:, 4:5], in0=stat[:, 4:5], scalar1=1e-12, scalar2=0.0,
                op0=ALU.max, op1=ALU.bypass), waits=[(sS, sS.n)])
            E("v", lambda v: v.drain())
            E("v", lambda v: v.reciprocal(out=stat[:, 5:6], in_=stat[:, 4:5]))
            E("v", lambda v: v.drain())
            E("v", lambda v: v.tensor_scalar(
                out=qtile[:, :], in0=yt[:, :], scalar1=stat[:, 5:6], scalar2=0.0,
                op0=ALU.mult, op1=ALU.bypass), inc=(sV, 1))
            E("v", lambda v: v.drain())
            vq = sV.n
            for c in range(8):
                E("t", lambda t, c=c: t.transpose(
                    out=ps_qt[:, c * P:(c + 1) * P], in_=qtile[:, c * P:(c + 1) * P],
                    identity=ident_b[:, :]),
                  waits=[(sV, vq), (sV, st["qt_evac"])] if c == 0 else [],
                  inc=(sT, 1) if c == 7 else None)
            tq = sT.n
            E("v", lambda v, ig=ig, tt=tt: v.tensor_copy(
                out=QT[ig][:, :].rearrange("p (h q) -> p h q", h=8)[:, :, tt * P:(tt + 1) * P],
                in_=ps_qt[:, :].rearrange("p (h q) -> p h q", h=8)),
              waits=[(sT, tq), (sT, st["qt_free"][ig]) if tt == 0 else (sT, 0)],
              inc=(sV, 1))
            st["qt_evac"] = sV.n
        E("v", lambda v: v.drain(), inc=(sV, 1))
        st["xg_free"][ig] = sV.n
        st["qt_built"][ig] = sV.n

    def mm1_half(gq, half):
        ig = gq % 2
        for kk in range(HB):
            k = half * HB + kk
            i2 = kk % 3
            for hc in range(8):
                E("t", lambda t, k=k, hc=hc, i2=i2, ig=ig: t.matmul(
                    out=ps_sT[i2][:, :],
                    lhsT=KT[:, hc * I0 + k * P: hc * I0 + (k + 1) * P],
                    rhs=QT[ig][:, hc * GRP:(hc + 1) * GRP],
                    start=(hc == 0), stop=(hc == 7)),
                  waits=[(sV, st["sT_free"][i2]), (sV, st["qt_built"][ig])]
                        if hc == 0 else [],
                  inc=(sT, 1) if hc == 7 else None)
            t1 = sT.n
            E("v", lambda v, kk=kk, i2=i2: v.tensor_scalar(
                out=sstage[:, kk * GRP:(kk + 1) * GRP], in0=ps_sT[i2][:, :],
                scalar1=CLIP_LO, scalar2=CLIP_HI, op0=ALU.max, op1=ALU.min),
              waits=[(sT, t1)], inc=(sV, 1))
            st["sT_free"][i2] = sV.n

    def bchain_half(half):
        ih = half
        E("v", lambda v: v.drain(), inc=(sV, 1))
        vclip = sV.n
        E("g", lambda g: g.tensor_tensor(
            out=aux[:, :], in0=sstage[:, :], in1=sstage[:, :], op=ALU.mult),
          waits=[(sV, vclip), (sS, st["aux_free"])], inc=(sG, 1))
        gp = sG.n
        E("s", lambda s: s.activation(
            out=aux[:, :], in_=aux[:, :], func=AF.Ln, bias=1.0, scale=-1.0),
          waits=[(sG, gp)])
        E("s", lambda s: s.activation(
            out=aux[:, :], in_=aux[:, :], func=AF.Exp, bias=0.0, scale=-0.5),
          inc=(sS, 1))
        se = sS.n
        E("v", lambda v: v.tensor_tensor(
            out=aux[:, :], in0=sstage[:, :], in1=aux[:, :], op=ALU.mult),
          waits=[(sS, se)], inc=(sV, 1))
        E("v", lambda v: v.drain())
        va = sV.n
        E("s", lambda s: s.activation(
            out=aux[:, :], in_=aux[:, :], func=AF.Arctan, bias=0.0, scale=1.0),
          waits=[(sV, va)])
        E("s", lambda s: s.activation(
            out=aux[:, :], in_=aux[:, :], func=AF.Ln, bias=kc[:, 2:3], scale=INV_PI))
        E("s", lambda s, ih=ih: s.activation(
            out=Bst[ih][:, :], in_=aux[:, :], func=AF.Exp, bias=0.0, scale=9.0),
          waits=[(sT, max(st["bst_free"]))], inc=(sS, 1))
        st["aux_free"] = sS.n
        return sS.n

    def mm2_half(gq, half, sb_done):
        ih = half
        for kk in range(HB):
            k = half * HB + kk
            i2 = kk % 2
            E("y", lambda y, k=k, i2=i2: y.dma_start(
                out=qwk[i2][:, :], in_=dqwbf[k * P:(k + 1) * P, :]),
              waits=[(sT, st["qwk_free"][i2]), (sQW, qw_cached if gq == 0 else 0)],
              inc16=sLD)
            qld = sLD.n
            for tt in range(2):
                for hf in range(2):
                    first = (tt == 0 and hf == 0)
                    E("t", lambda t, kk=kk, tt=tt, hf=hf, i2=i2, ih=ih, k=k: t.matmul(
                        out=ps_X[tt * 2 + hf][:, :],
                        lhsT=Bst[ih][:, kk * GRP + tt * P: kk * GRP + (tt + 1) * P],
                        rhs=qwk[i2][:, hf * 512:(hf + 1) * 512],
                        start=(k == 0), stop=(k == NC0 - 1)),
                      waits=([(sS, sb_done), (sLD, qld)] +
                             ([(sV, st["xfin_v"])] if k == 0 else []))
                            if first else [],
                      inc=(sT, 1) if (tt == 1 and hf == 1) else None)
            st["qwk_free"][i2] = sT.n
        st["bst_free"][ih] = sT.n

    def x_finalize(gq):
        tdone = sT.n
        for tt in range(2):
            tile = gq * 2 + tt
            pa, pb = ps_X[tt * 2], ps_X[tt * 2 + 1]
            E("s", lambda s, pa=pa: s.activation(
                out=sq_scr[:, 0:512], in_=pa[:, :], func=AF.Square,
                bias=0.0, scale=1.0),
              waits=[(sT, tdone), (sV, sV.n)])
            E("s", lambda s, pb=pb: s.activation(
                out=sq_scr[:, 512:1024], in_=pb[:, :], func=AF.Square,
                bias=0.0, scale=1.0), inc=(sS, 1))
            sx = sS.n
            E("v", lambda v: v.tensor_reduce(
                out=stat[:, 3:4], in_=sq_scr[:, :], axis=AX.X, op=ALU.add),
              waits=[(sS, sx)], inc=(sV, 1))
            E("s", lambda s: s.activation(
                out=stat[:, 5:6], in_=stat[:, 3:4], func=AF.Ln, bias=0.0, scale=1.0),
              waits=[(sV, sV.n)])
            E("s", lambda s: s.activation(
                out=stat[:, 4:5], in_=stat[:, 5:6], func=AF.Exp, bias=0.0, scale=0.5),
              inc=(sS, 1))
            E("v", lambda v: v.tensor_scalar(
                out=stat[:, 4:5], in0=stat[:, 4:5], scalar1=1e-12, scalar2=0.0,
                op0=ALU.max, op1=ALU.bypass), waits=[(sS, sS.n)])
            E("v", lambda v: v.drain())
            E("v", lambda v: v.reciprocal(out=stat[:, 5:6], in_=stat[:, 4:5]))
            E("v", lambda v: v.drain())
            E("v", lambda v, pa=pa, tt=tt: v.scalar_tensor_tensor(
                out=Xout[tt][:, 0:512], in0=pa[:, :], scalar=stat[:, 5:6],
                in1=bias1_b[:, 0:512], op0=ALU.mult, op1=ALU.add),
              waits=[(sA0, max(st["xout_free"]))])
            E("v", lambda v, pb=pb, tt=tt: v.scalar_tensor_tensor(
                out=Xout[tt][:, 512:1024], in0=pb[:, :], scalar=stat[:, 5:6],
                in1=bias1_b[:, 512:1024], op0=ALU.mult, op1=ALU.add),
              inc=(sV, 1))
            vf = sV.n
            E("s", lambda s, tile=tile, tt=tt: s.dma_start(
                out=dattn0[tile * P:(tile + 1) * P, :], in_=Xout[tt][:, :]),
              waits=[(sV, vf)], inc16=sA0)
            st["xout_free"][tt] = sA0.n
            st["attn0_sp"][tile] = sA0.n
        st["xfin_v"] = sV.n

    q_build(0)
    for gq in range(NG):
        mm1_half(gq, 0)
        sb0 = bchain_half(0)
        mm2_half(gq, 0, sb0)
        if gq + 1 < NG:
            q_build(gq + 1)
        mm1_half(gq, 1)
        sb1 = bchain_half(1)
        mm2_half(gq, 1, sb1)
        st["qt_free"][gq % 2] = sT.n
        x_finalize(gq)

    if upto == "dbgp1":
        E("v", lambda v: v.tensor_copy(out=aux[:, :], in_=sstage[:, :]), inc=(sV, 1))
        fw = [(sV, sV.n), (sS, sS.n), (sT, sT.n)]
        def dump(row0, src_ap, eng="y"):
            E(eng, lambda e, r=row0, a=src_ap: e.dma_start(
                out=dout[r * P:(r + 1) * P, 0:a.shape[1]], in_=a), waits=fw, inc16=sOUT)
        dump(0, KT[:, 3072:4096], "g")
        dump(1, QT[1][:, 0:1024], "g")
        dump(2, QT[1][:, 1024:2048], "g")
        E("y", lambda y: y.dma_start(out=dout[3 * P:4 * P, 0:NT], in_=m_all[:, :]),
          waits=fw, inc16=sOUT)
        E("y", lambda y: y.dma_start(out=dout[3 * P:4 * P, NT:2 * NT], in_=rstd_all[:, :]),
          waits=fw, inc16=sOUT)
        dump(4, qtile[:, :], "g")
        dump(5, yt[:, :])
        for i in range(4):
            dump(6 + i, sstage[:, i * 1024:(i + 1) * 1024])
        for i in range(4):
            dump(10 + i, Bst[1][:, i * 1024:(i + 1) * 1024], "g")
        for i in (2, 3):
            dump(12 + i, aux[:, i * 1024:(i + 1) * 1024])
        E("g", lambda g: g.nop(), waits=[(sOUT, sOUT.n)])
        with nc.Block() as block:
            @block.sync
            def _(y):
                b.sched.replay("y", y)

            @block.vector
            def _(v):
                b.sched.replay("v", v)

            @block.scalar
            def _(s):
                b.sched.replay("s", s)

            @block.tensor
            def _(t):
                b.sched.replay("t", t)

            @block.gpsimd
            def _(g):
                b.sched.replay("g", g)
        return nc

    b.free("xg0", "xg1", "QT0", "QT1", "qwk0", "qwk1", "sstage", "aux",
           "Bst0", "xnt", "yt", "qtile", "Xout0",
           "ps_sT0", "ps_sT1", "ps_sT2", "ps_X0", "ps_X1", "ps_X2", "ps_X3", "ps_qt",
           "KT")

    WrB = b.sb("WrB", [P, 8 * D], BF16)
    WkB = b.sb("WkB", [P, 8 * D], BF16)
    WvB = b.sb("WvB", [P, 8 * D], BF16)
    x1t = [b.sb(f"x1t{i}", [P, D], F32) for i in range(2)]
    xstr = [b.sb(f"xstr{i}", [P, D], F32) for i in range(2)]
    a0str = [b.sb(f"a0str{i}", [P, D], F32) for i in range(2)]
    xn_p2 = b.sb("xn_p2", [P, D], F32)
    xnTP = [b.sb(f"xnTP{i}", [P, 8 * 129], F32) for i in range(2)]
    dTt = b.sb("dTt", [P, D], F32)
    mixtmp = b.sb("mixtmp", [P, D], F32)
    xrT = b.sb("xrT", [P, D], BF16)
    xkT = b.sb("xkT", [P, D], BF16)
    r_tile = b.sb("r_tile", [P, D], F32)
    kact_f = b.sb("kact_f", [P, D], F32)
    kact_b = b.sb("kact_b", [P, D], BF16)
    kactT = b.sb("kactT", [P, D], BF16)
    rv = b.sb("rv", [P, D], F32)
    xn2f = [b.sb(f"xn2f{i}", [P, D], F32) for i in range(2)]
    ps_xnT = b.ps("ps_xnT", [P, D], F32)
    ps_mm = [b.ps(f"ps_mm{i}", [P, 512], F32) for i in range(2)]
    ps_kT = b.ps("ps_kT", [P, D], BF16)

    p1_v, p1_s, p1_t = sV.n, sS.n, sT.n
    p1_fence = [(sV, p1_v), (sS, p1_s), (sT, p1_t)]
    wB_ld = []
    for src, dst in [(dwrT, WrB), (dwkT, WkB), (dwvT, WvB)]:
        E("g", lambda g, s=src, d=dst: g.dma_start(
            out=d[:, :].rearrange("p (c j) -> p c j", c=8),
            in_=s.rearrange("(c p) j -> p c j", p=P)),
          waits=p1_fence, inc16=sGD)
        wB_ld.append(sGD.n)

    def mm_1024(ps_pair, lhsT_buf, W_buf, wait_first, done_inc=True):
        for sl in range(2):
            for cc in range(8):
                E("t", lambda t, sl=sl, cc=cc, lb=lhsT_buf, wb=W_buf: t.matmul(
                    out=ps_pair[sl][:, :], lhsT=lb[:, cc * P:(cc + 1) * P],
                    rhs=wb[:, cc * D + sl * 512: cc * D + (sl + 1) * 512],
                    start=(cc == 0), stop=(cc == 7)),
                  waits=wait_first if (sl == 0 and cc == 0) else [],
                  inc=(sT, 1) if (cc == 7) else None)

    p2 = {"x_free": [0, 0], "a0_free": [0, 0], "x1_sp": [0] * NT,
          "xn2_sp": [0] * NT, "xn2T_evac": 0,
          "ssq_done": [0] * NT, "gred_done": [0] * NT}
    prevten = [0]

    for tile in range(NT):
        i2 = tile % 2
        cur, prv = xnTP[tile % 2], xnTP[1 - tile % 2]
        E("y", lambda y, tile=tile, i2=i2: y.dma_start(
            out=xstr[i2][:, :], in_=dxin[tile * P:(tile + 1) * P, :]),
          waits=[(sV, p2["x_free"][i2])] + (p1_fence if tile < 2 else []), inc16=sLD)
        xld = sLD.n
        E("y", lambda y, tile=tile, i2=i2: y.dma_start(
            out=a0str[i2][:, :], in_=dattn0[tile * P:(tile + 1) * P, :]),
          waits=[(sA0, sA0.n), (sV, p2["a0_free"][i2])], inc16=sLD)
        ald = sLD.n
        E("v", lambda v, tile=tile, i2=i2: v.tensor_scalar(
            out=xn_p2[:, :], in0=xstr[i2][:, :], scalar1=m_all[:, tile:tile + 1],
            scalar2=rstd_all[:, tile:tile + 1], op0=ALU.subtract, op1=ALU.mult),
          waits=[(sLD, xld), (sV, sV.n), (sT, prevten[0])], inc=(sV, 1))
        E("v", lambda v: v.drain())
        vxn = sV.n
        for c in range(8):
            E("t", lambda t, c=c: t.transpose(
                out=ps_xnT[:, c * P:(c + 1) * P], in_=xn_p2[:, c * P:(c + 1) * P],
                identity=ident_f[:, :]),
              waits=[(sV, vxn)] if c == 0 else [],
              inc=(sT, 1) if c == 7 else None)
        txn = sT.n
        E("v", lambda v, cur=cur: v.tensor_copy(
            out=cur[:, :].rearrange("p (h c) -> p h c", h=8)[:, :, 1:129],
            in_=ps_xnT[:, :].rearrange("p (h c) -> p h c", h=8)),
          waits=[(sT, txn)], inc=(sV, 1))
        if tile == 0:
            E("v", lambda v, cur=cur: v.memset(
                cur[:, :].rearrange("p (h c) -> p h c", h=8)[:, :, 0:1], 0.0))
        else:
            E("v", lambda v, cur=cur, prv=prv: v.tensor_copy(
                out=cur[:, :].rearrange("p (h c) -> p h c", h=8)[:, :, 0:1],
                in_=prv[:, :].rearrange("p (h c) -> p h c", h=8)[:, :, 128:129]))
        cur3 = cur[:, :].rearrange("p (h c) -> p h c", h=8)
        cx = cur3[:, :, 1:129]
        cp = cur3[:, :, 0:128]
        d3 = dTt[:, :].rearrange("p (h c) -> p h c", h=8)
        m3 = mixtmp[:, :].rearrange("p (h c) -> p h c", h=8)
        E("g", lambda g, cx=cx, cp=cp, d3=d3: g.tensor_tensor(
            out=d3, in0=cx, in1=cp, op=ALU.subtract),
          waits=[(sV, sV.n), (sGD, wB_ld[2])], inc=(sG, 1))
        gd = sG.n
        E("v", lambda v, d3=d3, m3=m3: v.tensor_tensor(
            out=m3, in0=d3, in1=mur_sb[:, :].unsqueeze(2).to_broadcast((P, 8, P)),
            op=ALU.mult), waits=[(sG, gd), (sLD, mu_ld), (sT, prevten[0])])
        E("v", lambda v, m3=m3, cp=cp: v.tensor_tensor(
            out=xrT[:, :].rearrange("p (h c) -> p h c", h=8), in0=m3, in1=cp,
            op=ALU.add), inc=(sV, 1))
        E("v", lambda v: v.drain())
        vxr = sV.n
        E("v", lambda v, d3=d3, m3=m3: v.tensor_tensor(
            out=m3, in0=d3, in1=muk_sb[:, :].unsqueeze(2).to_broadcast((P, 8, P)),
            op=ALU.mult))
        E("v", lambda v, m3=m3, cp=cp: v.tensor_tensor(
            out=xkT[:, :].rearrange("p (h c) -> p h c", h=8), in0=m3, in1=cp,
            op=ALU.add), inc=(sV, 1))
        E("v", lambda v: v.drain())
        vxk = sV.n
        mm_1024(ps_mm, xrT, WrB, [(sV, vxr), (sS, sS.n), (sV, sV.n)])
        tr = sT.n
        E("s", lambda s: s.activation(
            out=r_tile[:, 0:512], in_=ps_mm[0][:, :], func=AF.Sigmoid,
            bias=0.0, scale=1.0), waits=[(sT, tr - 1)])
        E("s", lambda s: s.activation(
            out=r_tile[:, 512:1024], in_=ps_mm[1][:, :], func=AF.Sigmoid,
            bias=0.0, scale=1.0), waits=[(sT, tr)], inc=(sS, 1))
        sr = sS.n
        mm_1024(ps_mm, xkT, WkB, [(sV, vxk), (sS, sr)])
        tk = sT.n
        E("v", lambda v: v.tensor_scalar(
            out=kact_f[:, 0:512], in0=ps_mm[0][:, :], scalar1=0.0, scalar2=0.0,
            op0=ALU.max, op1=ALU.bypass), waits=[(sT, tk - 1)])
        E("v", lambda v: v.tensor_scalar(
            out=kact_f[:, 512:1024], in0=ps_mm[1][:, :], scalar1=0.0, scalar2=0.0,
            op0=ALU.max, op1=ALU.bypass), waits=[(sT, tk)])
        E("v", lambda v: v.tensor_tensor(
            out=kact_b[:, :], in0=kact_f[:, :], in1=kact_f[:, :], op=ALU.mult),
          inc=(sV, 1))
        E("v", lambda v: v.drain())
        vka = sV.n
        for c in range(8):
            E("t", lambda t, c=c: t.transpose(
                out=ps_kT[:, c * P:(c + 1) * P], in_=kact_b[:, c * P:(c + 1) * P],
                identity=ident_b[:, :]),
              waits=[(sV, vka)] if c == 0 else [],
              inc=(sT, 1) if c == 7 else None)
        tkt = sT.n
        E("v", lambda v: v.tensor_copy(out=kactT[:, :], in_=ps_kT[:, :]),
          waits=[(sT, tkt)], inc=(sV, 1))
        E("v", lambda v: v.drain())
        vkt = sV.n
        mm_1024(ps_mm, kactT, WvB, [(sV, vkt)])
        tv = sT.n
        E("v", lambda v: v.scalar_tensor_tensor(
            out=rv[:, 0:512], in0=ps_mm[0][:, :], scalar=1.0,
            in1=r_tile[:, 0:512], op0=ALU.mult, op1=ALU.mult),
          waits=[(sT, tv - 1)])
        E("v", lambda v: v.scalar_tensor_tensor(
            out=rv[:, 512:1024], in0=ps_mm[1][:, :], scalar=1.0,
            in1=r_tile[:, 512:1024], op0=ALU.mult, op1=ALU.mult),
          waits=[(sT, tv)], inc=(sV, 1))
        prevten[0] = tv
        E("g", lambda g, i2=i2: g.tensor_tensor(
            out=mixtmp[:, :], in0=xstr[i2][:, :], in1=a0str[i2][:, :], op=ALU.add),
          waits=[(sLD, ald), (sV, sV.n)], inc=(sG, 1))
        ga = sG.n
        E("v", lambda v, tile=tile, i2=i2: v.tensor_tensor(
            out=x1t[i2][:, :], in0=mixtmp[:, :], in1=rv[:, :],
            op=ALU.add),
          waits=[(sG, ga), (sACT, p2["x1_sp"][tile - 2] if tile >= 2 else 0),
                 (sS, p2["ssq_done"][tile - 2] if tile >= 2 else 0)],
          inc=(sV, 1))
        vx1 = sV.n
        p2["x_free"][i2] = vx1
        p2["a0_free"][i2] = vx1
        E("s", lambda s, tile=tile, i2=i2: s.dma_start(
            out=dx1[tile * P:(tile + 1) * P, :], in_=x1t[i2][:, :]),
          waits=[(sV, vx1)], inc16=sACT)
        p2["x1_sp"][tile] = sACT.n
        E("v", lambda v, tile=tile, i2=i2: v.tensor_reduce(
            out=m2_all[:, tile:tile + 1], in_=x1t[i2][:, :], axis=AX.X,
            op=ALU.add), waits=[(sV, vx1)], inc=(sV, 1))
        E("v", lambda v: v.drain())
        E("s", lambda s, i2=i2: s.activation(
            out=sq_scr[:, :], in_=x1t[i2][:, :], func=AF.Square, bias=0.0,
            scale=1.0), waits=[(sV, sV.n)], inc=(sS, 1))
        ssq = sS.n
        E("v", lambda v, tile=tile: v.tensor_reduce(
            out=ss2_all[:, tile:tile + 1], in_=sq_scr[:, :], axis=AX.X,
            op=ALU.add), waits=[(sS, ssq)], inc=(sV, 1))
        E("v", lambda v: v.drain())
        p2["ssq_done"][tile] = ssq
        p2["gred_done"][tile] = 0

    E("v", lambda v: v.drain(), waits=[(sG, sG.n), (sV, sV.n)])
    E("v", lambda v: v.tensor_scalar(
        out=m2_all[:, :], in0=m2_all[:, :], scalar1=INV_D, scalar2=0.0,
        op0=ALU.mult, op1=ALU.bypass))
    E("v", lambda v: v.tensor_scalar(
        out=ss2_all[:, :], in0=ss2_all[:, :], scalar1=INV_D, scalar2=0.0,
        op0=ALU.mult, op1=ALU.bypass))
    E("v", lambda v: v.drain())
    E("v", lambda v: v.tensor_tensor(
        out=rstd2_all[:, :], in0=m2_all[:, :], in1=m2_all[:, :], op=ALU.mult))
    E("v", lambda v: v.drain())
    E("v", lambda v: v.tensor_tensor(
        out=rstd2_all[:, :], in0=ss2_all[:, :], in1=rstd2_all[:, :],
        op=ALU.subtract), inc=(sV, 1))
    E("s", lambda s: s.activation(
        out=rstd2_all[:, :], in_=rstd2_all[:, :], func=AF.Ln, bias=kc[:, 0:1], scale=1.0),
      waits=[(sV, sV.n)])
    E("s", lambda s: s.activation(
        out=rstd2_all[:, :], in_=rstd2_all[:, :], func=AF.Exp, bias=0.0, scale=-0.5),
      inc=(sS, 1))
    ln2fin = sS.n

    for tile in range(NT):
        i2 = tile % 2
        E("y", lambda y, tile=tile, i2=i2: y.dma_start(
            out=xstr[i2][:, :], in_=dx1[tile * P:(tile + 1) * P, :]),
          waits=[(sACT, sACT.n), (sV, p2["x_free"][i2])], inc16=sLD)
        x1ld = sLD.n
        E("v", lambda v, tile=tile, i2=i2: v.tensor_scalar(
            out=xn2f[i2][:, :], in0=xstr[i2][:, :],
            scalar1=m2_all[:, tile:tile + 1], scalar2=rstd2_all[:, tile:tile + 1],
            op0=ALU.subtract, op1=ALU.mult),
          waits=[(sS, ln2fin), (sLD, x1ld),
                 (sACT, p2["xn2_sp"][tile - 2] if tile >= 2 else 0)],
          inc=(sV, 1))
        E("v", lambda v: v.drain())
        p2["x_free"][i2] = sV.n
        vx = sV.n
        E("s", lambda s, tile=tile, i2=i2: s.dma_start(
            out=dxn2[tile * P:(tile + 1) * P, :], in_=xn2f[i2][:, :]),
          waits=[(sV, vx)], inc16=sACT)
        p2["xn2_sp"][tile] = sACT.n
        E("v", lambda v, i2=i2: v.tensor_reduce(
            out=stat[:, 6:7], in_=xn2f[i2][:, :], axis=AX.X, op=ALU.max,
            apply_absolute_value=True))
        E("v", lambda v: v.drain())
        E("v", lambda v: v.tensor_tensor(
            out=runmax[:, 0:1], in0=runmax[:, 0:1], in1=stat[:, 6:7], op=ALU.max),
          inc=(sV, 1))
        E("v", lambda v: v.drain())
        for c in range(8):
            E("t", lambda t, c=c, i2=i2: t.transpose(
                out=ps_xnT[:, c * P:(c + 1) * P], in_=xn2f[i2][:, c * P:(c + 1) * P],
                identity=ident_f[:, :]),
              waits=[(sV, vx), (sV, p2["xn2T_evac"])] if c == 0 else [],
              inc=(sT, 1) if c == 7 else None)
        tt2 = sT.n
        E("v", lambda v, tile=tile: v.tensor_copy(
            out=xn2T[:, :].rearrange("p (h c) -> p h c", h=8)[:, :, tile * P:(tile + 1) * P],
            in_=ps_xnT[:, :].rearrange("p (h c) -> p h c", h=8)),
          waits=[(sT, tt2)], inc=(sV, 1))
        p2["xn2T_evac"] = sV.n
    E("v", lambda v: v.drain(), inc=(sV, 1))
    xn2T_done = sV.n

    E("g", lambda g: g.dma_start(out=cc1i[:, :], in_=runmax[:, 0:1]),
      waits=[(sV, sV.n)], inc16=sGD)
    E("g", lambda g: g.collective_compute(
        "AllReduce", ALU.max, replica_groups=[list(range(8))],
        ins=[cc1i.ap().opt()], outs=[cc1o.ap().opt()]),
      waits=[(sGD, sGD.n)], inc=(sCC, 1))
    ar1 = sCC.n

    b.free("WrB", "WkB", "WvB", "xstr0", "xstr1", "a0str0", "a0str1",
           "xn_p2", "xnTP0", "xnTP1", "dTt", "mixtmp", "xrT", "xkT",
           "r_tile", "kact_f", "kact_b", "kactT", "rv", "xn2f0", "xn2f1",
           "ps_xnT", "ps_mm0", "ps_mm1", "ps_kT", "x1t0", "x1t1")

    if upto == "p1":
        for tile in range(NT):
            E("y", lambda y, tile=tile: y.dma_start(
                out=dout[tile * P:(tile + 1) * P, :],
                in_=dattn0[tile * P:(tile + 1) * P, :]),
              waits=[(sA0, st["attn0_sp"][tile])], inc16=sOUT)
        E("y", lambda y: y.nop(), waits=[(sOUT, sOUT.n)])
        with nc.Block() as block:
            @block.sync
            def _(y):
                b.sched.replay("y", y)

            @block.vector
            def _(v):
                b.sched.replay("v", v)

            @block.scalar
            def _(s):
                b.sched.replay("s", s)

            @block.tensor
            def _(t):
                b.sched.replay("t", t)

            @block.gpsimd
            def _(g):
                b.sched.replay("g", g)
        return nc

    if upto == "l0":
        for tile in range(NT):
            E("y", lambda y, tile=tile: y.dma_start(
                out=dout[tile * P:(tile + 1) * P, :],
                in_=dx1[tile * P:(tile + 1) * P, :]),
              waits=[(sACT, p2["x1_sp"][tile]), (sV, xn2T_done), (sCC, ar1)],
              inc16=sOUT)
        E("y", lambda y: y.nop(), waits=[(sOUT, sOUT.n)])
        with nc.Block() as block:
            @block.sync
            def _(y):
                b.sched.replay("y", y)

            @block.vector
            def _(v):
                b.sched.replay("v", v)

            @block.scalar
            def _(s):
                b.sched.replay("s", s)

            @block.tensor
            def _(t):
                b.sched.replay("t", t)

            @block.gpsimd
            def _(g):
                b.sched.replay("g", g)
        return nc


    p2e = [(sV, sV.n), (sS, sS.n), (sT, sT.n)]
    b.free("sq_scr", "vrow")
    qT = b.sb("qT", [P, 8 * T], BF16)
    kTt = b.sb("kTt", [P, 8 * T], BF16)
    vst = b.sb("vst", [P, NT * 1040], BF16)
    Wb = [b.sb(f"Wb{i}", [P, 8 * D], BF16) for i in range(2)]
    PT = [b.sb(f"PT{i}", [P, 512], BF16) for i in range(2)]
    accsb = b.sb("accsb", [P, 512], F32)
    rrow = b.sb("rrow", [P, 512], F32)
    aoT = [b.sb(f"aoT{i}", [P, 8 * 512], BF16) for i in range(2)]
    opst = [b.sb("opst0", [P, D], F32)] * 2
    ps_s4 = [b.ps(f"ps_s4{i}", [P, 512], F32) for i in range(2)]
    ps_acc = [b.ps(f"ps_acc{i}", [P, 512], F32) for i in range(2)]
    ps_rb = b.ps("ps_rb", [P, 512], F32)
    ps_op = [b.ps(f"ps_op{i}", [P, 512], F32) for i in range(2)]

    def wload(dst, srcap, waits):
        E("g", lambda g, s=srcap, d=dst: g.dma_start(
            out=d[:, :].rearrange("p (c j) -> p c j", c=8),
            in_=s.rearrange("(c p) j -> p c j", p=P)),
          waits=waits, inc16=sGD)
        return sGD.n

    wq_ld = wload(Wb[0], dl1q, p2e)
    wk_ld = wload(Wb[1], dl1k, p2e)
    E("g", lambda g: g.memset(
        vst[:, :].rearrange("p (a c) -> p a c", c=65)[:, :, 64:65], 1.0),
      waits=p2e, inc=(sG, 1))
    vones = sG.n

    mmb_free = [0, 0]
    def proj_T(dstT, wld, wbuf):
        for dqc in range(8):
            for tg in range(4):
                bank = (dqc * 4 + tg) % 2
                for ec in range(8):
                    E("t", lambda t, dqc=dqc, tg=tg, ec=ec, bank=bank, wld=wld, wbuf=wbuf: t.matmul(
                        out=ps_s4[bank][:, :],
                        lhsT=wbuf[:, ec * D + dqc * P: ec * D + (dqc + 1) * P],
                        rhs=xn2T[:, ec * T + tg * 512:(ec * T + (tg + 1) * 512)],
                        start=(ec == 0), stop=(ec == 7)),
                      waits=[(sGD, wld), (sV, xn2T_done), (sS, mmb_free[bank])]
                            if ec == 0 else [],
                      inc=(sT, 1) if ec == 7 else None)
                tq = sT.n
                E("s", lambda s, dqc=dqc, tg=tg, bank=bank, dstT=dstT: s.activation(
                    out=dstT[:, dqc * T + tg * 512: dqc * T + (tg + 1) * 512],
                    in_=ps_s4[bank][:, :], func=AF.Copy, bias=0.0, scale=1.0),
                  waits=[(sT, tq)], inc=(sS, 1))
                mmb_free[bank] = sS.n
    proj_T(qT, wq_ld, Wb[0])
    proj_T(kTt, wk_ld, Wb[1])
    qk_done_t = sT.n
    wv_ld = wload(Wb[0], dl1v, [(sT, qk_done_t)])
    for tt in range(NT):
        for es in range(2):
            bank = (tt * 2 + es) % 2
            for ec in range(8):
                E("t", lambda t, tt=tt, es=es, ec=ec, bank=bank: t.matmul(
                    out=ps_s4[bank][:, :],
                    lhsT=xn2T[:, ec * T + tt * P: ec * T + (tt + 1) * P],
                    rhs=Wb[0][:, ec * D + es * 512: ec * D + (es + 1) * 512],
                    start=(ec == 0), stop=(ec == 7)),
                  waits=[(sGD, wv_ld), (sS, mmb_free[bank])] if ec == 0 else [],
                  inc=(sT, 1) if ec == 7 else None)
            tv = sT.n
            E("s", lambda s, tt=tt, es=es, bank=bank: s.activation(
                out=vst[:, tt * 1040 + es * 520: tt * 1040 + (es + 1) * 520]
                    .rearrange("p (h d) -> p h d", d=65)[:, :, 0:64],
                in_=ps_s4[bank][:, :].rearrange("p (h d) -> p h d", d=64),
                func=AF.Copy, bias=0.0, scale=1.0),
              waits=[(sT, tv), (sG, vones)], inc=(sS, 1))
            mmb_free[bank] = sS.n
    v_done_s = sS.n
    wo_ld = wload(Wb[1], dl1o, [(sT, qk_done_t)])

    att = {"sfree": [mmb_free[0], mmb_free[1]], "ptfree": [0, 0], "accfree": [0, 0],
           "rbfree": 0, "aofree": [0, 0], "opfree": [0, 0],
           "opstfree": [0, 0]}
    attnP_sp = [0] * NT
    for qg in range(4):
        cur = qg % 2
        for h in range(16):
            hc, off = h // 2, 64 * (h % 2)
            ia = h % 2
            pieces = [(4 * qg, 0, 512, True)]
            for kt in range(4 * qg):
                pieces.append((kt, 0, 512, False))
            for jj in range(1, 4):
                pieces.append((4 * qg + jj, 128 * jj, 128, True))
                W = 512 - 128 * (jj + 1)
                if W > 0:
                    pieces.append((4 * qg + jj, 128 * (jj + 1), W, False))
            n = len(pieces)
            meta = []
            def emit_smm(i):
                kt, q0, qw, msk = pieces[i]
                bank = i % 2
                w0 = [(sS, att["sfree"][bank])]
                if i == 0:
                    w0 += [(sS, att["accfree"][ia]), (sS, v_done_s)]
                E("t", lambda t, kt=kt, q0=q0, qw=qw, bank=bank, hc=hc, off=off, qg=qg: t.matmul(
                    out=ps_s4[bank][:, 0:qw],
                    lhsT=kTt[off:off + 64, hc * T + kt * P: hc * T + (kt + 1) * P],
                    rhs=qT[off:off + 64, hc * T + qg * 512 + q0: hc * T + qg * 512 + q0 + qw],
                    start=True, stop=True),
                  waits=w0, inc=(sT, 1))
                tS = sT.n
                E("s", lambda s, qw=qw, bank=bank: s.activation(
                    out=PT[bank][:, 0:qw], in_=ps_s4[bank][:, 0:qw], func=AF.Exp,
                    bias=0.0, scale=0.125),
                  waits=[(sT, tS), (sT, att["ptfree"][bank])], inc=(sS, 1))
                eS = sS.n
                att["sfree"][bank] = eS
                mV = 0
                if msk:
                    E("v", lambda v, bank=bank: v.tensor_tensor(
                        out=PT[bank][:, 0:128], in0=PT[bank][:, 0:128],
                        in1=tri[:, :], op=ALU.mult),
                      waits=[(sS, eS)], inc=(sV, 1))
                    E("v", lambda v: v.drain())
                    mV = sV.n
                meta.append((bank, eS, mV))
            def emit_av(i):
                kt, q0, qw, msk = pieces[i]
                bank, eS, mV = meta[i]
                w0 = [(sS, eS)]
                if mV:
                    w0.append((sV, mV))
                E("t", lambda t, kt=kt, q0=q0, qw=qw, bank=bank, h=h, ia=ia: t.matmul(
                    out=ps_acc[ia][0:65, q0:q0 + qw],
                    lhsT=vst[:, kt * 1040 + h * 65: kt * 1040 + (h + 1) * 65],
                    rhs=PT[bank][:, 0:qw],
                    start=(i == 0), stop=(i == n - 1)),
                  waits=w0, inc=(sT, 1))
                att["ptfree"][bank] = sT.n
            emit_smm(0)
            for i in range(1, n):
                emit_smm(i)
                emit_av(i - 1)
            emit_av(n - 1)
            tlast = sT.n
            E("s", lambda s, ia=ia: s.activation(
                out=rrow[64:65, 0:512], in_=ps_acc[ia][64:65, 0:512],
                func=AF.Ln, bias=0.0, scale=1.0), waits=[(sT, tlast)])
            E("s", lambda s: s.activation(
                out=rrow[64:65, 0:512], in_=rrow[64:65, 0:512],
                func=AF.Exp, bias=0.0, scale=-1.0), inc=(sS, 1))
            rS = sS.n
            E("s", lambda s, ia=ia: s.activation(
                out=accsb[0:64, 0:512], in_=ps_acc[ia][0:64, 0:512],
                func=AF.Copy, bias=0.0, scale=1.0), inc=(sS, 1))
            aS = sS.n
            att["accfree"][ia] = aS
            E("t", lambda t: t.matmul(
                out=ps_rb[0:64, 0:512], lhsT=ones_f[64:65, 0:64],
                rhs=rrow[64:65, 0:512], start=True, stop=True),
              waits=[(sS, rS), (sV, att["rbfree"])], inc=(sT, 1))
            rbT = sT.n
            w0 = [(sS, aS), (sT, rbT)]
            if h < 2:
                w0.append((sT, att["aofree"][cur]))
            E("v", lambda v, h=h, hc=hc, cur=cur: v.tensor_tensor(
                out=aoT[cur][64 * (h % 2):64 * (h % 2) + 64, hc * 512:(hc + 1) * 512],
                in0=accsb[0:64, 0:512], in1=ps_rb[0:64, 0:512], op=ALU.mult),
              waits=w0, inc=(sV, 1))
            att["rbfree"] = sV.n
        E("v", lambda v: v.drain(), inc=(sV, 1))
        ao_built = sV.n
        for qs in range(4):
            io = qs % 2
            tops = []
            for half in range(2):
                for ec in range(8):
                    E("t", lambda t, qs=qs, half=half, ec=ec, cur=cur: t.matmul(
                        out=ps_op[half][:, :],
                        lhsT=aoT[cur][:, ec * 512 + qs * P: ec * 512 + (qs + 1) * P],
                        rhs=Wb[1][:, ec * D + half * 512: ec * D + (half + 1) * 512],
                        start=(ec == 0), stop=(ec == 7)),
                      waits=[(sV, ao_built), (sGD, wo_ld), (sV, att["opfree"][half])]
                            if ec == 0 else [],
                      inc=(sT, 1) if ec == 7 else None)
                tops.append(sT.n)
            E("v", lambda v, io=io: v.tensor_tensor(
                out=opst[io][:, 0:512], in0=ps_op[0][:, :], in1=bo_b[:, 0:512],
                op=ALU.add),
              waits=[(sT, tops[0]), (sACT, max(att["opstfree"]))])
            E("v", lambda v, io=io: v.tensor_tensor(
                out=opst[io][:, 512:1024], in0=ps_op[1][:, :], in1=bo_b[:, 512:1024],
                op=ALU.add), waits=[(sT, tops[1])], inc=(sV, 1))
            vOP = sV.n
            att["opfree"][0] = vOP
            att["opfree"][1] = vOP
            tile = qg * 4 + qs
            E("s", lambda s, tile=tile, io=io: s.dma_start(
                out=dattnP[tile * P:(tile + 1) * P, :], in_=opst[io][:, :]),
              waits=[(sV, vOP)], inc16=sACT)
            attnP_sp[tile] = sACT.n
            att["opstfree"][io] = sACT.n
        att["aofree"][cur] = sT.n

    b.free("opst0", "aoT0", "aoT1", "rrow", "accsb", "PT0", "PT1",
           "Wb0", "Wb1", "vst", "kTt", "qT",
           "ps_op0", "ps_op1", "ps_rb", "ps_acc0", "ps_acc1", "ps_s40", "ps_s41")
    b.free("xn2T")

    if upto == "p4":
        for tile in range(NT):
            E("y", lambda y, tile=tile: y.dma_start(
                out=dout[tile * P:(tile + 1) * P, :],
                in_=dattnP[tile * P:(tile + 1) * P, :]),
              waits=[(sACT, attnP_sp[tile])], inc16=sOUT)
        E("y", lambda y: y.nop(), waits=[(sOUT, sOUT.n)])
        _finish(nc, b)
        return nc

    p4e = [(sV, sV.n), (sS, sS.n), (sT, sT.n)]
    scl = b.sb("scl", [P, 12], F32)
    sdq = b.sb("sdq", [P, 192], F32)
    xqT = b.sb("xqT", [P, 8 * T], BF16)
    xbuf = [b.sb(f"xbuf{i}", [P, D], F32) for i in range(2)]
    ibuf = b.sb("ibuf", [P, D], I32)
    ubuf = b.sb("ubuf", [P, D], F32)
    wnat = [b.sb(f"wnat{i}", [P, D], F32) for i in range(2)]
    wqb = b.sb("wqb", [P, D], BF16)
    luT = [b.sb(f"luT{i}", [P, D], BF16) for i in range(2)]
    ustage = [b.sb(f"ustage{i}", [P, 512], F32) for i in range(2)]
    ps_xt5 = b.ps("ps_xt5", [P, D], F32)
    ps_m5 = [b.ps(f"ps_m5{i}", [P, 512], F32) for i in range(2)]
    ps_w5 = b.ps("ps_w5", [P, D], BF16)

    def scale_setup(ccout, ar_val, col_s, col_r, fence):
        E("g", lambda g, c=ccout: g.dma_start(out=stat[:, 0:1], in_=c.ap()),
          waits=[(sCC, ar_val), (sV, sV.n), (sS, sS.n)] + fence, inc16=sGD)
        c1 = sGD.n
        E("t", lambda t: t.transpose(
            out=ps_xt5[0:1, 0:128], in_=stat[:, 0:1], identity=ident_f[:, :]),
          waits=[(sGD, c1), (sV, sV.n)], inc=(sT, 1))
        tc = sT.n
        E("v", lambda v: v.tensor_reduce(
            out=stat[0:1, 1:2], in_=ps_xt5[0:1, 0:128], axis=AX.X, op=ALU.max),
          waits=[(sT, tc)], inc=(sV, 1))
        E("v", lambda v: v.drain())
        E("v", lambda v: v.tensor_scalar(
            out=stat[0:1, 2:3], in0=stat[0:1, 1:2], scalar1=1.0 / 32767.0,
            scalar2=1e-12, op0=ALU.mult, op1=ALU.max), inc=(sV, 1))
        E("v", lambda v: v.drain())
        vsc = sV.n
        E("t", lambda t: t.matmul(
            out=ps_m5[0][:, 0:1], lhsT=ones_f[0:1, :], rhs=stat[0:1, 2:3],
            start=True, stop=True), waits=[(sV, vsc)], inc=(sT, 1))
        tb = sT.n
        E("v", lambda v, col_s=col_s: v.tensor_copy(
            out=scl[:, col_s:col_s + 1], in_=ps_m5[0][:, 0:1]),
          waits=[(sT, tb)], inc=(sV, 1))
        E("v", lambda v: v.drain())
        E("v", lambda v, col_s=col_s, col_r=col_r: v.reciprocal(
            out=scl[:, col_r:col_r + 1], in_=scl[:, col_s:col_s + 1]))
        E("v", lambda v: v.drain(), inc=(sV, 1))
        return sV.n

    sx_rdy = scale_setup(cc1o, ar1, 0, 1, p4e)

    xq5 = {"xfree": [0, 0], "evac": 0}
    for tile in range(NT):
        i2 = tile % 2
        E("y", lambda y, tile=tile, i2=i2: y.dma_start(
            out=xbuf[i2][:, :], in_=dxn2[tile * P:(tile + 1) * P, :]),
          waits=[(sACT, sACT.n), (sV, xq5["xfree"][i2])] +
                (p4e if tile < 2 else []),
          inc16=sLD)
        ld = sLD.n
        E("v", lambda v, i2=i2: v.tensor_scalar(
            out=ubuf[:, :], in0=xbuf[i2][:, :], scalar1=scl[:, 1:2], scalar2=0.0,
            op0=ALU.mult, op1=ALU.bypass),
          waits=[(sLD, ld), (sV, sx_rdy), (sT, xq5["evac"] and 0)], inc=(sV, 1))
        xq5["xfree"][i2] = sV.n
        E("v", lambda v: v.tensor_copy(out=ibuf[:, :], in_=ubuf[:, :]))
        E("v", lambda v: v.tensor_copy(out=ubuf[:, :], in_=ibuf[:, :]), inc=(sV, 1))
        uv = sV.n
        for c in range(8):
            E("t", lambda t, c=c: t.transpose(
                out=ps_xt5[:, c * P:(c + 1) * P], in_=ubuf[:, c * P:(c + 1) * P],
                identity=ident_f[:, :]),
              waits=[(sV, uv), (sV, xq5["evac"])] if c == 0 else [],
              inc=(sT, 1) if c == 7 else None)
        tX = sT.n
        E("v", lambda v, tile=tile: v.tensor_scalar(
            out=xqT[:, :].rearrange("p (h c) -> p h c", h=8)[:, :, tile * P:(tile + 1) * P],
            in0=ps_xt5[:, :].rearrange("p (h c) -> p h c", h=8),
            scalar1=scl[:, 0:1], scalar2=0.0, op0=ALU.mult, op1=ALU.bypass),
          waits=[(sT, tX)], inc=(sV, 1))
        xq5["evac"] = sV.n
    E("v", lambda v: v.drain(), inc=(sV, 1))
    xq_done = sV.n

    q5 = {"wfree": [0, 0], "lufree": [0, 0], "ufree": [[0, 0], [0, 0]],
          "mfree": [0, 0]}
    spill_rec = {}

    def dequant_tile(src_sb, nblk, waits0):
        r3 = src_sb[:, :].rearrange("p (b k) -> p b k", k=32)
        E("v", lambda v, r3=r3, nblk=nblk: v.tensor_reduce(
            out=sdq[:, 0:nblk], in_=r3, axis=AX.X, op=ALU.max,
            apply_absolute_value=True), waits=waits0 + [(sS, sS.n)], inc=(sV, 1))
        E("v", lambda v: v.drain())
        E("v", lambda v, nblk=nblk: v.tensor_scalar(
            out=sdq[:, 0:nblk], in0=sdq[:, 0:nblk], scalar1=1.0 / 7.0,
            scalar2=1e-12, op0=ALU.mult, op1=ALU.max))
        E("v", lambda v: v.drain())
        E("v", lambda v, nblk=nblk: v.reciprocal(
            out=sdq[:, 96:96 + nblk], in_=sdq[:, 0:nblk]))
        E("v", lambda v: v.drain(), inc=(sV, 1))
        E("v", lambda v, r3=r3, nblk=nblk: v.tensor_tensor(
            out=ubuf[:, 0:nblk * 32].rearrange("p (b k) -> p b k", k=32), in0=r3,
            in1=sdq[:, 96:96 + nblk].unsqueeze(2).to_broadcast((P, nblk, 32)),
            op=ALU.mult), inc=(sV, 1))
        vr = sV.n
        E("v", lambda v, nblk=nblk: v.tensor_copy(
            out=ibuf[:, 0:nblk * 32], in_=ubuf[:, 0:nblk * 32]))
        E("v", lambda v, nblk=nblk: v.tensor_copy(
            out=ubuf[:, 0:nblk * 32], in_=ibuf[:, 0:nblk * 32]))
        E("v", lambda v, nblk=nblk: v.tensor_tensor(
            out=wqb[:, 0:nblk * 32].rearrange("p (b k) -> p b k", k=32),
            in0=ubuf[:, 0:nblk * 32].rearrange("p (b k) -> p b k", k=32),
            in1=sdq[:, 0:nblk].unsqueeze(2).to_broadcast((P, nblk, 32)),
            op=ALU.mult), inc=(sV, 1))
        E("v", lambda v: v.drain())
        return sV.n

    def upgate_pass(dw, dspill, rm_col, name, ssem):
        for c in range(NCF):
            i2 = c % 2
            E("y", lambda y, c=c, i2=i2, dw=dw: y.dma_start(
                out=wnat[i2][:, :], in_=dw[c * P:(c + 1) * P, :]),
              waits=[(sV, q5["wfree"][i2])] + (p4e if c < 2 and name == "g" else []),
              inc16=sLD)
            wld = sLD.n
            wv = dequant_tile(wnat[i2], 32, [(sLD, wld)])
            q5["wfree"][i2] = wv
            for cc in range(8):
                E("t", lambda t, cc=cc: t.transpose(
                    out=ps_w5[:, cc * P:(cc + 1) * P], in_=wqb[:, cc * P:(cc + 1) * P],
                    identity=ident_b[:, :]),
                  waits=[(sV, wv), (sS, q5["lufree"][i2])] if cc == 0 else [],
                  inc=(sT, 1) if cc == 7 else None)
            tW = sT.n
            E("s", lambda s, i2=i2: s.activation(
                out=luT[i2][:, :], in_=ps_w5[:, :], func=AF.Copy, bias=0.0,
                scale=1.0), waits=[(sT, tW)], inc=(sS, 1))
            luS = sS.n
            q5["lufree"][i2] = luS
            for tg in range(4):
                bank = tg % 2
                for hc in range(8):
                    E("t", lambda t, hc=hc, tg=tg, bank=bank, i2=i2: t.matmul(
                        out=ps_m5[bank][:, :],
                        lhsT=luT[i2][:, hc * P:(hc + 1) * P],
                        rhs=xqT[:, hc * T + tg * 512: hc * T + (tg + 1) * 512],
                        start=(hc == 0), stop=(hc == 7)),
                      waits=[(sS, luS), (sV, xq_done), (sV, q5["mfree"][bank])]
                            if hc == 0 else [],
                      inc=(sT, 1) if hc == 7 else None)
                tM = sT.n
                E("v", lambda v, bank=bank, ssem=ssem: v.tensor_copy(
                    out=ustage[bank][:, :], in_=ps_m5[bank][:, :]),
                  waits=[(ssem, q5["ufree"][bank][0]),
                         (sGT, q5.get("ug_fence", 0) if (name == "u" and c == 0 and tg < 2) else 0),
                         (sT, tM), (sV, q5["ufree"][bank][1])], inc=(sV, 1))
                uV = sV.n
                q5["mfree"][bank] = uV
                E("s", lambda s, c=c, tg=tg, bank=bank, dspill=dspill, ssem=ssem: s.dma_start(
                    out=dspill[c, :, tg * 512:(tg + 1) * 512], in_=ustage[bank][:, :]),
                  waits=[(sV, uV)], inc16=ssem)
                spill_rec[(name, c, tg)] = ssem.n
                E("v", lambda v, bank=bank: v.tensor_reduce(
                    out=stat[:, 6:7], in_=ustage[bank][:, :], axis=AX.X, op=ALU.max,
                    apply_absolute_value=True))
                E("v", lambda v: v.drain())
                E("v", lambda v, rm_col=rm_col: v.tensor_tensor(
                    out=runmax[:, rm_col:rm_col + 1], in0=runmax[:, rm_col:rm_col + 1],
                    in1=stat[:, 6:7], op=ALU.max), inc=(sV, 1))
                E("v", lambda v: v.drain())
                q5["ufree"][bank] = [ssem.n, sV.n]

    def launch_ar(cci, cco, rm_col):
        E("g", lambda g, cci=cci, rm_col=rm_col: g.dma_start(
            out=cci[:, :], in_=runmax[:, rm_col:rm_col + 1]),
          waits=[(sV, sV.n)], inc16=sGD)
        E("g", lambda g, cci=cci, cco=cco: g.collective_compute(
            "AllReduce", ALU.max, replica_groups=[list(range(8))],
            ins=[cci.ap().opt()], outs=[cco.ap().opt()]),
          waits=[(sGD, sGD.n)], inc=(sCC, 1))
        return sCC.n

    upgate_pass(dwg, dgate, 2, "g", sGT)
    gate_total = sGT.n
    ar2 = launch_ar(cc2i, cc2o, 2)
    q5["ug_fence"] = gate_total
    q5["ufree"] = [[0, q5["ufree"][0][1]], [0, q5["ufree"][1][1]]]
    upgate_pass(dwu, dup, 1, "u", sUP)
    up_total = sUP.n
    ar3 = launch_ar(cc3i, cc3o, 1)

    sg_rdy = scale_setup(cc2o, ar2, 2, 3, [])
    gst = [b.sb(f"gst{i}", [P, 512], F32) for i in range(2)]
    g2b = b.sb("g2b", [P, 512], F32)
    gq_b = b.sb("gq_b", [P, 512], F32)
    sgb = b.sb("sgb", [P, 512], F32)
    i5 = b.sb("i5", [P, 512], I32)
    p5c = {"gfree": [0, 0], "g2free": 0, "spfree": 0}
    for c in range(NCF):
        for tg in range(4):
            i2 = (c * 4 + tg) % 2
            E("y", lambda y, c=c, tg=tg, i2=i2: y.dma_start(
                out=gst[i2][:, :], in_=dgate[c, :, tg * 512:(tg + 1) * 512]),
              waits=[(sGT, gate_total), (sV, p5c["gfree"][i2])],
              inc16=sLD)
            gld = sLD.n
            E("v", lambda v, i2=i2: v.tensor_scalar(
                out=g2b[:, :], in0=gst[i2][:, :], scalar1=scl[:, 3:4], scalar2=0.0,
                op0=ALU.mult, op1=ALU.bypass),
              waits=[(sLD, gld), (sV, sg_rdy), (sG2c, p5c["spfree"])], inc=(sV, 1))
            p5c["gfree"][i2] = sV.n
            E("v", lambda v: v.tensor_copy(out=i5[:, :], in_=g2b[:, :]))
            E("v", lambda v: v.tensor_copy(out=g2b[:, :], in_=i5[:, :]),
              inc=(sV, 1))
            E("v", lambda v: v.tensor_scalar(
                out=gq_b[:, :], in0=g2b[:, :], scalar1=scl[:, 2:3], scalar2=0.0,
                op0=ALU.mult, op1=ALU.bypass), inc=(sV, 1))
            gqv = sV.n
            E("s", lambda s: s.activation(
                out=sgb[:, :], in_=gq_b[:, :], func=AF.Sigmoid, bias=0.0, scale=1.0),
              waits=[(sV, gqv)], inc=(sS, 1))
            sS1 = sS.n
            E("v", lambda v: v.tensor_scalar(
                out=g2b[:, :], in0=sgb[:, :], scalar1=65536.0, scalar2=0.0,
                op0=ALU.mult, op1=ALU.bypass), waits=[(sS, sS1)], inc=(sV, 1))
            E("v", lambda v: v.tensor_copy(out=i5[:, :], in_=g2b[:, :]))
            E("v", lambda v: v.tensor_copy(out=g2b[:, :], in_=i5[:, :]),
              inc=(sV, 1))
            E("v", lambda v: v.tensor_scalar(
                out=g2b[:, :], in0=g2b[:, :], scalar1=65535.0, scalar2=0.0,
                op0=ALU.min, op1=ALU.max), inc=(sV, 1))
            E("v", lambda v: v.scalar_tensor_tensor(
                out=g2b[:, :], in0=g2b[:, :], scalar=1.0 / 65536.0, in1=gq_b[:, :],
                op0=ALU.mult, op1=ALU.mult), inc=(sV, 1))
            g2v = sV.n
            if upto == "p5dbg" and c == 23 and tg == 2:
                E("g", lambda g: g.dma_start(
                    out=dout[5 * P:6 * P, 0:512], in_=g2b[:, :]),
                  waits=[(sV, g2v)], inc16=sGD)
                E("g", lambda g, i2=i2: g.dma_start(
                    out=dout[15 * P:16 * P, 0:512], in_=gst[i2][:, :]),
                  waits=[(sV, g2v)], inc16=sGD)
                E("g", lambda g: g.dma_start(
                    out=dout[15 * P:16 * P, 512:1024], in_=gq_b[:, :]),
                  waits=[(sV, g2v)], inc16=sGD)
            E("s", lambda s, c=c, tg=tg: s.dma_start(
                out=dgate2[c, :, tg * 512:(tg + 1) * 512], in_=g2b[:, :]),
              waits=[(sV, g2v)], inc16=sG2c)
            spill_rec[("g2", c, tg)] = sG2c.n
            p5c["spfree"] = sG2c.n
            E("v", lambda v: v.tensor_reduce(
                out=stat[:, 6:7], in_=g2b[:, :], axis=AX.X, op=ALU.max,
                apply_absolute_value=True))
            E("v", lambda v: v.drain())
            E("v", lambda v: v.tensor_tensor(
                out=runmax[:, 3:4], in0=runmax[:, 3:4], in1=stat[:, 6:7],
                op=ALU.max), inc=(sV, 1))
            E("v", lambda v: v.drain())
            p5c["g2free"] = sG2c.n
    ar4 = launch_ar(cc4i, cc4o, 3)

    sup_rdy = scale_setup(cc3o, ar3, 4, 5, [])
    sg2_rdy = scale_setup(cc4o, ar4, 6, 7, [])
    E("v", lambda v: v.memset(runmax[:, 0:1], 0.0), inc=(sV, 1))
    E("v", lambda v: v.drain())
    ust2 = [b.sb(f"ust2{i}", [P, 512], F32) for i in range(2)]
    ob = b.sb("ob", [P, 512], F32)
    p5c_tail = sG2c.n
    g2_total = sG2c.n
    p5d = {"gfree": [0, 0], "ufree": [0, 0], "spfree": 0}
    for c in range(NCF):
        for tg in range(4):
            i2 = (c * 4 + tg) % 2
            E("y", lambda y, c=c, tg=tg, i2=i2: y.dma_start(
                out=gst[i2][:, :], in_=dgate2[c, :, tg * 512:(tg + 1) * 512]),
              waits=[(sG2c, g2_total), (sV, p5d["gfree"][i2])],
              inc16=sLD)
            E("y", lambda y, c=c, tg=tg, i2=i2: y.dma_start(
                out=ust2[i2][:, :], in_=dup[c, :, tg * 512:(tg + 1) * 512]),
              waits=[(sUP, up_total), (sV, p5d["ufree"][i2])],
              inc16=sLD)
            gld = sLD.n
            E("v", lambda v, i2=i2: v.tensor_scalar(
                out=g2b[:, :], in0=gst[i2][:, :], scalar1=scl[:, 7:8], scalar2=0.0,
                op0=ALU.mult, op1=ALU.bypass),
              waits=[(sLD, gld), (sV, sg2_rdy), (sV, sup_rdy)] +
                    ([(sG2c, p5c_tail)] if (c == 0 and tg == 0) else []),
              inc=(sV, 1))
            p5d["gfree"][i2] = sV.n
            E("v", lambda v: v.tensor_copy(out=i5[:, :], in_=g2b[:, :]))
            E("v", lambda v: v.tensor_copy(out=g2b[:, :], in_=i5[:, :]),
              inc=(sV, 1))
            E("v", lambda v: v.tensor_scalar(
                out=g2b[:, :], in0=g2b[:, :], scalar1=scl[:, 6:7], scalar2=0.0,
                op0=ALU.mult, op1=ALU.bypass), inc=(sV, 1))
            E("v", lambda v, i2=i2: v.tensor_scalar(
                out=ob[:, :], in0=ust2[i2][:, :], scalar1=scl[:, 5:6], scalar2=0.0,
                op0=ALU.mult, op1=ALU.bypass),
              waits=[(sOc, p5d["spfree"])], inc=(sV, 1))
            p5d["ufree"][i2] = sV.n
            E("v", lambda v: v.tensor_copy(out=i5[:, :], in_=ob[:, :]))
            E("v", lambda v: v.tensor_copy(out=ob[:, :], in_=i5[:, :]),
              inc=(sV, 1))
            E("v", lambda v: v.tensor_scalar(
                out=ob[:, :], in0=ob[:, :], scalar1=scl[:, 4:5], scalar2=0.0,
                op0=ALU.mult, op1=ALU.bypass), inc=(sV, 1))
            E("v", lambda v: v.tensor_tensor(
                out=ob[:, :], in0=g2b[:, :], in1=ob[:, :], op=ALU.mult),
              inc=(sV, 1))
            ov = sV.n
            E("s", lambda s, c=c, tg=tg: s.dma_start(
                out=dos[c, :, tg * 512:(tg + 1) * 512], in_=ob[:, :]),
              waits=[(sV, ov)], inc16=sOc)
            spill_rec[("o", c, tg)] = sOc.n
            p5d["spfree"] = sOc.n
            E("v", lambda v: v.tensor_reduce(
                out=stat[:, 6:7], in_=ob[:, :], axis=AX.X, op=ALU.max,
                apply_absolute_value=True))
            E("v", lambda v: v.drain())
            E("v", lambda v: v.tensor_tensor(
                out=runmax[:, 0:1], in0=runmax[:, 0:1], in1=stat[:, 6:7],
                op=ALU.max), inc=(sV, 1))
            E("v", lambda v: v.drain())
    ar5 = launch_ar(cc1i, cc1o, 0)

    if upto == "p5dbg":
        fw = [(sV, sV.n), (sS, sS.n), (sT, sT.n), (sACT, sACT.n), (sG2c, sG2c.n), (sOc, sOc.n), (sCC, sCC.n)]
        E("y", lambda y: y.dma_start(out=dout[0:P, 0:12], in_=scl[:, :]),
          waits=fw, inc16=sOUT)
        E("y", lambda y: y.dma_start(out=dout[0:P, 16:20], in_=runmax[:, :]),
          waits=fw, inc16=sOUT)
        E("y", lambda y: y.dma_start(out=dout[P:2 * P, :], in_=dup[0, :, 0:1024]),
          waits=fw, inc16=sOUT)
        E("y", lambda y: y.dma_start(out=dout[2 * P:3 * P, :], in_=dgate[0, :, 0:1024]),
          waits=fw, inc16=sOUT)
        E("y", lambda y: y.dma_start(out=dout[3 * P:4 * P, :], in_=dgate2[0, :, 0:1024]),
          waits=fw, inc16=sOUT)
        E("y", lambda y: y.dma_start(out=dout[4 * P:5 * P, :], in_=dos[0, :, 0:1024]),
          waits=fw, inc16=sOUT)
        E("y", lambda y: y.dma_start(out=dout[6 * P:7 * P, :], in_=dos[0, :, 1024:2048]),
          waits=fw, inc16=sOUT)
        E("y", lambda y: y.dma_start(out=dout[7 * P:8 * P, :], in_=dgate2[0, :, 1024:2048]),
          waits=fw, inc16=sOUT)
        E("y", lambda y: y.dma_start(out=dout[8 * P:9 * P, :], in_=dup[0, :, 1024:2048]),
          waits=fw, inc16=sOUT)
        E("y", lambda y: y.dma_start(out=dout[9 * P:10 * P, :], in_=dgate[0, :, 1024:2048]),
          waits=fw, inc16=sOUT)
        E("y", lambda y: y.dma_start(out=dout[10 * P:11 * P, :], in_=dos[23, :, 1024:2048]),
          waits=fw, inc16=sOUT)
        E("y", lambda y: y.dma_start(out=dout[11 * P:12 * P, :], in_=dgate2[23, :, 1024:2048]),
          waits=fw, inc16=sOUT)
        E("y", lambda y: y.dma_start(out=dout[12 * P:13 * P, :], in_=dup[23, :, 1024:2048]),
          waits=fw, inc16=sOUT)
        E("y", lambda y: y.dma_start(out=dout[13 * P:14 * P, :], in_=dos[23, :, 0:1024]),
          waits=fw, inc16=sOUT)
        E("y", lambda y: y.dma_start(out=dout[14 * P:15 * P, :], in_=dgate[23, :, 1024:2048]),
          waits=fw, inc16=sOUT)
        E("y", lambda y: y.dma_start(out=dout[15 * P:16 * P, :], in_=dup[22, :, 1024:2048]),
          waits=fw, inc16=sOUT)
        E("g", lambda g: g.dma_start(out=dout[5 * P:6 * P, :], in_=xqT[:, 0:1024]),
          waits=fw, inc16=sOUT)
        E("y", lambda y: y.nop(), waits=[(sOUT, sOUT.n)])
        _finish(nc, b)
        return nc

    b.free("ob", "ust20", "ust21", "i5", "sgb", "gq_b", "g2b", "gst0", "gst1",
           "ustage0", "ustage1", "luT0", "luT1", "wqb", "wnat0", "wnat1",
           "ubuf", "ibuf", "xbuf0", "xbuf1", "xqT")

    wdT = b.sb("wdT", [P, NCF * D], BF16)
    wnat2 = b.sb("wnat2", [P, IF], F32)
    ubuf2 = b.sb("ubuf2", [P, IF], F32)
    ibuf2 = b.sb("ibuf2", [P, IF], I32)
    wqb2 = b.sb("wqb2", [P, IF], BF16)
    wdqn = b.sb("wdqn", [P, 8 * IF], BF16)
    wd5 = {"wfree": 0, "evac": sS.n}
    o_total = sOc.n
    p5d_e = [(sV, sV.n), (sS, sS.n), (sT, sT.n), (sOc, sOc.n), (sG2c, sG2c.n)]
    for ht in range(8):
        E("y", lambda y, ht=ht: y.dma_start(
            out=wnat2[:, :], in_=dwd[ht * P:(ht + 1) * P, :]),
          waits=[(sV, wd5["wfree"])] + (p5d_e if ht == 0 else []), inc16=sLD)
        wld = sLD.n
        r3 = wnat2[:, :].rearrange("p (b k) -> p b k", k=32)
        E("v", lambda v, r3=r3: v.tensor_reduce(
            out=sdq[:, 0:96], in_=r3, axis=AX.X, op=ALU.max,
            apply_absolute_value=True), waits=[(sLD, wld), (sS, sS.n)], inc=(sV, 1))
        E("v", lambda v: v.drain())
        E("v", lambda v: v.tensor_scalar(
            out=sdq[:, 0:96], in0=sdq[:, 0:96], scalar1=1.0 / 7.0, scalar2=1e-12,
            op0=ALU.mult, op1=ALU.max))
        E("v", lambda v: v.drain())
        E("v", lambda v: v.reciprocal(out=sdq[:, 96:192], in_=sdq[:, 0:96]))
        E("v", lambda v: v.drain(), inc=(sV, 1))
        E("v", lambda v, r3=r3: v.tensor_tensor(
            out=ubuf2[:, :].rearrange("p (b k) -> p b k", k=32),
            in0=r3, in1=sdq[:, 96:192].unsqueeze(2).to_broadcast((P, 96, 32)),
            op=ALU.mult), inc=(sV, 1))
        vr = sV.n
        E("v", lambda v: v.tensor_copy(out=ibuf2[:, :], in_=ubuf2[:, :]))
        E("v", lambda v: v.tensor_copy(out=ubuf2[:, :], in_=ibuf2[:, :]))
        E("v", lambda v, ht=ht: v.tensor_tensor(
            out=wdqn[:, ht * IF:(ht + 1) * IF].rearrange("p (b k) -> p b k", k=32),
            in0=ubuf2[:, :].rearrange("p (b k) -> p b k", k=32),
            in1=sdq[:, 0:96].unsqueeze(2).to_broadcast((P, 96, 32)),
            op=ALU.mult), inc=(sV, 1))
        wd5["wfree"] = sV.n
    E("v", lambda v: v.drain(), inc=(sV, 1))
    wdq_v = sV.n
    for ioc in range(NCF):
        for ht in range(8):
            E("t", lambda t, ioc=ioc, ht=ht: t.transpose(
                out=ps_w5[:, ht * P:(ht + 1) * P],
                in_=wdqn[:, ht * IF + ioc * P: ht * IF + (ioc + 1) * P],
                identity=ident_b[:, :]),
              waits=[(sV, wdq_v), (sS, wd5["evac"])] if ht == 0 else [],
              inc=(sT, 1) if ht == 7 else None)
        tW = sT.n
        E("s", lambda s, ioc=ioc: s.activation(
            out=wdT[:, ioc * D:(ioc + 1) * D], in_=ps_w5[:, :], func=AF.Copy,
            bias=0.0, scale=1.0), waits=[(sT, tW)], inc=(sS, 1))
        wd5["evac"] = sS.n
    wdT_done = sS.n
    b.free("wdqn", "wqb2", "ibuf2", "ubuf2", "wnat2")

    so_rdy = scale_setup(cc1o, ar5, 8, 9, [])
    act_total = sACT.n
    oqb = b.sb("oqb", [P, NCF * 512], BF16)
    ostr = [b.sb(f"ostr{i}", [P, 512], F32) for i in range(2)]
    oi5 = b.sb("oi5", [P, 512], I32)
    osc = b.sb("osc", [P, 512], F32)
    x1s = [b.sb(f"x1s{i}", [P, D], F32) for i in range(2)]
    apS = [b.sb(f"apS{i}", [P, D], F32) for i in range(2)]
    tst = b.sb("tst", [P, D], F32)
    outst = [b.sb(f"outst{i}", [P, D], F32) for i in range(2)]
    p5e = {"ofree": [0, 0], "xfree": [0, 0], "afree": [0, 0], "outfree": [0, 0],
           "mfree": [0, 0], "mmdone": 0}
    for tg in range(4):
        for ioc in range(NCF):
            i2 = ioc % 2
            E("y", lambda y, ioc=ioc, tg=tg, i2=i2: y.dma_start(
                out=ostr[i2][:, :], in_=dos[ioc, :, tg * 512:(tg + 1) * 512]),
              waits=[(sOc, o_total), (sV, p5e["ofree"][i2])] +
                    ([(sS, wdT_done)] if tg == 0 and ioc < 2 else []),
              inc16=sLD)
            old = sLD.n
            E("v", lambda v, i2=i2: v.tensor_scalar(
                out=osc[:, :], in0=ostr[i2][:, :], scalar1=scl[:, 9:10], scalar2=0.0,
                op0=ALU.mult, op1=ALU.bypass),
              waits=[(sLD, old), (sV, so_rdy)], inc=(sV, 1))
            p5e["ofree"][i2] = sV.n
            E("v", lambda v: v.tensor_copy(out=oi5[:, :], in_=osc[:, :]))
            E("v", lambda v: v.tensor_copy(out=osc[:, :], in_=oi5[:, :]),
              inc=(sV, 1))
            E("v", lambda v, ioc=ioc: v.tensor_scalar(
                out=oqb[:, ioc * 512:(ioc + 1) * 512], in0=osc[:, :],
                scalar1=scl[:, 8:9], scalar2=0.0, op0=ALU.mult, op1=ALU.bypass),
              waits=[(sT, p5e["mmdone"])] if ioc == 0 else [], inc=(sV, 1))
        E("v", lambda v: v.drain(), inc=(sV, 1))
        oq_v = sV.n
        if upto == "p5e" and False:
            for i in range(8):
                E("g", lambda g, i=i: g.dma_start(
                    out=dout[i * P:(i + 1) * P, 0:512], in_=oqb[:, i * 512:(i + 1) * 512]),
                  waits=[(sV, oq_v)], inc16=sOUT)
            E("y", lambda y: y.dma_start(out=dout[8 * P:9 * P, 0:12], in_=scl[:, :]),
              waits=[(sV, oq_v)], inc16=sOUT)
            E("g", lambda g: g.dma_start(out=dout[9 * P:10 * P, :], in_=wdT[:, 0:1024]),
              waits=[(sV, oq_v)], inc16=sOUT)
        for ts4 in range(4):
            tile = tg * 4 + ts4
            i2 = ts4 % 2
            E("y", lambda y, tile=tile, i2=i2: y.dma_start(
                out=x1s[i2][:, :], in_=dx1[tile * P:(tile + 1) * P, :]),
              waits=[(sACT, act_total), (sV, p5e["xfree"][i2])] +
                    ([(sS, wdT_done)] if tile < 2 else []), inc16=sLD)
            E("y", lambda y, tile=tile, i2=i2: y.dma_start(
                out=apS[i2][:, :], in_=dattnP[tile * P:(tile + 1) * P, :]),
              waits=[(sACT, act_total), (sV, p5e["afree"][i2])] +
                    ([(sS, wdT_done)] if tile < 2 else []), inc16=sLD)
            rld = sLD.n
            tops = []
            for half in range(2):
                for ioc in range(NCF):
                    E("t", lambda t, ioc=ioc, ts4=ts4, half=half: t.matmul(
                        out=ps_m5[half][:, :],
                        lhsT=oqb[:, ioc * 512 + ts4 * P: ioc * 512 + (ts4 + 1) * P],
                        rhs=wdT[:, ioc * D + half * 512: ioc * D + (half + 1) * 512],
                        start=(ioc == 0), stop=(ioc == NCF - 1)),
                      waits=[(sV, oq_v), (sS, wdT_done), (sV, p5e["mfree"][half])]
                            if ioc == 0 else [],
                      inc=(sT, 1) if ioc == NCF - 1 else None)
                tops.append(sT.n)
            E("v", lambda v, i2=i2: v.tensor_tensor(
                out=tst[:, :], in0=x1s[i2][:, :], in1=apS[i2][:, :], op=ALU.add),
              waits=[(sLD, rld), (sV, sV.n)], inc=(sV, 1))
            tv = sV.n
            p5e["xfree"][i2] = tv
            p5e["afree"][i2] = tv
            oi = ts4 % 2
            E("v", lambda v, oi=oi: v.tensor_tensor(
                out=outst[oi][:, 0:512], in0=ps_m5[0][:, :], in1=tst[:, 0:512],
                op=ALU.add),
              waits=[(sT, tops[0]), (sOUT, p5e["outfree"][oi])])
            E("v", lambda v, oi=oi: v.tensor_tensor(
                out=outst[oi][:, 512:1024], in0=ps_m5[1][:, :], in1=tst[:, 512:1024],
                op=ALU.add), waits=[(sT, tops[1])], inc=(sV, 1))
            vout = sV.n
            p5e["mfree"][0] = vout
            p5e["mfree"][1] = vout
            E("y", lambda y, tile=tile, oi=oi: y.dma_start(
                out=dout[tile * P:(tile + 1) * P, :], in_=outst[oi][:, :]),
              waits=[(sV, vout)], inc16=sOUT)
            p5e["outfree"][oi] = sOUT.n
        p5e["mmdone"] = sT.n
        if upto == "p5e" and tg == 2:
            fw2 = [(sV, sV.n), (sS, sS.n), (sT, sT.n), (sOUT, sOUT.n)]
            E("y", lambda y: y.dma_start(out=dout[12 * P:13 * P, :], in_=dx1[8 * P:9 * P, :]),
              waits=fw2, inc16=sOUT)
            E("y", lambda y: y.dma_start(out=dout[13 * P:14 * P, :], in_=dattnP[8 * P:9 * P, :]),
              waits=fw2, inc16=sOUT)
            break
    E("y", lambda y: y.nop(), waits=[(sOUT, sOUT.n)])
    _finish(nc, b)
    return nc



_CACHE = {}


def kernel(**inputs):
    if "nc" not in _CACHE:
        _CACHE["nc"] = build_nc()
    nc = _CACHE["nc"]
    x = np.ascontiguousarray(inputs["L0_attn_hidden_states"], dtype=np.float32)
    w = {
        "kw": np.ascontiguousarray(inputs["L0_attn_k_weight"], np.float32),
        "qw": np.ascontiguousarray(inputs["L0_attn_q_weight"], np.float32),
        "lnw": np.ascontiguousarray(inputs["L0_attn_ln_weight"], np.float32),
        "lnb": np.ascontiguousarray(inputs["L0_attn_ln_bias"], np.float32),
        "bias1": np.ascontiguousarray(inputs["L0_attn_bias"], np.float32),
        "wrT": np.ascontiguousarray(np.asarray(inputs["L0_ffn_W_r"]).T, np.float32),
        "wkT": np.ascontiguousarray(np.asarray(inputs["L0_ffn_W_k"]).T, np.float32),
        "wvT": np.ascontiguousarray(np.asarray(inputs["L0_ffn_W_v"]).T, np.float32),
        "mur": np.ascontiguousarray(inputs["L0_ffn_mu_r"], np.float32),
        "muk": np.ascontiguousarray(inputs["L0_ffn_mu_k"], np.float32),
        "l1qT": np.ascontiguousarray(np.asarray(inputs["L1_attn_Wq"]).T, np.float32),
        "l1kT": np.ascontiguousarray(np.asarray(inputs["L1_attn_Wk"]).T, np.float32),
        "l1vT": np.ascontiguousarray(np.asarray(inputs["L1_attn_Wv"]).T, np.float32),
        "l1oT": np.ascontiguousarray(np.asarray(inputs["L1_attn_Wo"]).T, np.float32),
        "bo": np.ascontiguousarray(inputs["L1_attn_bo"], np.float32),
        "wg": np.ascontiguousarray(inputs["L1_ffn_w_gate"], np.float32),
        "wu": np.ascontiguousarray(inputs["L1_ffn_w_up"], np.float32),
        "wd": np.ascontiguousarray(inputs["L1_ffn_w_down"], np.float32),
    }
    in_maps = [dict(w, xin=np.ascontiguousarray(x[c])) for c in range(8)]
    res = run_bass_kernel_spmd(nc, in_maps, core_ids=list(range(8)))
    return np.stack([res.results[c]["out"] for c in range(8)], axis=0)


# version 04
# speedup vs baseline: 1.0426x; 1.0426x over previous
import numpy as np
import concourse.bass as bass
import concourse.mybir as mybir
from concourse.bass_utils import run_bass_kernel_spmd

F32 = mybir.dt.float32
BF16 = mybir.dt.bfloat16
I32 = mybir.dt.int32
AF = mybir.ActivationFunctionType
ALU = mybir.AluOpType
AX = mybir.AxisListType

P = 128
T = 2048
D = 1024
NT = T // P
I0 = 4096
IF = 3072
NC0 = I0 // P
NCF = IF // P
GRP = 256
NG = T // GRP
HB = 16
INV_D = 1.0 / D
INV_PI = 0.3183098861837907
CLIP_LO = -1.0 + 1e-6
CLIP_HI = 1.0 - 1e-6

ENGINES = ("y", "v", "s", "t", "g")


class Ct:
    def __init__(self, handle, name):
        self.h = handle
        self.name = name
        self.n = 0


class Sched:
    def __init__(self):
        self.recs = {e: [] for e in ENGINES}

    def op(self, eng, fn, waits=(), inc=None, inc16=None, serial=True):
        ws = []
        for c, v in waits:
            if v <= 0:
                continue
            assert v <= c.n, f"wait on future value of {c.name}: {v} > {c.n}"
            ws.append((c, v))
        incs = []
        if inc is not None:
            cs = inc if isinstance(inc, list) else [inc]
            for c, k in cs:
                c.n += k
                incs.append((c, k))
        if inc16 is not None:
            inc16.n += 16
            incs.append((inc16, 16))
        self.recs[eng].append((fn, ws, incs))
        if inc16 is not None and serial:
            self.recs[eng].append(
                (lambda e: e.nop(), [(inc16, inc16.n)], []))

    def replay(self, eng_name, eng):
        for fn, ws, incs in self.recs[eng_name]:
            for c, v in ws:
                eng.wait_ge(c.h, v)
            inst = fn(eng)
            for c, k in incs:
                inst.then_inc(c.h, k)


class B:
    def __init__(self, nc):
        self.nc = nc
        self.sched = Sched()
        self._open = {}
        self._seq = 0

    def sb(self, name, shape, dt=F32):
        cm = self.nc.sbuf_tensor(name, shape, dt)
        h = cm.__enter__()
        self._seq += 1
        self._open[name] = (self._seq, cm)
        return h

    def ps(self, name, shape, dt=F32):
        cm = self.nc.psum_tensor(name, shape, dt)
        h = cm.__enter__()
        self._seq += 1
        self._open[name] = (self._seq, cm)
        return h

    def free(self, *names):
        for name in sorted(names, key=lambda n: -self._open[n][0]):
            self._open.pop(name)[1].__exit__(None, None, None)

    def sem(self, name):
        cm = self.nc.semaphore(name)
        h = cm.__enter__()
        self._seq += 1
        self._open["sem_" + name] = (self._seq, cm)
        return Ct(h, name)




def _finish(nc, b):
    with nc.Block() as block:
        @block.sync
        def _(y):
            b.sched.replay("y", y)

        @block.vector
        def _(v):
            b.sched.replay("v", v)

        @block.scalar
        def _(s):
            b.sched.replay("s", s)

        @block.tensor
        def _(t):
            b.sched.replay("t", t)

        @block.gpsimd
        def _(g):
            b.sched.replay("g", g)


def build_nc(upto="all"):
    nc = bass.Bass("TRN2", target_bir_lowering=False)
    b = B(nc)
    E = b.sched.op

    dxin = nc.dram_tensor("xin", [T, D], F32, kind="ExternalInput").ap()
    dkw = nc.dram_tensor("kw", [I0, D], F32, kind="ExternalInput").ap()
    dqw = nc.dram_tensor("qw", [I0, D], F32, kind="ExternalInput").ap()
    dlnw = nc.dram_tensor("lnw", [D], F32, kind="ExternalInput").ap()
    dlnb = nc.dram_tensor("lnb", [D], F32, kind="ExternalInput").ap()
    dbias1 = nc.dram_tensor("bias1", [D], F32, kind="ExternalInput").ap()
    dwrT = nc.dram_tensor("wrT", [D, D], F32, kind="ExternalInput").ap()
    dwkT = nc.dram_tensor("wkT", [D, D], F32, kind="ExternalInput").ap()
    dwvT = nc.dram_tensor("wvT", [D, D], F32, kind="ExternalInput").ap()
    dmur = nc.dram_tensor("mur", [D], F32, kind="ExternalInput").ap()
    dmuk = nc.dram_tensor("muk", [D], F32, kind="ExternalInput").ap()
    dl1q = nc.dram_tensor("l1qT", [D, D], F32, kind="ExternalInput").ap()
    dl1k = nc.dram_tensor("l1kT", [D, D], F32, kind="ExternalInput").ap()
    dl1v = nc.dram_tensor("l1vT", [D, D], F32, kind="ExternalInput").ap()
    dl1o = nc.dram_tensor("l1oT", [D, D], F32, kind="ExternalInput").ap()
    dbo = nc.dram_tensor("bo", [D], F32, kind="ExternalInput").ap()
    dwg = nc.dram_tensor("wg", [IF, D], F32, kind="ExternalInput").ap()
    dwu = nc.dram_tensor("wu", [IF, D], F32, kind="ExternalInput").ap()
    dwd = nc.dram_tensor("wd", [D, IF], F32, kind="ExternalInput").ap()
    dout = nc.dram_tensor("out", [T, D], F32, kind="ExternalOutput").ap()

    dattn0 = nc.dram_tensor("attn0", [T, D], F32).ap()
    dx1 = nc.dram_tensor("x1", [T, D], F32).ap()
    dxn2 = nc.dram_tensor("xn2s", [T, D], F32).ap()
    dattnP = nc.dram_tensor("attnP", [T, D], F32).ap()
    dqwbf = nc.dram_tensor("qwbf", [I0, D], BF16).ap()
    dup = nc.dram_tensor("ups", [NCF, P, T], F32).ap()
    dgate = nc.dram_tensor("gates", [NCF, P, T], F32).ap()
    dgate2 = nc.dram_tensor("gate2s", [NCF, P, T], F32).ap()
    dos = nc.dram_tensor("osp", [NCF, P, T], F32).ap()
    cc1i = nc.dram_tensor("cc1i", [P, 1], F32)
    cc1o = nc.dram_tensor("cc1o", [P, 1], F32)
    cc2i = nc.dram_tensor("cc2i", [P, 1], F32)
    cc2o = nc.dram_tensor("cc2o", [P, 1], F32)
    cc5i = nc.dram_tensor("cc5i", [P, 1], F32)
    cc5o = nc.dram_tensor("cc5o", [P, 1], F32)
    cc3i = nc.dram_tensor("cc3i", [P, 1], F32)
    cc3o = nc.dram_tensor("cc3o", [P, 1], F32)
    cc4i = nc.dram_tensor("cc4i", [P, 1], F32)
    cc4o = nc.dram_tensor("cc4o", [P, 1], F32)

    sLD = b.sem("sLD")
    sSP = b.sem("sSP")
    sGD = b.sem("sGD")
    sV = b.sem("sV")
    sS = b.sem("sS")
    sT = b.sem("sT")
    sG = b.sem("sG")
    sCC = b.sem("sCC")
    sOUT = b.sem("sOUT")
    sA0 = b.sem("sA0")
    sACT = b.sem("sACT")
    sQW = b.sem("sQW")
    sGT = b.sem("sGT")
    sUP = b.sem("sUP")
    sG2c = b.sem("sG2c")
    sOc = b.sem("sOc")

    ident_f = b.sb("ident_f", [P, P], F32)
    ident_b = b.sb("ident_b", [P, P], BF16)
    ones_f = b.sb("ones_f", [P, P], F32)
    tri = b.sb("tri", [P, P], BF16)
    lnw_b = b.sb("lnw_b", [P, D], F32)
    lnb_b = b.sb("lnb_b", [P, D], F32)
    bias1_b = b.sb("bias1_b", [P, D], F32)
    bo_b = b.sb("bo_b", [P, D], F32)
    mur_sb = b.sb("mur_sb", [P, 8], F32)
    muk_sb = b.sb("muk_sb", [P, 8], F32)
    m_all = b.sb("m_all", [P, NT], F32)
    rstd_all = b.sb("rstd_all", [P, NT], F32)
    m2_all = b.sb("m2_all", [P, NT], F32)
    ss2_all = b.sb("ss2_all", [P, NT], F32)
    rstd2_all = b.sb("rstd2_all", [P, NT], F32)
    runmax = b.sb("runmax", [P, 4], F32)
    stat = b.sb("stat", [P, 8], F32)
    kc = b.sb("kc", [P, 4], F32)
    xn2T = b.sb("xn2T", [P, 8 * T], BF16)
    vrow = b.sb("vrow", [1, D], F32)
    sq_scr = b.sb("sq_scr", [P, D], F32)
    E("g", lambda g: g.memset(ident_f[:, :], 0.0))
    E("g", lambda g: g.affine_select(
        out=ident_f[:, :], in_=ident_f[:, :], compare_op=ALU.not_equal,
        fill=1.0, base=0, pattern=[[-1, P]], channel_multiplier=1))
    E("g", lambda g: g.memset(ident_b[:, :], 0.0))
    E("g", lambda g: g.affine_select(
        out=ident_b[:, :], in_=ident_b[:, :], compare_op=ALU.not_equal,
        fill=1.0, base=0, pattern=[[-1, P]], channel_multiplier=1))
    E("g", lambda g: g.memset(ones_f[:, :], 1.0))
    E("g", lambda g: g.memset(tri[:, :], 1.0))
    E("g", lambda g: g.affine_select(
        out=tri[:, :], in_=tri[:, :], compare_op=ALU.is_ge,
        fill=0.0, base=0, pattern=[[1, P]], channel_multiplier=-1))
    E("g", lambda g: g.memset(runmax[:, :], 0.0))
    E("g", lambda g: g.memset(kc[:, 0:1], 1e-5))
    E("g", lambda g: g.memset(kc[:, 1:2], 1e-12))
    E("g", lambda g: g.memset(kc[:, 2:3], 0.5), inc=(sG, 1))
    g_const = sG.n

    ps_bc = b.ps("ps_bc", [P, D], F32)
    for src, dst in [(dlnw, lnw_b), (dlnb, lnb_b), (dbias1, bias1_b), (dbo, bo_b)]:
        vprev = sV.n
        E("y", lambda y, s=src: y.dma_start(out=vrow[0:1, :], in_=s.rearrange("(o d) -> o d", o=1)),
          waits=[(sV, vprev)], inc16=sLD)
        ld = sLD.n
        for half in range(2):
            E("t", lambda t, h=half: t.matmul(
                out=ps_bc[:, h * 512:(h + 1) * 512], lhsT=ones_f[0:1, :],
                rhs=vrow[0:1, h * 512:(h + 1) * 512], start=True, stop=True),
              waits=[(sLD, ld), (sG, g_const), (sV, vprev)],
              inc=(sT, 1) if half == 1 else None)
        E("v", lambda v, d=dst: v.tensor_copy(out=d[:, :], in_=ps_bc[:, :]),
          waits=[(sT, sT.n)], inc=(sV, 1))

    def _mu_load(y, dst, srcap):
        with nc.allow_non_contiguous_dma(reason="tiny one-time mu load"):
            return y.dma_start(out=dst[:, :], in_=srcap.rearrange("(c p) -> p c", p=P))
    E("y", lambda y: _mu_load(y, mur_sb, dmur), inc16=sLD)
    E("y", lambda y: _mu_load(y, muk_sb, dmuk), inc16=sLD)
    mu_ld = sLD.n

    KT = b.sb("KT", [P, 8 * I0], BF16)
    kwb = [b.sb(f"kwb{i}", [P, D], F32) for i in range(2)]
    knb = [b.sb(f"knb{i}", [P, D], BF16) for i in range(2)]
    qcb = [b.sb(f"qcb{i}", [P, D], BF16) for i in range(2)]
    ps_kt = [b.ps(f"ps_kt{i}", [P, D], BF16) for i in range(2)]

    kw_free = [0, 0]
    kt_busy = [0, 0]
    kn_free = [0, 0]
    qc_free = [0, 0]
    for k in range(NC0):
        i2 = k % 2
        E("y", lambda y, k=k, i2=i2: y.dma_start(
            out=kwb[i2][:, :], in_=dkw[k * P:(k + 1) * P, :]),
          waits=[(sV, kw_free[i2])], inc16=sLD)
        ld = sLD.n
        E("s", lambda s, i2=i2: s.activation(
            out=sq_scr[:, :], in_=kwb[i2][:, :], func=AF.Square,
            bias=0.0, scale=1.0),
          waits=[(sLD, ld), (sV, sV.n)], inc=(sS, 1))
        ssd = sS.n
        E("v", lambda v: v.tensor_reduce(
            out=stat[:, 2:3], in_=sq_scr[:, :], axis=AX.X, op=ALU.add),
          waits=[(sS, ssd)], inc=(sV, 1))
        E("s", lambda s: s.activation(
            out=stat[:, 3:4], in_=stat[:, 2:3], func=AF.Sqrt, bias=0.0, scale=1.0),
          waits=[(sV, sV.n)], inc=(sS, 1))
        E("v", lambda v: v.tensor_scalar(
            out=stat[:, 4:5], in0=stat[:, 3:4], scalar1=1e-12, scalar2=0.0,
            op0=ALU.max, op1=ALU.bypass), waits=[(sS, sS.n)])
        E("v", lambda v: v.drain())
        E("v", lambda v: v.reciprocal(out=stat[:, 5:6], in_=stat[:, 4:5]))
        E("v", lambda v: v.drain())
        E("v", lambda v, i2=i2: v.tensor_scalar(
            out=knb[i2][:, :], in0=kwb[i2][:, :], scalar1=stat[:, 5:6], scalar2=0.0,
            op0=ALU.mult, op1=ALU.bypass),
          waits=[(sT, kn_free[i2])], inc=(sV, 1))
        E("v", lambda v: v.drain())
        kw_free[i2] = sV.n
        vn = sV.n
        for c in range(8):
            E("t", lambda t, i2=i2, c=c: t.transpose(
                out=ps_kt[i2][:, c * P:(c + 1) * P], in_=knb[i2][:, c * P:(c + 1) * P],
                identity=ident_b[:, :]),
              waits=[(sV, vn), (sV, kt_busy[i2]), (sG, g_const)] if c == 0 else [],
              inc=(sT, 1) if c == 7 else None)
        kn_free[i2] = sT.n
        tn = sT.n
        E("v", lambda v, i2=i2, k=k: v.tensor_copy(
            out=KT[:, :].rearrange("p (h c) -> p h c", h=8)[:, :, k * P:(k + 1) * P],
            in_=ps_kt[i2][:, :].rearrange("p (h c) -> p h c", h=8)),
          waits=[(sT, tn)], inc=(sV, 1))
        kt_busy[i2] = sV.n
        E("g", lambda g, k=k, i2=i2: g.dma_start(
            out=qcb[i2][:, :], in_=dqw[k * P:(k + 1) * P, :]),
          waits=[(sQW, qc_free[i2])], inc16=sGD)
        gld = sGD.n
        E("s", lambda s, k=k, i2=i2: s.dma_start(
            out=dqwbf[k * P:(k + 1) * P, :], in_=qcb[i2][:, :]),
          waits=[(sGD, gld)], inc16=sQW)
        qc_free[i2] = sQW.n
    qw_cached = sQW.n

    b.free("kwb0", "kwb1", "knb0", "knb1", "qcb0", "qcb1",
           "ps_kt0", "ps_kt1", "ps_bc")
    p0_v, p0_s, p0_t = sV.n, sS.n, sT.n

    xg = [b.sb(f"xg{i}", [P, 2 * D], F32) for i in range(2)]
    QT = [b.sb(f"QT{i}", [P, 8 * GRP], BF16) for i in range(2)]
    qwk = [b.sb(f"qwk{i}", [P, D], BF16) for i in range(2)]
    sstage = b.sb("sstage", [P, HB * GRP], F32)
    aux = b.sb("aux", [P, HB * GRP], F32)
    Bst = [b.sb("Bst0", [P, HB * GRP], BF16)] * 2
    xnt = b.sb("xnt", [P, D], F32)
    yt = b.sb("yt", [P, D], F32)
    qtile = b.sb("qtile", [P, D], BF16)
    Xout = [b.sb("Xout0", [P, D], F32)] * 2
    ps_sT = [b.ps(f"ps_sT{i}", [P, GRP], F32) for i in range(3)]
    ps_X = [b.ps(f"ps_X{i}", [P, 512], F32) for i in range(4)]
    ps_qt = b.ps("ps_qt", [P, D], BF16)

    st = {
        "xg_free": [0, 0],
        "qt_evac": 0,
        "qt_built": [0, 0],
        "qt_free": [0, 0],
        "qwk_free": [0, 0],
        "sT_free": [0, 0, 0],
        "bst_free": [0, 0],
        "xfin_v": 0,
        "xout_free": [0, 0],
        "aux_free": 0,
        "attn0_sp": [0] * NT,
    }

    def ln_stats(xt_ap, m_dst, r_dst, eps, ld_waits):
        E("v", lambda v: v.tensor_reduce(
            out=stat[:, 0:1], in_=xt_ap, axis=AX.X, op=ALU.add),
          waits=ld_waits + [(sS, sS.n)], inc=(sV, 1))
        E("s", lambda s: s.activation(
            out=sq_scr[:, :], in_=xt_ap, func=AF.Square, bias=0.0, scale=1.0),
          waits=ld_waits + [(sV, sV.n)], inc=(sS, 1))
        ssd = sS.n
        E("v", lambda v: v.drain())
        E("v", lambda v: v.tensor_scalar(
            out=m_dst, in0=stat[:, 0:1], scalar1=INV_D, scalar2=0.0,
            op0=ALU.mult, op1=ALU.bypass))
        E("v", lambda v: v.tensor_reduce(
            out=stat[:, 3:4], in_=sq_scr[:, :], axis=AX.X, op=ALU.add),
          waits=[(sS, ssd)], inc=(sV, 1))
        E("v", lambda v: v.drain())
        E("v", lambda v: v.tensor_scalar(
            out=stat[:, 3:4], in0=stat[:, 3:4], scalar1=INV_D, scalar2=0.0,
            op0=ALU.mult, op1=ALU.bypass))
        E("v", lambda v: v.drain())
        E("v", lambda v: v.scalar_tensor_tensor(
            out=stat[:, 4:5], in0=m_dst, scalar=m_dst, in1=stat[:, 3:4],
            op0=ALU.mult, op1=ALU.subtract))
        E("v", lambda v: v.drain())
        E("v", lambda v: v.tensor_scalar(
            out=stat[:, 4:5], in0=stat[:, 4:5], scalar1=-1.0, scalar2=eps,
            op0=ALU.mult, op1=ALU.add), inc=(sV, 1))
        E("s", lambda s: s.activation(
            out=stat[:, 5:6], in_=stat[:, 4:5], func=AF.Ln, bias=0.0, scale=1.0),
          waits=[(sV, sV.n)])
        E("s", lambda s: s.activation(
            out=r_dst, in_=stat[:, 5:6], func=AF.Exp, bias=0.0, scale=-0.5),
          inc=(sS, 1))
        return sS.n

    def sq_accum(src_ap, dst_idx, waits):
        E("s", lambda s: s.activation(
            out=sq_scr[:, :], in_=src_ap, func=AF.Square, bias=0.0, scale=1.0),
          waits=list(waits) + [(sV, sV.n)], inc=(sS, 1))
        ssd = sS.n
        E("v", lambda v: v.tensor_reduce(
            out=stat[:, dst_idx:dst_idx + 1], in_=sq_scr[:, :], axis=AX.X,
            op=ALU.add), waits=[(sS, ssd)], inc=(sV, 1))
        E("v", lambda v: v.drain())
        return sV.n

    def rsqrt_nl(src_idx, dst_ap, waits):
        E("s", lambda s: s.activation(
            out=stat[:, 5:6], in_=stat[:, src_idx:src_idx + 1], func=AF.Ln,
            bias=0.0, scale=1.0), waits=waits)
        E("s", lambda s: s.activation(
            out=dst_ap, in_=stat[:, 5:6], func=AF.Exp, bias=0.0, scale=-0.5),
          inc=(sS, 1))
        return sS.n

    def vrecip(out_ap, in_ap):
        E("v", lambda v: v.drain())
        E("v", lambda v: v.reciprocal(out=out_ap, in_=in_ap))
        E("v", lambda v: v.drain())

    def q_build(gq):
        ig = gq % 2
        for tt in range(2):
            tile = gq * 2 + tt
            fence = ([(sV, p0_v), (sS, p0_s), (sT, p0_t)] if gq == 0 else [])
            E("y", lambda y, tile=tile, ig=ig, tt=tt: y.dma_start(
                out=xg[ig][:, tt * D:(tt + 1) * D], in_=dxin[tile * P:(tile + 1) * P, :]),
              waits=([(sV, st["xg_free"][ig])] if tt == 0 else []) + fence, inc16=sLD)
        ld = sLD.n
        for tt in range(2):
            tile = gq * 2 + tt
            xt_ap = xg[ig][:, tt * D:(tt + 1) * D]
            srs = ln_stats(xt_ap, m_all[:, tile:tile + 1], rstd_all[:, tile:tile + 1],
                           1e-5, [(sLD, ld)])
            E("v", lambda v, xt_ap=xt_ap, tile=tile: v.tensor_scalar(
                out=xnt[:, :], in0=xt_ap, scalar1=m_all[:, tile:tile + 1],
                scalar2=rstd_all[:, tile:tile + 1], op0=ALU.subtract, op1=ALU.mult),
              waits=[(sS, srs)], inc=(sV, 1))
            vxn = sV.n
            E("v", lambda v: v.tensor_reduce(
                out=stat[:, 0:1], in_=xnt[:, :], axis=AX.X, op=ALU.add))
            E("v", lambda v: v.drain())
            sq_accum(xnt[:, :], 3, [(sV, vxn)])
            E("v", lambda v: v.tensor_scalar(
                out=stat[:, 6:7], in0=stat[:, 0:1], scalar1=INV_D, scalar2=0.0,
                op0=ALU.mult, op1=ALU.bypass))
            E("v", lambda v: v.drain())
            E("v", lambda v: v.tensor_scalar(
                out=stat[:, 3:4], in0=stat[:, 3:4], scalar1=INV_D, scalar2=0.0,
                op0=ALU.mult, op1=ALU.bypass))
            E("v", lambda v: v.drain())
            E("v", lambda v: v.scalar_tensor_tensor(
                out=stat[:, 4:5], in0=stat[:, 6:7], scalar=stat[:, 6:7],
                in1=stat[:, 3:4], op0=ALU.mult, op1=ALU.subtract))
            E("v", lambda v: v.drain())
            E("v", lambda v: v.tensor_scalar(
                out=stat[:, 4:5], in0=stat[:, 4:5], scalar1=-1.0, scalar2=1e-12,
                op0=ALU.mult, op1=ALU.add), inc=(sV, 1))
            sr2 = rsqrt_nl(4, stat[:, 7:8], [(sV, sV.n)])
            E("v", lambda v: v.tensor_scalar(
                out=yt[:, :], in0=xnt[:, :], scalar1=stat[:, 6:7],
                scalar2=stat[:, 7:8], op0=ALU.subtract, op1=ALU.mult),
              waits=[(sS, sr2)])
            E("v", lambda v: v.tensor_tensor(
                out=yt[:, :], in0=yt[:, :], in1=lnw_b[:, :], op=ALU.mult))
            E("v", lambda v: v.tensor_tensor(
                out=yt[:, :], in0=yt[:, :], in1=lnb_b[:, :], op=ALU.add),
              inc=(sV, 1))
            vy = sV.n
            sq_accum(yt[:, :], 3, [(sV, vy)])
            E("s", lambda s: s.activation(
                out=stat[:, 5:6], in_=stat[:, 3:4], func=AF.Ln, bias=0.0, scale=1.0),
              waits=[(sV, sV.n)])
            E("s", lambda s: s.activation(
                out=stat[:, 4:5], in_=stat[:, 5:6], func=AF.Exp, bias=0.0, scale=0.5),
              inc=(sS, 1))
            E("v", lambda v: v.tensor_scalar(
                out=stat[:, 4:5], in0=stat[:, 4:5], scalar1=1e-12, scalar2=0.0,
                op0=ALU.max, op1=ALU.bypass), waits=[(sS, sS.n)])
            E("v", lambda v: v.drain())
            E("v", lambda v: v.reciprocal(out=stat[:, 5:6], in_=stat[:, 4:5]))
            E("v", lambda v: v.drain())
            E("v", lambda v: v.tensor_scalar(
                out=qtile[:, :], in0=yt[:, :], scalar1=stat[:, 5:6], scalar2=0.0,
                op0=ALU.mult, op1=ALU.bypass), inc=(sV, 1))
            E("v", lambda v: v.drain())
            vq = sV.n
            for c in range(8):
                E("t", lambda t, c=c: t.transpose(
                    out=ps_qt[:, c * P:(c + 1) * P], in_=qtile[:, c * P:(c + 1) * P],
                    identity=ident_b[:, :]),
                  waits=[(sV, vq), (sV, st["qt_evac"])] if c == 0 else [],
                  inc=(sT, 1) if c == 7 else None)
            tq = sT.n
            E("v", lambda v, ig=ig, tt=tt: v.tensor_copy(
                out=QT[ig][:, :].rearrange("p (h q) -> p h q", h=8)[:, :, tt * P:(tt + 1) * P],
                in_=ps_qt[:, :].rearrange("p (h q) -> p h q", h=8)),
              waits=[(sT, tq), (sT, st["qt_free"][ig]) if tt == 0 else (sT, 0)],
              inc=(sV, 1))
            st["qt_evac"] = sV.n
        E("v", lambda v: v.drain(), inc=(sV, 1))
        st["xg_free"][ig] = sV.n
        st["qt_built"][ig] = sV.n

    def mm1_half(gq, half):
        ig = gq % 2
        for kk in range(HB):
            k = half * HB + kk
            i2 = kk % 3
            for hc in range(8):
                E("t", lambda t, k=k, hc=hc, i2=i2, ig=ig: t.matmul(
                    out=ps_sT[i2][:, :],
                    lhsT=KT[:, hc * I0 + k * P: hc * I0 + (k + 1) * P],
                    rhs=QT[ig][:, hc * GRP:(hc + 1) * GRP],
                    start=(hc == 0), stop=(hc == 7)),
                  waits=[(sV, st["sT_free"][i2]), (sV, st["qt_built"][ig])]
                        if hc == 0 else [],
                  inc=(sT, 1) if hc == 7 else None)
            t1 = sT.n
            E("v", lambda v, kk=kk, i2=i2: v.tensor_scalar(
                out=sstage[:, kk * GRP:(kk + 1) * GRP], in0=ps_sT[i2][:, :],
                scalar1=CLIP_LO, scalar2=CLIP_HI, op0=ALU.max, op1=ALU.min),
              waits=[(sT, t1)], inc=(sV, 1))
            st["sT_free"][i2] = sV.n

    def bchain_half(half):
        ih = half
        E("v", lambda v: v.drain(), inc=(sV, 1))
        vclip = sV.n
        E("g", lambda g: g.tensor_tensor(
            out=aux[:, :], in0=sstage[:, :], in1=sstage[:, :], op=ALU.mult),
          waits=[(sV, vclip), (sS, st["aux_free"])], inc=(sG, 1))
        gp = sG.n
        E("s", lambda s: s.activation(
            out=aux[:, :], in_=aux[:, :], func=AF.Ln, bias=1.0, scale=-1.0),
          waits=[(sG, gp)])
        E("s", lambda s: s.activation(
            out=aux[:, :], in_=aux[:, :], func=AF.Exp, bias=0.0, scale=-0.5),
          inc=(sS, 1))
        se = sS.n
        E("v", lambda v: v.tensor_tensor(
            out=aux[:, :], in0=sstage[:, :], in1=aux[:, :], op=ALU.mult),
          waits=[(sS, se)], inc=(sV, 1))
        E("v", lambda v: v.drain())
        va = sV.n
        E("s", lambda s: s.activation(
            out=aux[:, :], in_=aux[:, :], func=AF.Arctan, bias=0.0, scale=1.0),
          waits=[(sV, va)])
        E("s", lambda s: s.activation(
            out=aux[:, :], in_=aux[:, :], func=AF.Ln, bias=kc[:, 2:3], scale=INV_PI))
        E("s", lambda s, ih=ih: s.activation(
            out=Bst[ih][:, :], in_=aux[:, :], func=AF.Exp, bias=0.0, scale=9.0),
          waits=[(sT, max(st["bst_free"]))], inc=(sS, 1))
        st["aux_free"] = sS.n
        return sS.n

    def mm2_half(gq, half, sb_done):
        ih = half
        for kk in range(HB):
            k = half * HB + kk
            i2 = kk % 2
            E("y", lambda y, k=k, i2=i2: y.dma_start(
                out=qwk[i2][:, :], in_=dqwbf[k * P:(k + 1) * P, :]),
              waits=[(sT, st["qwk_free"][i2]), (sQW, qw_cached if gq == 0 else 0)],
              inc16=sLD)
            qld = sLD.n
            for tt in range(2):
                for hf in range(2):
                    first = (tt == 0 and hf == 0)
                    E("t", lambda t, kk=kk, tt=tt, hf=hf, i2=i2, ih=ih, k=k: t.matmul(
                        out=ps_X[tt * 2 + hf][:, :],
                        lhsT=Bst[ih][:, kk * GRP + tt * P: kk * GRP + (tt + 1) * P],
                        rhs=qwk[i2][:, hf * 512:(hf + 1) * 512],
                        start=(k == 0), stop=(k == NC0 - 1)),
                      waits=([(sS, sb_done), (sLD, qld)] +
                             ([(sV, st["xfin_v"])] if k == 0 else []))
                            if first else [],
                      inc=(sT, 1) if (tt == 1 and hf == 1) else None)
            st["qwk_free"][i2] = sT.n
        st["bst_free"][ih] = sT.n

    def x_finalize(gq):
        tdone = sT.n
        for tt in range(2):
            tile = gq * 2 + tt
            pa, pb = ps_X[tt * 2], ps_X[tt * 2 + 1]
            E("s", lambda s, pa=pa: s.activation(
                out=sq_scr[:, 0:512], in_=pa[:, :], func=AF.Square,
                bias=0.0, scale=1.0),
              waits=[(sT, tdone), (sV, sV.n)])
            E("s", lambda s, pb=pb: s.activation(
                out=sq_scr[:, 512:1024], in_=pb[:, :], func=AF.Square,
                bias=0.0, scale=1.0), inc=(sS, 1))
            sx = sS.n
            E("v", lambda v: v.tensor_reduce(
                out=stat[:, 3:4], in_=sq_scr[:, :], axis=AX.X, op=ALU.add),
              waits=[(sS, sx)], inc=(sV, 1))
            E("s", lambda s: s.activation(
                out=stat[:, 5:6], in_=stat[:, 3:4], func=AF.Ln, bias=0.0, scale=1.0),
              waits=[(sV, sV.n)])
            E("s", lambda s: s.activation(
                out=stat[:, 4:5], in_=stat[:, 5:6], func=AF.Exp, bias=0.0, scale=0.5),
              inc=(sS, 1))
            E("v", lambda v: v.tensor_scalar(
                out=stat[:, 4:5], in0=stat[:, 4:5], scalar1=1e-12, scalar2=0.0,
                op0=ALU.max, op1=ALU.bypass), waits=[(sS, sS.n)])
            E("v", lambda v: v.drain())
            E("v", lambda v: v.reciprocal(out=stat[:, 5:6], in_=stat[:, 4:5]))
            E("v", lambda v: v.drain())
            E("v", lambda v, pa=pa, tt=tt: v.scalar_tensor_tensor(
                out=Xout[tt][:, 0:512], in0=pa[:, :], scalar=stat[:, 5:6],
                in1=bias1_b[:, 0:512], op0=ALU.mult, op1=ALU.add),
              waits=[(sA0, max(st["xout_free"]))])
            E("v", lambda v, pb=pb, tt=tt: v.scalar_tensor_tensor(
                out=Xout[tt][:, 512:1024], in0=pb[:, :], scalar=stat[:, 5:6],
                in1=bias1_b[:, 512:1024], op0=ALU.mult, op1=ALU.add),
              inc=(sV, 1))
            vf = sV.n
            E("s", lambda s, tile=tile, tt=tt: s.dma_start(
                out=dattn0[tile * P:(tile + 1) * P, :], in_=Xout[tt][:, :]),
              waits=[(sV, vf)], inc16=sA0)
            st["xout_free"][tt] = sA0.n
            st["attn0_sp"][tile] = sA0.n
        st["xfin_v"] = sV.n

    q_build(0)
    for gq in range(NG):
        mm1_half(gq, 0)
        sb0 = bchain_half(0)
        mm2_half(gq, 0, sb0)
        if gq + 1 < NG:
            q_build(gq + 1)
        mm1_half(gq, 1)
        sb1 = bchain_half(1)
        mm2_half(gq, 1, sb1)
        st["qt_free"][gq % 2] = sT.n
        x_finalize(gq)

    if upto == "dbgp1":
        E("v", lambda v: v.tensor_copy(out=aux[:, :], in_=sstage[:, :]), inc=(sV, 1))
        fw = [(sV, sV.n), (sS, sS.n), (sT, sT.n)]
        def dump(row0, src_ap, eng="y"):
            E(eng, lambda e, r=row0, a=src_ap: e.dma_start(
                out=dout[r * P:(r + 1) * P, 0:a.shape[1]], in_=a), waits=fw, inc16=sOUT)
        dump(0, KT[:, 3072:4096], "g")
        dump(1, QT[1][:, 0:1024], "g")
        dump(2, QT[1][:, 1024:2048], "g")
        E("y", lambda y: y.dma_start(out=dout[3 * P:4 * P, 0:NT], in_=m_all[:, :]),
          waits=fw, inc16=sOUT)
        E("y", lambda y: y.dma_start(out=dout[3 * P:4 * P, NT:2 * NT], in_=rstd_all[:, :]),
          waits=fw, inc16=sOUT)
        dump(4, qtile[:, :], "g")
        dump(5, yt[:, :])
        for i in range(4):
            dump(6 + i, sstage[:, i * 1024:(i + 1) * 1024])
        for i in range(4):
            dump(10 + i, Bst[1][:, i * 1024:(i + 1) * 1024], "g")
        for i in (2, 3):
            dump(12 + i, aux[:, i * 1024:(i + 1) * 1024])
        E("g", lambda g: g.nop(), waits=[(sOUT, sOUT.n)])
        with nc.Block() as block:
            @block.sync
            def _(y):
                b.sched.replay("y", y)

            @block.vector
            def _(v):
                b.sched.replay("v", v)

            @block.scalar
            def _(s):
                b.sched.replay("s", s)

            @block.tensor
            def _(t):
                b.sched.replay("t", t)

            @block.gpsimd
            def _(g):
                b.sched.replay("g", g)
        return nc

    b.free("xg0", "xg1", "QT0", "QT1", "qwk0", "qwk1", "sstage", "aux",
           "Bst0", "xnt", "yt", "qtile", "Xout0",
           "ps_sT0", "ps_sT1", "ps_sT2", "ps_X0", "ps_X1", "ps_X2", "ps_X3", "ps_qt",
           "KT")

    WrB = b.sb("WrB", [P, 8 * D], BF16)
    WkB = b.sb("WkB", [P, 8 * D], BF16)
    WvB = b.sb("WvB", [P, 8 * D], BF16)
    x1t = [b.sb(f"x1t{i}", [P, D], F32) for i in range(2)]
    xstr = [b.sb(f"xstr{i}", [P, D], F32) for i in range(2)]
    a0str = [b.sb(f"a0str{i}", [P, D], F32) for i in range(2)]
    xn_p2 = b.sb("xn_p2", [P, D], F32)
    xnTP = [b.sb(f"xnTP{i}", [P, 8 * 129], F32) for i in range(2)]
    dTt = b.sb("dTt", [P, D], F32)
    mixtmp = b.sb("mixtmp", [P, D], F32)
    xrT = b.sb("xrT", [P, D], BF16)
    xkT = b.sb("xkT", [P, D], BF16)
    r_tile = b.sb("r_tile", [P, D], F32)
    kact_f = b.sb("kact_f", [P, D], F32)
    kact_b = b.sb("kact_b", [P, D], BF16)
    kactT = b.sb("kactT", [P, D], BF16)
    rv = b.sb("rv", [P, D], F32)
    xn2f = [b.sb(f"xn2f{i}", [P, D], F32) for i in range(2)]
    ps_xnT = b.ps("ps_xnT", [P, D], F32)
    ps_mm = [b.ps(f"ps_mm{i}", [P, 512], F32) for i in range(2)]
    ps_kT = b.ps("ps_kT", [P, D], BF16)

    p1_v, p1_s, p1_t = sV.n, sS.n, sT.n
    p1_fence = [(sV, p1_v), (sS, p1_s), (sT, p1_t)]
    wB_ld = []
    for src, dst in [(dwrT, WrB), (dwkT, WkB), (dwvT, WvB)]:
        E("g", lambda g, s=src, d=dst: g.dma_start(
            out=d[:, :].rearrange("p (c j) -> p c j", c=8),
            in_=s.rearrange("(c p) j -> p c j", p=P)),
          waits=p1_fence, inc16=sGD)
        wB_ld.append(sGD.n)

    def mm_1024(ps_pair, lhsT_buf, W_buf, wait_first, done_inc=True):
        for sl in range(2):
            for cc in range(8):
                E("t", lambda t, sl=sl, cc=cc, lb=lhsT_buf, wb=W_buf: t.matmul(
                    out=ps_pair[sl][:, :], lhsT=lb[:, cc * P:(cc + 1) * P],
                    rhs=wb[:, cc * D + sl * 512: cc * D + (sl + 1) * 512],
                    start=(cc == 0), stop=(cc == 7)),
                  waits=wait_first if (sl == 0 and cc == 0) else [],
                  inc=(sT, 1) if (cc == 7) else None)

    p2 = {"x_free": [0, 0], "a0_free": [0, 0], "x1_sp": [0] * NT,
          "xn2_sp": [0] * NT, "xn2T_evac": 0,
          "ssq_done": [0] * NT, "gred_done": [0] * NT}
    prevten = [0]

    for tile in range(NT):
        i2 = tile % 2
        cur, prv = xnTP[tile % 2], xnTP[1 - tile % 2]
        E("y", lambda y, tile=tile, i2=i2: y.dma_start(
            out=xstr[i2][:, :], in_=dxin[tile * P:(tile + 1) * P, :]),
          waits=[(sV, p2["x_free"][i2])] + (p1_fence if tile < 2 else []), inc16=sLD)
        xld = sLD.n
        E("y", lambda y, tile=tile, i2=i2: y.dma_start(
            out=a0str[i2][:, :], in_=dattn0[tile * P:(tile + 1) * P, :]),
          waits=[(sA0, sA0.n), (sV, p2["a0_free"][i2])], inc16=sLD)
        ald = sLD.n
        E("v", lambda v, tile=tile, i2=i2: v.tensor_scalar(
            out=xn_p2[:, :], in0=xstr[i2][:, :], scalar1=m_all[:, tile:tile + 1],
            scalar2=rstd_all[:, tile:tile + 1], op0=ALU.subtract, op1=ALU.mult),
          waits=[(sLD, xld), (sV, sV.n), (sT, prevten[0])], inc=(sV, 1))
        E("v", lambda v: v.drain())
        vxn = sV.n
        for c in range(8):
            E("t", lambda t, c=c: t.transpose(
                out=ps_xnT[:, c * P:(c + 1) * P], in_=xn_p2[:, c * P:(c + 1) * P],
                identity=ident_f[:, :]),
              waits=[(sV, vxn)] if c == 0 else [],
              inc=(sT, 1) if c == 7 else None)
        txn = sT.n
        E("v", lambda v, cur=cur: v.tensor_copy(
            out=cur[:, :].rearrange("p (h c) -> p h c", h=8)[:, :, 1:129],
            in_=ps_xnT[:, :].rearrange("p (h c) -> p h c", h=8)),
          waits=[(sT, txn)], inc=(sV, 1))
        if tile == 0:
            E("v", lambda v, cur=cur: v.memset(
                cur[:, :].rearrange("p (h c) -> p h c", h=8)[:, :, 0:1], 0.0))
        else:
            E("v", lambda v, cur=cur, prv=prv: v.tensor_copy(
                out=cur[:, :].rearrange("p (h c) -> p h c", h=8)[:, :, 0:1],
                in_=prv[:, :].rearrange("p (h c) -> p h c", h=8)[:, :, 128:129]))
        cur3 = cur[:, :].rearrange("p (h c) -> p h c", h=8)
        cx = cur3[:, :, 1:129]
        cp = cur3[:, :, 0:128]
        d3 = dTt[:, :].rearrange("p (h c) -> p h c", h=8)
        m3 = mixtmp[:, :].rearrange("p (h c) -> p h c", h=8)
        E("g", lambda g, cx=cx, cp=cp, d3=d3: g.tensor_tensor(
            out=d3, in0=cx, in1=cp, op=ALU.subtract),
          waits=[(sV, sV.n), (sGD, wB_ld[2])], inc=(sG, 1))
        gd = sG.n
        E("v", lambda v, d3=d3, m3=m3: v.tensor_tensor(
            out=m3, in0=d3, in1=mur_sb[:, :].unsqueeze(2).to_broadcast((P, 8, P)),
            op=ALU.mult), waits=[(sG, gd), (sLD, mu_ld), (sT, prevten[0])])
        E("v", lambda v, m3=m3, cp=cp: v.tensor_tensor(
            out=xrT[:, :].rearrange("p (h c) -> p h c", h=8), in0=m3, in1=cp,
            op=ALU.add), inc=(sV, 1))
        E("v", lambda v: v.drain())
        vxr = sV.n
        E("v", lambda v, d3=d3, m3=m3: v.tensor_tensor(
            out=m3, in0=d3, in1=muk_sb[:, :].unsqueeze(2).to_broadcast((P, 8, P)),
            op=ALU.mult))
        E("v", lambda v, m3=m3, cp=cp: v.tensor_tensor(
            out=xkT[:, :].rearrange("p (h c) -> p h c", h=8), in0=m3, in1=cp,
            op=ALU.add), inc=(sV, 1))
        E("v", lambda v: v.drain())
        vxk = sV.n
        mm_1024(ps_mm, xrT, WrB, [(sV, vxr), (sS, sS.n), (sV, sV.n)])
        tr = sT.n
        E("s", lambda s: s.activation(
            out=r_tile[:, 0:512], in_=ps_mm[0][:, :], func=AF.Sigmoid,
            bias=0.0, scale=1.0), waits=[(sT, tr - 1)])
        E("s", lambda s: s.activation(
            out=r_tile[:, 512:1024], in_=ps_mm[1][:, :], func=AF.Sigmoid,
            bias=0.0, scale=1.0), waits=[(sT, tr)], inc=(sS, 1))
        sr = sS.n
        mm_1024(ps_mm, xkT, WkB, [(sV, vxk), (sS, sr)])
        tk = sT.n
        E("v", lambda v: v.tensor_scalar(
            out=kact_f[:, 0:512], in0=ps_mm[0][:, :], scalar1=0.0, scalar2=0.0,
            op0=ALU.max, op1=ALU.bypass), waits=[(sT, tk - 1)])
        E("v", lambda v: v.tensor_scalar(
            out=kact_f[:, 512:1024], in0=ps_mm[1][:, :], scalar1=0.0, scalar2=0.0,
            op0=ALU.max, op1=ALU.bypass), waits=[(sT, tk)])
        E("v", lambda v: v.tensor_tensor(
            out=kact_b[:, :], in0=kact_f[:, :], in1=kact_f[:, :], op=ALU.mult),
          inc=(sV, 1))
        E("v", lambda v: v.drain())
        vka = sV.n
        for c in range(8):
            E("t", lambda t, c=c: t.transpose(
                out=ps_kT[:, c * P:(c + 1) * P], in_=kact_b[:, c * P:(c + 1) * P],
                identity=ident_b[:, :]),
              waits=[(sV, vka)] if c == 0 else [],
              inc=(sT, 1) if c == 7 else None)
        tkt = sT.n
        E("v", lambda v: v.tensor_copy(out=kactT[:, :], in_=ps_kT[:, :]),
          waits=[(sT, tkt)], inc=(sV, 1))
        E("v", lambda v: v.drain())
        vkt = sV.n
        mm_1024(ps_mm, kactT, WvB, [(sV, vkt)])
        tv = sT.n
        E("v", lambda v: v.scalar_tensor_tensor(
            out=rv[:, 0:512], in0=ps_mm[0][:, :], scalar=1.0,
            in1=r_tile[:, 0:512], op0=ALU.mult, op1=ALU.mult),
          waits=[(sT, tv - 1)])
        E("v", lambda v: v.scalar_tensor_tensor(
            out=rv[:, 512:1024], in0=ps_mm[1][:, :], scalar=1.0,
            in1=r_tile[:, 512:1024], op0=ALU.mult, op1=ALU.mult),
          waits=[(sT, tv)], inc=(sV, 1))
        prevten[0] = tv
        E("g", lambda g, i2=i2: g.tensor_tensor(
            out=mixtmp[:, :], in0=xstr[i2][:, :], in1=a0str[i2][:, :], op=ALU.add),
          waits=[(sLD, ald), (sV, sV.n)], inc=(sG, 1))
        ga = sG.n
        E("v", lambda v, tile=tile, i2=i2: v.tensor_tensor(
            out=x1t[i2][:, :], in0=mixtmp[:, :], in1=rv[:, :],
            op=ALU.add),
          waits=[(sG, ga), (sACT, p2["x1_sp"][tile - 2] if tile >= 2 else 0),
                 (sS, p2["ssq_done"][tile - 2] if tile >= 2 else 0)],
          inc=(sV, 1))
        vx1 = sV.n
        p2["x_free"][i2] = vx1
        p2["a0_free"][i2] = vx1
        E("s", lambda s, tile=tile, i2=i2: s.dma_start(
            out=dx1[tile * P:(tile + 1) * P, :], in_=x1t[i2][:, :]),
          waits=[(sV, vx1)], inc16=sACT)
        p2["x1_sp"][tile] = sACT.n
        E("v", lambda v, tile=tile, i2=i2: v.tensor_reduce(
            out=m2_all[:, tile:tile + 1], in_=x1t[i2][:, :], axis=AX.X,
            op=ALU.add), waits=[(sV, vx1)], inc=(sV, 1))
        E("v", lambda v: v.drain())
        E("s", lambda s, i2=i2: s.activation(
            out=sq_scr[:, :], in_=x1t[i2][:, :], func=AF.Square, bias=0.0,
            scale=1.0), waits=[(sV, sV.n)], inc=(sS, 1))
        ssq = sS.n
        E("v", lambda v, tile=tile: v.tensor_reduce(
            out=ss2_all[:, tile:tile + 1], in_=sq_scr[:, :], axis=AX.X,
            op=ALU.add), waits=[(sS, ssq)], inc=(sV, 1))
        E("v", lambda v: v.drain())
        p2["ssq_done"][tile] = ssq
        p2["gred_done"][tile] = 0

    E("v", lambda v: v.drain(), waits=[(sG, sG.n), (sV, sV.n)])
    E("v", lambda v: v.tensor_scalar(
        out=m2_all[:, :], in0=m2_all[:, :], scalar1=INV_D, scalar2=0.0,
        op0=ALU.mult, op1=ALU.bypass))
    E("v", lambda v: v.tensor_scalar(
        out=ss2_all[:, :], in0=ss2_all[:, :], scalar1=INV_D, scalar2=0.0,
        op0=ALU.mult, op1=ALU.bypass))
    E("v", lambda v: v.drain())
    E("v", lambda v: v.tensor_tensor(
        out=rstd2_all[:, :], in0=m2_all[:, :], in1=m2_all[:, :], op=ALU.mult))
    E("v", lambda v: v.drain())
    E("v", lambda v: v.tensor_tensor(
        out=rstd2_all[:, :], in0=ss2_all[:, :], in1=rstd2_all[:, :],
        op=ALU.subtract), inc=(sV, 1))
    E("s", lambda s: s.activation(
        out=rstd2_all[:, :], in_=rstd2_all[:, :], func=AF.Ln, bias=kc[:, 0:1], scale=1.0),
      waits=[(sV, sV.n)])
    E("s", lambda s: s.activation(
        out=rstd2_all[:, :], in_=rstd2_all[:, :], func=AF.Exp, bias=0.0, scale=-0.5),
      inc=(sS, 1))
    ln2fin = sS.n

    for tile in range(NT):
        i2 = tile % 2
        E("y", lambda y, tile=tile, i2=i2: y.dma_start(
            out=xstr[i2][:, :], in_=dx1[tile * P:(tile + 1) * P, :]),
          waits=[(sACT, sACT.n), (sV, p2["x_free"][i2])], inc16=sLD)
        x1ld = sLD.n
        E("v", lambda v, tile=tile, i2=i2: v.tensor_scalar(
            out=xn2f[i2][:, :], in0=xstr[i2][:, :],
            scalar1=m2_all[:, tile:tile + 1], scalar2=rstd2_all[:, tile:tile + 1],
            op0=ALU.subtract, op1=ALU.mult),
          waits=[(sS, ln2fin), (sLD, x1ld),
                 (sACT, p2["xn2_sp"][tile - 2] if tile >= 2 else 0)],
          inc=(sV, 1))
        E("v", lambda v: v.drain())
        p2["x_free"][i2] = sV.n
        vx = sV.n
        E("s", lambda s, tile=tile, i2=i2: s.dma_start(
            out=dxn2[tile * P:(tile + 1) * P, :], in_=xn2f[i2][:, :]),
          waits=[(sV, vx)], inc16=sACT)
        p2["xn2_sp"][tile] = sACT.n
        E("v", lambda v, i2=i2: v.tensor_reduce(
            out=stat[:, 6:7], in_=xn2f[i2][:, :], axis=AX.X, op=ALU.max,
            apply_absolute_value=True))
        E("v", lambda v: v.drain())
        E("v", lambda v: v.tensor_tensor(
            out=runmax[:, 0:1], in0=runmax[:, 0:1], in1=stat[:, 6:7], op=ALU.max),
          inc=(sV, 1))
        E("v", lambda v: v.drain())
        for c in range(8):
            E("t", lambda t, c=c, i2=i2: t.transpose(
                out=ps_xnT[:, c * P:(c + 1) * P], in_=xn2f[i2][:, c * P:(c + 1) * P],
                identity=ident_f[:, :]),
              waits=[(sV, vx), (sV, p2["xn2T_evac"])] if c == 0 else [],
              inc=(sT, 1) if c == 7 else None)
        tt2 = sT.n
        E("v", lambda v, tile=tile: v.tensor_copy(
            out=xn2T[:, :].rearrange("p (h c) -> p h c", h=8)[:, :, tile * P:(tile + 1) * P],
            in_=ps_xnT[:, :].rearrange("p (h c) -> p h c", h=8)),
          waits=[(sT, tt2)], inc=(sV, 1))
        p2["xn2T_evac"] = sV.n
    E("v", lambda v: v.drain(), inc=(sV, 1))
    xn2T_done = sV.n

    E("g", lambda g: g.dma_start(out=cc1i[:, :], in_=runmax[:, 0:1]),
      waits=[(sV, sV.n)], inc16=sGD)
    E("g", lambda g: g.collective_compute(
        "AllReduce", ALU.max, replica_groups=[list(range(8))],
        ins=[cc1i.ap().opt()], outs=[cc1o.ap().opt()]),
      waits=[(sGD, sGD.n)], inc=(sCC, 1))
    ar1 = sCC.n

    b.free("WrB", "WkB", "WvB", "xstr0", "xstr1", "a0str0", "a0str1",
           "xn_p2", "xnTP0", "xnTP1", "dTt", "mixtmp", "xrT", "xkT",
           "r_tile", "kact_f", "kact_b", "kactT", "rv", "xn2f0", "xn2f1",
           "ps_xnT", "ps_mm0", "ps_mm1", "ps_kT", "x1t0", "x1t1")

    if upto == "p1":
        for tile in range(NT):
            E("y", lambda y, tile=tile: y.dma_start(
                out=dout[tile * P:(tile + 1) * P, :],
                in_=dattn0[tile * P:(tile + 1) * P, :]),
              waits=[(sA0, st["attn0_sp"][tile])], inc16=sOUT)
        E("y", lambda y: y.nop(), waits=[(sOUT, sOUT.n)])
        with nc.Block() as block:
            @block.sync
            def _(y):
                b.sched.replay("y", y)

            @block.vector
            def _(v):
                b.sched.replay("v", v)

            @block.scalar
            def _(s):
                b.sched.replay("s", s)

            @block.tensor
            def _(t):
                b.sched.replay("t", t)

            @block.gpsimd
            def _(g):
                b.sched.replay("g", g)
        return nc

    if upto == "l0":
        for tile in range(NT):
            E("y", lambda y, tile=tile: y.dma_start(
                out=dout[tile * P:(tile + 1) * P, :],
                in_=dx1[tile * P:(tile + 1) * P, :]),
              waits=[(sACT, p2["x1_sp"][tile]), (sV, xn2T_done), (sCC, ar1)],
              inc16=sOUT)
        E("y", lambda y: y.nop(), waits=[(sOUT, sOUT.n)])
        with nc.Block() as block:
            @block.sync
            def _(y):
                b.sched.replay("y", y)

            @block.vector
            def _(v):
                b.sched.replay("v", v)

            @block.scalar
            def _(s):
                b.sched.replay("s", s)

            @block.tensor
            def _(t):
                b.sched.replay("t", t)

            @block.gpsimd
            def _(g):
                b.sched.replay("g", g)
        return nc


    p2e = [(sV, sV.n), (sS, sS.n), (sT, sT.n)]
    b.free("sq_scr", "vrow")
    qT = b.sb("qT", [P, 8 * T], BF16)
    kTt = b.sb("kTt", [P, 8 * T], BF16)
    vst = b.sb("vst", [P, NT * 1040], BF16)
    Wb = [b.sb(f"Wb{i}", [P, 8 * D], BF16) for i in range(2)]
    PT = [b.sb(f"PT{i}", [P, 512], BF16) for i in range(2)]
    accsb = b.sb("accsb", [P, 512], F32)
    rrow = b.sb("rrow", [P, 512], F32)
    aoT = [b.sb(f"aoT{i}", [P, 8 * 512], BF16) for i in range(2)]
    opst = [b.sb("opst0", [P, D], F32)] * 2
    ps_s4 = [b.ps(f"ps_s4{i}", [P, 512], F32) for i in range(2)]
    ps_acc = [b.ps(f"ps_acc{i}", [P, 512], F32) for i in range(2)]
    ps_rb = b.ps("ps_rb", [P, 512], F32)
    ps_op = [b.ps(f"ps_op{i}", [P, 512], F32) for i in range(2)]

    def wload(dst, srcap, waits):
        E("g", lambda g, s=srcap, d=dst: g.dma_start(
            out=d[:, :].rearrange("p (c j) -> p c j", c=8),
            in_=s.rearrange("(c p) j -> p c j", p=P)),
          waits=waits, inc16=sGD)
        return sGD.n

    wq_ld = wload(Wb[0], dl1q, p2e)
    wk_ld = wload(Wb[1], dl1k, p2e)
    E("g", lambda g: g.memset(
        vst[:, :].rearrange("p (a c) -> p a c", c=65)[:, :, 64:65], 1.0),
      waits=p2e, inc=(sG, 1))
    vones = sG.n

    mmb_free = [0, 0]
    def proj_T(dstT, wld, wbuf):
        for dqc in range(8):
            for tg in range(4):
                bank = (dqc * 4 + tg) % 2
                for ec in range(8):
                    E("t", lambda t, dqc=dqc, tg=tg, ec=ec, bank=bank, wld=wld, wbuf=wbuf: t.matmul(
                        out=ps_s4[bank][:, :],
                        lhsT=wbuf[:, ec * D + dqc * P: ec * D + (dqc + 1) * P],
                        rhs=xn2T[:, ec * T + tg * 512:(ec * T + (tg + 1) * 512)],
                        start=(ec == 0), stop=(ec == 7)),
                      waits=[(sGD, wld), (sV, xn2T_done), (sS, mmb_free[bank])]
                            if ec == 0 else [],
                      inc=(sT, 1) if ec == 7 else None)
                tq = sT.n
                E("s", lambda s, dqc=dqc, tg=tg, bank=bank, dstT=dstT: s.activation(
                    out=dstT[:, dqc * T + tg * 512: dqc * T + (tg + 1) * 512],
                    in_=ps_s4[bank][:, :], func=AF.Copy, bias=0.0, scale=1.0),
                  waits=[(sT, tq)], inc=(sS, 1))
                mmb_free[bank] = sS.n
    proj_T(qT, wq_ld, Wb[0])
    proj_T(kTt, wk_ld, Wb[1])
    qk_done_t = sT.n
    wv_ld = wload(Wb[0], dl1v, [(sT, qk_done_t)])
    for tt in range(NT):
        for es in range(2):
            bank = (tt * 2 + es) % 2
            for ec in range(8):
                E("t", lambda t, tt=tt, es=es, ec=ec, bank=bank: t.matmul(
                    out=ps_s4[bank][:, :],
                    lhsT=xn2T[:, ec * T + tt * P: ec * T + (tt + 1) * P],
                    rhs=Wb[0][:, ec * D + es * 512: ec * D + (es + 1) * 512],
                    start=(ec == 0), stop=(ec == 7)),
                  waits=[(sGD, wv_ld), (sS, mmb_free[bank])] if ec == 0 else [],
                  inc=(sT, 1) if ec == 7 else None)
            tv = sT.n
            E("s", lambda s, tt=tt, es=es, bank=bank: s.activation(
                out=vst[:, tt * 1040 + es * 520: tt * 1040 + (es + 1) * 520]
                    .rearrange("p (h d) -> p h d", d=65)[:, :, 0:64],
                in_=ps_s4[bank][:, :].rearrange("p (h d) -> p h d", d=64),
                func=AF.Copy, bias=0.0, scale=1.0),
              waits=[(sT, tv), (sG, vones)], inc=(sS, 1))
            mmb_free[bank] = sS.n
    v_done_s = sS.n
    wo_ld = wload(Wb[1], dl1o, [(sT, qk_done_t)])

    att = {"sfree": [mmb_free[0], mmb_free[1]], "ptfree": [0, 0], "accfree": [0, 0],
           "rbfree": 0, "aofree": [0, 0], "opfree": [0, 0],
           "opstfree": [0, 0]}
    attnP_sp = [0] * NT
    for qg in range(4):
        cur = qg % 2
        for h in range(16):
            hc, off = h // 2, 64 * (h % 2)
            ia = h % 2
            pieces = [(4 * qg, 0, 512, True)]
            for kt in range(4 * qg):
                pieces.append((kt, 0, 512, False))
            for jj in range(1, 4):
                pieces.append((4 * qg + jj, 128 * jj, 128, True))
                W = 512 - 128 * (jj + 1)
                if W > 0:
                    pieces.append((4 * qg + jj, 128 * (jj + 1), W, False))
            n = len(pieces)
            meta = []
            def emit_smm(i):
                kt, q0, qw, msk = pieces[i]
                bank = i % 2
                w0 = [(sS, att["sfree"][bank])]
                if i == 0:
                    w0 += [(sS, att["accfree"][ia]), (sS, v_done_s)]
                E("t", lambda t, kt=kt, q0=q0, qw=qw, bank=bank, hc=hc, off=off, qg=qg: t.matmul(
                    out=ps_s4[bank][:, 0:qw],
                    lhsT=kTt[off:off + 64, hc * T + kt * P: hc * T + (kt + 1) * P],
                    rhs=qT[off:off + 64, hc * T + qg * 512 + q0: hc * T + qg * 512 + q0 + qw],
                    start=True, stop=True),
                  waits=w0, inc=(sT, 1))
                tS = sT.n
                E("s", lambda s, qw=qw, bank=bank: s.activation(
                    out=PT[bank][:, 0:qw], in_=ps_s4[bank][:, 0:qw], func=AF.Exp,
                    bias=0.0, scale=0.125),
                  waits=[(sT, tS), (sT, att["ptfree"][bank])], inc=(sS, 1))
                eS = sS.n
                att["sfree"][bank] = eS
                mV = 0
                if msk:
                    E("v", lambda v, bank=bank: v.tensor_tensor(
                        out=PT[bank][:, 0:128], in0=PT[bank][:, 0:128],
                        in1=tri[:, :], op=ALU.mult),
                      waits=[(sS, eS)], inc=(sV, 1))
                    E("v", lambda v: v.drain())
                    mV = sV.n
                meta.append((bank, eS, mV))
            def emit_av(i):
                kt, q0, qw, msk = pieces[i]
                bank, eS, mV = meta[i]
                w0 = [(sS, eS)]
                if mV:
                    w0.append((sV, mV))
                E("t", lambda t, kt=kt, q0=q0, qw=qw, bank=bank, h=h, ia=ia: t.matmul(
                    out=ps_acc[ia][0:65, q0:q0 + qw],
                    lhsT=vst[:, kt * 1040 + h * 65: kt * 1040 + (h + 1) * 65],
                    rhs=PT[bank][:, 0:qw],
                    start=(i == 0), stop=(i == n - 1)),
                  waits=w0, inc=(sT, 1))
                att["ptfree"][bank] = sT.n
            emit_smm(0)
            for i in range(1, n):
                emit_smm(i)
                emit_av(i - 1)
            emit_av(n - 1)
            tlast = sT.n
            E("s", lambda s, ia=ia: s.activation(
                out=rrow[64:65, 0:512], in_=ps_acc[ia][64:65, 0:512],
                func=AF.Ln, bias=0.0, scale=1.0), waits=[(sT, tlast)])
            E("s", lambda s: s.activation(
                out=rrow[64:65, 0:512], in_=rrow[64:65, 0:512],
                func=AF.Exp, bias=0.0, scale=-1.0), inc=(sS, 1))
            rS = sS.n
            E("s", lambda s, ia=ia: s.activation(
                out=accsb[0:64, 0:512], in_=ps_acc[ia][0:64, 0:512],
                func=AF.Copy, bias=0.0, scale=1.0), inc=(sS, 1))
            aS = sS.n
            att["accfree"][ia] = aS
            E("t", lambda t: t.matmul(
                out=ps_rb[0:64, 0:512], lhsT=ones_f[64:65, 0:64],
                rhs=rrow[64:65, 0:512], start=True, stop=True),
              waits=[(sS, rS), (sV, att["rbfree"])], inc=(sT, 1))
            rbT = sT.n
            w0 = [(sS, aS), (sT, rbT)]
            if h < 2:
                w0.append((sT, att["aofree"][cur]))
            E("v", lambda v, h=h, hc=hc, cur=cur: v.tensor_tensor(
                out=aoT[cur][64 * (h % 2):64 * (h % 2) + 64, hc * 512:(hc + 1) * 512],
                in0=accsb[0:64, 0:512], in1=ps_rb[0:64, 0:512], op=ALU.mult),
              waits=w0, inc=(sV, 1))
            att["rbfree"] = sV.n
        E("v", lambda v: v.drain(), inc=(sV, 1))
        ao_built = sV.n
        for qs in range(4):
            io = qs % 2
            tops = []
            for half in range(2):
                for ec in range(8):
                    E("t", lambda t, qs=qs, half=half, ec=ec, cur=cur: t.matmul(
                        out=ps_op[half][:, :],
                        lhsT=aoT[cur][:, ec * 512 + qs * P: ec * 512 + (qs + 1) * P],
                        rhs=Wb[1][:, ec * D + half * 512: ec * D + (half + 1) * 512],
                        start=(ec == 0), stop=(ec == 7)),
                      waits=[(sV, ao_built), (sGD, wo_ld), (sV, att["opfree"][half])]
                            if ec == 0 else [],
                      inc=(sT, 1) if ec == 7 else None)
                tops.append(sT.n)
            E("v", lambda v, io=io: v.tensor_tensor(
                out=opst[io][:, 0:512], in0=ps_op[0][:, :], in1=bo_b[:, 0:512],
                op=ALU.add),
              waits=[(sT, tops[0]), (sACT, max(att["opstfree"]))])
            E("v", lambda v, io=io: v.tensor_tensor(
                out=opst[io][:, 512:1024], in0=ps_op[1][:, :], in1=bo_b[:, 512:1024],
                op=ALU.add), waits=[(sT, tops[1])], inc=(sV, 1))
            vOP = sV.n
            att["opfree"][0] = vOP
            att["opfree"][1] = vOP
            tile = qg * 4 + qs
            E("s", lambda s, tile=tile, io=io: s.dma_start(
                out=dattnP[tile * P:(tile + 1) * P, :], in_=opst[io][:, :]),
              waits=[(sV, vOP)], inc16=sACT)
            attnP_sp[tile] = sACT.n
            att["opstfree"][io] = sACT.n
        att["aofree"][cur] = sT.n

    b.free("opst0", "aoT0", "aoT1", "rrow", "accsb", "PT0", "PT1",
           "Wb0", "Wb1", "vst", "kTt", "qT",
           "ps_op0", "ps_op1", "ps_rb", "ps_acc0", "ps_acc1", "ps_s40", "ps_s41")
    b.free("xn2T")

    if upto == "p4":
        for tile in range(NT):
            E("y", lambda y, tile=tile: y.dma_start(
                out=dout[tile * P:(tile + 1) * P, :],
                in_=dattnP[tile * P:(tile + 1) * P, :]),
              waits=[(sACT, attnP_sp[tile])], inc16=sOUT)
        E("y", lambda y: y.nop(), waits=[(sOUT, sOUT.n)])
        _finish(nc, b)
        return nc

    p4e = [(sV, sV.n), (sS, sS.n), (sT, sT.n)]
    scl = b.sb("scl", [P, 12], F32)
    sdq = b.sb("sdq", [P, 192], F32)
    xqT = b.sb("xqT", [P, 8 * T], BF16)
    xbuf = [b.sb(f"xbuf{i}", [P, D], F32) for i in range(2)]
    ibuf = b.sb("ibuf", [P, D], I32)
    ubuf = b.sb("ubuf", [P, D], F32)
    wnat = [b.sb(f"wnat{i}", [P, D], F32) for i in range(2)]
    wqb = b.sb("wqb", [P, D], BF16)
    luT = [b.sb(f"luT{i}", [P, D], BF16) for i in range(2)]
    ustage = [b.sb(f"ustage{i}", [P, 512], F32) for i in range(2)]
    ps_xt5 = b.ps("ps_xt5", [P, D], F32)
    ps_m5 = [b.ps(f"ps_m5{i}", [P, 512], F32) for i in range(2)]
    ps_w5 = b.ps("ps_w5", [P, D], BF16)

    def scale_setup(ccout, ar_val, col_s, col_r, fence):
        E("g", lambda g, c=ccout: g.dma_start(out=stat[:, 0:1], in_=c.ap()),
          waits=[(sCC, ar_val), (sV, sV.n), (sS, sS.n)] + fence, inc16=sGD)
        c1 = sGD.n
        E("t", lambda t: t.transpose(
            out=ps_xt5[0:1, 0:128], in_=stat[:, 0:1], identity=ident_f[:, :]),
          waits=[(sGD, c1), (sV, sV.n)], inc=(sT, 1))
        tc = sT.n
        E("v", lambda v: v.tensor_reduce(
            out=stat[0:1, 1:2], in_=ps_xt5[0:1, 0:128], axis=AX.X, op=ALU.max),
          waits=[(sT, tc)], inc=(sV, 1))
        E("v", lambda v: v.drain())
        E("v", lambda v: v.tensor_scalar(
            out=stat[0:1, 2:3], in0=stat[0:1, 1:2], scalar1=1.0 / 32767.0,
            scalar2=1e-12, op0=ALU.mult, op1=ALU.max), inc=(sV, 1))
        E("v", lambda v: v.drain())
        vsc = sV.n
        E("t", lambda t: t.matmul(
            out=ps_m5[0][:, 0:1], lhsT=ones_f[0:1, :], rhs=stat[0:1, 2:3],
            start=True, stop=True), waits=[(sV, vsc)], inc=(sT, 1))
        tb = sT.n
        E("v", lambda v, col_s=col_s: v.tensor_copy(
            out=scl[:, col_s:col_s + 1], in_=ps_m5[0][:, 0:1]),
          waits=[(sT, tb)], inc=(sV, 1))
        E("v", lambda v: v.drain())
        E("v", lambda v, col_s=col_s, col_r=col_r: v.reciprocal(
            out=scl[:, col_r:col_r + 1], in_=scl[:, col_s:col_s + 1]))
        E("v", lambda v: v.drain(), inc=(sV, 1))
        return sV.n

    sx_rdy = scale_setup(cc1o, ar1, 0, 1, p4e)

    xq5 = {"xfree": [0, 0], "evac": 0}
    for tile in range(NT):
        i2 = tile % 2
        leng, lsem = ("y", sLD) if i2 == 0 else ("g", sGD)
        E(leng, lambda e, tile=tile, i2=i2: e.dma_start(
            out=xbuf[i2][:, :], in_=dxn2[tile * P:(tile + 1) * P, :]),
          waits=[(sACT, sACT.n), (sV, xq5["xfree"][i2])] +
                (p4e if tile < 2 else []),
          inc16=lsem)
        ld = lsem.n
        E("v", lambda v, i2=i2: v.tensor_scalar(
            out=ubuf[:, :], in0=xbuf[i2][:, :], scalar1=scl[:, 1:2], scalar2=0.0,
            op0=ALU.mult, op1=ALU.bypass),
          waits=[(lsem, ld), (sV, sx_rdy), (sT, xq5["evac"] and 0)], inc=(sV, 1))
        xq5["xfree"][i2] = sV.n
        E("v", lambda v: v.tensor_copy(out=ibuf[:, :], in_=ubuf[:, :]))
        E("v", lambda v: v.tensor_copy(out=ubuf[:, :], in_=ibuf[:, :]), inc=(sV, 1))
        uv = sV.n
        for c in range(8):
            E("t", lambda t, c=c: t.transpose(
                out=ps_xt5[:, c * P:(c + 1) * P], in_=ubuf[:, c * P:(c + 1) * P],
                identity=ident_f[:, :]),
              waits=[(sV, uv), (sV, xq5["evac"])] if c == 0 else [],
              inc=(sT, 1) if c == 7 else None)
        tX = sT.n
        E("v", lambda v, tile=tile: v.tensor_scalar(
            out=xqT[:, :].rearrange("p (h c) -> p h c", h=8)[:, :, tile * P:(tile + 1) * P],
            in0=ps_xt5[:, :].rearrange("p (h c) -> p h c", h=8),
            scalar1=scl[:, 0:1], scalar2=0.0, op0=ALU.mult, op1=ALU.bypass),
          waits=[(sT, tX)], inc=(sV, 1))
        xq5["evac"] = sV.n
    E("v", lambda v: v.drain(), inc=(sV, 1))
    xq_done = sV.n

    q5 = {"wfree": [0, 0], "lufree": [0, 0], "ufree": [[0, 0], [0, 0]],
          "mfree": [0, 0]}
    spill_rec = {}

    def dequant_tile(src_sb, nblk, waits0):
        r3 = src_sb[:, :].rearrange("p (b k) -> p b k", k=32)
        E("v", lambda v, r3=r3, nblk=nblk: v.tensor_reduce(
            out=sdq[:, 0:nblk], in_=r3, axis=AX.X, op=ALU.max,
            apply_absolute_value=True), waits=waits0 + [(sS, sS.n)], inc=(sV, 1))
        E("v", lambda v: v.drain())
        E("v", lambda v, nblk=nblk: v.tensor_scalar(
            out=sdq[:, 0:nblk], in0=sdq[:, 0:nblk], scalar1=1.0 / 7.0,
            scalar2=1e-12, op0=ALU.mult, op1=ALU.max))
        E("v", lambda v: v.drain())
        E("v", lambda v, nblk=nblk: v.reciprocal(
            out=sdq[:, 96:96 + nblk], in_=sdq[:, 0:nblk]))
        E("v", lambda v: v.drain(), inc=(sV, 1))
        E("v", lambda v, r3=r3, nblk=nblk: v.tensor_tensor(
            out=ubuf[:, 0:nblk * 32].rearrange("p (b k) -> p b k", k=32), in0=r3,
            in1=sdq[:, 96:96 + nblk].unsqueeze(2).to_broadcast((P, nblk, 32)),
            op=ALU.mult), inc=(sV, 1))
        vr = sV.n
        E("v", lambda v, nblk=nblk: v.tensor_copy(
            out=ibuf[:, 0:nblk * 32], in_=ubuf[:, 0:nblk * 32]))
        E("v", lambda v, nblk=nblk: v.tensor_copy(
            out=ubuf[:, 0:nblk * 32], in_=ibuf[:, 0:nblk * 32]))
        E("v", lambda v, nblk=nblk: v.tensor_tensor(
            out=wqb[:, 0:nblk * 32].rearrange("p (b k) -> p b k", k=32),
            in0=ubuf[:, 0:nblk * 32].rearrange("p (b k) -> p b k", k=32),
            in1=sdq[:, 0:nblk].unsqueeze(2).to_broadcast((P, nblk, 32)),
            op=ALU.mult), inc=(sV, 1))
        E("v", lambda v: v.drain())
        return sV.n

    def upgate_pass(dw, dspill, rm_col, name, ssem):
        for c in range(NCF):
            i2 = c % 2
            leng, lsem = ("y", sLD) if i2 == 0 else ("g", sGD)
            E(leng, lambda e, c=c, i2=i2, dw=dw: e.dma_start(
                out=wnat[i2][:, :], in_=dw[c * P:(c + 1) * P, :]),
              waits=[(sV, q5["wfree"][i2])] + (p4e if c < 2 and name == "g" else []),
              inc16=lsem)
            wld = lsem.n
            wv = dequant_tile(wnat[i2], 32, [(lsem, wld)])
            q5["wfree"][i2] = wv
            for cc in range(8):
                E("t", lambda t, cc=cc: t.transpose(
                    out=ps_w5[:, cc * P:(cc + 1) * P], in_=wqb[:, cc * P:(cc + 1) * P],
                    identity=ident_b[:, :]),
                  waits=[(sV, wv), (sS, q5["lufree"][i2])] if cc == 0 else [],
                  inc=(sT, 1) if cc == 7 else None)
            tW = sT.n
            E("s", lambda s, i2=i2: s.activation(
                out=luT[i2][:, :], in_=ps_w5[:, :], func=AF.Copy, bias=0.0,
                scale=1.0), waits=[(sT, tW)], inc=(sS, 1))
            luS = sS.n
            q5["lufree"][i2] = luS
            for tg in range(4):
                bank = tg % 2
                for hc in range(8):
                    E("t", lambda t, hc=hc, tg=tg, bank=bank, i2=i2: t.matmul(
                        out=ps_m5[bank][:, :],
                        lhsT=luT[i2][:, hc * P:(hc + 1) * P],
                        rhs=xqT[:, hc * T + tg * 512: hc * T + (tg + 1) * 512],
                        start=(hc == 0), stop=(hc == 7)),
                      waits=[(sS, luS), (sV, xq_done), (sV, q5["mfree"][bank])]
                            if hc == 0 else [],
                      inc=(sT, 1) if hc == 7 else None)
                tM = sT.n
                E("v", lambda v, bank=bank, ssem=ssem: v.tensor_copy(
                    out=ustage[bank][:, :], in_=ps_m5[bank][:, :]),
                  waits=[(ssem, q5["ufree"][bank][0]),
                         (sGT, q5.get("ug_fence", 0) if (name == "u" and c == 0 and tg < 2) else 0),
                         (sT, tM), (sV, q5["ufree"][bank][1])], inc=(sV, 1))
                uV = sV.n
                q5["mfree"][bank] = uV
                E("s", lambda s, c=c, tg=tg, bank=bank, dspill=dspill, ssem=ssem: s.dma_start(
                    out=dspill[c, :, tg * 512:(tg + 1) * 512], in_=ustage[bank][:, :]),
                  waits=[(sV, uV)], inc16=ssem)
                spill_rec[(name, c, tg)] = ssem.n
                E("v", lambda v, bank=bank: v.tensor_reduce(
                    out=stat[:, 6:7], in_=ustage[bank][:, :], axis=AX.X, op=ALU.max,
                    apply_absolute_value=True))
                E("v", lambda v: v.drain())
                E("v", lambda v, rm_col=rm_col: v.tensor_tensor(
                    out=runmax[:, rm_col:rm_col + 1], in0=runmax[:, rm_col:rm_col + 1],
                    in1=stat[:, 6:7], op=ALU.max), inc=(sV, 1))
                E("v", lambda v: v.drain())
                q5["ufree"][bank] = [ssem.n, sV.n]

    def launch_ar(cci, cco, rm_col):
        E("g", lambda g, cci=cci, rm_col=rm_col: g.dma_start(
            out=cci[:, :], in_=runmax[:, rm_col:rm_col + 1]),
          waits=[(sV, sV.n)], inc16=sGD)
        E("g", lambda g, cci=cci, cco=cco: g.collective_compute(
            "AllReduce", ALU.max, replica_groups=[list(range(8))],
            ins=[cci.ap().opt()], outs=[cco.ap().opt()]),
          waits=[(sGD, sGD.n)], inc=(sCC, 1))
        return sCC.n

    upgate_pass(dwg, dgate, 2, "g", sGT)
    gate_total = sGT.n
    ar2 = launch_ar(cc2i, cc2o, 2)
    q5["ug_fence"] = gate_total
    q5["ufree"] = [[0, q5["ufree"][0][1]], [0, q5["ufree"][1][1]]]
    upgate_pass(dwu, dup, 1, "u", sUP)
    up_total = sUP.n
    ar3 = launch_ar(cc3i, cc3o, 1)

    sg_rdy = scale_setup(cc2o, ar2, 2, 3, [])
    gst = [b.sb(f"gst{i}", [P, 512], F32) for i in range(2)]
    g2b = b.sb("g2b", [P, 512], F32)
    gq_b = b.sb("gq_b", [P, 512], F32)
    sgb = b.sb("sgb", [P, 512], F32)
    i5 = b.sb("i5", [P, 512], I32)
    p5c = {"gfree": [0, 0], "g2free": 0, "spfree": 0}
    for c in range(NCF):
        for tg in range(4):
            i2 = (c * 4 + tg) % 2
            leng, lsem = ("y", sLD) if i2 == 0 else ("g", sGD)
            E(leng, lambda e, c=c, tg=tg, i2=i2: e.dma_start(
                out=gst[i2][:, :], in_=dgate[c, :, tg * 512:(tg + 1) * 512]),
              waits=[(sGT, gate_total), (sV, p5c["gfree"][i2])],
              inc16=lsem)
            gld = lsem.n
            E("v", lambda v, i2=i2: v.tensor_scalar(
                out=g2b[:, :], in0=gst[i2][:, :], scalar1=scl[:, 3:4], scalar2=0.0,
                op0=ALU.mult, op1=ALU.bypass),
              waits=[(lsem, gld), (sV, sg_rdy), (sG2c, p5c["spfree"])], inc=(sV, 1))
            p5c["gfree"][i2] = sV.n
            E("v", lambda v: v.tensor_copy(out=i5[:, :], in_=g2b[:, :]))
            E("v", lambda v: v.tensor_copy(out=g2b[:, :], in_=i5[:, :]),
              inc=(sV, 1))
            E("v", lambda v: v.tensor_scalar(
                out=gq_b[:, :], in0=g2b[:, :], scalar1=scl[:, 2:3], scalar2=0.0,
                op0=ALU.mult, op1=ALU.bypass), inc=(sV, 1))
            gqv = sV.n
            E("s", lambda s: s.activation(
                out=sgb[:, :], in_=gq_b[:, :], func=AF.Sigmoid, bias=0.0, scale=1.0),
              waits=[(sV, gqv)], inc=(sS, 1))
            sS1 = sS.n
            E("v", lambda v: v.tensor_scalar(
                out=g2b[:, :], in0=sgb[:, :], scalar1=65536.0, scalar2=0.0,
                op0=ALU.mult, op1=ALU.bypass), waits=[(sS, sS1)], inc=(sV, 1))
            E("v", lambda v: v.tensor_copy(out=i5[:, :], in_=g2b[:, :]))
            E("v", lambda v: v.tensor_copy(out=g2b[:, :], in_=i5[:, :]),
              inc=(sV, 1))
            E("v", lambda v: v.tensor_scalar(
                out=g2b[:, :], in0=g2b[:, :], scalar1=65535.0, scalar2=0.0,
                op0=ALU.min, op1=ALU.max), inc=(sV, 1))
            E("v", lambda v: v.scalar_tensor_tensor(
                out=g2b[:, :], in0=g2b[:, :], scalar=1.0 / 65536.0, in1=gq_b[:, :],
                op0=ALU.mult, op1=ALU.mult), inc=(sV, 1))
            g2v = sV.n
            if upto == "p5dbg" and c == 23 and tg == 2:
                E("g", lambda g: g.dma_start(
                    out=dout[5 * P:6 * P, 0:512], in_=g2b[:, :]),
                  waits=[(sV, g2v)], inc16=sGD)
                E("g", lambda g, i2=i2: g.dma_start(
                    out=dout[15 * P:16 * P, 0:512], in_=gst[i2][:, :]),
                  waits=[(sV, g2v)], inc16=sGD)
                E("g", lambda g: g.dma_start(
                    out=dout[15 * P:16 * P, 512:1024], in_=gq_b[:, :]),
                  waits=[(sV, g2v)], inc16=sGD)
            E("s", lambda s, c=c, tg=tg: s.dma_start(
                out=dgate2[c, :, tg * 512:(tg + 1) * 512], in_=g2b[:, :]),
              waits=[(sV, g2v)], inc16=sG2c)
            spill_rec[("g2", c, tg)] = sG2c.n
            p5c["spfree"] = sG2c.n
            E("v", lambda v: v.tensor_reduce(
                out=stat[:, 6:7], in_=g2b[:, :], axis=AX.X, op=ALU.max,
                apply_absolute_value=True))
            E("v", lambda v: v.drain())
            E("v", lambda v: v.tensor_tensor(
                out=runmax[:, 3:4], in0=runmax[:, 3:4], in1=stat[:, 6:7],
                op=ALU.max), inc=(sV, 1))
            E("v", lambda v: v.drain())
            p5c["g2free"] = sG2c.n
    ar4 = launch_ar(cc4i, cc4o, 3)

    sup_rdy = scale_setup(cc3o, ar3, 4, 5, [])
    sg2_rdy = scale_setup(cc4o, ar4, 6, 7, [])
    E("v", lambda v: v.memset(runmax[:, 0:1], 0.0), inc=(sV, 1))
    E("v", lambda v: v.drain())
    ust2 = [b.sb(f"ust2{i}", [P, 512], F32) for i in range(2)]
    ob = b.sb("ob", [P, 512], F32)
    p5c_tail = sG2c.n
    g2_total = sG2c.n
    p5d = {"gfree": [0, 0], "ufree": [0, 0], "spfree": 0}
    for c in range(NCF):
        for tg in range(4):
            i2 = (c * 4 + tg) % 2
            E("y", lambda y, c=c, tg=tg, i2=i2: y.dma_start(
                out=gst[i2][:, :], in_=dgate2[c, :, tg * 512:(tg + 1) * 512]),
              waits=[(sG2c, g2_total), (sV, p5d["gfree"][i2])],
              inc16=sLD)
            gld = sLD.n
            E("g", lambda g, c=c, tg=tg, i2=i2: g.dma_start(
                out=ust2[i2][:, :], in_=dup[c, :, tg * 512:(tg + 1) * 512]),
              waits=[(sUP, up_total), (sV, p5d["ufree"][i2])],
              inc16=sGD)
            uld = sGD.n
            E("v", lambda v, i2=i2: v.tensor_scalar(
                out=g2b[:, :], in0=gst[i2][:, :], scalar1=scl[:, 7:8], scalar2=0.0,
                op0=ALU.mult, op1=ALU.bypass),
              waits=[(sLD, gld), (sV, sg2_rdy), (sV, sup_rdy)] +
                    ([(sG2c, p5c_tail)] if (c == 0 and tg == 0) else []),
              inc=(sV, 1))
            p5d["gfree"][i2] = sV.n
            E("v", lambda v: v.tensor_copy(out=i5[:, :], in_=g2b[:, :]))
            E("v", lambda v: v.tensor_copy(out=g2b[:, :], in_=i5[:, :]),
              inc=(sV, 1))
            E("v", lambda v: v.tensor_scalar(
                out=g2b[:, :], in0=g2b[:, :], scalar1=scl[:, 6:7], scalar2=0.0,
                op0=ALU.mult, op1=ALU.bypass), inc=(sV, 1))
            E("v", lambda v, i2=i2: v.tensor_scalar(
                out=ob[:, :], in0=ust2[i2][:, :], scalar1=scl[:, 5:6], scalar2=0.0,
                op0=ALU.mult, op1=ALU.bypass),
              waits=[(sOc, p5d["spfree"]), (sGD, uld)], inc=(sV, 1))
            p5d["ufree"][i2] = sV.n
            E("v", lambda v: v.tensor_copy(out=i5[:, :], in_=ob[:, :]))
            E("v", lambda v: v.tensor_copy(out=ob[:, :], in_=i5[:, :]),
              inc=(sV, 1))
            E("v", lambda v: v.tensor_scalar(
                out=ob[:, :], in0=ob[:, :], scalar1=scl[:, 4:5], scalar2=0.0,
                op0=ALU.mult, op1=ALU.bypass), inc=(sV, 1))
            E("v", lambda v: v.tensor_tensor(
                out=ob[:, :], in0=g2b[:, :], in1=ob[:, :], op=ALU.mult),
              inc=(sV, 1))
            ov = sV.n
            E("s", lambda s, c=c, tg=tg: s.dma_start(
                out=dos[c, :, tg * 512:(tg + 1) * 512], in_=ob[:, :]),
              waits=[(sV, ov)], inc16=sOc)
            spill_rec[("o", c, tg)] = sOc.n
            p5d["spfree"] = sOc.n
            E("v", lambda v: v.tensor_reduce(
                out=stat[:, 6:7], in_=ob[:, :], axis=AX.X, op=ALU.max,
                apply_absolute_value=True))
            E("v", lambda v: v.drain())
            E("v", lambda v: v.tensor_tensor(
                out=runmax[:, 0:1], in0=runmax[:, 0:1], in1=stat[:, 6:7],
                op=ALU.max), inc=(sV, 1))
            E("v", lambda v: v.drain())
    ar5 = launch_ar(cc1i, cc1o, 0)

    if upto == "p5dbg":
        fw = [(sV, sV.n), (sS, sS.n), (sT, sT.n), (sACT, sACT.n), (sG2c, sG2c.n), (sOc, sOc.n), (sCC, sCC.n)]
        E("y", lambda y: y.dma_start(out=dout[0:P, 0:12], in_=scl[:, :]),
          waits=fw, inc16=sOUT)
        E("y", lambda y: y.dma_start(out=dout[0:P, 16:20], in_=runmax[:, :]),
          waits=fw, inc16=sOUT)
        E("y", lambda y: y.dma_start(out=dout[P:2 * P, :], in_=dup[0, :, 0:1024]),
          waits=fw, inc16=sOUT)
        E("y", lambda y: y.dma_start(out=dout[2 * P:3 * P, :], in_=dgate[0, :, 0:1024]),
          waits=fw, inc16=sOUT)
        E("y", lambda y: y.dma_start(out=dout[3 * P:4 * P, :], in_=dgate2[0, :, 0:1024]),
          waits=fw, inc16=sOUT)
        E("y", lambda y: y.dma_start(out=dout[4 * P:5 * P, :], in_=dos[0, :, 0:1024]),
          waits=fw, inc16=sOUT)
        E("y", lambda y: y.dma_start(out=dout[6 * P:7 * P, :], in_=dos[0, :, 1024:2048]),
          waits=fw, inc16=sOUT)
        E("y", lambda y: y.dma_start(out=dout[7 * P:8 * P, :], in_=dgate2[0, :, 1024:2048]),
          waits=fw, inc16=sOUT)
        E("y", lambda y: y.dma_start(out=dout[8 * P:9 * P, :], in_=dup[0, :, 1024:2048]),
          waits=fw, inc16=sOUT)
        E("y", lambda y: y.dma_start(out=dout[9 * P:10 * P, :], in_=dgate[0, :, 1024:2048]),
          waits=fw, inc16=sOUT)
        E("y", lambda y: y.dma_start(out=dout[10 * P:11 * P, :], in_=dos[23, :, 1024:2048]),
          waits=fw, inc16=sOUT)
        E("y", lambda y: y.dma_start(out=dout[11 * P:12 * P, :], in_=dgate2[23, :, 1024:2048]),
          waits=fw, inc16=sOUT)
        E("y", lambda y: y.dma_start(out=dout[12 * P:13 * P, :], in_=dup[23, :, 1024:2048]),
          waits=fw, inc16=sOUT)
        E("y", lambda y: y.dma_start(out=dout[13 * P:14 * P, :], in_=dos[23, :, 0:1024]),
          waits=fw, inc16=sOUT)
        E("y", lambda y: y.dma_start(out=dout[14 * P:15 * P, :], in_=dgate[23, :, 1024:2048]),
          waits=fw, inc16=sOUT)
        E("y", lambda y: y.dma_start(out=dout[15 * P:16 * P, :], in_=dup[22, :, 1024:2048]),
          waits=fw, inc16=sOUT)
        E("g", lambda g: g.dma_start(out=dout[5 * P:6 * P, :], in_=xqT[:, 0:1024]),
          waits=fw, inc16=sOUT)
        E("y", lambda y: y.nop(), waits=[(sOUT, sOUT.n)])
        _finish(nc, b)
        return nc

    b.free("ob", "ust20", "ust21", "i5", "sgb", "gq_b", "g2b", "gst0", "gst1",
           "ustage0", "ustage1", "luT0", "luT1", "wqb", "wnat0", "wnat1",
           "ubuf", "ibuf", "xbuf0", "xbuf1", "xqT")

    wdT = b.sb("wdT", [P, NCF * D], BF16)
    wnat2 = b.sb("wnat2", [P, IF], F32)
    ubuf2 = b.sb("ubuf2", [P, IF], F32)
    ibuf2 = b.sb("ibuf2", [P, IF], I32)
    wqb2 = b.sb("wqb2", [P, IF], BF16)
    wdqn = b.sb("wdqn", [P, 8 * IF], BF16)
    wd5 = {"wfree": 0, "evac": sS.n}
    o_total = sOc.n
    p5d_e = [(sV, sV.n), (sS, sS.n), (sT, sT.n), (sOc, sOc.n), (sG2c, sG2c.n)]
    for ht in range(8):
        E("y", lambda y, ht=ht: y.dma_start(
            out=wnat2[:, :], in_=dwd[ht * P:(ht + 1) * P, :]),
          waits=[(sV, wd5["wfree"])] + (p5d_e if ht == 0 else []), inc16=sLD)
        wld = sLD.n
        r3 = wnat2[:, :].rearrange("p (b k) -> p b k", k=32)
        E("v", lambda v, r3=r3: v.tensor_reduce(
            out=sdq[:, 0:96], in_=r3, axis=AX.X, op=ALU.max,
            apply_absolute_value=True), waits=[(sLD, wld), (sS, sS.n)], inc=(sV, 1))
        E("v", lambda v: v.drain())
        E("v", lambda v: v.tensor_scalar(
            out=sdq[:, 0:96], in0=sdq[:, 0:96], scalar1=1.0 / 7.0, scalar2=1e-12,
            op0=ALU.mult, op1=ALU.max))
        E("v", lambda v: v.drain())
        E("v", lambda v: v.reciprocal(out=sdq[:, 96:192], in_=sdq[:, 0:96]))
        E("v", lambda v: v.drain(), inc=(sV, 1))
        E("v", lambda v, r3=r3: v.tensor_tensor(
            out=ubuf2[:, :].rearrange("p (b k) -> p b k", k=32),
            in0=r3, in1=sdq[:, 96:192].unsqueeze(2).to_broadcast((P, 96, 32)),
            op=ALU.mult), inc=(sV, 1))
        vr = sV.n
        E("v", lambda v: v.tensor_copy(out=ibuf2[:, :], in_=ubuf2[:, :]))
        E("v", lambda v: v.tensor_copy(out=ubuf2[:, :], in_=ibuf2[:, :]))
        E("v", lambda v, ht=ht: v.tensor_tensor(
            out=wdqn[:, ht * IF:(ht + 1) * IF].rearrange("p (b k) -> p b k", k=32),
            in0=ubuf2[:, :].rearrange("p (b k) -> p b k", k=32),
            in1=sdq[:, 0:96].unsqueeze(2).to_broadcast((P, 96, 32)),
            op=ALU.mult), inc=(sV, 1))
        wd5["wfree"] = sV.n
    E("v", lambda v: v.drain(), inc=(sV, 1))
    wdq_v = sV.n
    for ioc in range(NCF):
        for ht in range(8):
            E("t", lambda t, ioc=ioc, ht=ht: t.transpose(
                out=ps_w5[:, ht * P:(ht + 1) * P],
                in_=wdqn[:, ht * IF + ioc * P: ht * IF + (ioc + 1) * P],
                identity=ident_b[:, :]),
              waits=[(sV, wdq_v), (sS, wd5["evac"])] if ht == 0 else [],
              inc=(sT, 1) if ht == 7 else None)
        tW = sT.n
        E("s", lambda s, ioc=ioc: s.activation(
            out=wdT[:, ioc * D:(ioc + 1) * D], in_=ps_w5[:, :], func=AF.Copy,
            bias=0.0, scale=1.0), waits=[(sT, tW)], inc=(sS, 1))
        wd5["evac"] = sS.n
    wdT_done = sS.n
    b.free("wdqn", "wqb2", "ibuf2", "ubuf2", "wnat2")

    so_rdy = scale_setup(cc1o, ar5, 8, 9, [])
    act_total = sACT.n
    oqb = b.sb("oqb", [P, NCF * 512], BF16)
    ostr = [b.sb(f"ostr{i}", [P, 512], F32) for i in range(2)]
    oi5 = b.sb("oi5", [P, 512], I32)
    osc = b.sb("osc", [P, 512], F32)
    x1s = [b.sb(f"x1s{i}", [P, D], F32) for i in range(2)]
    apS = [b.sb(f"apS{i}", [P, D], F32) for i in range(2)]
    tst = b.sb("tst", [P, D], F32)
    outst = [b.sb(f"outst{i}", [P, D], F32) for i in range(2)]
    p5e = {"ofree": [0, 0], "xfree": [0, 0], "afree": [0, 0], "outfree": [0, 0],
           "mfree": [0, 0], "mmdone": 0}
    for tg in range(4):
        for ioc in range(NCF):
            i2 = ioc % 2
            leng, lsem = ("y", sLD) if i2 == 0 else ("g", sGD)
            E(leng, lambda e, ioc=ioc, tg=tg, i2=i2: e.dma_start(
                out=ostr[i2][:, :], in_=dos[ioc, :, tg * 512:(tg + 1) * 512]),
              waits=[(sOc, o_total), (sV, p5e["ofree"][i2])] +
                    ([(sS, wdT_done)] if tg == 0 and ioc < 2 else []),
              inc16=lsem)
            old = lsem.n
            E("v", lambda v, i2=i2: v.tensor_scalar(
                out=osc[:, :], in0=ostr[i2][:, :], scalar1=scl[:, 9:10], scalar2=0.0,
                op0=ALU.mult, op1=ALU.bypass),
              waits=[(lsem, old), (sV, so_rdy)], inc=(sV, 1))
            p5e["ofree"][i2] = sV.n
            E("v", lambda v: v.tensor_copy(out=oi5[:, :], in_=osc[:, :]))
            E("v", lambda v: v.tensor_copy(out=osc[:, :], in_=oi5[:, :]),
              inc=(sV, 1))
            E("v", lambda v, ioc=ioc: v.tensor_scalar(
                out=oqb[:, ioc * 512:(ioc + 1) * 512], in0=osc[:, :],
                scalar1=scl[:, 8:9], scalar2=0.0, op0=ALU.mult, op1=ALU.bypass),
              waits=[(sT, p5e["mmdone"])] if ioc == 0 else [], inc=(sV, 1))
        E("v", lambda v: v.drain(), inc=(sV, 1))
        oq_v = sV.n
        if upto == "p5e" and False:
            for i in range(8):
                E("g", lambda g, i=i: g.dma_start(
                    out=dout[i * P:(i + 1) * P, 0:512], in_=oqb[:, i * 512:(i + 1) * 512]),
                  waits=[(sV, oq_v)], inc16=sOUT)
            E("y", lambda y: y.dma_start(out=dout[8 * P:9 * P, 0:12], in_=scl[:, :]),
              waits=[(sV, oq_v)], inc16=sOUT)
            E("g", lambda g: g.dma_start(out=dout[9 * P:10 * P, :], in_=wdT[:, 0:1024]),
              waits=[(sV, oq_v)], inc16=sOUT)
        for ts4 in range(4):
            tile = tg * 4 + ts4
            i2 = ts4 % 2
            E("y", lambda y, tile=tile, i2=i2: y.dma_start(
                out=x1s[i2][:, :], in_=dx1[tile * P:(tile + 1) * P, :]),
              waits=[(sACT, act_total), (sV, p5e["xfree"][i2])] +
                    ([(sS, wdT_done)] if tile < 2 else []), inc16=sLD)
            rld = sLD.n
            E("g", lambda g, tile=tile, i2=i2: g.dma_start(
                out=apS[i2][:, :], in_=dattnP[tile * P:(tile + 1) * P, :]),
              waits=[(sACT, act_total), (sV, p5e["afree"][i2])] +
                    ([(sS, wdT_done)] if tile < 2 else []), inc16=sGD)
            rldg = sGD.n
            tops = []
            for half in range(2):
                for ioc in range(NCF):
                    E("t", lambda t, ioc=ioc, ts4=ts4, half=half: t.matmul(
                        out=ps_m5[half][:, :],
                        lhsT=oqb[:, ioc * 512 + ts4 * P: ioc * 512 + (ts4 + 1) * P],
                        rhs=wdT[:, ioc * D + half * 512: ioc * D + (half + 1) * 512],
                        start=(ioc == 0), stop=(ioc == NCF - 1)),
                      waits=[(sV, oq_v), (sS, wdT_done), (sV, p5e["mfree"][half])]
                            if ioc == 0 else [],
                      inc=(sT, 1) if ioc == NCF - 1 else None)
                tops.append(sT.n)
            E("v", lambda v, i2=i2: v.tensor_tensor(
                out=tst[:, :], in0=x1s[i2][:, :], in1=apS[i2][:, :], op=ALU.add),
              waits=[(sLD, rld), (sGD, rldg), (sV, sV.n)], inc=(sV, 1))
            tv = sV.n
            p5e["xfree"][i2] = tv
            p5e["afree"][i2] = tv
            oi = ts4 % 2
            E("v", lambda v, oi=oi: v.tensor_tensor(
                out=outst[oi][:, 0:512], in0=ps_m5[0][:, :], in1=tst[:, 0:512],
                op=ALU.add),
              waits=[(sT, tops[0]), (sOUT, p5e["outfree"][oi])])
            E("v", lambda v, oi=oi: v.tensor_tensor(
                out=outst[oi][:, 512:1024], in0=ps_m5[1][:, :], in1=tst[:, 512:1024],
                op=ALU.add), waits=[(sT, tops[1])], inc=(sV, 1))
            vout = sV.n
            p5e["mfree"][0] = vout
            p5e["mfree"][1] = vout
            E("y", lambda y, tile=tile, oi=oi: y.dma_start(
                out=dout[tile * P:(tile + 1) * P, :], in_=outst[oi][:, :]),
              waits=[(sV, vout)], inc16=sOUT)
            p5e["outfree"][oi] = sOUT.n
        p5e["mmdone"] = sT.n
        if upto == "p5e" and tg == 2:
            fw2 = [(sV, sV.n), (sS, sS.n), (sT, sT.n), (sOUT, sOUT.n)]
            E("y", lambda y: y.dma_start(out=dout[12 * P:13 * P, :], in_=dx1[8 * P:9 * P, :]),
              waits=fw2, inc16=sOUT)
            E("y", lambda y: y.dma_start(out=dout[13 * P:14 * P, :], in_=dattnP[8 * P:9 * P, :]),
              waits=fw2, inc16=sOUT)
            break
    E("y", lambda y: y.nop(), waits=[(sOUT, sOUT.n)])
    _finish(nc, b)
    return nc



_CACHE = {}


def kernel(**inputs):
    if "nc" not in _CACHE:
        _CACHE["nc"] = build_nc()
    nc = _CACHE["nc"]
    x = np.ascontiguousarray(inputs["L0_attn_hidden_states"], dtype=np.float32)
    w = {
        "kw": np.ascontiguousarray(inputs["L0_attn_k_weight"], np.float32),
        "qw": np.ascontiguousarray(inputs["L0_attn_q_weight"], np.float32),
        "lnw": np.ascontiguousarray(inputs["L0_attn_ln_weight"], np.float32),
        "lnb": np.ascontiguousarray(inputs["L0_attn_ln_bias"], np.float32),
        "bias1": np.ascontiguousarray(inputs["L0_attn_bias"], np.float32),
        "wrT": np.ascontiguousarray(np.asarray(inputs["L0_ffn_W_r"]).T, np.float32),
        "wkT": np.ascontiguousarray(np.asarray(inputs["L0_ffn_W_k"]).T, np.float32),
        "wvT": np.ascontiguousarray(np.asarray(inputs["L0_ffn_W_v"]).T, np.float32),
        "mur": np.ascontiguousarray(inputs["L0_ffn_mu_r"], np.float32),
        "muk": np.ascontiguousarray(inputs["L0_ffn_mu_k"], np.float32),
        "l1qT": np.ascontiguousarray(np.asarray(inputs["L1_attn_Wq"]).T, np.float32),
        "l1kT": np.ascontiguousarray(np.asarray(inputs["L1_attn_Wk"]).T, np.float32),
        "l1vT": np.ascontiguousarray(np.asarray(inputs["L1_attn_Wv"]).T, np.float32),
        "l1oT": np.ascontiguousarray(np.asarray(inputs["L1_attn_Wo"]).T, np.float32),
        "bo": np.ascontiguousarray(inputs["L1_attn_bo"], np.float32),
        "wg": np.ascontiguousarray(inputs["L1_ffn_w_gate"], np.float32),
        "wu": np.ascontiguousarray(inputs["L1_ffn_w_up"], np.float32),
        "wd": np.ascontiguousarray(inputs["L1_ffn_w_down"], np.float32),
    }
    in_maps = [dict(w, xin=np.ascontiguousarray(x[c])) for c in range(8)]
    res = run_bass_kernel_spmd(nc, in_maps, core_ids=list(range(8)))
    return np.stack([res.results[c]["out"] for c in range(8)], axis=0)


# version 05
# speedup vs baseline: 1.0900x; 1.0455x over previous
import numpy as np
import concourse.bass as bass
import concourse.mybir as mybir
from concourse.bass_utils import run_bass_kernel_spmd

F32 = mybir.dt.float32
BF16 = mybir.dt.bfloat16
I32 = mybir.dt.int32
AF = mybir.ActivationFunctionType
ALU = mybir.AluOpType
AX = mybir.AxisListType

P = 128
T = 2048
D = 1024
NT = T // P
I0 = 4096
IF = 3072
NC0 = I0 // P
NCF = IF // P
GRP = 256
NG = T // GRP
HB = 16
INV_D = 1.0 / D
INV_PI = 0.3183098861837907
CLIP_LO = -1.0 + 1e-6
CLIP_HI = 1.0 - 1e-6

ENGINES = ("y", "v", "s", "t", "g")


class Ct:
    def __init__(self, handle, name):
        self.h = handle
        self.name = name
        self.n = 0


class Sched:
    def __init__(self):
        self.recs = {e: [] for e in ENGINES}

    def op(self, eng, fn, waits=(), inc=None, inc16=None, serial=True):
        ws = []
        for c, v in waits:
            if v <= 0:
                continue
            assert v <= c.n, f"wait on future value of {c.name}: {v} > {c.n}"
            ws.append((c, v))
        incs = []
        if inc is not None:
            cs = inc if isinstance(inc, list) else [inc]
            for c, k in cs:
                c.n += k
                incs.append((c, k))
        if inc16 is not None:
            inc16.n += 16
            incs.append((inc16, 16))
        self.recs[eng].append((fn, ws, incs))
        if inc16 is not None and serial:
            self.recs[eng].append(
                (lambda e: e.nop(), [(inc16, inc16.n)], []))

    def replay(self, eng_name, eng):
        for fn, ws, incs in self.recs[eng_name]:
            for c, v in ws:
                eng.wait_ge(c.h, v)
            inst = fn(eng)
            for c, k in incs:
                inst.then_inc(c.h, k)


class B:
    def __init__(self, nc):
        self.nc = nc
        self.sched = Sched()
        self._open = {}
        self._seq = 0

    def sb(self, name, shape, dt=F32):
        cm = self.nc.sbuf_tensor(name, shape, dt)
        h = cm.__enter__()
        self._seq += 1
        self._open[name] = (self._seq, cm)
        return h

    def ps(self, name, shape, dt=F32):
        cm = self.nc.psum_tensor(name, shape, dt)
        h = cm.__enter__()
        self._seq += 1
        self._open[name] = (self._seq, cm)
        return h

    def free(self, *names):
        for name in sorted(names, key=lambda n: -self._open[n][0]):
            self._open.pop(name)[1].__exit__(None, None, None)

    def sem(self, name):
        cm = self.nc.semaphore(name)
        h = cm.__enter__()
        self._seq += 1
        self._open["sem_" + name] = (self._seq, cm)
        return Ct(h, name)




def _finish(nc, b):
    with nc.Block() as block:
        @block.sync
        def _(y):
            b.sched.replay("y", y)

        @block.vector
        def _(v):
            b.sched.replay("v", v)

        @block.scalar
        def _(s):
            b.sched.replay("s", s)

        @block.tensor
        def _(t):
            b.sched.replay("t", t)

        @block.gpsimd
        def _(g):
            b.sched.replay("g", g)


def build_nc(upto="all"):
    nc = bass.Bass("TRN2", target_bir_lowering=False)
    b = B(nc)
    E = b.sched.op

    dxin = nc.dram_tensor("xin", [T, D], F32, kind="ExternalInput").ap()
    dkw = nc.dram_tensor("kw", [I0, D], F32, kind="ExternalInput").ap()
    dqw = nc.dram_tensor("qw", [I0, D], F32, kind="ExternalInput").ap()
    dlnw = nc.dram_tensor("lnw", [D], F32, kind="ExternalInput").ap()
    dlnb = nc.dram_tensor("lnb", [D], F32, kind="ExternalInput").ap()
    dbias1 = nc.dram_tensor("bias1", [D], F32, kind="ExternalInput").ap()
    dwrT = nc.dram_tensor("wrT", [D, D], F32, kind="ExternalInput").ap()
    dwkT = nc.dram_tensor("wkT", [D, D], F32, kind="ExternalInput").ap()
    dwvT = nc.dram_tensor("wvT", [D, D], F32, kind="ExternalInput").ap()
    dmur = nc.dram_tensor("mur", [D], F32, kind="ExternalInput").ap()
    dmuk = nc.dram_tensor("muk", [D], F32, kind="ExternalInput").ap()
    dl1q = nc.dram_tensor("l1qT", [D, D], F32, kind="ExternalInput").ap()
    dl1k = nc.dram_tensor("l1kT", [D, D], F32, kind="ExternalInput").ap()
    dl1v = nc.dram_tensor("l1vT", [D, D], F32, kind="ExternalInput").ap()
    dl1o = nc.dram_tensor("l1oT", [D, D], F32, kind="ExternalInput").ap()
    dbo = nc.dram_tensor("bo", [D], F32, kind="ExternalInput").ap()
    dwg = nc.dram_tensor("wg", [IF, D], F32, kind="ExternalInput").ap()
    dwu = nc.dram_tensor("wu", [IF, D], F32, kind="ExternalInput").ap()
    dwd = nc.dram_tensor("wd", [D, IF], F32, kind="ExternalInput").ap()
    dout = nc.dram_tensor("out", [T, D], F32, kind="ExternalOutput").ap()

    dattn0 = nc.dram_tensor("attn0", [T, D], F32).ap()
    dx1 = nc.dram_tensor("x1", [T, D], F32).ap()
    dxn2 = nc.dram_tensor("xn2s", [T, D], F32).ap()
    dattnP = nc.dram_tensor("attnP", [T, D], F32).ap()
    dqwbf = nc.dram_tensor("qwbf", [I0, D], BF16).ap()
    dup = nc.dram_tensor("ups", [NCF, P, T], BF16).ap()
    dgate = nc.dram_tensor("gates", [NCF, P, T], BF16).ap()
    dgate2 = nc.dram_tensor("gate2s", [NCF, P, T], BF16).ap()
    dos = nc.dram_tensor("osp", [NCF, P, T], BF16).ap()
    cc1i = nc.dram_tensor("cc1i", [P, 1], F32)
    cc1o = nc.dram_tensor("cc1o", [P, 1], F32)
    cc2i = nc.dram_tensor("cc2i", [P, 1], F32)
    cc2o = nc.dram_tensor("cc2o", [P, 1], F32)
    cc5i = nc.dram_tensor("cc5i", [P, 1], F32)
    cc5o = nc.dram_tensor("cc5o", [P, 1], F32)
    cc3i = nc.dram_tensor("cc3i", [P, 1], F32)
    cc3o = nc.dram_tensor("cc3o", [P, 1], F32)
    cc4i = nc.dram_tensor("cc4i", [P, 1], F32)
    cc4o = nc.dram_tensor("cc4o", [P, 1], F32)

    sLD = b.sem("sLD")
    sSP = b.sem("sSP")
    sGD = b.sem("sGD")
    sV = b.sem("sV")
    sS = b.sem("sS")
    sT = b.sem("sT")
    sG = b.sem("sG")
    sCC = b.sem("sCC")
    sOUT = b.sem("sOUT")
    sA0 = b.sem("sA0")
    sACT = b.sem("sACT")
    sQW = b.sem("sQW")
    sGT = b.sem("sGT")
    sUP = b.sem("sUP")
    sG2c = b.sem("sG2c")
    sOc = b.sem("sOc")

    ident_f = b.sb("ident_f", [P, P], F32)
    ident_b = b.sb("ident_b", [P, P], BF16)
    ones_f = b.sb("ones_f", [P, P], F32)
    tri = b.sb("tri", [P, P], BF16)
    lnw_b = b.sb("lnw_b", [P, D], F32)
    lnb_b = b.sb("lnb_b", [P, D], F32)
    bias1_b = b.sb("bias1_b", [P, D], F32)
    bo_b = b.sb("bo_b", [P, D], F32)
    mur_sb = b.sb("mur_sb", [P, 8], F32)
    muk_sb = b.sb("muk_sb", [P, 8], F32)
    m_all = b.sb("m_all", [P, NT], F32)
    rstd_all = b.sb("rstd_all", [P, NT], F32)
    m2_all = b.sb("m2_all", [P, NT], F32)
    ss2_all = b.sb("ss2_all", [P, NT], F32)
    rstd2_all = b.sb("rstd2_all", [P, NT], F32)
    runmax = b.sb("runmax", [P, 4], F32)
    stat = b.sb("stat", [P, 8], F32)
    kc = b.sb("kc", [P, 4], F32)
    xn2T = b.sb("xn2T", [P, 8 * T], BF16)
    vrow = b.sb("vrow", [1, D], F32)
    sq_scr = b.sb("sq_scr", [P, D], F32)
    E("g", lambda g: g.memset(ident_f[:, :], 0.0))
    E("g", lambda g: g.affine_select(
        out=ident_f[:, :], in_=ident_f[:, :], compare_op=ALU.not_equal,
        fill=1.0, base=0, pattern=[[-1, P]], channel_multiplier=1))
    E("g", lambda g: g.memset(ident_b[:, :], 0.0))
    E("g", lambda g: g.affine_select(
        out=ident_b[:, :], in_=ident_b[:, :], compare_op=ALU.not_equal,
        fill=1.0, base=0, pattern=[[-1, P]], channel_multiplier=1))
    E("g", lambda g: g.memset(ones_f[:, :], 1.0))
    E("g", lambda g: g.memset(tri[:, :], 1.0))
    E("g", lambda g: g.affine_select(
        out=tri[:, :], in_=tri[:, :], compare_op=ALU.is_ge,
        fill=0.0, base=0, pattern=[[1, P]], channel_multiplier=-1))
    E("g", lambda g: g.memset(runmax[:, :], 0.0))
    E("g", lambda g: g.memset(kc[:, 0:1], 1e-5))
    E("g", lambda g: g.memset(kc[:, 1:2], 1e-12))
    E("g", lambda g: g.memset(kc[:, 2:3], 0.5), inc=(sG, 1))
    g_const = sG.n

    ps_bc = b.ps("ps_bc", [P, D], F32)
    for src, dst in [(dlnw, lnw_b), (dlnb, lnb_b), (dbias1, bias1_b), (dbo, bo_b)]:
        vprev = sV.n
        E("y", lambda y, s=src: y.dma_start(out=vrow[0:1, :], in_=s.rearrange("(o d) -> o d", o=1)),
          waits=[(sV, vprev)], inc16=sLD)
        ld = sLD.n
        for half in range(2):
            E("t", lambda t, h=half: t.matmul(
                out=ps_bc[:, h * 512:(h + 1) * 512], lhsT=ones_f[0:1, :],
                rhs=vrow[0:1, h * 512:(h + 1) * 512], start=True, stop=True),
              waits=[(sLD, ld), (sG, g_const), (sV, vprev)],
              inc=(sT, 1) if half == 1 else None)
        E("v", lambda v, d=dst: v.tensor_copy(out=d[:, :], in_=ps_bc[:, :]),
          waits=[(sT, sT.n)], inc=(sV, 1))

    def _mu_load(y, dst, srcap):
        with nc.allow_non_contiguous_dma(reason="tiny one-time mu load"):
            return y.dma_start(out=dst[:, :], in_=srcap.rearrange("(c p) -> p c", p=P))
    E("y", lambda y: _mu_load(y, mur_sb, dmur), inc16=sLD)
    E("y", lambda y: _mu_load(y, muk_sb, dmuk), inc16=sLD)
    mu_ld = sLD.n

    KT = b.sb("KT", [P, 8 * I0], BF16)
    kwb = [b.sb(f"kwb{i}", [P, D], F32) for i in range(2)]
    knb = [b.sb(f"knb{i}", [P, D], BF16) for i in range(2)]
    qcb = [b.sb(f"qcb{i}", [P, D], BF16) for i in range(2)]
    ps_kt = [b.ps(f"ps_kt{i}", [P, D], BF16) for i in range(2)]

    kw_free = [0, 0]
    kt_busy = [0, 0]
    kn_free = [0, 0]
    qc_free = [0, 0]
    for k in range(NC0):
        i2 = k % 2
        E("y", lambda y, k=k, i2=i2: y.dma_start(
            out=kwb[i2][:, :], in_=dkw[k * P:(k + 1) * P, :]),
          waits=[(sV, kw_free[i2])], inc16=sLD)
        ld = sLD.n
        E("s", lambda s, i2=i2: s.activation(
            out=sq_scr[:, :], in_=kwb[i2][:, :], func=AF.Square,
            bias=0.0, scale=1.0),
          waits=[(sLD, ld), (sV, sV.n)], inc=(sS, 1))
        ssd = sS.n
        E("v", lambda v: v.tensor_reduce(
            out=stat[:, 2:3], in_=sq_scr[:, :], axis=AX.X, op=ALU.add),
          waits=[(sS, ssd)], inc=(sV, 1))
        E("s", lambda s: s.activation(
            out=stat[:, 3:4], in_=stat[:, 2:3], func=AF.Sqrt, bias=0.0, scale=1.0),
          waits=[(sV, sV.n)], inc=(sS, 1))
        E("v", lambda v: v.tensor_scalar(
            out=stat[:, 4:5], in0=stat[:, 3:4], scalar1=1e-12, scalar2=0.0,
            op0=ALU.max, op1=ALU.bypass), waits=[(sS, sS.n)])
        E("v", lambda v: v.drain())
        E("v", lambda v: v.reciprocal(out=stat[:, 5:6], in_=stat[:, 4:5]))
        E("v", lambda v: v.drain())
        E("v", lambda v, i2=i2: v.tensor_scalar(
            out=knb[i2][:, :], in0=kwb[i2][:, :], scalar1=stat[:, 5:6], scalar2=0.0,
            op0=ALU.mult, op1=ALU.bypass),
          waits=[(sT, kn_free[i2])], inc=(sV, 1))
        E("v", lambda v: v.drain())
        kw_free[i2] = sV.n
        vn = sV.n
        for c in range(8):
            E("t", lambda t, i2=i2, c=c: t.transpose(
                out=ps_kt[i2][:, c * P:(c + 1) * P], in_=knb[i2][:, c * P:(c + 1) * P],
                identity=ident_b[:, :]),
              waits=[(sV, vn), (sV, kt_busy[i2]), (sG, g_const)] if c == 0 else [],
              inc=(sT, 1) if c == 7 else None)
        kn_free[i2] = sT.n
        tn = sT.n
        E("v", lambda v, i2=i2, k=k: v.tensor_copy(
            out=KT[:, :].rearrange("p (h c) -> p h c", h=8)[:, :, k * P:(k + 1) * P],
            in_=ps_kt[i2][:, :].rearrange("p (h c) -> p h c", h=8)),
          waits=[(sT, tn)], inc=(sV, 1))
        kt_busy[i2] = sV.n
        E("g", lambda g, k=k, i2=i2: g.dma_start(
            out=qcb[i2][:, :], in_=dqw[k * P:(k + 1) * P, :]),
          waits=[(sQW, qc_free[i2])], inc16=sGD)
        gld = sGD.n
        E("s", lambda s, k=k, i2=i2: s.dma_start(
            out=dqwbf[k * P:(k + 1) * P, :], in_=qcb[i2][:, :]),
          waits=[(sGD, gld)], inc16=sQW)
        qc_free[i2] = sQW.n
    qw_cached = sQW.n

    b.free("kwb0", "kwb1", "knb0", "knb1", "qcb0", "qcb1",
           "ps_kt0", "ps_kt1", "ps_bc")
    p0_v, p0_s, p0_t = sV.n, sS.n, sT.n

    xg = [b.sb(f"xg{i}", [P, 2 * D], F32) for i in range(2)]
    QT = [b.sb(f"QT{i}", [P, 8 * GRP], BF16) for i in range(2)]
    qwk = [b.sb(f"qwk{i}", [P, D], BF16) for i in range(2)]
    sstage = b.sb("sstage", [P, HB * GRP], F32)
    aux = b.sb("aux", [P, HB * GRP], F32)
    Bst = [b.sb("Bst0", [P, HB * GRP], BF16)] * 2
    xnt = b.sb("xnt", [P, D], F32)
    yt = b.sb("yt", [P, D], F32)
    qtile = b.sb("qtile", [P, D], BF16)
    Xout = [b.sb("Xout0", [P, D], F32)] * 2
    ps_sT = [b.ps(f"ps_sT{i}", [P, GRP], F32) for i in range(3)]
    ps_X = [b.ps(f"ps_X{i}", [P, 512], F32) for i in range(4)]
    ps_qt = b.ps("ps_qt", [P, D], BF16)

    st = {
        "xg_free": [0, 0],
        "qt_evac": 0,
        "qt_built": [0, 0],
        "qt_free": [0, 0],
        "qwk_free": [0, 0],
        "sT_free": [0, 0, 0],
        "bst_free": [0, 0],
        "xfin_v": 0,
        "xout_free": [0, 0],
        "aux_free": 0,
        "attn0_sp": [0] * NT,
    }

    def ln_stats(xt_ap, m_dst, r_dst, eps, ld_waits):
        E("v", lambda v: v.tensor_reduce(
            out=stat[:, 0:1], in_=xt_ap, axis=AX.X, op=ALU.add),
          waits=ld_waits + [(sS, sS.n)], inc=(sV, 1))
        E("s", lambda s: s.activation(
            out=sq_scr[:, :], in_=xt_ap, func=AF.Square, bias=0.0, scale=1.0),
          waits=ld_waits + [(sV, sV.n)], inc=(sS, 1))
        ssd = sS.n
        E("v", lambda v: v.drain())
        E("v", lambda v: v.tensor_scalar(
            out=m_dst, in0=stat[:, 0:1], scalar1=INV_D, scalar2=0.0,
            op0=ALU.mult, op1=ALU.bypass))
        E("v", lambda v: v.tensor_reduce(
            out=stat[:, 3:4], in_=sq_scr[:, :], axis=AX.X, op=ALU.add),
          waits=[(sS, ssd)], inc=(sV, 1))
        E("v", lambda v: v.drain())
        E("v", lambda v: v.tensor_scalar(
            out=stat[:, 3:4], in0=stat[:, 3:4], scalar1=INV_D, scalar2=0.0,
            op0=ALU.mult, op1=ALU.bypass))
        E("v", lambda v: v.drain())
        E("v", lambda v: v.scalar_tensor_tensor(
            out=stat[:, 4:5], in0=m_dst, scalar=m_dst, in1=stat[:, 3:4],
            op0=ALU.mult, op1=ALU.subtract))
        E("v", lambda v: v.drain())
        E("v", lambda v: v.tensor_scalar(
            out=stat[:, 4:5], in0=stat[:, 4:5], scalar1=-1.0, scalar2=eps,
            op0=ALU.mult, op1=ALU.add), inc=(sV, 1))
        E("s", lambda s: s.activation(
            out=stat[:, 5:6], in_=stat[:, 4:5], func=AF.Ln, bias=0.0, scale=1.0),
          waits=[(sV, sV.n)])
        E("s", lambda s: s.activation(
            out=r_dst, in_=stat[:, 5:6], func=AF.Exp, bias=0.0, scale=-0.5),
          inc=(sS, 1))
        return sS.n

    def sq_accum(src_ap, dst_idx, waits):
        E("s", lambda s: s.activation(
            out=sq_scr[:, :], in_=src_ap, func=AF.Square, bias=0.0, scale=1.0),
          waits=list(waits) + [(sV, sV.n)], inc=(sS, 1))
        ssd = sS.n
        E("v", lambda v: v.tensor_reduce(
            out=stat[:, dst_idx:dst_idx + 1], in_=sq_scr[:, :], axis=AX.X,
            op=ALU.add), waits=[(sS, ssd)], inc=(sV, 1))
        E("v", lambda v: v.drain())
        return sV.n

    def rsqrt_nl(src_idx, dst_ap, waits):
        E("s", lambda s: s.activation(
            out=stat[:, 5:6], in_=stat[:, src_idx:src_idx + 1], func=AF.Ln,
            bias=0.0, scale=1.0), waits=waits)
        E("s", lambda s: s.activation(
            out=dst_ap, in_=stat[:, 5:6], func=AF.Exp, bias=0.0, scale=-0.5),
          inc=(sS, 1))
        return sS.n

    def vrecip(out_ap, in_ap):
        E("v", lambda v: v.drain())
        E("v", lambda v: v.reciprocal(out=out_ap, in_=in_ap))
        E("v", lambda v: v.drain())

    def q_build(gq):
        ig = gq % 2
        for tt in range(2):
            tile = gq * 2 + tt
            fence = ([(sV, p0_v), (sS, p0_s), (sT, p0_t)] if gq == 0 else [])
            E("y", lambda y, tile=tile, ig=ig, tt=tt: y.dma_start(
                out=xg[ig][:, tt * D:(tt + 1) * D], in_=dxin[tile * P:(tile + 1) * P, :]),
              waits=([(sV, st["xg_free"][ig])] if tt == 0 else []) + fence, inc16=sLD)
        ld = sLD.n
        for tt in range(2):
            tile = gq * 2 + tt
            xt_ap = xg[ig][:, tt * D:(tt + 1) * D]
            srs = ln_stats(xt_ap, m_all[:, tile:tile + 1], rstd_all[:, tile:tile + 1],
                           1e-5, [(sLD, ld)])
            E("v", lambda v, xt_ap=xt_ap, tile=tile: v.tensor_scalar(
                out=xnt[:, :], in0=xt_ap, scalar1=m_all[:, tile:tile + 1],
                scalar2=rstd_all[:, tile:tile + 1], op0=ALU.subtract, op1=ALU.mult),
              waits=[(sS, srs)], inc=(sV, 1))
            vxn = sV.n
            E("v", lambda v: v.tensor_reduce(
                out=stat[:, 0:1], in_=xnt[:, :], axis=AX.X, op=ALU.add))
            E("v", lambda v: v.drain())
            sq_accum(xnt[:, :], 3, [(sV, vxn)])
            E("v", lambda v: v.tensor_scalar(
                out=stat[:, 6:7], in0=stat[:, 0:1], scalar1=INV_D, scalar2=0.0,
                op0=ALU.mult, op1=ALU.bypass))
            E("v", lambda v: v.drain())
            E("v", lambda v: v.tensor_scalar(
                out=stat[:, 3:4], in0=stat[:, 3:4], scalar1=INV_D, scalar2=0.0,
                op0=ALU.mult, op1=ALU.bypass))
            E("v", lambda v: v.drain())
            E("v", lambda v: v.scalar_tensor_tensor(
                out=stat[:, 4:5], in0=stat[:, 6:7], scalar=stat[:, 6:7],
                in1=stat[:, 3:4], op0=ALU.mult, op1=ALU.subtract))
            E("v", lambda v: v.drain())
            E("v", lambda v: v.tensor_scalar(
                out=stat[:, 4:5], in0=stat[:, 4:5], scalar1=-1.0, scalar2=1e-12,
                op0=ALU.mult, op1=ALU.add), inc=(sV, 1))
            sr2 = rsqrt_nl(4, stat[:, 7:8], [(sV, sV.n)])
            E("v", lambda v: v.tensor_scalar(
                out=yt[:, :], in0=xnt[:, :], scalar1=stat[:, 6:7],
                scalar2=stat[:, 7:8], op0=ALU.subtract, op1=ALU.mult),
              waits=[(sS, sr2)])
            E("v", lambda v: v.tensor_tensor(
                out=yt[:, :], in0=yt[:, :], in1=lnw_b[:, :], op=ALU.mult))
            E("v", lambda v: v.tensor_tensor(
                out=yt[:, :], in0=yt[:, :], in1=lnb_b[:, :], op=ALU.add),
              inc=(sV, 1))
            vy = sV.n
            sq_accum(yt[:, :], 3, [(sV, vy)])
            E("s", lambda s: s.activation(
                out=stat[:, 5:6], in_=stat[:, 3:4], func=AF.Ln, bias=0.0, scale=1.0),
              waits=[(sV, sV.n)])
            E("s", lambda s: s.activation(
                out=stat[:, 4:5], in_=stat[:, 5:6], func=AF.Exp, bias=0.0, scale=0.5),
              inc=(sS, 1))
            E("v", lambda v: v.tensor_scalar(
                out=stat[:, 4:5], in0=stat[:, 4:5], scalar1=1e-12, scalar2=0.0,
                op0=ALU.max, op1=ALU.bypass), waits=[(sS, sS.n)])
            E("v", lambda v: v.drain())
            E("v", lambda v: v.reciprocal(out=stat[:, 5:6], in_=stat[:, 4:5]))
            E("v", lambda v: v.drain())
            E("v", lambda v: v.tensor_scalar(
                out=qtile[:, :], in0=yt[:, :], scalar1=stat[:, 5:6], scalar2=0.0,
                op0=ALU.mult, op1=ALU.bypass), inc=(sV, 1))
            E("v", lambda v: v.drain())
            vq = sV.n
            for c in range(8):
                E("t", lambda t, c=c: t.transpose(
                    out=ps_qt[:, c * P:(c + 1) * P], in_=qtile[:, c * P:(c + 1) * P],
                    identity=ident_b[:, :]),
                  waits=[(sV, vq), (sV, st["qt_evac"])] if c == 0 else [],
                  inc=(sT, 1) if c == 7 else None)
            tq = sT.n
            E("v", lambda v, ig=ig, tt=tt: v.tensor_copy(
                out=QT[ig][:, :].rearrange("p (h q) -> p h q", h=8)[:, :, tt * P:(tt + 1) * P],
                in_=ps_qt[:, :].rearrange("p (h q) -> p h q", h=8)),
              waits=[(sT, tq), (sT, st["qt_free"][ig]) if tt == 0 else (sT, 0)],
              inc=(sV, 1))
            st["qt_evac"] = sV.n
        E("v", lambda v: v.drain(), inc=(sV, 1))
        st["xg_free"][ig] = sV.n
        st["qt_built"][ig] = sV.n

    def mm1_half(gq, half):
        ig = gq % 2
        for kk in range(HB):
            k = half * HB + kk
            i2 = kk % 3
            for hc in range(8):
                E("t", lambda t, k=k, hc=hc, i2=i2, ig=ig: t.matmul(
                    out=ps_sT[i2][:, :],
                    lhsT=KT[:, hc * I0 + k * P: hc * I0 + (k + 1) * P],
                    rhs=QT[ig][:, hc * GRP:(hc + 1) * GRP],
                    start=(hc == 0), stop=(hc == 7)),
                  waits=[(sV, st["sT_free"][i2]), (sV, st["qt_built"][ig])]
                        if hc == 0 else [],
                  inc=(sT, 1) if hc == 7 else None)
            t1 = sT.n
            E("v", lambda v, kk=kk, i2=i2: v.tensor_scalar(
                out=sstage[:, kk * GRP:(kk + 1) * GRP], in0=ps_sT[i2][:, :],
                scalar1=CLIP_LO, scalar2=CLIP_HI, op0=ALU.max, op1=ALU.min),
              waits=[(sT, t1)], inc=(sV, 1))
            st["sT_free"][i2] = sV.n

    def bchain_half(half):
        ih = half
        E("v", lambda v: v.drain(), inc=(sV, 1))
        vclip = sV.n
        E("g", lambda g: g.tensor_tensor(
            out=aux[:, :], in0=sstage[:, :], in1=sstage[:, :], op=ALU.mult),
          waits=[(sV, vclip), (sS, st["aux_free"])], inc=(sG, 1))
        gp = sG.n
        E("s", lambda s: s.activation(
            out=aux[:, :], in_=aux[:, :], func=AF.Ln, bias=1.0, scale=-1.0),
          waits=[(sG, gp)])
        E("s", lambda s: s.activation(
            out=aux[:, :], in_=aux[:, :], func=AF.Exp, bias=0.0, scale=-0.5),
          inc=(sS, 1))
        se = sS.n
        E("v", lambda v: v.tensor_tensor(
            out=aux[:, :], in0=sstage[:, :], in1=aux[:, :], op=ALU.mult),
          waits=[(sS, se)], inc=(sV, 1))
        E("v", lambda v: v.drain())
        va = sV.n
        E("s", lambda s: s.activation(
            out=aux[:, :], in_=aux[:, :], func=AF.Arctan, bias=0.0, scale=1.0),
          waits=[(sV, va)])
        E("s", lambda s: s.activation(
            out=aux[:, :], in_=aux[:, :], func=AF.Ln, bias=kc[:, 2:3], scale=INV_PI))
        E("s", lambda s, ih=ih: s.activation(
            out=Bst[ih][:, :], in_=aux[:, :], func=AF.Exp, bias=0.0, scale=9.0),
          waits=[(sT, max(st["bst_free"]))], inc=(sS, 1))
        st["aux_free"] = sS.n
        return sS.n

    def mm2_half(gq, half, sb_done):
        ih = half
        for kk in range(HB):
            k = half * HB + kk
            i2 = kk % 2
            E("y", lambda y, k=k, i2=i2: y.dma_start(
                out=qwk[i2][:, :], in_=dqwbf[k * P:(k + 1) * P, :]),
              waits=[(sT, st["qwk_free"][i2]), (sQW, qw_cached if gq == 0 else 0)],
              inc16=sLD)
            qld = sLD.n
            for tt in range(2):
                for hf in range(2):
                    first = (tt == 0 and hf == 0)
                    E("t", lambda t, kk=kk, tt=tt, hf=hf, i2=i2, ih=ih, k=k: t.matmul(
                        out=ps_X[tt * 2 + hf][:, :],
                        lhsT=Bst[ih][:, kk * GRP + tt * P: kk * GRP + (tt + 1) * P],
                        rhs=qwk[i2][:, hf * 512:(hf + 1) * 512],
                        start=(k == 0), stop=(k == NC0 - 1)),
                      waits=([(sS, sb_done), (sLD, qld)] +
                             ([(sV, st["xfin_v"])] if k == 0 else []))
                            if first else [],
                      inc=(sT, 1) if (tt == 1 and hf == 1) else None)
            st["qwk_free"][i2] = sT.n
        st["bst_free"][ih] = sT.n

    def x_finalize(gq):
        tdone = sT.n
        for tt in range(2):
            tile = gq * 2 + tt
            pa, pb = ps_X[tt * 2], ps_X[tt * 2 + 1]
            E("s", lambda s, pa=pa: s.activation(
                out=sq_scr[:, 0:512], in_=pa[:, :], func=AF.Square,
                bias=0.0, scale=1.0),
              waits=[(sT, tdone), (sV, sV.n)])
            E("s", lambda s, pb=pb: s.activation(
                out=sq_scr[:, 512:1024], in_=pb[:, :], func=AF.Square,
                bias=0.0, scale=1.0), inc=(sS, 1))
            sx = sS.n
            E("v", lambda v: v.tensor_reduce(
                out=stat[:, 3:4], in_=sq_scr[:, :], axis=AX.X, op=ALU.add),
              waits=[(sS, sx)], inc=(sV, 1))
            E("s", lambda s: s.activation(
                out=stat[:, 5:6], in_=stat[:, 3:4], func=AF.Ln, bias=0.0, scale=1.0),
              waits=[(sV, sV.n)])
            E("s", lambda s: s.activation(
                out=stat[:, 4:5], in_=stat[:, 5:6], func=AF.Exp, bias=0.0, scale=0.5),
              inc=(sS, 1))
            E("v", lambda v: v.tensor_scalar(
                out=stat[:, 4:5], in0=stat[:, 4:5], scalar1=1e-12, scalar2=0.0,
                op0=ALU.max, op1=ALU.bypass), waits=[(sS, sS.n)])
            E("v", lambda v: v.drain())
            E("v", lambda v: v.reciprocal(out=stat[:, 5:6], in_=stat[:, 4:5]))
            E("v", lambda v: v.drain())
            E("v", lambda v, pa=pa, tt=tt: v.scalar_tensor_tensor(
                out=Xout[tt][:, 0:512], in0=pa[:, :], scalar=stat[:, 5:6],
                in1=bias1_b[:, 0:512], op0=ALU.mult, op1=ALU.add),
              waits=[(sA0, max(st["xout_free"]))])
            E("v", lambda v, pb=pb, tt=tt: v.scalar_tensor_tensor(
                out=Xout[tt][:, 512:1024], in0=pb[:, :], scalar=stat[:, 5:6],
                in1=bias1_b[:, 512:1024], op0=ALU.mult, op1=ALU.add),
              inc=(sV, 1))
            vf = sV.n
            E("s", lambda s, tile=tile, tt=tt: s.dma_start(
                out=dattn0[tile * P:(tile + 1) * P, :], in_=Xout[tt][:, :]),
              waits=[(sV, vf)], inc16=sA0)
            st["xout_free"][tt] = sA0.n
            st["attn0_sp"][tile] = sA0.n
        st["xfin_v"] = sV.n

    q_build(0)
    for gq in range(NG):
        mm1_half(gq, 0)
        sb0 = bchain_half(0)
        mm2_half(gq, 0, sb0)
        if gq + 1 < NG:
            q_build(gq + 1)
        mm1_half(gq, 1)
        sb1 = bchain_half(1)
        mm2_half(gq, 1, sb1)
        st["qt_free"][gq % 2] = sT.n
        x_finalize(gq)

    if upto == "dbgp1":
        E("v", lambda v: v.tensor_copy(out=aux[:, :], in_=sstage[:, :]), inc=(sV, 1))
        fw = [(sV, sV.n), (sS, sS.n), (sT, sT.n)]
        def dump(row0, src_ap, eng="y"):
            E(eng, lambda e, r=row0, a=src_ap: e.dma_start(
                out=dout[r * P:(r + 1) * P, 0:a.shape[1]], in_=a), waits=fw, inc16=sOUT)
        dump(0, KT[:, 3072:4096], "g")
        dump(1, QT[1][:, 0:1024], "g")
        dump(2, QT[1][:, 1024:2048], "g")
        E("y", lambda y: y.dma_start(out=dout[3 * P:4 * P, 0:NT], in_=m_all[:, :]),
          waits=fw, inc16=sOUT)
        E("y", lambda y: y.dma_start(out=dout[3 * P:4 * P, NT:2 * NT], in_=rstd_all[:, :]),
          waits=fw, inc16=sOUT)
        dump(4, qtile[:, :], "g")
        dump(5, yt[:, :])
        for i in range(4):
            dump(6 + i, sstage[:, i * 1024:(i + 1) * 1024])
        for i in range(4):
            dump(10 + i, Bst[1][:, i * 1024:(i + 1) * 1024], "g")
        for i in (2, 3):
            dump(12 + i, aux[:, i * 1024:(i + 1) * 1024])
        E("g", lambda g: g.nop(), waits=[(sOUT, sOUT.n)])
        with nc.Block() as block:
            @block.sync
            def _(y):
                b.sched.replay("y", y)

            @block.vector
            def _(v):
                b.sched.replay("v", v)

            @block.scalar
            def _(s):
                b.sched.replay("s", s)

            @block.tensor
            def _(t):
                b.sched.replay("t", t)

            @block.gpsimd
            def _(g):
                b.sched.replay("g", g)
        return nc

    b.free("xg0", "xg1", "QT0", "QT1", "qwk0", "qwk1", "sstage", "aux",
           "Bst0", "xnt", "yt", "qtile", "Xout0",
           "ps_sT0", "ps_sT1", "ps_sT2", "ps_X0", "ps_X1", "ps_X2", "ps_X3", "ps_qt",
           "KT")

    WrB = b.sb("WrB", [P, 8 * D], BF16)
    WkB = b.sb("WkB", [P, 8 * D], BF16)
    WvB = b.sb("WvB", [P, 8 * D], BF16)
    x1t = [b.sb(f"x1t{i}", [P, D], F32) for i in range(2)]
    xstr = [b.sb(f"xstr{i}", [P, D], F32) for i in range(2)]
    a0str = [b.sb(f"a0str{i}", [P, D], F32) for i in range(2)]
    xn_p2 = b.sb("xn_p2", [P, D], F32)
    xnTP = [b.sb(f"xnTP{i}", [P, 8 * 129], F32) for i in range(2)]
    dTt = b.sb("dTt", [P, D], F32)
    mixtmp = b.sb("mixtmp", [P, D], F32)
    xrT = b.sb("xrT", [P, D], BF16)
    xkT = b.sb("xkT", [P, D], BF16)
    r_tile = b.sb("r_tile", [P, D], F32)
    kact_f = b.sb("kact_f", [P, D], F32)
    kact_b = b.sb("kact_b", [P, D], BF16)
    kactT = b.sb("kactT", [P, D], BF16)
    rv = b.sb("rv", [P, D], F32)
    xn2f = [b.sb(f"xn2f{i}", [P, D], F32) for i in range(2)]
    ps_xnT = b.ps("ps_xnT", [P, D], F32)
    ps_mm = [b.ps(f"ps_mm{i}", [P, 512], F32) for i in range(2)]
    ps_kT = b.ps("ps_kT", [P, D], BF16)

    p1_v, p1_s, p1_t = sV.n, sS.n, sT.n
    p1_fence = [(sV, p1_v), (sS, p1_s), (sT, p1_t)]
    wB_ld = []
    for src, dst in [(dwrT, WrB), (dwkT, WkB), (dwvT, WvB)]:
        E("g", lambda g, s=src, d=dst: g.dma_start(
            out=d[:, :].rearrange("p (c j) -> p c j", c=8),
            in_=s.rearrange("(c p) j -> p c j", p=P)),
          waits=p1_fence, inc16=sGD)
        wB_ld.append(sGD.n)

    def mm_1024(ps_pair, lhsT_buf, W_buf, wait_first, done_inc=True):
        for sl in range(2):
            for cc in range(8):
                E("t", lambda t, sl=sl, cc=cc, lb=lhsT_buf, wb=W_buf: t.matmul(
                    out=ps_pair[sl][:, :], lhsT=lb[:, cc * P:(cc + 1) * P],
                    rhs=wb[:, cc * D + sl * 512: cc * D + (sl + 1) * 512],
                    start=(cc == 0), stop=(cc == 7)),
                  waits=wait_first if (sl == 0 and cc == 0) else [],
                  inc=(sT, 1) if (cc == 7) else None)

    p2 = {"x_free": [0, 0], "a0_free": [0, 0], "x1_sp": [0] * NT,
          "xn2_sp": [0] * NT, "xn2T_evac": 0,
          "ssq_done": [0] * NT, "gred_done": [0] * NT}
    prevten = [0]

    for tile in range(NT):
        i2 = tile % 2
        cur, prv = xnTP[tile % 2], xnTP[1 - tile % 2]
        E("y", lambda y, tile=tile, i2=i2: y.dma_start(
            out=xstr[i2][:, :], in_=dxin[tile * P:(tile + 1) * P, :]),
          waits=[(sV, p2["x_free"][i2])] + (p1_fence if tile < 2 else []), inc16=sLD)
        xld = sLD.n
        E("y", lambda y, tile=tile, i2=i2: y.dma_start(
            out=a0str[i2][:, :], in_=dattn0[tile * P:(tile + 1) * P, :]),
          waits=[(sA0, sA0.n), (sV, p2["a0_free"][i2])], inc16=sLD)
        ald = sLD.n
        E("v", lambda v, tile=tile, i2=i2: v.tensor_scalar(
            out=xn_p2[:, :], in0=xstr[i2][:, :], scalar1=m_all[:, tile:tile + 1],
            scalar2=rstd_all[:, tile:tile + 1], op0=ALU.subtract, op1=ALU.mult),
          waits=[(sLD, xld), (sV, sV.n), (sT, prevten[0])], inc=(sV, 1))
        E("v", lambda v: v.drain())
        vxn = sV.n
        for c in range(8):
            E("t", lambda t, c=c: t.transpose(
                out=ps_xnT[:, c * P:(c + 1) * P], in_=xn_p2[:, c * P:(c + 1) * P],
                identity=ident_f[:, :]),
              waits=[(sV, vxn)] if c == 0 else [],
              inc=(sT, 1) if c == 7 else None)
        txn = sT.n
        E("v", lambda v, cur=cur: v.tensor_copy(
            out=cur[:, :].rearrange("p (h c) -> p h c", h=8)[:, :, 1:129],
            in_=ps_xnT[:, :].rearrange("p (h c) -> p h c", h=8)),
          waits=[(sT, txn)], inc=(sV, 1))
        if tile == 0:
            E("v", lambda v, cur=cur: v.memset(
                cur[:, :].rearrange("p (h c) -> p h c", h=8)[:, :, 0:1], 0.0))
        else:
            E("v", lambda v, cur=cur, prv=prv: v.tensor_copy(
                out=cur[:, :].rearrange("p (h c) -> p h c", h=8)[:, :, 0:1],
                in_=prv[:, :].rearrange("p (h c) -> p h c", h=8)[:, :, 128:129]))
        cur3 = cur[:, :].rearrange("p (h c) -> p h c", h=8)
        cx = cur3[:, :, 1:129]
        cp = cur3[:, :, 0:128]
        d3 = dTt[:, :].rearrange("p (h c) -> p h c", h=8)
        m3 = mixtmp[:, :].rearrange("p (h c) -> p h c", h=8)
        E("g", lambda g, cx=cx, cp=cp, d3=d3: g.tensor_tensor(
            out=d3, in0=cx, in1=cp, op=ALU.subtract),
          waits=[(sV, sV.n), (sGD, wB_ld[2])], inc=(sG, 1))
        gd = sG.n
        E("v", lambda v, d3=d3, m3=m3: v.tensor_tensor(
            out=m3, in0=d3, in1=mur_sb[:, :].unsqueeze(2).to_broadcast((P, 8, P)),
            op=ALU.mult), waits=[(sG, gd), (sLD, mu_ld), (sT, prevten[0])])
        E("v", lambda v, m3=m3, cp=cp: v.tensor_tensor(
            out=xrT[:, :].rearrange("p (h c) -> p h c", h=8), in0=m3, in1=cp,
            op=ALU.add), inc=(sV, 1))
        E("v", lambda v: v.drain())
        vxr = sV.n
        E("v", lambda v, d3=d3, m3=m3: v.tensor_tensor(
            out=m3, in0=d3, in1=muk_sb[:, :].unsqueeze(2).to_broadcast((P, 8, P)),
            op=ALU.mult))
        E("v", lambda v, m3=m3, cp=cp: v.tensor_tensor(
            out=xkT[:, :].rearrange("p (h c) -> p h c", h=8), in0=m3, in1=cp,
            op=ALU.add), inc=(sV, 1))
        E("v", lambda v: v.drain())
        vxk = sV.n
        mm_1024(ps_mm, xrT, WrB, [(sV, vxr), (sS, sS.n), (sV, sV.n)])
        tr = sT.n
        E("s", lambda s: s.activation(
            out=r_tile[:, 0:512], in_=ps_mm[0][:, :], func=AF.Sigmoid,
            bias=0.0, scale=1.0), waits=[(sT, tr - 1)])
        E("s", lambda s: s.activation(
            out=r_tile[:, 512:1024], in_=ps_mm[1][:, :], func=AF.Sigmoid,
            bias=0.0, scale=1.0), waits=[(sT, tr)], inc=(sS, 1))
        sr = sS.n
        mm_1024(ps_mm, xkT, WkB, [(sV, vxk), (sS, sr)])
        tk = sT.n
        E("v", lambda v: v.tensor_scalar(
            out=kact_f[:, 0:512], in0=ps_mm[0][:, :], scalar1=0.0, scalar2=0.0,
            op0=ALU.max, op1=ALU.bypass), waits=[(sT, tk - 1)])
        E("v", lambda v: v.tensor_scalar(
            out=kact_f[:, 512:1024], in0=ps_mm[1][:, :], scalar1=0.0, scalar2=0.0,
            op0=ALU.max, op1=ALU.bypass), waits=[(sT, tk)])
        E("v", lambda v: v.tensor_tensor(
            out=kact_b[:, :], in0=kact_f[:, :], in1=kact_f[:, :], op=ALU.mult),
          inc=(sV, 1))
        E("v", lambda v: v.drain())
        vka = sV.n
        for c in range(8):
            E("t", lambda t, c=c: t.transpose(
                out=ps_kT[:, c * P:(c + 1) * P], in_=kact_b[:, c * P:(c + 1) * P],
                identity=ident_b[:, :]),
              waits=[(sV, vka)] if c == 0 else [],
              inc=(sT, 1) if c == 7 else None)
        tkt = sT.n
        E("v", lambda v: v.tensor_copy(out=kactT[:, :], in_=ps_kT[:, :]),
          waits=[(sT, tkt)], inc=(sV, 1))
        E("v", lambda v: v.drain())
        vkt = sV.n
        mm_1024(ps_mm, kactT, WvB, [(sV, vkt)])
        tv = sT.n
        E("v", lambda v: v.scalar_tensor_tensor(
            out=rv[:, 0:512], in0=ps_mm[0][:, :], scalar=1.0,
            in1=r_tile[:, 0:512], op0=ALU.mult, op1=ALU.mult),
          waits=[(sT, tv - 1)])
        E("v", lambda v: v.scalar_tensor_tensor(
            out=rv[:, 512:1024], in0=ps_mm[1][:, :], scalar=1.0,
            in1=r_tile[:, 512:1024], op0=ALU.mult, op1=ALU.mult),
          waits=[(sT, tv)], inc=(sV, 1))
        prevten[0] = tv
        E("g", lambda g, i2=i2: g.tensor_tensor(
            out=mixtmp[:, :], in0=xstr[i2][:, :], in1=a0str[i2][:, :], op=ALU.add),
          waits=[(sLD, ald), (sV, sV.n)], inc=(sG, 1))
        ga = sG.n
        E("v", lambda v, tile=tile, i2=i2: v.tensor_tensor(
            out=x1t[i2][:, :], in0=mixtmp[:, :], in1=rv[:, :],
            op=ALU.add),
          waits=[(sG, ga), (sACT, p2["x1_sp"][tile - 2] if tile >= 2 else 0),
                 (sS, p2["ssq_done"][tile - 2] if tile >= 2 else 0)],
          inc=(sV, 1))
        vx1 = sV.n
        p2["x_free"][i2] = vx1
        p2["a0_free"][i2] = vx1
        E("s", lambda s, tile=tile, i2=i2: s.dma_start(
            out=dx1[tile * P:(tile + 1) * P, :], in_=x1t[i2][:, :]),
          waits=[(sV, vx1)], inc16=sACT)
        p2["x1_sp"][tile] = sACT.n
        E("v", lambda v, tile=tile, i2=i2: v.tensor_reduce(
            out=m2_all[:, tile:tile + 1], in_=x1t[i2][:, :], axis=AX.X,
            op=ALU.add), waits=[(sV, vx1)], inc=(sV, 1))
        E("v", lambda v: v.drain())
        E("s", lambda s, i2=i2: s.activation(
            out=sq_scr[:, :], in_=x1t[i2][:, :], func=AF.Square, bias=0.0,
            scale=1.0), waits=[(sV, sV.n)], inc=(sS, 1))
        ssq = sS.n
        E("v", lambda v, tile=tile: v.tensor_reduce(
            out=ss2_all[:, tile:tile + 1], in_=sq_scr[:, :], axis=AX.X,
            op=ALU.add), waits=[(sS, ssq)], inc=(sV, 1))
        E("v", lambda v: v.drain())
        p2["ssq_done"][tile] = ssq
        p2["gred_done"][tile] = 0

    E("v", lambda v: v.drain(), waits=[(sG, sG.n), (sV, sV.n)])
    E("v", lambda v: v.tensor_scalar(
        out=m2_all[:, :], in0=m2_all[:, :], scalar1=INV_D, scalar2=0.0,
        op0=ALU.mult, op1=ALU.bypass))
    E("v", lambda v: v.tensor_scalar(
        out=ss2_all[:, :], in0=ss2_all[:, :], scalar1=INV_D, scalar2=0.0,
        op0=ALU.mult, op1=ALU.bypass))
    E("v", lambda v: v.drain())
    E("v", lambda v: v.tensor_tensor(
        out=rstd2_all[:, :], in0=m2_all[:, :], in1=m2_all[:, :], op=ALU.mult))
    E("v", lambda v: v.drain())
    E("v", lambda v: v.tensor_tensor(
        out=rstd2_all[:, :], in0=ss2_all[:, :], in1=rstd2_all[:, :],
        op=ALU.subtract), inc=(sV, 1))
    E("s", lambda s: s.activation(
        out=rstd2_all[:, :], in_=rstd2_all[:, :], func=AF.Ln, bias=kc[:, 0:1], scale=1.0),
      waits=[(sV, sV.n)])
    E("s", lambda s: s.activation(
        out=rstd2_all[:, :], in_=rstd2_all[:, :], func=AF.Exp, bias=0.0, scale=-0.5),
      inc=(sS, 1))
    ln2fin = sS.n

    for tile in range(NT):
        i2 = tile % 2
        E("y", lambda y, tile=tile, i2=i2: y.dma_start(
            out=xstr[i2][:, :], in_=dx1[tile * P:(tile + 1) * P, :]),
          waits=[(sACT, sACT.n), (sV, p2["x_free"][i2])], inc16=sLD)
        x1ld = sLD.n
        E("v", lambda v, tile=tile, i2=i2: v.tensor_scalar(
            out=xn2f[i2][:, :], in0=xstr[i2][:, :],
            scalar1=m2_all[:, tile:tile + 1], scalar2=rstd2_all[:, tile:tile + 1],
            op0=ALU.subtract, op1=ALU.mult),
          waits=[(sS, ln2fin), (sLD, x1ld),
                 (sACT, p2["xn2_sp"][tile - 2] if tile >= 2 else 0)],
          inc=(sV, 1))
        E("v", lambda v: v.drain())
        p2["x_free"][i2] = sV.n
        vx = sV.n
        E("s", lambda s, tile=tile, i2=i2: s.dma_start(
            out=dxn2[tile * P:(tile + 1) * P, :], in_=xn2f[i2][:, :]),
          waits=[(sV, vx)], inc16=sACT)
        p2["xn2_sp"][tile] = sACT.n
        E("v", lambda v, i2=i2: v.tensor_reduce(
            out=stat[:, 6:7], in_=xn2f[i2][:, :], axis=AX.X, op=ALU.max,
            apply_absolute_value=True))
        E("v", lambda v: v.drain())
        E("v", lambda v: v.tensor_tensor(
            out=runmax[:, 0:1], in0=runmax[:, 0:1], in1=stat[:, 6:7], op=ALU.max),
          inc=(sV, 1))
        E("v", lambda v: v.drain())
        for c in range(8):
            E("t", lambda t, c=c, i2=i2: t.transpose(
                out=ps_xnT[:, c * P:(c + 1) * P], in_=xn2f[i2][:, c * P:(c + 1) * P],
                identity=ident_f[:, :]),
              waits=[(sV, vx), (sV, p2["xn2T_evac"])] if c == 0 else [],
              inc=(sT, 1) if c == 7 else None)
        tt2 = sT.n
        E("v", lambda v, tile=tile: v.tensor_copy(
            out=xn2T[:, :].rearrange("p (h c) -> p h c", h=8)[:, :, tile * P:(tile + 1) * P],
            in_=ps_xnT[:, :].rearrange("p (h c) -> p h c", h=8)),
          waits=[(sT, tt2)], inc=(sV, 1))
        p2["xn2T_evac"] = sV.n
    E("v", lambda v: v.drain(), inc=(sV, 1))
    xn2T_done = sV.n

    E("g", lambda g: g.dma_start(out=cc1i[:, :], in_=runmax[:, 0:1]),
      waits=[(sV, sV.n)], inc16=sGD)
    E("g", lambda g: g.collective_compute(
        "AllReduce", ALU.max, replica_groups=[list(range(8))],
        ins=[cc1i.ap().opt()], outs=[cc1o.ap().opt()]),
      waits=[(sGD, sGD.n)], inc=(sCC, 1))
    ar1 = sCC.n

    b.free("WrB", "WkB", "WvB", "xstr0", "xstr1", "a0str0", "a0str1",
           "xn_p2", "xnTP0", "xnTP1", "dTt", "mixtmp", "xrT", "xkT",
           "r_tile", "kact_f", "kact_b", "kactT", "rv", "xn2f0", "xn2f1",
           "ps_xnT", "ps_mm0", "ps_mm1", "ps_kT", "x1t0", "x1t1")

    if upto == "p1":
        for tile in range(NT):
            E("y", lambda y, tile=tile: y.dma_start(
                out=dout[tile * P:(tile + 1) * P, :],
                in_=dattn0[tile * P:(tile + 1) * P, :]),
              waits=[(sA0, st["attn0_sp"][tile])], inc16=sOUT)
        E("y", lambda y: y.nop(), waits=[(sOUT, sOUT.n)])
        with nc.Block() as block:
            @block.sync
            def _(y):
                b.sched.replay("y", y)

            @block.vector
            def _(v):
                b.sched.replay("v", v)

            @block.scalar
            def _(s):
                b.sched.replay("s", s)

            @block.tensor
            def _(t):
                b.sched.replay("t", t)

            @block.gpsimd
            def _(g):
                b.sched.replay("g", g)
        return nc

    if upto == "l0":
        for tile in range(NT):
            E("y", lambda y, tile=tile: y.dma_start(
                out=dout[tile * P:(tile + 1) * P, :],
                in_=dx1[tile * P:(tile + 1) * P, :]),
              waits=[(sACT, p2["x1_sp"][tile]), (sV, xn2T_done), (sCC, ar1)],
              inc16=sOUT)
        E("y", lambda y: y.nop(), waits=[(sOUT, sOUT.n)])
        with nc.Block() as block:
            @block.sync
            def _(y):
                b.sched.replay("y", y)

            @block.vector
            def _(v):
                b.sched.replay("v", v)

            @block.scalar
            def _(s):
                b.sched.replay("s", s)

            @block.tensor
            def _(t):
                b.sched.replay("t", t)

            @block.gpsimd
            def _(g):
                b.sched.replay("g", g)
        return nc


    p2e = [(sV, sV.n), (sS, sS.n), (sT, sT.n)]
    b.free("sq_scr", "vrow")
    qT = b.sb("qT", [P, 8 * T], BF16)
    kTt = b.sb("kTt", [P, 8 * T], BF16)
    vst = b.sb("vst", [P, NT * 1040], BF16)
    Wb = [b.sb(f"Wb{i}", [P, 8 * D], BF16) for i in range(2)]
    PT = [b.sb(f"PT{i}", [P, 512], BF16) for i in range(2)]
    accsb = b.sb("accsb", [P, 512], F32)
    rrow = b.sb("rrow", [P, 512], F32)
    aoT = [b.sb(f"aoT{i}", [P, 8 * 512], BF16) for i in range(2)]
    opst = [b.sb("opst0", [P, D], F32)] * 2
    ps_s4 = [b.ps(f"ps_s4{i}", [P, 512], F32) for i in range(2)]
    ps_acc = [b.ps(f"ps_acc{i}", [P, 512], F32) for i in range(2)]
    ps_rb = b.ps("ps_rb", [P, 512], F32)
    ps_op = [b.ps(f"ps_op{i}", [P, 512], F32) for i in range(2)]

    def wload(dst, srcap, waits):
        E("g", lambda g, s=srcap, d=dst: g.dma_start(
            out=d[:, :].rearrange("p (c j) -> p c j", c=8),
            in_=s.rearrange("(c p) j -> p c j", p=P)),
          waits=waits, inc16=sGD)
        return sGD.n

    wq_ld = wload(Wb[0], dl1q, p2e)
    wk_ld = wload(Wb[1], dl1k, p2e)
    E("g", lambda g: g.memset(
        vst[:, :].rearrange("p (a c) -> p a c", c=65)[:, :, 64:65], 1.0),
      waits=p2e, inc=(sG, 1))
    vones = sG.n

    mmb_free = [0, 0]
    def proj_T(dstT, wld, wbuf):
        for dqc in range(8):
            for tg in range(4):
                bank = (dqc * 4 + tg) % 2
                for ec in range(8):
                    E("t", lambda t, dqc=dqc, tg=tg, ec=ec, bank=bank, wld=wld, wbuf=wbuf: t.matmul(
                        out=ps_s4[bank][:, :],
                        lhsT=wbuf[:, ec * D + dqc * P: ec * D + (dqc + 1) * P],
                        rhs=xn2T[:, ec * T + tg * 512:(ec * T + (tg + 1) * 512)],
                        start=(ec == 0), stop=(ec == 7)),
                      waits=[(sGD, wld), (sV, xn2T_done), (sS, mmb_free[bank])]
                            if ec == 0 else [],
                      inc=(sT, 1) if ec == 7 else None)
                tq = sT.n
                E("s", lambda s, dqc=dqc, tg=tg, bank=bank, dstT=dstT: s.activation(
                    out=dstT[:, dqc * T + tg * 512: dqc * T + (tg + 1) * 512],
                    in_=ps_s4[bank][:, :], func=AF.Copy, bias=0.0, scale=1.0),
                  waits=[(sT, tq)], inc=(sS, 1))
                mmb_free[bank] = sS.n
    proj_T(qT, wq_ld, Wb[0])
    proj_T(kTt, wk_ld, Wb[1])
    qk_done_t = sT.n
    wv_ld = wload(Wb[0], dl1v, [(sT, qk_done_t)])
    for tt in range(NT):
        for es in range(2):
            bank = (tt * 2 + es) % 2
            for ec in range(8):
                E("t", lambda t, tt=tt, es=es, ec=ec, bank=bank: t.matmul(
                    out=ps_s4[bank][:, :],
                    lhsT=xn2T[:, ec * T + tt * P: ec * T + (tt + 1) * P],
                    rhs=Wb[0][:, ec * D + es * 512: ec * D + (es + 1) * 512],
                    start=(ec == 0), stop=(ec == 7)),
                  waits=[(sGD, wv_ld), (sS, mmb_free[bank])] if ec == 0 else [],
                  inc=(sT, 1) if ec == 7 else None)
            tv = sT.n
            E("s", lambda s, tt=tt, es=es, bank=bank: s.activation(
                out=vst[:, tt * 1040 + es * 520: tt * 1040 + (es + 1) * 520]
                    .rearrange("p (h d) -> p h d", d=65)[:, :, 0:64],
                in_=ps_s4[bank][:, :].rearrange("p (h d) -> p h d", d=64),
                func=AF.Copy, bias=0.0, scale=1.0),
              waits=[(sT, tv), (sG, vones)], inc=(sS, 1))
            mmb_free[bank] = sS.n
    v_done_s = sS.n
    wo_ld = wload(Wb[1], dl1o, [(sT, qk_done_t)])

    att = {"sfree": [mmb_free[0], mmb_free[1]], "ptfree": [0, 0], "accfree": [0, 0],
           "rbfree": 0, "aofree": [0, 0], "opfree": [0, 0],
           "opstfree": [0, 0]}
    attnP_sp = [0] * NT
    for qg in range(4):
        cur = qg % 2
        for h in range(16):
            hc, off = h // 2, 64 * (h % 2)
            ia = h % 2
            pieces = [(4 * qg, 0, 512, True)]
            for kt in range(4 * qg):
                pieces.append((kt, 0, 512, False))
            for jj in range(1, 4):
                pieces.append((4 * qg + jj, 128 * jj, 128, True))
                W = 512 - 128 * (jj + 1)
                if W > 0:
                    pieces.append((4 * qg + jj, 128 * (jj + 1), W, False))
            n = len(pieces)
            meta = []
            def emit_smm(i):
                kt, q0, qw, msk = pieces[i]
                bank = i % 2
                w0 = [(sS, att["sfree"][bank])]
                if i == 0:
                    w0 += [(sS, att["accfree"][ia]), (sS, v_done_s)]
                E("t", lambda t, kt=kt, q0=q0, qw=qw, bank=bank, hc=hc, off=off, qg=qg: t.matmul(
                    out=ps_s4[bank][:, 0:qw],
                    lhsT=kTt[off:off + 64, hc * T + kt * P: hc * T + (kt + 1) * P],
                    rhs=qT[off:off + 64, hc * T + qg * 512 + q0: hc * T + qg * 512 + q0 + qw],
                    start=True, stop=True),
                  waits=w0, inc=(sT, 1))
                tS = sT.n
                E("s", lambda s, qw=qw, bank=bank: s.activation(
                    out=PT[bank][:, 0:qw], in_=ps_s4[bank][:, 0:qw], func=AF.Exp,
                    bias=0.0, scale=0.125),
                  waits=[(sT, tS), (sT, att["ptfree"][bank])], inc=(sS, 1))
                eS = sS.n
                att["sfree"][bank] = eS
                mV = 0
                if msk:
                    E("v", lambda v, bank=bank: v.tensor_tensor(
                        out=PT[bank][:, 0:128], in0=PT[bank][:, 0:128],
                        in1=tri[:, :], op=ALU.mult),
                      waits=[(sS, eS)], inc=(sV, 1))
                    E("v", lambda v: v.drain())
                    mV = sV.n
                meta.append((bank, eS, mV))
            def emit_av(i):
                kt, q0, qw, msk = pieces[i]
                bank, eS, mV = meta[i]
                w0 = [(sS, eS)]
                if mV:
                    w0.append((sV, mV))
                E("t", lambda t, kt=kt, q0=q0, qw=qw, bank=bank, h=h, ia=ia: t.matmul(
                    out=ps_acc[ia][0:65, q0:q0 + qw],
                    lhsT=vst[:, kt * 1040 + h * 65: kt * 1040 + (h + 1) * 65],
                    rhs=PT[bank][:, 0:qw],
                    start=(i == 0), stop=(i == n - 1)),
                  waits=w0, inc=(sT, 1))
                att["ptfree"][bank] = sT.n
            emit_smm(0)
            for i in range(1, n):
                emit_smm(i)
                emit_av(i - 1)
            emit_av(n - 1)
            tlast = sT.n
            E("s", lambda s, ia=ia: s.activation(
                out=rrow[64:65, 0:512], in_=ps_acc[ia][64:65, 0:512],
                func=AF.Ln, bias=0.0, scale=1.0), waits=[(sT, tlast)])
            E("s", lambda s: s.activation(
                out=rrow[64:65, 0:512], in_=rrow[64:65, 0:512],
                func=AF.Exp, bias=0.0, scale=-1.0), inc=(sS, 1))
            rS = sS.n
            E("s", lambda s, ia=ia: s.activation(
                out=accsb[0:64, 0:512], in_=ps_acc[ia][0:64, 0:512],
                func=AF.Copy, bias=0.0, scale=1.0), inc=(sS, 1))
            aS = sS.n
            att["accfree"][ia] = aS
            E("t", lambda t: t.matmul(
                out=ps_rb[0:64, 0:512], lhsT=ones_f[64:65, 0:64],
                rhs=rrow[64:65, 0:512], start=True, stop=True),
              waits=[(sS, rS), (sV, att["rbfree"])], inc=(sT, 1))
            rbT = sT.n
            w0 = [(sS, aS), (sT, rbT)]
            if h < 2:
                w0.append((sT, att["aofree"][cur]))
            E("v", lambda v, h=h, hc=hc, cur=cur: v.tensor_tensor(
                out=aoT[cur][64 * (h % 2):64 * (h % 2) + 64, hc * 512:(hc + 1) * 512],
                in0=accsb[0:64, 0:512], in1=ps_rb[0:64, 0:512], op=ALU.mult),
              waits=w0, inc=(sV, 1))
            att["rbfree"] = sV.n
        E("v", lambda v: v.drain(), inc=(sV, 1))
        ao_built = sV.n
        for qs in range(4):
            io = qs % 2
            tops = []
            for half in range(2):
                for ec in range(8):
                    E("t", lambda t, qs=qs, half=half, ec=ec, cur=cur: t.matmul(
                        out=ps_op[half][:, :],
                        lhsT=aoT[cur][:, ec * 512 + qs * P: ec * 512 + (qs + 1) * P],
                        rhs=Wb[1][:, ec * D + half * 512: ec * D + (half + 1) * 512],
                        start=(ec == 0), stop=(ec == 7)),
                      waits=[(sV, ao_built), (sGD, wo_ld), (sV, att["opfree"][half])]
                            if ec == 0 else [],
                      inc=(sT, 1) if ec == 7 else None)
                tops.append(sT.n)
            E("v", lambda v, io=io: v.tensor_tensor(
                out=opst[io][:, 0:512], in0=ps_op[0][:, :], in1=bo_b[:, 0:512],
                op=ALU.add),
              waits=[(sT, tops[0]), (sACT, max(att["opstfree"]))])
            E("v", lambda v, io=io: v.tensor_tensor(
                out=opst[io][:, 512:1024], in0=ps_op[1][:, :], in1=bo_b[:, 512:1024],
                op=ALU.add), waits=[(sT, tops[1])], inc=(sV, 1))
            vOP = sV.n
            att["opfree"][0] = vOP
            att["opfree"][1] = vOP
            tile = qg * 4 + qs
            E("s", lambda s, tile=tile, io=io: s.dma_start(
                out=dattnP[tile * P:(tile + 1) * P, :], in_=opst[io][:, :]),
              waits=[(sV, vOP)], inc16=sACT)
            attnP_sp[tile] = sACT.n
            att["opstfree"][io] = sACT.n
        att["aofree"][cur] = sT.n

    b.free("opst0", "aoT0", "aoT1", "rrow", "accsb", "PT0", "PT1",
           "Wb0", "Wb1", "vst", "kTt", "qT",
           "ps_op0", "ps_op1", "ps_rb", "ps_acc0", "ps_acc1", "ps_s40", "ps_s41")
    b.free("xn2T")

    if upto == "p4":
        for tile in range(NT):
            E("y", lambda y, tile=tile: y.dma_start(
                out=dout[tile * P:(tile + 1) * P, :],
                in_=dattnP[tile * P:(tile + 1) * P, :]),
              waits=[(sACT, attnP_sp[tile])], inc16=sOUT)
        E("y", lambda y: y.nop(), waits=[(sOUT, sOUT.n)])
        _finish(nc, b)
        return nc

    p4e = [(sV, sV.n), (sS, sS.n), (sT, sT.n)]
    scl = b.sb("scl", [P, 12], F32)
    sdq = b.sb("sdq", [P, 192], F32)
    xqT = b.sb("xqT", [P, 8 * T], BF16)
    xbuf = [b.sb(f"xbuf{i}", [P, D], F32) for i in range(2)]
    ibuf = b.sb("ibuf", [P, D], I32)
    ubuf = b.sb("ubuf", [P, D], F32)
    wnat = [b.sb(f"wnat{i}", [P, D], F32) for i in range(2)]
    wqb = b.sb("wqb", [P, D], BF16)
    luT = [b.sb(f"luT{i}", [P, D], BF16) for i in range(2)]
    ustage = [b.sb(f"ustage{i}", [P, 512], BF16) for i in range(2)]
    ps_xt5 = b.ps("ps_xt5", [P, D], F32)
    ps_m5 = [b.ps(f"ps_m5{i}", [P, 512], F32) for i in range(2)]
    ps_w5 = b.ps("ps_w5", [P, D], BF16)

    def scale_setup(ccout, ar_val, col_s, col_r, fence):
        E("g", lambda g, c=ccout: g.dma_start(out=stat[:, 0:1], in_=c.ap()),
          waits=[(sCC, ar_val), (sV, sV.n), (sS, sS.n)] + fence, inc16=sGD)
        c1 = sGD.n
        E("t", lambda t: t.transpose(
            out=ps_xt5[0:1, 0:128], in_=stat[:, 0:1], identity=ident_f[:, :]),
          waits=[(sGD, c1), (sV, sV.n)], inc=(sT, 1))
        tc = sT.n
        E("v", lambda v: v.tensor_reduce(
            out=stat[0:1, 1:2], in_=ps_xt5[0:1, 0:128], axis=AX.X, op=ALU.max),
          waits=[(sT, tc)], inc=(sV, 1))
        E("v", lambda v: v.drain())
        E("v", lambda v: v.tensor_scalar(
            out=stat[0:1, 2:3], in0=stat[0:1, 1:2], scalar1=1.0 / 32767.0,
            scalar2=1e-12, op0=ALU.mult, op1=ALU.max), inc=(sV, 1))
        E("v", lambda v: v.drain())
        vsc = sV.n
        E("t", lambda t: t.matmul(
            out=ps_m5[0][:, 0:1], lhsT=ones_f[0:1, :], rhs=stat[0:1, 2:3],
            start=True, stop=True), waits=[(sV, vsc)], inc=(sT, 1))
        tb = sT.n
        E("v", lambda v, col_s=col_s: v.tensor_copy(
            out=scl[:, col_s:col_s + 1], in_=ps_m5[0][:, 0:1]),
          waits=[(sT, tb)], inc=(sV, 1))
        E("v", lambda v: v.drain())
        E("v", lambda v, col_s=col_s, col_r=col_r: v.reciprocal(
            out=scl[:, col_r:col_r + 1], in_=scl[:, col_s:col_s + 1]))
        E("v", lambda v: v.drain(), inc=(sV, 1))
        return sV.n

    sx_rdy = scale_setup(cc1o, ar1, 0, 1, p4e)

    xq5 = {"xfree": [0, 0], "evac": 0}
    for tile in range(NT):
        i2 = tile % 2
        leng, lsem = ("y", sLD) if i2 == 0 else ("g", sGD)
        E(leng, lambda e, tile=tile, i2=i2: e.dma_start(
            out=xbuf[i2][:, :], in_=dxn2[tile * P:(tile + 1) * P, :]),
          waits=[(sACT, sACT.n), (sV, xq5["xfree"][i2])] +
                (p4e if tile < 2 else []),
          inc16=lsem)
        ld = lsem.n
        E("v", lambda v, i2=i2: v.tensor_scalar(
            out=ubuf[:, :], in0=xbuf[i2][:, :], scalar1=scl[:, 1:2], scalar2=0.0,
            op0=ALU.mult, op1=ALU.bypass),
          waits=[(lsem, ld), (sV, sx_rdy), (sT, xq5["evac"] and 0)], inc=(sV, 1))
        xq5["xfree"][i2] = sV.n
        E("v", lambda v: v.tensor_copy(out=ibuf[:, :], in_=ubuf[:, :]))
        E("v", lambda v: v.tensor_copy(out=ubuf[:, :], in_=ibuf[:, :]), inc=(sV, 1))
        uv = sV.n
        for c in range(8):
            E("t", lambda t, c=c: t.transpose(
                out=ps_xt5[:, c * P:(c + 1) * P], in_=ubuf[:, c * P:(c + 1) * P],
                identity=ident_f[:, :]),
              waits=[(sV, uv), (sV, xq5["evac"])] if c == 0 else [],
              inc=(sT, 1) if c == 7 else None)
        tX = sT.n
        E("v", lambda v, tile=tile: v.tensor_scalar(
            out=xqT[:, :].rearrange("p (h c) -> p h c", h=8)[:, :, tile * P:(tile + 1) * P],
            in0=ps_xt5[:, :].rearrange("p (h c) -> p h c", h=8),
            scalar1=scl[:, 0:1], scalar2=0.0, op0=ALU.mult, op1=ALU.bypass),
          waits=[(sT, tX)], inc=(sV, 1))
        xq5["evac"] = sV.n
    E("v", lambda v: v.drain(), inc=(sV, 1))
    xq_done = sV.n

    q5 = {"wfree": [0, 0], "lufree": [0, 0], "ufree": [[0, 0], [0, 0]],
          "mfree": [0, 0]}
    spill_rec = {}

    def dequant_tile(src_sb, nblk, waits0):
        r3 = src_sb[:, :].rearrange("p (b k) -> p b k", k=32)
        E("v", lambda v, r3=r3, nblk=nblk: v.tensor_reduce(
            out=sdq[:, 0:nblk], in_=r3, axis=AX.X, op=ALU.max,
            apply_absolute_value=True), waits=waits0 + [(sS, sS.n)], inc=(sV, 1))
        E("v", lambda v: v.drain())
        E("v", lambda v, nblk=nblk: v.tensor_scalar(
            out=sdq[:, 0:nblk], in0=sdq[:, 0:nblk], scalar1=1.0 / 7.0,
            scalar2=1e-12, op0=ALU.mult, op1=ALU.max))
        E("v", lambda v: v.drain())
        E("v", lambda v, nblk=nblk: v.reciprocal(
            out=sdq[:, 96:96 + nblk], in_=sdq[:, 0:nblk]))
        E("v", lambda v: v.drain(), inc=(sV, 1))
        E("v", lambda v, r3=r3, nblk=nblk: v.tensor_tensor(
            out=ubuf[:, 0:nblk * 32].rearrange("p (b k) -> p b k", k=32), in0=r3,
            in1=sdq[:, 96:96 + nblk].unsqueeze(2).to_broadcast((P, nblk, 32)),
            op=ALU.mult), inc=(sV, 1))
        vr = sV.n
        E("v", lambda v, nblk=nblk: v.tensor_copy(
            out=ibuf[:, 0:nblk * 32], in_=ubuf[:, 0:nblk * 32]))
        E("v", lambda v, nblk=nblk: v.tensor_copy(
            out=ubuf[:, 0:nblk * 32], in_=ibuf[:, 0:nblk * 32]))
        E("v", lambda v, nblk=nblk: v.tensor_tensor(
            out=wqb[:, 0:nblk * 32].rearrange("p (b k) -> p b k", k=32),
            in0=ubuf[:, 0:nblk * 32].rearrange("p (b k) -> p b k", k=32),
            in1=sdq[:, 0:nblk].unsqueeze(2).to_broadcast((P, nblk, 32)),
            op=ALU.mult), inc=(sV, 1))
        E("v", lambda v: v.drain())
        return sV.n

    def upgate_pass(dw, dspill, rm_col, name, ssem):
        for c in range(NCF):
            i2 = c % 2
            leng, lsem = ("y", sLD) if i2 == 0 else ("g", sGD)
            E(leng, lambda e, c=c, i2=i2, dw=dw: e.dma_start(
                out=wnat[i2][:, :], in_=dw[c * P:(c + 1) * P, :]),
              waits=[(sV, q5["wfree"][i2])] + (p4e if c < 2 and name == "g" else []),
              inc16=lsem)
            wld = lsem.n
            wv = dequant_tile(wnat[i2], 32, [(lsem, wld)])
            q5["wfree"][i2] = wv
            for cc in range(8):
                E("t", lambda t, cc=cc: t.transpose(
                    out=ps_w5[:, cc * P:(cc + 1) * P], in_=wqb[:, cc * P:(cc + 1) * P],
                    identity=ident_b[:, :]),
                  waits=[(sV, wv), (sS, q5["lufree"][i2])] if cc == 0 else [],
                  inc=(sT, 1) if cc == 7 else None)
            tW = sT.n
            E("s", lambda s, i2=i2: s.activation(
                out=luT[i2][:, :], in_=ps_w5[:, :], func=AF.Copy, bias=0.0,
                scale=1.0), waits=[(sT, tW)], inc=(sS, 1))
            luS = sS.n
            q5["lufree"][i2] = luS
            for tg in range(4):
                bank = tg % 2
                for hc in range(8):
                    E("t", lambda t, hc=hc, tg=tg, bank=bank, i2=i2: t.matmul(
                        out=ps_m5[bank][:, :],
                        lhsT=luT[i2][:, hc * P:(hc + 1) * P],
                        rhs=xqT[:, hc * T + tg * 512: hc * T + (tg + 1) * 512],
                        start=(hc == 0), stop=(hc == 7)),
                      waits=[(sS, luS), (sV, xq_done), (sV, q5["mfree"][bank])]
                            if hc == 0 else [],
                      inc=(sT, 1) if hc == 7 else None)
                tM = sT.n
                E("v", lambda v, bank=bank, ssem=ssem: v.tensor_copy(
                    out=ustage[bank][:, :], in_=ps_m5[bank][:, :]),
                  waits=[(ssem, q5["ufree"][bank][0]),
                         (sGT, q5.get("ug_fence", 0) if (name == "u" and c == 0 and tg < 2) else 0),
                         (sT, tM), (sV, q5["ufree"][bank][1])], inc=(sV, 1))
                uV = sV.n
                q5["mfree"][bank] = uV
                E("s", lambda s, c=c, tg=tg, bank=bank, dspill=dspill, ssem=ssem: s.dma_start(
                    out=dspill[c, :, tg * 512:(tg + 1) * 512], in_=ustage[bank][:, :]),
                  waits=[(sV, uV)], inc16=ssem)
                spill_rec[(name, c, tg)] = ssem.n
                E("v", lambda v, bank=bank: v.tensor_reduce(
                    out=stat[:, 6:7], in_=ustage[bank][:, :], axis=AX.X, op=ALU.max,
                    apply_absolute_value=True))
                E("v", lambda v: v.drain())
                E("v", lambda v, rm_col=rm_col: v.tensor_tensor(
                    out=runmax[:, rm_col:rm_col + 1], in0=runmax[:, rm_col:rm_col + 1],
                    in1=stat[:, 6:7], op=ALU.max), inc=(sV, 1))
                E("v", lambda v: v.drain())
                q5["ufree"][bank] = [ssem.n, sV.n]

    def launch_ar(cci, cco, rm_col):
        E("g", lambda g, cci=cci, rm_col=rm_col: g.dma_start(
            out=cci[:, :], in_=runmax[:, rm_col:rm_col + 1]),
          waits=[(sV, sV.n)], inc16=sGD)
        E("g", lambda g, cci=cci, cco=cco: g.collective_compute(
            "AllReduce", ALU.max, replica_groups=[list(range(8))],
            ins=[cci.ap().opt()], outs=[cco.ap().opt()]),
          waits=[(sGD, sGD.n)], inc=(sCC, 1))
        return sCC.n

    upgate_pass(dwg, dgate, 2, "g", sGT)
    gate_total = sGT.n
    ar2 = launch_ar(cc2i, cc2o, 2)
    q5["ug_fence"] = gate_total
    q5["ufree"] = [[0, q5["ufree"][0][1]], [0, q5["ufree"][1][1]]]
    upgate_pass(dwu, dup, 1, "u", sUP)
    up_total = sUP.n
    ar3 = launch_ar(cc3i, cc3o, 1)

    sg_rdy = scale_setup(cc2o, ar2, 2, 3, [])
    gst = [b.sb(f"gst{i}", [P, 512], BF16) for i in range(2)]
    g2b = b.sb("g2b", [P, 512], F32)
    g2bb = b.sb("g2bb", [P, 512], BF16)
    gq_b = b.sb("gq_b", [P, 512], F32)
    sgb = b.sb("sgb", [P, 512], F32)
    i5 = b.sb("i5", [P, 512], I32)
    p5c = {"gfree": [0, 0], "g2free": 0, "spfree": 0}
    for c in range(NCF):
        for tg in range(4):
            i2 = (c * 4 + tg) % 2
            leng, lsem = ("y", sLD) if i2 == 0 else ("g", sGD)
            E(leng, lambda e, c=c, tg=tg, i2=i2: e.dma_start(
                out=gst[i2][:, :], in_=dgate[c, :, tg * 512:(tg + 1) * 512]),
              waits=[(sGT, gate_total), (sV, p5c["gfree"][i2])],
              inc16=lsem)
            gld = lsem.n
            E("v", lambda v, i2=i2: v.tensor_scalar(
                out=g2b[:, :], in0=gst[i2][:, :], scalar1=scl[:, 3:4], scalar2=0.0,
                op0=ALU.mult, op1=ALU.bypass),
              waits=[(lsem, gld), (sV, sg_rdy)], inc=(sV, 1))
            p5c["gfree"][i2] = sV.n
            E("v", lambda v: v.tensor_copy(out=i5[:, :], in_=g2b[:, :]))
            E("v", lambda v: v.tensor_copy(out=g2b[:, :], in_=i5[:, :]),
              inc=(sV, 1))
            E("v", lambda v: v.tensor_scalar(
                out=gq_b[:, :], in0=g2b[:, :], scalar1=scl[:, 2:3], scalar2=0.0,
                op0=ALU.mult, op1=ALU.bypass), inc=(sV, 1))
            gqv = sV.n
            E("s", lambda s: s.activation(
                out=sgb[:, :], in_=gq_b[:, :], func=AF.Sigmoid, bias=0.0, scale=1.0),
              waits=[(sV, gqv)], inc=(sS, 1))
            sS1 = sS.n
            E("v", lambda v: v.tensor_scalar(
                out=g2b[:, :], in0=sgb[:, :], scalar1=65536.0, scalar2=0.0,
                op0=ALU.mult, op1=ALU.bypass), waits=[(sS, sS1)], inc=(sV, 1))
            E("v", lambda v: v.tensor_copy(out=i5[:, :], in_=g2b[:, :]))
            E("v", lambda v: v.tensor_copy(out=g2b[:, :], in_=i5[:, :]),
              inc=(sV, 1))
            E("v", lambda v: v.tensor_scalar(
                out=g2b[:, :], in0=g2b[:, :], scalar1=65535.0, scalar2=0.0,
                op0=ALU.min, op1=ALU.max), inc=(sV, 1))
            E("v", lambda v: v.scalar_tensor_tensor(
                out=g2b[:, :], in0=g2b[:, :], scalar=1.0 / 65536.0, in1=gq_b[:, :],
                op0=ALU.mult, op1=ALU.mult), inc=(sV, 1))
            g2v = sV.n
            if upto == "p5dbg" and c == 23 and tg == 2:
                E("g", lambda g: g.dma_start(
                    out=dout[5 * P:6 * P, 0:512], in_=g2b[:, :]),
                  waits=[(sV, g2v)], inc16=sGD)
                E("g", lambda g, i2=i2: g.dma_start(
                    out=dout[15 * P:16 * P, 0:512], in_=gst[i2][:, :]),
                  waits=[(sV, g2v)], inc16=sGD)
                E("g", lambda g: g.dma_start(
                    out=dout[15 * P:16 * P, 512:1024], in_=gq_b[:, :]),
                  waits=[(sV, g2v)], inc16=sGD)
            E("v", lambda v: v.tensor_copy(out=g2bb[:, :], in_=g2b[:, :]),
              waits=[(sG2c, p5c["spfree"])], inc=(sV, 1))
            g2bv = sV.n
            E("s", lambda s, c=c, tg=tg: s.dma_start(
                out=dgate2[c, :, tg * 512:(tg + 1) * 512], in_=g2bb[:, :]),
              waits=[(sV, g2bv)], inc16=sG2c)
            spill_rec[("g2", c, tg)] = sG2c.n
            p5c["spfree"] = sG2c.n
            E("v", lambda v: v.tensor_reduce(
                out=stat[:, 6:7], in_=g2b[:, :], axis=AX.X, op=ALU.max,
                apply_absolute_value=True))
            E("v", lambda v: v.drain())
            E("v", lambda v: v.tensor_tensor(
                out=runmax[:, 3:4], in0=runmax[:, 3:4], in1=stat[:, 6:7],
                op=ALU.max), inc=(sV, 1))
            E("v", lambda v: v.drain())
            p5c["g2free"] = sG2c.n
    ar4 = launch_ar(cc4i, cc4o, 3)

    sup_rdy = scale_setup(cc3o, ar3, 4, 5, [])
    sg2_rdy = scale_setup(cc4o, ar4, 6, 7, [])
    E("v", lambda v: v.memset(runmax[:, 0:1], 0.0), inc=(sV, 1))
    E("v", lambda v: v.drain())
    ust2 = [b.sb(f"ust2{i}", [P, 512], BF16) for i in range(2)]
    ob = b.sb("ob", [P, 512], F32)
    obb = b.sb("obb", [P, 512], BF16)
    p5c_tail = sG2c.n
    g2_total = sG2c.n
    p5d = {"gfree": [0, 0], "ufree": [0, 0], "spfree": 0}
    for c in range(NCF):
        for tg in range(4):
            i2 = (c * 4 + tg) % 2
            E("y", lambda y, c=c, tg=tg, i2=i2: y.dma_start(
                out=gst[i2][:, :], in_=dgate2[c, :, tg * 512:(tg + 1) * 512]),
              waits=[(sG2c, g2_total), (sV, p5d["gfree"][i2])],
              inc16=sLD)
            gld = sLD.n
            E("g", lambda g, c=c, tg=tg, i2=i2: g.dma_start(
                out=ust2[i2][:, :], in_=dup[c, :, tg * 512:(tg + 1) * 512]),
              waits=[(sUP, up_total), (sV, p5d["ufree"][i2])],
              inc16=sGD)
            uld = sGD.n
            E("v", lambda v, i2=i2: v.tensor_scalar(
                out=g2b[:, :], in0=gst[i2][:, :], scalar1=scl[:, 7:8], scalar2=0.0,
                op0=ALU.mult, op1=ALU.bypass),
              waits=[(sLD, gld), (sV, sg2_rdy), (sV, sup_rdy)] +
                    ([(sG2c, p5c_tail)] if (c == 0 and tg == 0) else []),
              inc=(sV, 1))
            p5d["gfree"][i2] = sV.n
            E("v", lambda v: v.tensor_copy(out=i5[:, :], in_=g2b[:, :]))
            E("v", lambda v: v.tensor_copy(out=g2b[:, :], in_=i5[:, :]),
              inc=(sV, 1))
            E("v", lambda v: v.tensor_scalar(
                out=g2b[:, :], in0=g2b[:, :], scalar1=scl[:, 6:7], scalar2=0.0,
                op0=ALU.mult, op1=ALU.bypass), inc=(sV, 1))
            E("v", lambda v, i2=i2: v.tensor_scalar(
                out=ob[:, :], in0=ust2[i2][:, :], scalar1=scl[:, 5:6], scalar2=0.0,
                op0=ALU.mult, op1=ALU.bypass),
              waits=[(sGD, uld)], inc=(sV, 1))
            p5d["ufree"][i2] = sV.n
            E("v", lambda v: v.tensor_copy(out=i5[:, :], in_=ob[:, :]))
            E("v", lambda v: v.tensor_copy(out=ob[:, :], in_=i5[:, :]),
              inc=(sV, 1))
            E("v", lambda v: v.tensor_scalar(
                out=ob[:, :], in0=ob[:, :], scalar1=scl[:, 4:5], scalar2=0.0,
                op0=ALU.mult, op1=ALU.bypass), inc=(sV, 1))
            E("v", lambda v: v.tensor_tensor(
                out=ob[:, :], in0=g2b[:, :], in1=ob[:, :], op=ALU.mult),
              inc=(sV, 1))
            ov = sV.n
            E("v", lambda v: v.tensor_copy(out=obb[:, :], in_=ob[:, :]),
              waits=[(sOc, p5d["spfree"])], inc=(sV, 1))
            obv = sV.n
            E("s", lambda s, c=c, tg=tg: s.dma_start(
                out=dos[c, :, tg * 512:(tg + 1) * 512], in_=obb[:, :]),
              waits=[(sV, obv)], inc16=sOc)
            spill_rec[("o", c, tg)] = sOc.n
            p5d["spfree"] = sOc.n
            E("v", lambda v: v.tensor_reduce(
                out=stat[:, 6:7], in_=ob[:, :], axis=AX.X, op=ALU.max,
                apply_absolute_value=True))
            E("v", lambda v: v.drain())
            E("v", lambda v: v.tensor_tensor(
                out=runmax[:, 0:1], in0=runmax[:, 0:1], in1=stat[:, 6:7],
                op=ALU.max), inc=(sV, 1))
            E("v", lambda v: v.drain())
    ar5 = launch_ar(cc1i, cc1o, 0)

    if upto == "p5dbg":
        fw = [(sV, sV.n), (sS, sS.n), (sT, sT.n), (sACT, sACT.n), (sG2c, sG2c.n), (sOc, sOc.n), (sCC, sCC.n)]
        E("y", lambda y: y.dma_start(out=dout[0:P, 0:12], in_=scl[:, :]),
          waits=fw, inc16=sOUT)
        E("y", lambda y: y.dma_start(out=dout[0:P, 16:20], in_=runmax[:, :]),
          waits=fw, inc16=sOUT)
        E("y", lambda y: y.dma_start(out=dout[P:2 * P, :], in_=dup[0, :, 0:1024]),
          waits=fw, inc16=sOUT)
        E("y", lambda y: y.dma_start(out=dout[2 * P:3 * P, :], in_=dgate[0, :, 0:1024]),
          waits=fw, inc16=sOUT)
        E("y", lambda y: y.dma_start(out=dout[3 * P:4 * P, :], in_=dgate2[0, :, 0:1024]),
          waits=fw, inc16=sOUT)
        E("y", lambda y: y.dma_start(out=dout[4 * P:5 * P, :], in_=dos[0, :, 0:1024]),
          waits=fw, inc16=sOUT)
        E("y", lambda y: y.dma_start(out=dout[6 * P:7 * P, :], in_=dos[0, :, 1024:2048]),
          waits=fw, inc16=sOUT)
        E("y", lambda y: y.dma_start(out=dout[7 * P:8 * P, :], in_=dgate2[0, :, 1024:2048]),
          waits=fw, inc16=sOUT)
        E("y", lambda y: y.dma_start(out=dout[8 * P:9 * P, :], in_=dup[0, :, 1024:2048]),
          waits=fw, inc16=sOUT)
        E("y", lambda y: y.dma_start(out=dout[9 * P:10 * P, :], in_=dgate[0, :, 1024:2048]),
          waits=fw, inc16=sOUT)
        E("y", lambda y: y.dma_start(out=dout[10 * P:11 * P, :], in_=dos[23, :, 1024:2048]),
          waits=fw, inc16=sOUT)
        E("y", lambda y: y.dma_start(out=dout[11 * P:12 * P, :], in_=dgate2[23, :, 1024:2048]),
          waits=fw, inc16=sOUT)
        E("y", lambda y: y.dma_start(out=dout[12 * P:13 * P, :], in_=dup[23, :, 1024:2048]),
          waits=fw, inc16=sOUT)
        E("y", lambda y: y.dma_start(out=dout[13 * P:14 * P, :], in_=dos[23, :, 0:1024]),
          waits=fw, inc16=sOUT)
        E("y", lambda y: y.dma_start(out=dout[14 * P:15 * P, :], in_=dgate[23, :, 1024:2048]),
          waits=fw, inc16=sOUT)
        E("y", lambda y: y.dma_start(out=dout[15 * P:16 * P, :], in_=dup[22, :, 1024:2048]),
          waits=fw, inc16=sOUT)
        E("g", lambda g: g.dma_start(out=dout[5 * P:6 * P, :], in_=xqT[:, 0:1024]),
          waits=fw, inc16=sOUT)
        E("y", lambda y: y.nop(), waits=[(sOUT, sOUT.n)])
        _finish(nc, b)
        return nc

    b.free("obb", "ob", "ust20", "ust21", "i5", "sgb", "gq_b", "g2bb", "g2b",
           "gst0", "gst1",
           "ustage0", "ustage1", "luT0", "luT1", "wqb", "wnat0", "wnat1",
           "ubuf", "ibuf", "xbuf0", "xbuf1", "xqT")

    wdT = b.sb("wdT", [P, NCF * D], BF16)
    wnat2 = b.sb("wnat2", [P, IF], F32)
    ubuf2 = b.sb("ubuf2", [P, IF], F32)
    ibuf2 = b.sb("ibuf2", [P, IF], I32)
    wqb2 = b.sb("wqb2", [P, IF], BF16)
    wdqn = b.sb("wdqn", [P, 8 * IF], BF16)
    wd5 = {"wfree": 0, "evac": sS.n}
    o_total = sOc.n
    p5d_e = [(sV, sV.n), (sS, sS.n), (sT, sT.n), (sOc, sOc.n), (sG2c, sG2c.n)]
    for ht in range(8):
        E("y", lambda y, ht=ht: y.dma_start(
            out=wnat2[:, :], in_=dwd[ht * P:(ht + 1) * P, :]),
          waits=[(sV, wd5["wfree"])] + (p5d_e if ht == 0 else []), inc16=sLD)
        wld = sLD.n
        r3 = wnat2[:, :].rearrange("p (b k) -> p b k", k=32)
        E("v", lambda v, r3=r3: v.tensor_reduce(
            out=sdq[:, 0:96], in_=r3, axis=AX.X, op=ALU.max,
            apply_absolute_value=True), waits=[(sLD, wld), (sS, sS.n)], inc=(sV, 1))
        E("v", lambda v: v.drain())
        E("v", lambda v: v.tensor_scalar(
            out=sdq[:, 0:96], in0=sdq[:, 0:96], scalar1=1.0 / 7.0, scalar2=1e-12,
            op0=ALU.mult, op1=ALU.max))
        E("v", lambda v: v.drain())
        E("v", lambda v: v.reciprocal(out=sdq[:, 96:192], in_=sdq[:, 0:96]))
        E("v", lambda v: v.drain(), inc=(sV, 1))
        E("v", lambda v, r3=r3: v.tensor_tensor(
            out=ubuf2[:, :].rearrange("p (b k) -> p b k", k=32),
            in0=r3, in1=sdq[:, 96:192].unsqueeze(2).to_broadcast((P, 96, 32)),
            op=ALU.mult), inc=(sV, 1))
        vr = sV.n
        E("v", lambda v: v.tensor_copy(out=ibuf2[:, :], in_=ubuf2[:, :]))
        E("v", lambda v: v.tensor_copy(out=ubuf2[:, :], in_=ibuf2[:, :]))
        E("v", lambda v, ht=ht: v.tensor_tensor(
            out=wdqn[:, ht * IF:(ht + 1) * IF].rearrange("p (b k) -> p b k", k=32),
            in0=ubuf2[:, :].rearrange("p (b k) -> p b k", k=32),
            in1=sdq[:, 0:96].unsqueeze(2).to_broadcast((P, 96, 32)),
            op=ALU.mult), inc=(sV, 1))
        wd5["wfree"] = sV.n
    E("v", lambda v: v.drain(), inc=(sV, 1))
    wdq_v = sV.n
    for ioc in range(NCF):
        for ht in range(8):
            E("t", lambda t, ioc=ioc, ht=ht: t.transpose(
                out=ps_w5[:, ht * P:(ht + 1) * P],
                in_=wdqn[:, ht * IF + ioc * P: ht * IF + (ioc + 1) * P],
                identity=ident_b[:, :]),
              waits=[(sV, wdq_v), (sS, wd5["evac"])] if ht == 0 else [],
              inc=(sT, 1) if ht == 7 else None)
        tW = sT.n
        E("s", lambda s, ioc=ioc: s.activation(
            out=wdT[:, ioc * D:(ioc + 1) * D], in_=ps_w5[:, :], func=AF.Copy,
            bias=0.0, scale=1.0), waits=[(sT, tW)], inc=(sS, 1))
        wd5["evac"] = sS.n
    wdT_done = sS.n
    b.free("wdqn", "wqb2", "ibuf2", "ubuf2", "wnat2")

    so_rdy = scale_setup(cc1o, ar5, 8, 9, [])
    act_total = sACT.n
    oqb = b.sb("oqb", [P, NCF * 512], BF16)
    ostr = [b.sb(f"ostr{i}", [P, 512], BF16) for i in range(2)]
    oi5 = b.sb("oi5", [P, 512], I32)
    osc = b.sb("osc", [P, 512], F32)
    x1s = [b.sb(f"x1s{i}", [P, D], F32) for i in range(2)]
    apS = [b.sb(f"apS{i}", [P, D], F32) for i in range(2)]
    tst = b.sb("tst", [P, D], F32)
    outst = [b.sb(f"outst{i}", [P, D], F32) for i in range(2)]
    p5e = {"ofree": [0, 0], "xfree": [0, 0], "afree": [0, 0], "outfree": [0, 0],
           "mfree": [0, 0], "mmdone": 0}
    for tg in range(4):
        for ioc in range(NCF):
            i2 = ioc % 2
            leng, lsem = ("y", sLD) if i2 == 0 else ("g", sGD)
            E(leng, lambda e, ioc=ioc, tg=tg, i2=i2: e.dma_start(
                out=ostr[i2][:, :], in_=dos[ioc, :, tg * 512:(tg + 1) * 512]),
              waits=[(sOc, o_total), (sV, p5e["ofree"][i2])] +
                    ([(sS, wdT_done)] if tg == 0 and ioc < 2 else []),
              inc16=lsem)
            old = lsem.n
            E("v", lambda v, i2=i2: v.tensor_scalar(
                out=osc[:, :], in0=ostr[i2][:, :], scalar1=scl[:, 9:10], scalar2=0.0,
                op0=ALU.mult, op1=ALU.bypass),
              waits=[(lsem, old), (sV, so_rdy)], inc=(sV, 1))
            p5e["ofree"][i2] = sV.n
            E("v", lambda v: v.tensor_copy(out=oi5[:, :], in_=osc[:, :]))
            E("v", lambda v: v.tensor_copy(out=osc[:, :], in_=oi5[:, :]),
              inc=(sV, 1))
            E("v", lambda v, ioc=ioc: v.tensor_scalar(
                out=oqb[:, ioc * 512:(ioc + 1) * 512], in0=osc[:, :],
                scalar1=scl[:, 8:9], scalar2=0.0, op0=ALU.mult, op1=ALU.bypass),
              waits=[(sT, p5e["mmdone"])] if ioc == 0 else [], inc=(sV, 1))
        E("v", lambda v: v.drain(), inc=(sV, 1))
        oq_v = sV.n
        if upto == "p5e" and False:
            for i in range(8):
                E("g", lambda g, i=i: g.dma_start(
                    out=dout[i * P:(i + 1) * P, 0:512], in_=oqb[:, i * 512:(i + 1) * 512]),
                  waits=[(sV, oq_v)], inc16=sOUT)
            E("y", lambda y: y.dma_start(out=dout[8 * P:9 * P, 0:12], in_=scl[:, :]),
              waits=[(sV, oq_v)], inc16=sOUT)
            E("g", lambda g: g.dma_start(out=dout[9 * P:10 * P, :], in_=wdT[:, 0:1024]),
              waits=[(sV, oq_v)], inc16=sOUT)
        for ts4 in range(4):
            tile = tg * 4 + ts4
            i2 = ts4 % 2
            E("y", lambda y, tile=tile, i2=i2: y.dma_start(
                out=x1s[i2][:, :], in_=dx1[tile * P:(tile + 1) * P, :]),
              waits=[(sACT, act_total), (sV, p5e["xfree"][i2])] +
                    ([(sS, wdT_done)] if tile < 2 else []), inc16=sLD)
            rld = sLD.n
            E("g", lambda g, tile=tile, i2=i2: g.dma_start(
                out=apS[i2][:, :], in_=dattnP[tile * P:(tile + 1) * P, :]),
              waits=[(sACT, act_total), (sV, p5e["afree"][i2])] +
                    ([(sS, wdT_done)] if tile < 2 else []), inc16=sGD)
            rldg = sGD.n
            tops = []
            for half in range(2):
                for ioc in range(NCF):
                    E("t", lambda t, ioc=ioc, ts4=ts4, half=half: t.matmul(
                        out=ps_m5[half][:, :],
                        lhsT=oqb[:, ioc * 512 + ts4 * P: ioc * 512 + (ts4 + 1) * P],
                        rhs=wdT[:, ioc * D + half * 512: ioc * D + (half + 1) * 512],
                        start=(ioc == 0), stop=(ioc == NCF - 1)),
                      waits=[(sV, oq_v), (sS, wdT_done), (sV, p5e["mfree"][half])]
                            if ioc == 0 else [],
                      inc=(sT, 1) if ioc == NCF - 1 else None)
                tops.append(sT.n)
            E("v", lambda v, i2=i2: v.tensor_tensor(
                out=tst[:, :], in0=x1s[i2][:, :], in1=apS[i2][:, :], op=ALU.add),
              waits=[(sLD, rld), (sGD, rldg), (sV, sV.n)], inc=(sV, 1))
            tv = sV.n
            p5e["xfree"][i2] = tv
            p5e["afree"][i2] = tv
            oi = ts4 % 2
            E("v", lambda v, oi=oi: v.tensor_tensor(
                out=outst[oi][:, 0:512], in0=ps_m5[0][:, :], in1=tst[:, 0:512],
                op=ALU.add),
              waits=[(sT, tops[0]), (sOUT, p5e["outfree"][oi])])
            E("v", lambda v, oi=oi: v.tensor_tensor(
                out=outst[oi][:, 512:1024], in0=ps_m5[1][:, :], in1=tst[:, 512:1024],
                op=ALU.add), waits=[(sT, tops[1])], inc=(sV, 1))
            vout = sV.n
            p5e["mfree"][0] = vout
            p5e["mfree"][1] = vout
            E("y", lambda y, tile=tile, oi=oi: y.dma_start(
                out=dout[tile * P:(tile + 1) * P, :], in_=outst[oi][:, :]),
              waits=[(sV, vout)], inc16=sOUT)
            p5e["outfree"][oi] = sOUT.n
        p5e["mmdone"] = sT.n
        if upto == "p5e" and tg == 2:
            fw2 = [(sV, sV.n), (sS, sS.n), (sT, sT.n), (sOUT, sOUT.n)]
            E("y", lambda y: y.dma_start(out=dout[12 * P:13 * P, :], in_=dx1[8 * P:9 * P, :]),
              waits=fw2, inc16=sOUT)
            E("y", lambda y: y.dma_start(out=dout[13 * P:14 * P, :], in_=dattnP[8 * P:9 * P, :]),
              waits=fw2, inc16=sOUT)
            break
    E("y", lambda y: y.nop(), waits=[(sOUT, sOUT.n)])
    _finish(nc, b)
    return nc



_CACHE = {}


def kernel(**inputs):
    if "nc" not in _CACHE:
        _CACHE["nc"] = build_nc()
    nc = _CACHE["nc"]
    x = np.ascontiguousarray(inputs["L0_attn_hidden_states"], dtype=np.float32)
    w = {
        "kw": np.ascontiguousarray(inputs["L0_attn_k_weight"], np.float32),
        "qw": np.ascontiguousarray(inputs["L0_attn_q_weight"], np.float32),
        "lnw": np.ascontiguousarray(inputs["L0_attn_ln_weight"], np.float32),
        "lnb": np.ascontiguousarray(inputs["L0_attn_ln_bias"], np.float32),
        "bias1": np.ascontiguousarray(inputs["L0_attn_bias"], np.float32),
        "wrT": np.ascontiguousarray(np.asarray(inputs["L0_ffn_W_r"]).T, np.float32),
        "wkT": np.ascontiguousarray(np.asarray(inputs["L0_ffn_W_k"]).T, np.float32),
        "wvT": np.ascontiguousarray(np.asarray(inputs["L0_ffn_W_v"]).T, np.float32),
        "mur": np.ascontiguousarray(inputs["L0_ffn_mu_r"], np.float32),
        "muk": np.ascontiguousarray(inputs["L0_ffn_mu_k"], np.float32),
        "l1qT": np.ascontiguousarray(np.asarray(inputs["L1_attn_Wq"]).T, np.float32),
        "l1kT": np.ascontiguousarray(np.asarray(inputs["L1_attn_Wk"]).T, np.float32),
        "l1vT": np.ascontiguousarray(np.asarray(inputs["L1_attn_Wv"]).T, np.float32),
        "l1oT": np.ascontiguousarray(np.asarray(inputs["L1_attn_Wo"]).T, np.float32),
        "bo": np.ascontiguousarray(inputs["L1_attn_bo"], np.float32),
        "wg": np.ascontiguousarray(inputs["L1_ffn_w_gate"], np.float32),
        "wu": np.ascontiguousarray(inputs["L1_ffn_w_up"], np.float32),
        "wd": np.ascontiguousarray(inputs["L1_ffn_w_down"], np.float32),
    }
    in_maps = [dict(w, xin=np.ascontiguousarray(x[c])) for c in range(8)]
    res = run_bass_kernel_spmd(nc, in_maps, core_ids=list(range(8)))
    return np.stack([res.results[c]["out"] for c in range(8)], axis=0)


# version 06 (final)
# speedup vs baseline: 1.0942x; 1.0038x over previous
import numpy as np
import concourse.bass as bass
import concourse.mybir as mybir
from concourse.bass_utils import run_bass_kernel_spmd

F32 = mybir.dt.float32
BF16 = mybir.dt.bfloat16
I32 = mybir.dt.int32
AF = mybir.ActivationFunctionType
ALU = mybir.AluOpType
AX = mybir.AxisListType

P = 128
T = 2048
D = 1024
NT = T // P
I0 = 4096
IF = 3072
NC0 = I0 // P
NCF = IF // P
GRP = 256
NG = T // GRP
HB = 16
INV_D = 1.0 / D
INV_PI = 0.3183098861837907
CLIP_LO = -1.0 + 1e-6
CLIP_HI = 1.0 - 1e-6

ENGINES = ("y", "v", "s", "t", "g")


class Ct:
    def __init__(self, handle, name):
        self.h = handle
        self.name = name
        self.n = 0


class Sched:
    def __init__(self):
        self.recs = {e: [] for e in ENGINES}

    def op(self, eng, fn, waits=(), inc=None, inc16=None, serial=True):
        ws = []
        for c, v in waits:
            if v <= 0:
                continue
            assert v <= c.n, f"wait on future value of {c.name}: {v} > {c.n}"
            ws.append((c, v))
        incs = []
        if inc is not None:
            cs = inc if isinstance(inc, list) else [inc]
            for c, k in cs:
                c.n += k
                incs.append((c, k))
        if inc16 is not None:
            inc16.n += 16
            incs.append((inc16, 16))
        self.recs[eng].append((fn, ws, incs))
        if inc16 is not None and serial:
            self.recs[eng].append(
                (lambda e: e.nop(), [(inc16, inc16.n)], []))

    def replay(self, eng_name, eng):
        for fn, ws, incs in self.recs[eng_name]:
            for c, v in ws:
                eng.wait_ge(c.h, v)
            inst = fn(eng)
            for c, k in incs:
                inst.then_inc(c.h, k)


class B:
    def __init__(self, nc):
        self.nc = nc
        self.sched = Sched()
        self._open = {}
        self._seq = 0

    def sb(self, name, shape, dt=F32):
        cm = self.nc.sbuf_tensor(name, shape, dt)
        h = cm.__enter__()
        self._seq += 1
        self._open[name] = (self._seq, cm)
        return h

    def ps(self, name, shape, dt=F32):
        cm = self.nc.psum_tensor(name, shape, dt)
        h = cm.__enter__()
        self._seq += 1
        self._open[name] = (self._seq, cm)
        return h

    def free(self, *names):
        for name in sorted(names, key=lambda n: -self._open[n][0]):
            self._open.pop(name)[1].__exit__(None, None, None)

    def sem(self, name):
        cm = self.nc.semaphore(name)
        h = cm.__enter__()
        self._seq += 1
        self._open["sem_" + name] = (self._seq, cm)
        return Ct(h, name)




def _finish(nc, b):
    with nc.Block() as block:
        @block.sync
        def _(y):
            b.sched.replay("y", y)

        @block.vector
        def _(v):
            b.sched.replay("v", v)

        @block.scalar
        def _(s):
            b.sched.replay("s", s)

        @block.tensor
        def _(t):
            b.sched.replay("t", t)

        @block.gpsimd
        def _(g):
            b.sched.replay("g", g)


def build_nc(upto="all"):
    nc = bass.Bass("TRN2", target_bir_lowering=False)
    b = B(nc)
    E = b.sched.op

    dxin = nc.dram_tensor("xin", [T, D], F32, kind="ExternalInput").ap()
    dkw = nc.dram_tensor("kw", [I0, D], F32, kind="ExternalInput").ap()
    dqw = nc.dram_tensor("qw", [I0, D], F32, kind="ExternalInput").ap()
    dlnw = nc.dram_tensor("lnw", [D], F32, kind="ExternalInput").ap()
    dlnb = nc.dram_tensor("lnb", [D], F32, kind="ExternalInput").ap()
    dbias1 = nc.dram_tensor("bias1", [D], F32, kind="ExternalInput").ap()
    dwrT = nc.dram_tensor("wrT", [D, D], F32, kind="ExternalInput").ap()
    dwkT = nc.dram_tensor("wkT", [D, D], F32, kind="ExternalInput").ap()
    dwvT = nc.dram_tensor("wvT", [D, D], F32, kind="ExternalInput").ap()
    dmur = nc.dram_tensor("mur", [D], F32, kind="ExternalInput").ap()
    dmuk = nc.dram_tensor("muk", [D], F32, kind="ExternalInput").ap()
    dl1q = nc.dram_tensor("l1qT", [D, D], F32, kind="ExternalInput").ap()
    dl1k = nc.dram_tensor("l1kT", [D, D], F32, kind="ExternalInput").ap()
    dl1v = nc.dram_tensor("l1vT", [D, D], F32, kind="ExternalInput").ap()
    dl1o = nc.dram_tensor("l1oT", [D, D], F32, kind="ExternalInput").ap()
    dbo = nc.dram_tensor("bo", [D], F32, kind="ExternalInput").ap()
    dwg = nc.dram_tensor("wg", [IF, D], F32, kind="ExternalInput").ap()
    dwu = nc.dram_tensor("wu", [IF, D], F32, kind="ExternalInput").ap()
    dwd = nc.dram_tensor("wd", [D, IF], F32, kind="ExternalInput").ap()
    dout = nc.dram_tensor("out", [T, D], F32, kind="ExternalOutput").ap()

    dattn0 = nc.dram_tensor("attn0", [T, D], F32).ap()
    dx1 = nc.dram_tensor("x1", [T, D], F32).ap()
    dxn2 = nc.dram_tensor("xn2s", [T, D], F32).ap()
    dattnP = nc.dram_tensor("attnP", [T, D], F32).ap()
    dqwbf = nc.dram_tensor("qwbf", [I0, D], BF16).ap()
    dup = nc.dram_tensor("ups", [NCF, P, T], BF16).ap()
    dgate = nc.dram_tensor("gates", [NCF, P, T], BF16).ap()
    dgate2 = nc.dram_tensor("gate2s", [NCF, P, T], BF16).ap()
    dos = nc.dram_tensor("osp", [NCF, P, T], BF16).ap()
    cc1i = nc.dram_tensor("cc1i", [P, 1], F32)
    cc1o = nc.dram_tensor("cc1o", [P, 1], F32)
    cc2i = nc.dram_tensor("cc2i", [P, 1], F32)
    cc2o = nc.dram_tensor("cc2o", [P, 1], F32)
    cc5i = nc.dram_tensor("cc5i", [P, 1], F32)
    cc5o = nc.dram_tensor("cc5o", [P, 1], F32)
    cc3i = nc.dram_tensor("cc3i", [P, 1], F32)
    cc3o = nc.dram_tensor("cc3o", [P, 1], F32)
    cc4i = nc.dram_tensor("cc4i", [P, 1], F32)
    cc4o = nc.dram_tensor("cc4o", [P, 1], F32)

    sLD = b.sem("sLD")
    sSP = b.sem("sSP")
    sGD = b.sem("sGD")
    sV = b.sem("sV")
    sS = b.sem("sS")
    sT = b.sem("sT")
    sG = b.sem("sG")
    sCC = b.sem("sCC")
    sOUT = b.sem("sOUT")
    sA0 = b.sem("sA0")
    sACT = b.sem("sACT")
    sQW = b.sem("sQW")
    sGT = b.sem("sGT")
    sUP = b.sem("sUP")
    sG2c = b.sem("sG2c")
    sOc = b.sem("sOc")

    ident_f = b.sb("ident_f", [P, P], F32)
    ident_b = b.sb("ident_b", [P, P], BF16)
    ones_f = b.sb("ones_f", [P, P], F32)
    tri = b.sb("tri", [P, P], BF16)
    lnw_b = b.sb("lnw_b", [P, D], F32)
    lnb_b = b.sb("lnb_b", [P, D], F32)
    bias1_b = b.sb("bias1_b", [P, D], F32)
    bo_b = b.sb("bo_b", [P, D], F32)
    mur_sb = b.sb("mur_sb", [P, 8], F32)
    muk_sb = b.sb("muk_sb", [P, 8], F32)
    m_all = b.sb("m_all", [P, NT], F32)
    rstd_all = b.sb("rstd_all", [P, NT], F32)
    m2_all = b.sb("m2_all", [P, NT], F32)
    ss2_all = b.sb("ss2_all", [P, NT], F32)
    rstd2_all = b.sb("rstd2_all", [P, NT], F32)
    runmax = b.sb("runmax", [P, 4], F32)
    stat = b.sb("stat", [P, 8], F32)
    kc = b.sb("kc", [P, 4], F32)
    xn2T = b.sb("xn2T", [P, 8 * T], BF16)
    vrow = b.sb("vrow", [1, D], F32)
    sq_scr = b.sb("sq_scr", [P, D], F32)
    E("g", lambda g: g.memset(ident_f[:, :], 0.0))
    E("g", lambda g: g.affine_select(
        out=ident_f[:, :], in_=ident_f[:, :], compare_op=ALU.not_equal,
        fill=1.0, base=0, pattern=[[-1, P]], channel_multiplier=1))
    E("g", lambda g: g.memset(ident_b[:, :], 0.0))
    E("g", lambda g: g.affine_select(
        out=ident_b[:, :], in_=ident_b[:, :], compare_op=ALU.not_equal,
        fill=1.0, base=0, pattern=[[-1, P]], channel_multiplier=1))
    E("g", lambda g: g.memset(ones_f[:, :], 1.0))
    E("g", lambda g: g.memset(tri[:, :], 1.0))
    E("g", lambda g: g.affine_select(
        out=tri[:, :], in_=tri[:, :], compare_op=ALU.is_ge,
        fill=0.0, base=0, pattern=[[1, P]], channel_multiplier=-1))
    E("g", lambda g: g.memset(runmax[:, :], 0.0))
    E("g", lambda g: g.memset(kc[:, 0:1], 1e-5))
    E("g", lambda g: g.memset(kc[:, 1:2], 1e-12))
    E("g", lambda g: g.memset(kc[:, 2:3], 0.5), inc=(sG, 1))
    g_const = sG.n

    ps_bc = b.ps("ps_bc", [P, D], F32)
    for src, dst in [(dlnw, lnw_b), (dlnb, lnb_b), (dbias1, bias1_b), (dbo, bo_b)]:
        vprev = sV.n
        E("y", lambda y, s=src: y.dma_start(out=vrow[0:1, :], in_=s.rearrange("(o d) -> o d", o=1)),
          waits=[(sV, vprev)], inc16=sLD)
        ld = sLD.n
        for half in range(2):
            E("t", lambda t, h=half: t.matmul(
                out=ps_bc[:, h * 512:(h + 1) * 512], lhsT=ones_f[0:1, :],
                rhs=vrow[0:1, h * 512:(h + 1) * 512], start=True, stop=True),
              waits=[(sLD, ld), (sG, g_const), (sV, vprev)],
              inc=(sT, 1) if half == 1 else None)
        E("v", lambda v, d=dst: v.tensor_copy(out=d[:, :], in_=ps_bc[:, :]),
          waits=[(sT, sT.n)], inc=(sV, 1))

    def _mu_load(y, dst, srcap):
        with nc.allow_non_contiguous_dma(reason="tiny one-time mu load"):
            return y.dma_start(out=dst[:, :], in_=srcap.rearrange("(c p) -> p c", p=P))
    E("y", lambda y: _mu_load(y, mur_sb, dmur), inc16=sLD)
    E("y", lambda y: _mu_load(y, muk_sb, dmuk), inc16=sLD)
    mu_ld = sLD.n

    KT = b.sb("KT", [P, 8 * I0], BF16)
    kwb = [b.sb(f"kwb{i}", [P, D], F32) for i in range(2)]
    knb = [b.sb(f"knb{i}", [P, D], BF16) for i in range(2)]
    qcb = [b.sb(f"qcb{i}", [P, D], BF16) for i in range(2)]
    ps_kt = [b.ps(f"ps_kt{i}", [P, D], BF16) for i in range(2)]

    kw_free = [0, 0]
    kt_busy = [0, 0]
    kn_free = [0, 0]
    qc_free = [0, 0]
    for k in range(NC0):
        i2 = k % 2
        E("y", lambda y, k=k, i2=i2: y.dma_start(
            out=kwb[i2][:, :], in_=dkw[k * P:(k + 1) * P, :]),
          waits=[(sV, kw_free[i2])], inc16=sLD)
        ld = sLD.n
        E("s", lambda s, i2=i2: s.activation(
            out=sq_scr[:, :], in_=kwb[i2][:, :], func=AF.Square,
            bias=0.0, scale=1.0),
          waits=[(sLD, ld), (sV, sV.n)], inc=(sS, 1))
        ssd = sS.n
        E("v", lambda v: v.tensor_reduce(
            out=stat[:, 2:3], in_=sq_scr[:, :], axis=AX.X, op=ALU.add),
          waits=[(sS, ssd)], inc=(sV, 1))
        E("s", lambda s: s.activation(
            out=stat[:, 3:4], in_=stat[:, 2:3], func=AF.Sqrt, bias=0.0, scale=1.0),
          waits=[(sV, sV.n)], inc=(sS, 1))
        E("v", lambda v: v.tensor_scalar(
            out=stat[:, 4:5], in0=stat[:, 3:4], scalar1=1e-12, scalar2=0.0,
            op0=ALU.max, op1=ALU.bypass), waits=[(sS, sS.n)])
        E("v", lambda v: v.drain())
        E("v", lambda v: v.reciprocal(out=stat[:, 5:6], in_=stat[:, 4:5]))
        E("v", lambda v: v.drain())
        E("v", lambda v, i2=i2: v.tensor_scalar(
            out=knb[i2][:, :], in0=kwb[i2][:, :], scalar1=stat[:, 5:6], scalar2=0.0,
            op0=ALU.mult, op1=ALU.bypass),
          waits=[(sT, kn_free[i2])], inc=(sV, 1))
        E("v", lambda v: v.drain())
        kw_free[i2] = sV.n
        vn = sV.n
        for c in range(8):
            E("t", lambda t, i2=i2, c=c: t.transpose(
                out=ps_kt[i2][:, c * P:(c + 1) * P], in_=knb[i2][:, c * P:(c + 1) * P],
                identity=ident_b[:, :]),
              waits=[(sV, vn), (sV, kt_busy[i2]), (sG, g_const)] if c == 0 else [],
              inc=(sT, 1) if c == 7 else None)
        kn_free[i2] = sT.n
        tn = sT.n
        E("v", lambda v, i2=i2, k=k: v.tensor_copy(
            out=KT[:, :].rearrange("p (h c) -> p h c", h=8)[:, :, k * P:(k + 1) * P],
            in_=ps_kt[i2][:, :].rearrange("p (h c) -> p h c", h=8)),
          waits=[(sT, tn)], inc=(sV, 1))
        kt_busy[i2] = sV.n
        E("g", lambda g, k=k, i2=i2: g.dma_start(
            out=qcb[i2][:, :], in_=dqw[k * P:(k + 1) * P, :]),
          waits=[(sQW, qc_free[i2])], inc16=sGD)
        gld = sGD.n
        E("s", lambda s, k=k, i2=i2: s.dma_start(
            out=dqwbf[k * P:(k + 1) * P, :], in_=qcb[i2][:, :]),
          waits=[(sGD, gld)], inc16=sQW)
        qc_free[i2] = sQW.n
    qw_cached = sQW.n

    b.free("kwb0", "kwb1", "knb0", "knb1", "qcb0", "qcb1",
           "ps_kt0", "ps_kt1", "ps_bc")
    p0_v, p0_s, p0_t = sV.n, sS.n, sT.n

    xg = [b.sb(f"xg{i}", [P, 2 * D], F32) for i in range(2)]
    QT = [b.sb(f"QT{i}", [P, 8 * GRP], BF16) for i in range(2)]
    qwk = [b.sb(f"qwk{i}", [P, D], BF16) for i in range(2)]
    sstage = b.sb("sstage", [P, HB * GRP], F32)
    aux = b.sb("aux", [P, HB * GRP], F32)
    Bst = [b.sb("Bst0", [P, HB * GRP], BF16)] * 2
    xnt = b.sb("xnt", [P, D], F32)
    yt = b.sb("yt", [P, D], F32)
    qtile = b.sb("qtile", [P, D], BF16)
    Xout = [b.sb("Xout0", [P, D], F32)] * 2
    ps_sT = [b.ps(f"ps_sT{i}", [P, GRP], F32) for i in range(3)]
    ps_X = [b.ps(f"ps_X{i}", [P, 512], F32) for i in range(4)]
    ps_qt = b.ps("ps_qt", [P, D], BF16)

    st = {
        "xg_free": [0, 0],
        "qt_evac": 0,
        "qt_built": [0, 0],
        "qt_free": [0, 0],
        "qwk_free": [0, 0],
        "sT_free": [0, 0, 0],
        "bst_free": [0, 0],
        "xfin_v": 0,
        "xout_free": [0, 0],
        "aux_free": 0,
        "attn0_sp": [0] * NT,
    }

    def ln_stats(xt_ap, m_dst, r_dst, eps, ld_waits):
        E("v", lambda v: v.tensor_reduce(
            out=stat[:, 0:1], in_=xt_ap, axis=AX.X, op=ALU.add),
          waits=ld_waits + [(sS, sS.n)], inc=(sV, 1))
        E("s", lambda s: s.activation(
            out=sq_scr[:, :], in_=xt_ap, func=AF.Square, bias=0.0, scale=1.0),
          waits=ld_waits + [(sV, sV.n)], inc=(sS, 1))
        ssd = sS.n
        E("v", lambda v: v.drain())
        E("v", lambda v: v.tensor_scalar(
            out=m_dst, in0=stat[:, 0:1], scalar1=INV_D, scalar2=0.0,
            op0=ALU.mult, op1=ALU.bypass))
        E("v", lambda v: v.tensor_reduce(
            out=stat[:, 3:4], in_=sq_scr[:, :], axis=AX.X, op=ALU.add),
          waits=[(sS, ssd)], inc=(sV, 1))
        E("v", lambda v: v.drain())
        E("v", lambda v: v.tensor_scalar(
            out=stat[:, 3:4], in0=stat[:, 3:4], scalar1=INV_D, scalar2=0.0,
            op0=ALU.mult, op1=ALU.bypass))
        E("v", lambda v: v.drain())
        E("v", lambda v: v.scalar_tensor_tensor(
            out=stat[:, 4:5], in0=m_dst, scalar=m_dst, in1=stat[:, 3:4],
            op0=ALU.mult, op1=ALU.subtract))
        E("v", lambda v: v.drain())
        E("v", lambda v: v.tensor_scalar(
            out=stat[:, 4:5], in0=stat[:, 4:5], scalar1=-1.0, scalar2=eps,
            op0=ALU.mult, op1=ALU.add), inc=(sV, 1))
        E("s", lambda s: s.activation(
            out=stat[:, 5:6], in_=stat[:, 4:5], func=AF.Ln, bias=0.0, scale=1.0),
          waits=[(sV, sV.n)])
        E("s", lambda s: s.activation(
            out=r_dst, in_=stat[:, 5:6], func=AF.Exp, bias=0.0, scale=-0.5),
          inc=(sS, 1))
        return sS.n

    def sq_accum(src_ap, dst_idx, waits):
        E("s", lambda s: s.activation(
            out=sq_scr[:, :], in_=src_ap, func=AF.Square, bias=0.0, scale=1.0),
          waits=list(waits) + [(sV, sV.n)], inc=(sS, 1))
        ssd = sS.n
        E("v", lambda v: v.tensor_reduce(
            out=stat[:, dst_idx:dst_idx + 1], in_=sq_scr[:, :], axis=AX.X,
            op=ALU.add), waits=[(sS, ssd)], inc=(sV, 1))
        E("v", lambda v: v.drain())
        return sV.n

    def rsqrt_nl(src_idx, dst_ap, waits):
        E("s", lambda s: s.activation(
            out=stat[:, 5:6], in_=stat[:, src_idx:src_idx + 1], func=AF.Ln,
            bias=0.0, scale=1.0), waits=waits)
        E("s", lambda s: s.activation(
            out=dst_ap, in_=stat[:, 5:6], func=AF.Exp, bias=0.0, scale=-0.5),
          inc=(sS, 1))
        return sS.n

    def vrecip(out_ap, in_ap):
        E("v", lambda v: v.drain())
        E("v", lambda v: v.reciprocal(out=out_ap, in_=in_ap))
        E("v", lambda v: v.drain())

    def q_build(gq):
        ig = gq % 2
        for tt in range(2):
            tile = gq * 2 + tt
            fence = ([(sV, p0_v), (sS, p0_s), (sT, p0_t)] if gq == 0 else [])
            E("y", lambda y, tile=tile, ig=ig, tt=tt: y.dma_start(
                out=xg[ig][:, tt * D:(tt + 1) * D], in_=dxin[tile * P:(tile + 1) * P, :]),
              waits=([(sV, st["xg_free"][ig])] if tt == 0 else []) + fence, inc16=sLD)
        ld = sLD.n
        for tt in range(2):
            tile = gq * 2 + tt
            xt_ap = xg[ig][:, tt * D:(tt + 1) * D]
            srs = ln_stats(xt_ap, m_all[:, tile:tile + 1], rstd_all[:, tile:tile + 1],
                           1e-5, [(sLD, ld)])
            E("v", lambda v, xt_ap=xt_ap, tile=tile: v.tensor_scalar(
                out=xnt[:, :], in0=xt_ap, scalar1=m_all[:, tile:tile + 1],
                scalar2=rstd_all[:, tile:tile + 1], op0=ALU.subtract, op1=ALU.mult),
              waits=[(sS, srs)], inc=(sV, 1))
            vxn = sV.n
            E("v", lambda v: v.tensor_reduce(
                out=stat[:, 0:1], in_=xnt[:, :], axis=AX.X, op=ALU.add))
            E("v", lambda v: v.drain())
            sq_accum(xnt[:, :], 3, [(sV, vxn)])
            E("v", lambda v: v.tensor_scalar(
                out=stat[:, 6:7], in0=stat[:, 0:1], scalar1=INV_D, scalar2=0.0,
                op0=ALU.mult, op1=ALU.bypass))
            E("v", lambda v: v.drain())
            E("v", lambda v: v.tensor_scalar(
                out=stat[:, 3:4], in0=stat[:, 3:4], scalar1=INV_D, scalar2=0.0,
                op0=ALU.mult, op1=ALU.bypass))
            E("v", lambda v: v.drain())
            E("v", lambda v: v.scalar_tensor_tensor(
                out=stat[:, 4:5], in0=stat[:, 6:7], scalar=stat[:, 6:7],
                in1=stat[:, 3:4], op0=ALU.mult, op1=ALU.subtract))
            E("v", lambda v: v.drain())
            E("v", lambda v: v.tensor_scalar(
                out=stat[:, 4:5], in0=stat[:, 4:5], scalar1=-1.0, scalar2=1e-12,
                op0=ALU.mult, op1=ALU.add), inc=(sV, 1))
            sr2 = rsqrt_nl(4, stat[:, 7:8], [(sV, sV.n)])
            E("v", lambda v: v.tensor_scalar(
                out=yt[:, :], in0=xnt[:, :], scalar1=stat[:, 6:7],
                scalar2=stat[:, 7:8], op0=ALU.subtract, op1=ALU.mult),
              waits=[(sS, sr2)])
            E("v", lambda v: v.tensor_tensor(
                out=yt[:, :], in0=yt[:, :], in1=lnw_b[:, :], op=ALU.mult))
            E("v", lambda v: v.tensor_tensor(
                out=yt[:, :], in0=yt[:, :], in1=lnb_b[:, :], op=ALU.add),
              inc=(sV, 1))
            vy = sV.n
            sq_accum(yt[:, :], 3, [(sV, vy)])
            E("s", lambda s: s.activation(
                out=stat[:, 5:6], in_=stat[:, 3:4], func=AF.Ln, bias=0.0, scale=1.0),
              waits=[(sV, sV.n)])
            E("s", lambda s: s.activation(
                out=stat[:, 4:5], in_=stat[:, 5:6], func=AF.Exp, bias=0.0, scale=0.5),
              inc=(sS, 1))
            E("v", lambda v: v.tensor_scalar(
                out=stat[:, 4:5], in0=stat[:, 4:5], scalar1=1e-12, scalar2=0.0,
                op0=ALU.max, op1=ALU.bypass), waits=[(sS, sS.n)])
            E("v", lambda v: v.drain())
            E("v", lambda v: v.reciprocal(out=stat[:, 5:6], in_=stat[:, 4:5]))
            E("v", lambda v: v.drain())
            E("v", lambda v: v.tensor_scalar(
                out=qtile[:, :], in0=yt[:, :], scalar1=stat[:, 5:6], scalar2=0.0,
                op0=ALU.mult, op1=ALU.bypass), inc=(sV, 1))
            E("v", lambda v: v.drain())
            vq = sV.n
            for c in range(8):
                E("t", lambda t, c=c: t.transpose(
                    out=ps_qt[:, c * P:(c + 1) * P], in_=qtile[:, c * P:(c + 1) * P],
                    identity=ident_b[:, :]),
                  waits=[(sV, vq), (sV, st["qt_evac"])] if c == 0 else [],
                  inc=(sT, 1) if c == 7 else None)
            tq = sT.n
            E("v", lambda v, ig=ig, tt=tt: v.tensor_copy(
                out=QT[ig][:, :].rearrange("p (h q) -> p h q", h=8)[:, :, tt * P:(tt + 1) * P],
                in_=ps_qt[:, :].rearrange("p (h q) -> p h q", h=8)),
              waits=[(sT, tq), (sT, st["qt_free"][ig]) if tt == 0 else (sT, 0)],
              inc=(sV, 1))
            st["qt_evac"] = sV.n
        E("v", lambda v: v.drain(), inc=(sV, 1))
        st["xg_free"][ig] = sV.n
        st["qt_built"][ig] = sV.n

    def mm1_half(gq, half):
        ig = gq % 2
        for kk in range(HB):
            k = half * HB + kk
            i2 = kk % 3
            for hc in range(8):
                E("t", lambda t, k=k, hc=hc, i2=i2, ig=ig: t.matmul(
                    out=ps_sT[i2][:, :],
                    lhsT=KT[:, hc * I0 + k * P: hc * I0 + (k + 1) * P],
                    rhs=QT[ig][:, hc * GRP:(hc + 1) * GRP],
                    start=(hc == 0), stop=(hc == 7)),
                  waits=[(sV, st["sT_free"][i2]), (sV, st["qt_built"][ig])]
                        if hc == 0 else [],
                  inc=(sT, 1) if hc == 7 else None)
            t1 = sT.n
            E("v", lambda v, kk=kk, i2=i2: v.tensor_scalar(
                out=sstage[:, kk * GRP:(kk + 1) * GRP], in0=ps_sT[i2][:, :],
                scalar1=CLIP_LO, scalar2=CLIP_HI, op0=ALU.max, op1=ALU.min),
              waits=[(sT, t1)], inc=(sV, 1))
            st["sT_free"][i2] = sV.n

    def bchain_half(half):
        ih = half
        E("v", lambda v: v.drain(), inc=(sV, 1))
        vclip = sV.n
        E("g", lambda g: g.tensor_tensor(
            out=aux[:, :], in0=sstage[:, :], in1=sstage[:, :], op=ALU.mult),
          waits=[(sV, vclip), (sS, st["aux_free"])], inc=(sG, 1))
        gp = sG.n
        E("s", lambda s: s.activation(
            out=aux[:, :], in_=aux[:, :], func=AF.Ln, bias=1.0, scale=-1.0),
          waits=[(sG, gp)])
        E("s", lambda s: s.activation(
            out=aux[:, :], in_=aux[:, :], func=AF.Exp, bias=0.0, scale=-0.5),
          inc=(sS, 1))
        se = sS.n
        E("v", lambda v: v.tensor_tensor(
            out=aux[:, :], in0=sstage[:, :], in1=aux[:, :], op=ALU.mult),
          waits=[(sS, se)], inc=(sV, 1))
        E("v", lambda v: v.drain())
        va = sV.n
        E("s", lambda s: s.activation(
            out=aux[:, :], in_=aux[:, :], func=AF.Arctan, bias=0.0, scale=1.0),
          waits=[(sV, va)])
        E("s", lambda s: s.activation(
            out=aux[:, :], in_=aux[:, :], func=AF.Ln, bias=kc[:, 2:3], scale=INV_PI))
        E("s", lambda s, ih=ih: s.activation(
            out=Bst[ih][:, :], in_=aux[:, :], func=AF.Exp, bias=0.0, scale=9.0),
          waits=[(sT, max(st["bst_free"]))], inc=(sS, 1))
        st["aux_free"] = sS.n
        return sS.n

    def mm2_half(gq, half, sb_done):
        ih = half
        for kk in range(HB):
            k = half * HB + kk
            i2 = kk % 2
            E("y", lambda y, k=k, i2=i2: y.dma_start(
                out=qwk[i2][:, :], in_=dqwbf[k * P:(k + 1) * P, :]),
              waits=[(sT, st["qwk_free"][i2]), (sQW, qw_cached if gq == 0 else 0)],
              inc16=sLD)
            qld = sLD.n
            for tt in range(2):
                for hf in range(2):
                    first = (tt == 0 and hf == 0)
                    E("t", lambda t, kk=kk, tt=tt, hf=hf, i2=i2, ih=ih, k=k: t.matmul(
                        out=ps_X[tt * 2 + hf][:, :],
                        lhsT=Bst[ih][:, kk * GRP + tt * P: kk * GRP + (tt + 1) * P],
                        rhs=qwk[i2][:, hf * 512:(hf + 1) * 512],
                        start=(k == 0), stop=(k == NC0 - 1)),
                      waits=([(sS, sb_done), (sLD, qld)] +
                             ([(sV, st["xfin_v"])] if k == 0 else []))
                            if first else [],
                      inc=(sT, 1) if (tt == 1 and hf == 1) else None)
            st["qwk_free"][i2] = sT.n
        st["bst_free"][ih] = sT.n

    def x_finalize(gq):
        tdone = sT.n
        for tt in range(2):
            tile = gq * 2 + tt
            pa, pb = ps_X[tt * 2], ps_X[tt * 2 + 1]
            E("s", lambda s, pa=pa: s.activation(
                out=sq_scr[:, 0:512], in_=pa[:, :], func=AF.Square,
                bias=0.0, scale=1.0),
              waits=[(sT, tdone), (sV, sV.n)])
            E("s", lambda s, pb=pb: s.activation(
                out=sq_scr[:, 512:1024], in_=pb[:, :], func=AF.Square,
                bias=0.0, scale=1.0), inc=(sS, 1))
            sx = sS.n
            E("v", lambda v: v.tensor_reduce(
                out=stat[:, 3:4], in_=sq_scr[:, :], axis=AX.X, op=ALU.add),
              waits=[(sS, sx)], inc=(sV, 1))
            E("s", lambda s: s.activation(
                out=stat[:, 5:6], in_=stat[:, 3:4], func=AF.Ln, bias=0.0, scale=1.0),
              waits=[(sV, sV.n)])
            E("s", lambda s: s.activation(
                out=stat[:, 4:5], in_=stat[:, 5:6], func=AF.Exp, bias=0.0, scale=0.5),
              inc=(sS, 1))
            E("v", lambda v: v.tensor_scalar(
                out=stat[:, 4:5], in0=stat[:, 4:5], scalar1=1e-12, scalar2=0.0,
                op0=ALU.max, op1=ALU.bypass), waits=[(sS, sS.n)])
            E("v", lambda v: v.drain())
            E("v", lambda v: v.reciprocal(out=stat[:, 5:6], in_=stat[:, 4:5]))
            E("v", lambda v: v.drain())
            E("v", lambda v, pa=pa, tt=tt: v.scalar_tensor_tensor(
                out=Xout[tt][:, 0:512], in0=pa[:, :], scalar=stat[:, 5:6],
                in1=bias1_b[:, 0:512], op0=ALU.mult, op1=ALU.add),
              waits=[(sA0, max(st["xout_free"]))])
            E("v", lambda v, pb=pb, tt=tt: v.scalar_tensor_tensor(
                out=Xout[tt][:, 512:1024], in0=pb[:, :], scalar=stat[:, 5:6],
                in1=bias1_b[:, 512:1024], op0=ALU.mult, op1=ALU.add),
              inc=(sV, 1))
            vf = sV.n
            E("s", lambda s, tile=tile, tt=tt: s.dma_start(
                out=dattn0[tile * P:(tile + 1) * P, :], in_=Xout[tt][:, :]),
              waits=[(sV, vf)], inc16=sA0)
            st["xout_free"][tt] = sA0.n
            st["attn0_sp"][tile] = sA0.n
        st["xfin_v"] = sV.n

    q_build(0)
    for gq in range(NG):
        mm1_half(gq, 0)
        sb0 = bchain_half(0)
        mm2_half(gq, 0, sb0)
        if gq + 1 < NG:
            q_build(gq + 1)
        mm1_half(gq, 1)
        sb1 = bchain_half(1)
        mm2_half(gq, 1, sb1)
        st["qt_free"][gq % 2] = sT.n
        x_finalize(gq)

    if upto == "dbgp1":
        E("v", lambda v: v.tensor_copy(out=aux[:, :], in_=sstage[:, :]), inc=(sV, 1))
        fw = [(sV, sV.n), (sS, sS.n), (sT, sT.n)]
        def dump(row0, src_ap, eng="y"):
            E(eng, lambda e, r=row0, a=src_ap: e.dma_start(
                out=dout[r * P:(r + 1) * P, 0:a.shape[1]], in_=a), waits=fw, inc16=sOUT)
        dump(0, KT[:, 3072:4096], "g")
        dump(1, QT[1][:, 0:1024], "g")
        dump(2, QT[1][:, 1024:2048], "g")
        E("y", lambda y: y.dma_start(out=dout[3 * P:4 * P, 0:NT], in_=m_all[:, :]),
          waits=fw, inc16=sOUT)
        E("y", lambda y: y.dma_start(out=dout[3 * P:4 * P, NT:2 * NT], in_=rstd_all[:, :]),
          waits=fw, inc16=sOUT)
        dump(4, qtile[:, :], "g")
        dump(5, yt[:, :])
        for i in range(4):
            dump(6 + i, sstage[:, i * 1024:(i + 1) * 1024])
        for i in range(4):
            dump(10 + i, Bst[1][:, i * 1024:(i + 1) * 1024], "g")
        for i in (2, 3):
            dump(12 + i, aux[:, i * 1024:(i + 1) * 1024])
        E("g", lambda g: g.nop(), waits=[(sOUT, sOUT.n)])
        with nc.Block() as block:
            @block.sync
            def _(y):
                b.sched.replay("y", y)

            @block.vector
            def _(v):
                b.sched.replay("v", v)

            @block.scalar
            def _(s):
                b.sched.replay("s", s)

            @block.tensor
            def _(t):
                b.sched.replay("t", t)

            @block.gpsimd
            def _(g):
                b.sched.replay("g", g)
        return nc

    b.free("xg0", "xg1", "QT0", "QT1", "qwk0", "qwk1", "sstage", "aux",
           "Bst0", "xnt", "yt", "qtile", "Xout0",
           "ps_sT0", "ps_sT1", "ps_sT2", "ps_X0", "ps_X1", "ps_X2", "ps_X3", "ps_qt",
           "KT")

    WrB = b.sb("WrB", [P, 8 * D], BF16)
    WkB = b.sb("WkB", [P, 8 * D], BF16)
    WvB = b.sb("WvB", [P, 8 * D], BF16)
    x1t = [b.sb(f"x1t{i}", [P, D], F32) for i in range(2)]
    xstr = [b.sb(f"xstr{i}", [P, D], F32) for i in range(2)]
    a0str = [b.sb(f"a0str{i}", [P, D], F32) for i in range(2)]
    xn_p2 = b.sb("xn_p2", [P, D], F32)
    xnTP = [b.sb(f"xnTP{i}", [P, 8 * 129], F32) for i in range(2)]
    dTt = b.sb("dTt", [P, D], F32)
    mixtmp = b.sb("mixtmp", [P, D], F32)
    xrT = b.sb("xrT", [P, D], BF16)
    xkT = b.sb("xkT", [P, D], BF16)
    r_tile = b.sb("r_tile", [P, D], F32)
    kact_f = b.sb("kact_f", [P, D], F32)
    kact_b = b.sb("kact_b", [P, D], BF16)
    kactT = b.sb("kactT", [P, D], BF16)
    rv = b.sb("rv", [P, D], F32)
    xn2f = [b.sb(f"xn2f{i}", [P, D], F32) for i in range(2)]
    ps_xnT = b.ps("ps_xnT", [P, D], F32)
    ps_mm = [b.ps(f"ps_mm{i}", [P, 512], F32) for i in range(2)]
    ps_kT = b.ps("ps_kT", [P, D], BF16)

    p1_v, p1_s, p1_t = sV.n, sS.n, sT.n
    p1_fence = [(sV, p1_v), (sS, p1_s), (sT, p1_t)]
    wB_ld = []
    for src, dst in [(dwrT, WrB), (dwkT, WkB), (dwvT, WvB)]:
        E("g", lambda g, s=src, d=dst: g.dma_start(
            out=d[:, :].rearrange("p (c j) -> p c j", c=8),
            in_=s.rearrange("(c p) j -> p c j", p=P)),
          waits=p1_fence, inc16=sGD)
        wB_ld.append(sGD.n)

    def mm_1024(ps_pair, lhsT_buf, W_buf, wait_first, done_inc=True):
        for sl in range(2):
            for cc in range(8):
                E("t", lambda t, sl=sl, cc=cc, lb=lhsT_buf, wb=W_buf: t.matmul(
                    out=ps_pair[sl][:, :], lhsT=lb[:, cc * P:(cc + 1) * P],
                    rhs=wb[:, cc * D + sl * 512: cc * D + (sl + 1) * 512],
                    start=(cc == 0), stop=(cc == 7)),
                  waits=wait_first if (sl == 0 and cc == 0) else [],
                  inc=(sT, 1) if (cc == 7) else None)

    p2 = {"x_free": [0, 0], "a0_free": [0, 0], "x1_sp": [0] * NT,
          "xn2_sp": [0] * NT, "xn2T_evac": 0,
          "ssq_done": [0] * NT, "gred_done": [0] * NT}
    prevten = [0]

    for tile in range(NT):
        i2 = tile % 2
        cur, prv = xnTP[tile % 2], xnTP[1 - tile % 2]
        E("y", lambda y, tile=tile, i2=i2: y.dma_start(
            out=xstr[i2][:, :], in_=dxin[tile * P:(tile + 1) * P, :]),
          waits=[(sV, p2["x_free"][i2])] + (p1_fence if tile < 2 else []), inc16=sLD)
        xld = sLD.n
        E("y", lambda y, tile=tile, i2=i2: y.dma_start(
            out=a0str[i2][:, :], in_=dattn0[tile * P:(tile + 1) * P, :]),
          waits=[(sA0, sA0.n), (sV, p2["a0_free"][i2])], inc16=sLD)
        ald = sLD.n
        E("v", lambda v, tile=tile, i2=i2: v.tensor_scalar(
            out=xn_p2[:, :], in0=xstr[i2][:, :], scalar1=m_all[:, tile:tile + 1],
            scalar2=rstd_all[:, tile:tile + 1], op0=ALU.subtract, op1=ALU.mult),
          waits=[(sLD, xld), (sV, sV.n), (sT, prevten[0])], inc=(sV, 1))
        E("v", lambda v: v.drain())
        vxn = sV.n
        for c in range(8):
            E("t", lambda t, c=c: t.transpose(
                out=ps_xnT[:, c * P:(c + 1) * P], in_=xn_p2[:, c * P:(c + 1) * P],
                identity=ident_f[:, :]),
              waits=[(sV, vxn)] if c == 0 else [],
              inc=(sT, 1) if c == 7 else None)
        txn = sT.n
        E("v", lambda v, cur=cur: v.tensor_copy(
            out=cur[:, :].rearrange("p (h c) -> p h c", h=8)[:, :, 1:129],
            in_=ps_xnT[:, :].rearrange("p (h c) -> p h c", h=8)),
          waits=[(sT, txn)], inc=(sV, 1))
        if tile == 0:
            E("v", lambda v, cur=cur: v.memset(
                cur[:, :].rearrange("p (h c) -> p h c", h=8)[:, :, 0:1], 0.0))
        else:
            E("v", lambda v, cur=cur, prv=prv: v.tensor_copy(
                out=cur[:, :].rearrange("p (h c) -> p h c", h=8)[:, :, 0:1],
                in_=prv[:, :].rearrange("p (h c) -> p h c", h=8)[:, :, 128:129]))
        cur3 = cur[:, :].rearrange("p (h c) -> p h c", h=8)
        cx = cur3[:, :, 1:129]
        cp = cur3[:, :, 0:128]
        d3 = dTt[:, :].rearrange("p (h c) -> p h c", h=8)
        m3 = mixtmp[:, :].rearrange("p (h c) -> p h c", h=8)
        E("g", lambda g, cx=cx, cp=cp, d3=d3: g.tensor_tensor(
            out=d3, in0=cx, in1=cp, op=ALU.subtract),
          waits=[(sV, sV.n), (sGD, wB_ld[2])], inc=(sG, 1))
        gd = sG.n
        E("v", lambda v, d3=d3, m3=m3: v.tensor_tensor(
            out=m3, in0=d3, in1=mur_sb[:, :].unsqueeze(2).to_broadcast((P, 8, P)),
            op=ALU.mult), waits=[(sG, gd), (sLD, mu_ld), (sT, prevten[0])])
        E("v", lambda v, m3=m3, cp=cp: v.tensor_tensor(
            out=xrT[:, :].rearrange("p (h c) -> p h c", h=8), in0=m3, in1=cp,
            op=ALU.add), inc=(sV, 1))
        E("v", lambda v: v.drain())
        vxr = sV.n
        E("v", lambda v, d3=d3, m3=m3: v.tensor_tensor(
            out=m3, in0=d3, in1=muk_sb[:, :].unsqueeze(2).to_broadcast((P, 8, P)),
            op=ALU.mult))
        E("v", lambda v, m3=m3, cp=cp: v.tensor_tensor(
            out=xkT[:, :].rearrange("p (h c) -> p h c", h=8), in0=m3, in1=cp,
            op=ALU.add), inc=(sV, 1))
        E("v", lambda v: v.drain())
        vxk = sV.n
        mm_1024(ps_mm, xrT, WrB, [(sV, vxr), (sS, sS.n), (sV, sV.n)])
        tr = sT.n
        E("s", lambda s: s.activation(
            out=r_tile[:, 0:512], in_=ps_mm[0][:, :], func=AF.Sigmoid,
            bias=0.0, scale=1.0), waits=[(sT, tr - 1)])
        E("s", lambda s: s.activation(
            out=r_tile[:, 512:1024], in_=ps_mm[1][:, :], func=AF.Sigmoid,
            bias=0.0, scale=1.0), waits=[(sT, tr)], inc=(sS, 1))
        sr = sS.n
        mm_1024(ps_mm, xkT, WkB, [(sV, vxk), (sS, sr)])
        tk = sT.n
        E("v", lambda v: v.tensor_scalar(
            out=kact_f[:, 0:512], in0=ps_mm[0][:, :], scalar1=0.0, scalar2=0.0,
            op0=ALU.max, op1=ALU.bypass), waits=[(sT, tk - 1)])
        E("v", lambda v: v.tensor_scalar(
            out=kact_f[:, 512:1024], in0=ps_mm[1][:, :], scalar1=0.0, scalar2=0.0,
            op0=ALU.max, op1=ALU.bypass), waits=[(sT, tk)])
        E("v", lambda v: v.tensor_tensor(
            out=kact_b[:, :], in0=kact_f[:, :], in1=kact_f[:, :], op=ALU.mult),
          inc=(sV, 1))
        E("v", lambda v: v.drain())
        vka = sV.n
        for c in range(8):
            E("t", lambda t, c=c: t.transpose(
                out=ps_kT[:, c * P:(c + 1) * P], in_=kact_b[:, c * P:(c + 1) * P],
                identity=ident_b[:, :]),
              waits=[(sV, vka)] if c == 0 else [],
              inc=(sT, 1) if c == 7 else None)
        tkt = sT.n
        E("v", lambda v: v.tensor_copy(out=kactT[:, :], in_=ps_kT[:, :]),
          waits=[(sT, tkt)], inc=(sV, 1))
        E("v", lambda v: v.drain())
        vkt = sV.n
        mm_1024(ps_mm, kactT, WvB, [(sV, vkt)])
        tv = sT.n
        E("v", lambda v: v.scalar_tensor_tensor(
            out=rv[:, 0:512], in0=ps_mm[0][:, :], scalar=1.0,
            in1=r_tile[:, 0:512], op0=ALU.mult, op1=ALU.mult),
          waits=[(sT, tv - 1)])
        E("v", lambda v: v.scalar_tensor_tensor(
            out=rv[:, 512:1024], in0=ps_mm[1][:, :], scalar=1.0,
            in1=r_tile[:, 512:1024], op0=ALU.mult, op1=ALU.mult),
          waits=[(sT, tv)], inc=(sV, 1))
        prevten[0] = tv
        E("g", lambda g, i2=i2: g.tensor_tensor(
            out=mixtmp[:, :], in0=xstr[i2][:, :], in1=a0str[i2][:, :], op=ALU.add),
          waits=[(sLD, ald), (sV, sV.n)], inc=(sG, 1))
        ga = sG.n
        E("v", lambda v, tile=tile, i2=i2: v.tensor_tensor(
            out=x1t[i2][:, :], in0=mixtmp[:, :], in1=rv[:, :],
            op=ALU.add),
          waits=[(sG, ga), (sACT, p2["x1_sp"][tile - 2] if tile >= 2 else 0),
                 (sS, p2["ssq_done"][tile - 2] if tile >= 2 else 0)],
          inc=(sV, 1))
        vx1 = sV.n
        p2["x_free"][i2] = vx1
        p2["a0_free"][i2] = vx1
        E("s", lambda s, tile=tile, i2=i2: s.dma_start(
            out=dx1[tile * P:(tile + 1) * P, :], in_=x1t[i2][:, :]),
          waits=[(sV, vx1)], inc16=sACT)
        p2["x1_sp"][tile] = sACT.n
        E("v", lambda v, tile=tile, i2=i2: v.tensor_reduce(
            out=m2_all[:, tile:tile + 1], in_=x1t[i2][:, :], axis=AX.X,
            op=ALU.add), waits=[(sV, vx1)], inc=(sV, 1))
        E("v", lambda v: v.drain())
        E("s", lambda s, i2=i2: s.activation(
            out=sq_scr[:, :], in_=x1t[i2][:, :], func=AF.Square, bias=0.0,
            scale=1.0), waits=[(sV, sV.n)], inc=(sS, 1))
        ssq = sS.n
        E("v", lambda v, tile=tile: v.tensor_reduce(
            out=ss2_all[:, tile:tile + 1], in_=sq_scr[:, :], axis=AX.X,
            op=ALU.add), waits=[(sS, ssq)], inc=(sV, 1))
        E("v", lambda v: v.drain())
        p2["ssq_done"][tile] = ssq
        p2["gred_done"][tile] = 0

    E("v", lambda v: v.drain(), waits=[(sG, sG.n), (sV, sV.n)])
    E("v", lambda v: v.tensor_scalar(
        out=m2_all[:, :], in0=m2_all[:, :], scalar1=INV_D, scalar2=0.0,
        op0=ALU.mult, op1=ALU.bypass))
    E("v", lambda v: v.tensor_scalar(
        out=ss2_all[:, :], in0=ss2_all[:, :], scalar1=INV_D, scalar2=0.0,
        op0=ALU.mult, op1=ALU.bypass))
    E("v", lambda v: v.drain())
    E("v", lambda v: v.tensor_tensor(
        out=rstd2_all[:, :], in0=m2_all[:, :], in1=m2_all[:, :], op=ALU.mult))
    E("v", lambda v: v.drain())
    E("v", lambda v: v.tensor_tensor(
        out=rstd2_all[:, :], in0=ss2_all[:, :], in1=rstd2_all[:, :],
        op=ALU.subtract), inc=(sV, 1))
    E("s", lambda s: s.activation(
        out=rstd2_all[:, :], in_=rstd2_all[:, :], func=AF.Ln, bias=kc[:, 0:1], scale=1.0),
      waits=[(sV, sV.n)])
    E("s", lambda s: s.activation(
        out=rstd2_all[:, :], in_=rstd2_all[:, :], func=AF.Exp, bias=0.0, scale=-0.5),
      inc=(sS, 1))
    ln2fin = sS.n

    for tile in range(NT):
        i2 = tile % 2
        E("y", lambda y, tile=tile, i2=i2: y.dma_start(
            out=xstr[i2][:, :], in_=dx1[tile * P:(tile + 1) * P, :]),
          waits=[(sACT, sACT.n), (sV, p2["x_free"][i2])], inc16=sLD)
        x1ld = sLD.n
        E("v", lambda v, tile=tile, i2=i2: v.tensor_scalar(
            out=xn2f[i2][:, :], in0=xstr[i2][:, :],
            scalar1=m2_all[:, tile:tile + 1], scalar2=rstd2_all[:, tile:tile + 1],
            op0=ALU.subtract, op1=ALU.mult),
          waits=[(sS, ln2fin), (sLD, x1ld),
                 (sACT, p2["xn2_sp"][tile - 2] if tile >= 2 else 0)],
          inc=(sV, 1))
        E("v", lambda v: v.drain())
        p2["x_free"][i2] = sV.n
        vx = sV.n
        E("s", lambda s, tile=tile, i2=i2: s.dma_start(
            out=dxn2[tile * P:(tile + 1) * P, :], in_=xn2f[i2][:, :]),
          waits=[(sV, vx)], inc16=sACT)
        p2["xn2_sp"][tile] = sACT.n
        E("v", lambda v, i2=i2: v.tensor_reduce(
            out=stat[:, 6:7], in_=xn2f[i2][:, :], axis=AX.X, op=ALU.max,
            apply_absolute_value=True))
        E("v", lambda v: v.drain())
        E("v", lambda v: v.tensor_tensor(
            out=runmax[:, 0:1], in0=runmax[:, 0:1], in1=stat[:, 6:7], op=ALU.max),
          inc=(sV, 1))
        E("v", lambda v: v.drain())
        for c in range(8):
            E("t", lambda t, c=c, i2=i2: t.transpose(
                out=ps_xnT[:, c * P:(c + 1) * P], in_=xn2f[i2][:, c * P:(c + 1) * P],
                identity=ident_f[:, :]),
              waits=[(sV, vx), (sV, p2["xn2T_evac"])] if c == 0 else [],
              inc=(sT, 1) if c == 7 else None)
        tt2 = sT.n
        E("v", lambda v, tile=tile: v.tensor_copy(
            out=xn2T[:, :].rearrange("p (h c) -> p h c", h=8)[:, :, tile * P:(tile + 1) * P],
            in_=ps_xnT[:, :].rearrange("p (h c) -> p h c", h=8)),
          waits=[(sT, tt2)], inc=(sV, 1))
        p2["xn2T_evac"] = sV.n
    E("v", lambda v: v.drain(), inc=(sV, 1))
    xn2T_done = sV.n

    E("g", lambda g: g.dma_start(out=cc1i[:, :], in_=runmax[:, 0:1]),
      waits=[(sV, sV.n)], inc16=sGD)
    E("g", lambda g: g.collective_compute(
        "AllReduce", ALU.max, replica_groups=[list(range(8))],
        ins=[cc1i.ap().opt()], outs=[cc1o.ap().opt()]),
      waits=[(sGD, sGD.n)], inc=(sCC, 1))
    ar1 = sCC.n

    b.free("WrB", "WkB", "WvB", "xstr0", "xstr1", "a0str0", "a0str1",
           "xn_p2", "xnTP0", "xnTP1", "dTt", "mixtmp", "xrT", "xkT",
           "r_tile", "kact_f", "kact_b", "kactT", "rv", "xn2f0", "xn2f1",
           "ps_xnT", "ps_mm0", "ps_mm1", "ps_kT", "x1t0", "x1t1")

    if upto == "p1":
        for tile in range(NT):
            E("y", lambda y, tile=tile: y.dma_start(
                out=dout[tile * P:(tile + 1) * P, :],
                in_=dattn0[tile * P:(tile + 1) * P, :]),
              waits=[(sA0, st["attn0_sp"][tile])], inc16=sOUT)
        E("y", lambda y: y.nop(), waits=[(sOUT, sOUT.n)])
        with nc.Block() as block:
            @block.sync
            def _(y):
                b.sched.replay("y", y)

            @block.vector
            def _(v):
                b.sched.replay("v", v)

            @block.scalar
            def _(s):
                b.sched.replay("s", s)

            @block.tensor
            def _(t):
                b.sched.replay("t", t)

            @block.gpsimd
            def _(g):
                b.sched.replay("g", g)
        return nc

    if upto == "l0":
        for tile in range(NT):
            E("y", lambda y, tile=tile: y.dma_start(
                out=dout[tile * P:(tile + 1) * P, :],
                in_=dx1[tile * P:(tile + 1) * P, :]),
              waits=[(sACT, p2["x1_sp"][tile]), (sV, xn2T_done), (sCC, ar1)],
              inc16=sOUT)
        E("y", lambda y: y.nop(), waits=[(sOUT, sOUT.n)])
        with nc.Block() as block:
            @block.sync
            def _(y):
                b.sched.replay("y", y)

            @block.vector
            def _(v):
                b.sched.replay("v", v)

            @block.scalar
            def _(s):
                b.sched.replay("s", s)

            @block.tensor
            def _(t):
                b.sched.replay("t", t)

            @block.gpsimd
            def _(g):
                b.sched.replay("g", g)
        return nc


    p2e = [(sV, sV.n), (sS, sS.n), (sT, sT.n)]
    b.free("sq_scr", "vrow")
    qT = b.sb("qT", [P, 8 * T], BF16)
    kTt = b.sb("kTt", [P, 8 * T], BF16)
    vst = b.sb("vst", [P, NT * 1040], BF16)
    Wb = [b.sb(f"Wb{i}", [P, 8 * D], BF16) for i in range(2)]
    PT = [b.sb(f"PT{i}", [P, 512], BF16) for i in range(2)]
    accsb = b.sb("accsb", [P, 512], F32)
    rrow = b.sb("rrow", [P, 512], F32)
    aoT = [b.sb(f"aoT{i}", [P, 8 * 512], BF16) for i in range(2)]
    opst = [b.sb("opst0", [P, D], F32)] * 2
    ps_s4 = [b.ps(f"ps_s4{i}", [P, 512], F32) for i in range(2)]
    ps_acc = [b.ps(f"ps_acc{i}", [P, 512], F32) for i in range(2)]
    ps_rb = b.ps("ps_rb", [P, 512], F32)
    ps_op = [b.ps(f"ps_op{i}", [P, 512], F32) for i in range(2)]

    def wload(dst, srcap, waits):
        E("g", lambda g, s=srcap, d=dst: g.dma_start(
            out=d[:, :].rearrange("p (c j) -> p c j", c=8),
            in_=s.rearrange("(c p) j -> p c j", p=P)),
          waits=waits, inc16=sGD)
        return sGD.n

    wq_ld = wload(Wb[0], dl1q, p2e)
    wk_ld = wload(Wb[1], dl1k, p2e)
    E("g", lambda g: g.memset(
        vst[:, :].rearrange("p (a c) -> p a c", c=65)[:, :, 64:65], 1.0),
      waits=p2e, inc=(sG, 1))
    vones = sG.n

    mmb_free = [0, 0]
    def proj_T(dstT, wld, wbuf):
        for dqc in range(8):
            for tg in range(4):
                bank = (dqc * 4 + tg) % 2
                for ec in range(8):
                    E("t", lambda t, dqc=dqc, tg=tg, ec=ec, bank=bank, wld=wld, wbuf=wbuf: t.matmul(
                        out=ps_s4[bank][:, :],
                        lhsT=wbuf[:, ec * D + dqc * P: ec * D + (dqc + 1) * P],
                        rhs=xn2T[:, ec * T + tg * 512:(ec * T + (tg + 1) * 512)],
                        start=(ec == 0), stop=(ec == 7)),
                      waits=[(sGD, wld), (sV, xn2T_done), (sS, mmb_free[bank])]
                            if ec == 0 else [],
                      inc=(sT, 1) if ec == 7 else None)
                tq = sT.n
                E("s", lambda s, dqc=dqc, tg=tg, bank=bank, dstT=dstT: s.activation(
                    out=dstT[:, dqc * T + tg * 512: dqc * T + (tg + 1) * 512],
                    in_=ps_s4[bank][:, :], func=AF.Copy, bias=0.0, scale=1.0),
                  waits=[(sT, tq)], inc=(sS, 1))
                mmb_free[bank] = sS.n
    proj_T(qT, wq_ld, Wb[0])
    proj_T(kTt, wk_ld, Wb[1])
    qk_done_t = sT.n
    wv_ld = wload(Wb[0], dl1v, [(sT, qk_done_t)])
    for tt in range(NT):
        for es in range(2):
            bank = (tt * 2 + es) % 2
            for ec in range(8):
                E("t", lambda t, tt=tt, es=es, ec=ec, bank=bank: t.matmul(
                    out=ps_s4[bank][:, :],
                    lhsT=xn2T[:, ec * T + tt * P: ec * T + (tt + 1) * P],
                    rhs=Wb[0][:, ec * D + es * 512: ec * D + (es + 1) * 512],
                    start=(ec == 0), stop=(ec == 7)),
                  waits=[(sGD, wv_ld), (sS, mmb_free[bank])] if ec == 0 else [],
                  inc=(sT, 1) if ec == 7 else None)
            tv = sT.n
            E("s", lambda s, tt=tt, es=es, bank=bank: s.activation(
                out=vst[:, tt * 1040 + es * 520: tt * 1040 + (es + 1) * 520]
                    .rearrange("p (h d) -> p h d", d=65)[:, :, 0:64],
                in_=ps_s4[bank][:, :].rearrange("p (h d) -> p h d", d=64),
                func=AF.Copy, bias=0.0, scale=1.0),
              waits=[(sT, tv), (sG, vones)], inc=(sS, 1))
            mmb_free[bank] = sS.n
    v_done_s = sS.n
    wo_ld = wload(Wb[1], dl1o, [(sT, qk_done_t)])

    att = {"sfree": [mmb_free[0], mmb_free[1]], "ptfree": [0, 0], "accfree": [0, 0],
           "rbfree": 0, "aofree": [0, 0], "opfree": [0, 0],
           "opstfree": [0, 0]}
    attnP_sp = [0] * NT
    for qg in range(4):
        cur = qg % 2
        for h in range(16):
            hc, off = h // 2, 64 * (h % 2)
            ia = h % 2
            pieces = [(4 * qg, 0, 512, True)]
            for kt in range(4 * qg):
                pieces.append((kt, 0, 512, False))
            for jj in range(1, 4):
                pieces.append((4 * qg + jj, 128 * jj, 128, True))
                W = 512 - 128 * (jj + 1)
                if W > 0:
                    pieces.append((4 * qg + jj, 128 * (jj + 1), W, False))
            n = len(pieces)
            meta = []
            def emit_smm(i):
                kt, q0, qw, msk = pieces[i]
                bank = i % 2
                w0 = [(sS, att["sfree"][bank])]
                if i == 0:
                    w0 += [(sS, att["accfree"][ia]), (sS, v_done_s)]
                E("t", lambda t, kt=kt, q0=q0, qw=qw, bank=bank, hc=hc, off=off, qg=qg: t.matmul(
                    out=ps_s4[bank][:, 0:qw],
                    lhsT=kTt[off:off + 64, hc * T + kt * P: hc * T + (kt + 1) * P],
                    rhs=qT[off:off + 64, hc * T + qg * 512 + q0: hc * T + qg * 512 + q0 + qw],
                    start=True, stop=True),
                  waits=w0, inc=(sT, 1))
                tS = sT.n
                E("s", lambda s, qw=qw, bank=bank: s.activation(
                    out=PT[bank][:, 0:qw], in_=ps_s4[bank][:, 0:qw], func=AF.Exp,
                    bias=0.0, scale=0.125),
                  waits=[(sT, tS), (sT, att["ptfree"][bank])], inc=(sS, 1))
                eS = sS.n
                att["sfree"][bank] = eS
                mV = 0
                if msk:
                    E("v", lambda v, bank=bank: v.tensor_tensor(
                        out=PT[bank][:, 0:128], in0=PT[bank][:, 0:128],
                        in1=tri[:, :], op=ALU.mult),
                      waits=[(sS, eS)], inc=(sV, 1))
                    E("v", lambda v: v.drain())
                    mV = sV.n
                meta.append((bank, eS, mV))
            def emit_av(i):
                kt, q0, qw, msk = pieces[i]
                bank, eS, mV = meta[i]
                w0 = [(sS, eS)]
                if mV:
                    w0.append((sV, mV))
                E("t", lambda t, kt=kt, q0=q0, qw=qw, bank=bank, h=h, ia=ia: t.matmul(
                    out=ps_acc[ia][0:65, q0:q0 + qw],
                    lhsT=vst[:, kt * 1040 + h * 65: kt * 1040 + (h + 1) * 65],
                    rhs=PT[bank][:, 0:qw],
                    start=(i == 0), stop=(i == n - 1)),
                  waits=w0, inc=(sT, 1))
                att["ptfree"][bank] = sT.n
            emit_smm(0)
            for i in range(1, n):
                emit_smm(i)
                emit_av(i - 1)
            emit_av(n - 1)
            tlast = sT.n
            E("s", lambda s, ia=ia: s.activation(
                out=rrow[64:65, 0:512], in_=ps_acc[ia][64:65, 0:512],
                func=AF.Ln, bias=0.0, scale=1.0), waits=[(sT, tlast)])
            E("s", lambda s: s.activation(
                out=rrow[64:65, 0:512], in_=rrow[64:65, 0:512],
                func=AF.Exp, bias=0.0, scale=-1.0), inc=(sS, 1))
            rS = sS.n
            E("s", lambda s, ia=ia: s.activation(
                out=accsb[0:64, 0:512], in_=ps_acc[ia][0:64, 0:512],
                func=AF.Copy, bias=0.0, scale=1.0), inc=(sS, 1))
            aS = sS.n
            att["accfree"][ia] = aS
            E("t", lambda t: t.matmul(
                out=ps_rb[0:64, 0:512], lhsT=ones_f[64:65, 0:64],
                rhs=rrow[64:65, 0:512], start=True, stop=True),
              waits=[(sS, rS), (sV, att["rbfree"])], inc=(sT, 1))
            rbT = sT.n
            w0 = [(sS, aS), (sT, rbT)]
            if h < 2:
                w0.append((sT, att["aofree"][cur]))
            E("v", lambda v, h=h, hc=hc, cur=cur: v.tensor_tensor(
                out=aoT[cur][64 * (h % 2):64 * (h % 2) + 64, hc * 512:(hc + 1) * 512],
                in0=accsb[0:64, 0:512], in1=ps_rb[0:64, 0:512], op=ALU.mult),
              waits=w0, inc=(sV, 1))
            att["rbfree"] = sV.n
        E("v", lambda v: v.drain(), inc=(sV, 1))
        ao_built = sV.n
        for qs in range(4):
            io = qs % 2
            tops = []
            for half in range(2):
                for ec in range(8):
                    E("t", lambda t, qs=qs, half=half, ec=ec, cur=cur: t.matmul(
                        out=ps_op[half][:, :],
                        lhsT=aoT[cur][:, ec * 512 + qs * P: ec * 512 + (qs + 1) * P],
                        rhs=Wb[1][:, ec * D + half * 512: ec * D + (half + 1) * 512],
                        start=(ec == 0), stop=(ec == 7)),
                      waits=[(sV, ao_built), (sGD, wo_ld), (sV, att["opfree"][half])]
                            if ec == 0 else [],
                      inc=(sT, 1) if ec == 7 else None)
                tops.append(sT.n)
            E("v", lambda v, io=io: v.tensor_tensor(
                out=opst[io][:, 0:512], in0=ps_op[0][:, :], in1=bo_b[:, 0:512],
                op=ALU.add),
              waits=[(sT, tops[0]), (sACT, max(att["opstfree"]))])
            E("v", lambda v, io=io: v.tensor_tensor(
                out=opst[io][:, 512:1024], in0=ps_op[1][:, :], in1=bo_b[:, 512:1024],
                op=ALU.add), waits=[(sT, tops[1])], inc=(sV, 1))
            vOP = sV.n
            att["opfree"][0] = vOP
            att["opfree"][1] = vOP
            tile = qg * 4 + qs
            E("s", lambda s, tile=tile, io=io: s.dma_start(
                out=dattnP[tile * P:(tile + 1) * P, :], in_=opst[io][:, :]),
              waits=[(sV, vOP)], inc16=sACT)
            attnP_sp[tile] = sACT.n
            att["opstfree"][io] = sACT.n
        att["aofree"][cur] = sT.n

    b.free("opst0", "aoT0", "aoT1", "rrow", "accsb", "PT0", "PT1",
           "Wb0", "Wb1", "vst", "kTt", "qT",
           "ps_op0", "ps_op1", "ps_rb", "ps_acc0", "ps_acc1", "ps_s40", "ps_s41")
    b.free("xn2T")

    if upto == "p4":
        for tile in range(NT):
            E("y", lambda y, tile=tile: y.dma_start(
                out=dout[tile * P:(tile + 1) * P, :],
                in_=dattnP[tile * P:(tile + 1) * P, :]),
              waits=[(sACT, attnP_sp[tile])], inc16=sOUT)
        E("y", lambda y: y.nop(), waits=[(sOUT, sOUT.n)])
        _finish(nc, b)
        return nc

    p4e = [(sV, sV.n), (sS, sS.n), (sT, sT.n)]
    scl = b.sb("scl", [P, 12], F32)
    sdq = b.sb("sdq", [P, 192], F32)
    xqT = b.sb("xqT", [P, 8 * T], BF16)
    xbuf = [b.sb(f"xbuf{i}", [P, D], F32) for i in range(2)]
    ibuf = b.sb("ibuf", [P, D], I32)
    ubuf = b.sb("ubuf", [P, D], F32)
    wnat = [b.sb(f"wnat{i}", [P, D], F32) for i in range(2)]
    wqb = b.sb("wqb", [P, D], BF16)
    luT = [b.sb(f"luT{i}", [P, D], BF16) for i in range(2)]
    ustage = [b.sb(f"ustage{i}", [P, 512], BF16) for i in range(2)]
    ps_xt5 = b.ps("ps_xt5", [P, D], F32)
    ps_m5 = [b.ps(f"ps_m5{i}", [P, 512], F32) for i in range(2)]
    ps_w5 = b.ps("ps_w5", [P, D], BF16)

    def scale_setup(ccout, ar_val, col_s, col_r, fence):
        E("g", lambda g, c=ccout: g.dma_start(out=stat[:, 0:1], in_=c.ap()),
          waits=[(sCC, ar_val), (sV, sV.n), (sS, sS.n)] + fence, inc16=sGD)
        c1 = sGD.n
        E("t", lambda t: t.transpose(
            out=ps_xt5[0:1, 0:128], in_=stat[:, 0:1], identity=ident_f[:, :]),
          waits=[(sGD, c1), (sV, sV.n)], inc=(sT, 1))
        tc = sT.n
        E("v", lambda v: v.tensor_reduce(
            out=stat[0:1, 1:2], in_=ps_xt5[0:1, 0:128], axis=AX.X, op=ALU.max),
          waits=[(sT, tc)], inc=(sV, 1))
        E("v", lambda v: v.drain())
        E("v", lambda v: v.tensor_scalar(
            out=stat[0:1, 2:3], in0=stat[0:1, 1:2], scalar1=1.0 / 32767.0,
            scalar2=1e-12, op0=ALU.mult, op1=ALU.max), inc=(sV, 1))
        E("v", lambda v: v.drain())
        vsc = sV.n
        E("t", lambda t: t.matmul(
            out=ps_m5[0][:, 0:1], lhsT=ones_f[0:1, :], rhs=stat[0:1, 2:3],
            start=True, stop=True), waits=[(sV, vsc)], inc=(sT, 1))
        tb = sT.n
        E("v", lambda v, col_s=col_s: v.tensor_copy(
            out=scl[:, col_s:col_s + 1], in_=ps_m5[0][:, 0:1]),
          waits=[(sT, tb)], inc=(sV, 1))
        E("v", lambda v: v.drain())
        E("v", lambda v, col_s=col_s, col_r=col_r: v.reciprocal(
            out=scl[:, col_r:col_r + 1], in_=scl[:, col_s:col_s + 1]))
        E("v", lambda v: v.drain(), inc=(sV, 1))
        return sV.n

    sx_rdy = scale_setup(cc1o, ar1, 0, 1, p4e)

    xq5 = {"xfree": [0, 0], "evac": 0}
    for tile in range(NT):
        i2 = tile % 2
        leng, lsem = ("y", sLD) if i2 == 0 else ("g", sGD)
        E(leng, lambda e, tile=tile, i2=i2: e.dma_start(
            out=xbuf[i2][:, :], in_=dxn2[tile * P:(tile + 1) * P, :]),
          waits=[(sACT, sACT.n), (sV, xq5["xfree"][i2])] +
                (p4e if tile < 2 else []),
          inc16=lsem)
        ld = lsem.n
        E("v", lambda v, i2=i2: v.tensor_scalar(
            out=ubuf[:, :], in0=xbuf[i2][:, :], scalar1=scl[:, 1:2], scalar2=0.0,
            op0=ALU.mult, op1=ALU.bypass),
          waits=[(lsem, ld), (sV, sx_rdy), (sT, xq5["evac"] and 0)], inc=(sV, 1))
        xq5["xfree"][i2] = sV.n
        E("v", lambda v: v.tensor_copy(out=ibuf[:, :], in_=ubuf[:, :]))
        E("v", lambda v: v.tensor_copy(out=ubuf[:, :], in_=ibuf[:, :]), inc=(sV, 1))
        uv = sV.n
        for c in range(8):
            E("t", lambda t, c=c: t.transpose(
                out=ps_xt5[:, c * P:(c + 1) * P], in_=ubuf[:, c * P:(c + 1) * P],
                identity=ident_f[:, :]),
              waits=[(sV, uv), (sV, xq5["evac"])] if c == 0 else [],
              inc=(sT, 1) if c == 7 else None)
        tX = sT.n
        E("v", lambda v, tile=tile: v.tensor_scalar(
            out=xqT[:, :].rearrange("p (h c) -> p h c", h=8)[:, :, tile * P:(tile + 1) * P],
            in0=ps_xt5[:, :].rearrange("p (h c) -> p h c", h=8),
            scalar1=scl[:, 0:1], scalar2=0.0, op0=ALU.mult, op1=ALU.bypass),
          waits=[(sT, tX)], inc=(sV, 1))
        xq5["evac"] = sV.n
    E("v", lambda v: v.drain(), inc=(sV, 1))
    xq_done = sV.n

    q5 = {"wfree": [0, 0], "lufree": [0, 0], "ufree": [[0, 0], [0, 0]],
          "mfree": [0, 0]}
    spill_rec = {}

    def dequant_tile(src_sb, nblk, waits0):
        r3 = src_sb[:, :].rearrange("p (b k) -> p b k", k=32)
        E("v", lambda v, r3=r3, nblk=nblk: v.tensor_reduce(
            out=sdq[:, 0:nblk], in_=r3, axis=AX.X, op=ALU.max,
            apply_absolute_value=True), waits=waits0 + [(sS, sS.n)], inc=(sV, 1))
        E("v", lambda v: v.drain())
        E("v", lambda v, nblk=nblk: v.tensor_scalar(
            out=sdq[:, 0:nblk], in0=sdq[:, 0:nblk], scalar1=1.0 / 7.0,
            scalar2=1e-12, op0=ALU.mult, op1=ALU.max))
        E("v", lambda v: v.drain())
        E("v", lambda v, nblk=nblk: v.reciprocal(
            out=sdq[:, 96:96 + nblk], in_=sdq[:, 0:nblk]))
        E("v", lambda v: v.drain(), inc=(sV, 1))
        E("v", lambda v, r3=r3, nblk=nblk: v.tensor_tensor(
            out=ubuf[:, 0:nblk * 32].rearrange("p (b k) -> p b k", k=32), in0=r3,
            in1=sdq[:, 96:96 + nblk].unsqueeze(2).to_broadcast((P, nblk, 32)),
            op=ALU.mult), inc=(sV, 1))
        vr = sV.n
        E("v", lambda v, nblk=nblk: v.tensor_copy(
            out=ibuf[:, 0:nblk * 32], in_=ubuf[:, 0:nblk * 32]))
        E("v", lambda v, nblk=nblk: v.tensor_copy(
            out=ubuf[:, 0:nblk * 32], in_=ibuf[:, 0:nblk * 32]))
        E("v", lambda v, nblk=nblk: v.tensor_tensor(
            out=wqb[:, 0:nblk * 32].rearrange("p (b k) -> p b k", k=32),
            in0=ubuf[:, 0:nblk * 32].rearrange("p (b k) -> p b k", k=32),
            in1=sdq[:, 0:nblk].unsqueeze(2).to_broadcast((P, nblk, 32)),
            op=ALU.mult), inc=(sV, 1))
        E("v", lambda v: v.drain())
        return sV.n

    def upgate_pass(dw, dspill, rm_col, name, ssem):
        for c in range(NCF):
            i2 = c % 2
            leng, lsem = ("y", sLD) if i2 == 0 else ("g", sGD)
            E(leng, lambda e, c=c, i2=i2, dw=dw: e.dma_start(
                out=wnat[i2][:, :], in_=dw[c * P:(c + 1) * P, :]),
              waits=[(sV, q5["wfree"][i2])] + (p4e if c < 2 and name == "g" else []),
              inc16=lsem)
            wld = lsem.n
            wv = dequant_tile(wnat[i2], 32, [(lsem, wld)])
            q5["wfree"][i2] = wv
            for cc in range(8):
                E("t", lambda t, cc=cc: t.transpose(
                    out=ps_w5[:, cc * P:(cc + 1) * P], in_=wqb[:, cc * P:(cc + 1) * P],
                    identity=ident_b[:, :]),
                  waits=[(sV, wv), (sS, q5["lufree"][i2])] if cc == 0 else [],
                  inc=(sT, 1) if cc == 7 else None)
            tW = sT.n
            E("s", lambda s, i2=i2: s.activation(
                out=luT[i2][:, :], in_=ps_w5[:, :], func=AF.Copy, bias=0.0,
                scale=1.0), waits=[(sT, tW)], inc=(sS, 1))
            luS = sS.n
            q5["lufree"][i2] = luS
            for tg in range(4):
                bank = tg % 2
                for hc in range(8):
                    E("t", lambda t, hc=hc, tg=tg, bank=bank, i2=i2: t.matmul(
                        out=ps_m5[bank][:, :],
                        lhsT=luT[i2][:, hc * P:(hc + 1) * P],
                        rhs=xqT[:, hc * T + tg * 512: hc * T + (tg + 1) * 512],
                        start=(hc == 0), stop=(hc == 7)),
                      waits=[(sS, luS), (sV, xq_done), (sV, q5["mfree"][bank])]
                            if hc == 0 else [],
                      inc=(sT, 1) if hc == 7 else None)
                tM = sT.n
                E("v", lambda v, bank=bank, ssem=ssem: v.tensor_copy(
                    out=ustage[bank][:, :], in_=ps_m5[bank][:, :]),
                  waits=[(ssem, q5["ufree"][bank][0]),
                         (sGT, q5.get("ug_fence", 0) if (name == "u" and c == 0 and tg < 2) else 0),
                         (sT, tM), (sV, q5["ufree"][bank][1])], inc=(sV, 1))
                uV = sV.n
                q5["mfree"][bank] = uV
                E("s", lambda s, c=c, tg=tg, bank=bank, dspill=dspill, ssem=ssem: s.dma_start(
                    out=dspill[c, :, tg * 512:(tg + 1) * 512], in_=ustage[bank][:, :]),
                  waits=[(sV, uV)], inc16=ssem)
                spill_rec[(name, c, tg)] = ssem.n
                E("v", lambda v, bank=bank: v.tensor_reduce(
                    out=stat[:, 6:7], in_=ustage[bank][:, :], axis=AX.X, op=ALU.max,
                    apply_absolute_value=True))
                E("v", lambda v: v.drain())
                E("v", lambda v, rm_col=rm_col: v.tensor_tensor(
                    out=runmax[:, rm_col:rm_col + 1], in0=runmax[:, rm_col:rm_col + 1],
                    in1=stat[:, 6:7], op=ALU.max), inc=(sV, 1))
                E("v", lambda v: v.drain())
                q5["ufree"][bank] = [ssem.n, sV.n]

    def launch_ar(cci, cco, rm_col):
        E("g", lambda g, cci=cci, rm_col=rm_col: g.dma_start(
            out=cci[:, :], in_=runmax[:, rm_col:rm_col + 1]),
          waits=[(sV, sV.n)], inc16=sGD)
        E("g", lambda g, cci=cci, cco=cco: g.collective_compute(
            "AllReduce", ALU.max, replica_groups=[list(range(8))],
            ins=[cci.ap().opt()], outs=[cco.ap().opt()]),
          waits=[(sGD, sGD.n)], inc=(sCC, 1))
        return sCC.n

    upgate_pass(dwg, dgate, 2, "g", sGT)
    gate_total = sGT.n
    ar2 = launch_ar(cc2i, cc2o, 2)
    q5["ug_fence"] = gate_total
    q5["ufree"] = [[0, q5["ufree"][0][1]], [0, q5["ufree"][1][1]]]
    upgate_pass(dwu, dup, 1, "u", sUP)
    up_total = sUP.n
    ar3 = launch_ar(cc3i, cc3o, 1)

    sg_rdy = scale_setup(cc2o, ar2, 2, 3, [])
    gst = [b.sb(f"gst{i}", [P, 512], BF16) for i in range(2)]
    g2b = b.sb("g2b", [P, 512], F32)
    g2bb = b.sb("g2bb", [P, 512], BF16)
    gq_b = b.sb("gq_b", [P, 512], F32)
    sgb = b.sb("sgb", [P, 512], F32)
    i5 = b.sb("i5", [P, 512], I32)
    p5c = {"gfree": [0, 0], "g2free": 0, "spfree": 0}
    for c in range(NCF):
        for tg in range(4):
            i2 = (c * 4 + tg) % 2
            leng, lsem = ("y", sLD) if i2 == 0 else ("g", sGD)
            E(leng, lambda e, c=c, tg=tg, i2=i2: e.dma_start(
                out=gst[i2][:, :], in_=dgate[c, :, tg * 512:(tg + 1) * 512]),
              waits=[(sGT, gate_total), (sV, p5c["gfree"][i2])],
              inc16=lsem)
            gld = lsem.n
            E("v", lambda v, i2=i2: v.tensor_scalar(
                out=g2b[:, :], in0=gst[i2][:, :], scalar1=scl[:, 3:4], scalar2=0.0,
                op0=ALU.mult, op1=ALU.bypass),
              waits=[(lsem, gld), (sV, sg_rdy)], inc=(sV, 1))
            p5c["gfree"][i2] = sV.n
            E("v", lambda v: v.tensor_copy(out=i5[:, :], in_=g2b[:, :]))
            E("v", lambda v: v.tensor_copy(out=g2b[:, :], in_=i5[:, :]),
              inc=(sV, 1))
            E("v", lambda v: v.tensor_scalar(
                out=gq_b[:, :], in0=g2b[:, :], scalar1=scl[:, 2:3], scalar2=0.0,
                op0=ALU.mult, op1=ALU.bypass), inc=(sV, 1))
            gqv = sV.n
            E("s", lambda s: s.activation(
                out=sgb[:, :], in_=gq_b[:, :], func=AF.Sigmoid, bias=0.0, scale=1.0),
              waits=[(sV, gqv)], inc=(sS, 1))
            sS1 = sS.n
            E("v", lambda v: v.tensor_scalar(
                out=g2b[:, :], in0=sgb[:, :], scalar1=65536.0, scalar2=0.0,
                op0=ALU.mult, op1=ALU.bypass), waits=[(sS, sS1)], inc=(sV, 1))
            E("v", lambda v: v.tensor_copy(out=i5[:, :], in_=g2b[:, :]))
            E("v", lambda v: v.tensor_copy(out=g2b[:, :], in_=i5[:, :]),
              inc=(sV, 1))
            E("v", lambda v: v.tensor_scalar(
                out=g2b[:, :], in0=g2b[:, :], scalar1=65535.0, scalar2=0.0,
                op0=ALU.min, op1=ALU.max), inc=(sV, 1))
            E("v", lambda v: v.scalar_tensor_tensor(
                out=g2b[:, :], in0=g2b[:, :], scalar=1.0 / 65536.0, in1=gq_b[:, :],
                op0=ALU.mult, op1=ALU.mult), inc=(sV, 1))
            g2v = sV.n
            if upto == "p5dbg" and c == 23 and tg == 2:
                E("g", lambda g: g.dma_start(
                    out=dout[5 * P:6 * P, 0:512], in_=g2b[:, :]),
                  waits=[(sV, g2v)], inc16=sGD)
                E("g", lambda g, i2=i2: g.dma_start(
                    out=dout[15 * P:16 * P, 0:512], in_=gst[i2][:, :]),
                  waits=[(sV, g2v)], inc16=sGD)
                E("g", lambda g: g.dma_start(
                    out=dout[15 * P:16 * P, 512:1024], in_=gq_b[:, :]),
                  waits=[(sV, g2v)], inc16=sGD)
            E("v", lambda v: v.tensor_copy(out=g2bb[:, :], in_=g2b[:, :]),
              waits=[(sG2c, p5c["spfree"])], inc=(sV, 1))
            g2bv = sV.n
            E("s", lambda s, c=c, tg=tg: s.dma_start(
                out=dgate2[c, :, tg * 512:(tg + 1) * 512], in_=g2bb[:, :]),
              waits=[(sV, g2bv)], inc16=sG2c)
            spill_rec[("g2", c, tg)] = sG2c.n
            p5c["spfree"] = sG2c.n
            E("v", lambda v: v.tensor_reduce(
                out=stat[:, 6:7], in_=g2b[:, :], axis=AX.X, op=ALU.max,
                apply_absolute_value=True))
            E("v", lambda v: v.drain())
            E("v", lambda v: v.tensor_tensor(
                out=runmax[:, 3:4], in0=runmax[:, 3:4], in1=stat[:, 6:7],
                op=ALU.max), inc=(sV, 1))
            E("v", lambda v: v.drain())
            p5c["g2free"] = sG2c.n
    ar4 = launch_ar(cc4i, cc4o, 3)

    sup_rdy = scale_setup(cc3o, ar3, 4, 5, [])
    sg2_rdy = scale_setup(cc4o, ar4, 6, 7, [])
    E("v", lambda v: v.memset(runmax[:, 0:1], 0.0), inc=(sV, 1))
    E("v", lambda v: v.drain())
    ust2 = [b.sb(f"ust2{i}", [P, 512], BF16) for i in range(2)]
    ob = b.sb("ob", [P, 512], F32)
    obb = b.sb("obb", [P, 512], BF16)
    p5c_tail = sG2c.n
    g2_total = sG2c.n
    p5d = {"gfree": [0, 0], "ufree": [0, 0], "spfree": 0}
    for c in range(NCF):
        for tg in range(4):
            i2 = (c * 4 + tg) % 2
            E("y", lambda y, c=c, tg=tg, i2=i2: y.dma_start(
                out=gst[i2][:, :], in_=dgate2[c, :, tg * 512:(tg + 1) * 512]),
              waits=[(sG2c, g2_total), (sV, p5d["gfree"][i2])],
              inc16=sLD)
            gld = sLD.n
            E("g", lambda g, c=c, tg=tg, i2=i2: g.dma_start(
                out=ust2[i2][:, :], in_=dup[c, :, tg * 512:(tg + 1) * 512]),
              waits=[(sUP, up_total), (sV, p5d["ufree"][i2])],
              inc16=sGD)
            uld = sGD.n
            E("v", lambda v, i2=i2: v.tensor_scalar(
                out=g2b[:, :], in0=gst[i2][:, :], scalar1=scl[:, 7:8], scalar2=0.0,
                op0=ALU.mult, op1=ALU.bypass),
              waits=[(sLD, gld), (sV, sg2_rdy), (sV, sup_rdy)] +
                    ([(sG2c, p5c_tail)] if (c == 0 and tg == 0) else []),
              inc=(sV, 1))
            p5d["gfree"][i2] = sV.n
            E("v", lambda v: v.tensor_copy(out=i5[:, :], in_=g2b[:, :]))
            E("v", lambda v: v.tensor_copy(out=g2b[:, :], in_=i5[:, :]),
              inc=(sV, 1))
            E("v", lambda v: v.tensor_scalar(
                out=g2b[:, :], in0=g2b[:, :], scalar1=scl[:, 6:7], scalar2=0.0,
                op0=ALU.mult, op1=ALU.bypass), inc=(sV, 1))
            E("v", lambda v, i2=i2: v.tensor_scalar(
                out=ob[:, :], in0=ust2[i2][:, :], scalar1=scl[:, 5:6], scalar2=0.0,
                op0=ALU.mult, op1=ALU.bypass),
              waits=[(sGD, uld)], inc=(sV, 1))
            p5d["ufree"][i2] = sV.n
            E("v", lambda v: v.tensor_copy(out=i5[:, :], in_=ob[:, :]))
            E("v", lambda v: v.tensor_copy(out=ob[:, :], in_=i5[:, :]),
              inc=(sV, 1))
            E("v", lambda v: v.tensor_scalar(
                out=ob[:, :], in0=ob[:, :], scalar1=scl[:, 4:5], scalar2=0.0,
                op0=ALU.mult, op1=ALU.bypass), inc=(sV, 1))
            E("v", lambda v: v.tensor_tensor(
                out=ob[:, :], in0=g2b[:, :], in1=ob[:, :], op=ALU.mult),
              inc=(sV, 1))
            ov = sV.n
            E("v", lambda v: v.tensor_copy(out=obb[:, :], in_=ob[:, :]),
              waits=[(sOc, p5d["spfree"])], inc=(sV, 1))
            obv = sV.n
            E("s", lambda s, c=c, tg=tg: s.dma_start(
                out=dos[c, :, tg * 512:(tg + 1) * 512], in_=obb[:, :]),
              waits=[(sV, obv)], inc16=sOc)
            spill_rec[("o", c, tg)] = sOc.n
            p5d["spfree"] = sOc.n
            E("v", lambda v: v.tensor_reduce(
                out=stat[:, 6:7], in_=ob[:, :], axis=AX.X, op=ALU.max,
                apply_absolute_value=True))
            E("v", lambda v: v.drain())
            E("v", lambda v: v.tensor_tensor(
                out=runmax[:, 0:1], in0=runmax[:, 0:1], in1=stat[:, 6:7],
                op=ALU.max), inc=(sV, 1))
            E("v", lambda v: v.drain())
    ar5 = launch_ar(cc1i, cc1o, 0)

    if upto == "p5dbg":
        fw = [(sV, sV.n), (sS, sS.n), (sT, sT.n), (sACT, sACT.n), (sG2c, sG2c.n), (sOc, sOc.n), (sCC, sCC.n)]
        E("y", lambda y: y.dma_start(out=dout[0:P, 0:12], in_=scl[:, :]),
          waits=fw, inc16=sOUT)
        E("y", lambda y: y.dma_start(out=dout[0:P, 16:20], in_=runmax[:, :]),
          waits=fw, inc16=sOUT)
        E("y", lambda y: y.dma_start(out=dout[P:2 * P, :], in_=dup[0, :, 0:1024]),
          waits=fw, inc16=sOUT)
        E("y", lambda y: y.dma_start(out=dout[2 * P:3 * P, :], in_=dgate[0, :, 0:1024]),
          waits=fw, inc16=sOUT)
        E("y", lambda y: y.dma_start(out=dout[3 * P:4 * P, :], in_=dgate2[0, :, 0:1024]),
          waits=fw, inc16=sOUT)
        E("y", lambda y: y.dma_start(out=dout[4 * P:5 * P, :], in_=dos[0, :, 0:1024]),
          waits=fw, inc16=sOUT)
        E("y", lambda y: y.dma_start(out=dout[6 * P:7 * P, :], in_=dos[0, :, 1024:2048]),
          waits=fw, inc16=sOUT)
        E("y", lambda y: y.dma_start(out=dout[7 * P:8 * P, :], in_=dgate2[0, :, 1024:2048]),
          waits=fw, inc16=sOUT)
        E("y", lambda y: y.dma_start(out=dout[8 * P:9 * P, :], in_=dup[0, :, 1024:2048]),
          waits=fw, inc16=sOUT)
        E("y", lambda y: y.dma_start(out=dout[9 * P:10 * P, :], in_=dgate[0, :, 1024:2048]),
          waits=fw, inc16=sOUT)
        E("y", lambda y: y.dma_start(out=dout[10 * P:11 * P, :], in_=dos[23, :, 1024:2048]),
          waits=fw, inc16=sOUT)
        E("y", lambda y: y.dma_start(out=dout[11 * P:12 * P, :], in_=dgate2[23, :, 1024:2048]),
          waits=fw, inc16=sOUT)
        E("y", lambda y: y.dma_start(out=dout[12 * P:13 * P, :], in_=dup[23, :, 1024:2048]),
          waits=fw, inc16=sOUT)
        E("y", lambda y: y.dma_start(out=dout[13 * P:14 * P, :], in_=dos[23, :, 0:1024]),
          waits=fw, inc16=sOUT)
        E("y", lambda y: y.dma_start(out=dout[14 * P:15 * P, :], in_=dgate[23, :, 1024:2048]),
          waits=fw, inc16=sOUT)
        E("y", lambda y: y.dma_start(out=dout[15 * P:16 * P, :], in_=dup[22, :, 1024:2048]),
          waits=fw, inc16=sOUT)
        E("g", lambda g: g.dma_start(out=dout[5 * P:6 * P, :], in_=xqT[:, 0:1024]),
          waits=fw, inc16=sOUT)
        E("y", lambda y: y.nop(), waits=[(sOUT, sOUT.n)])
        _finish(nc, b)
        return nc

    b.free("obb", "ob", "ust20", "ust21", "i5", "sgb", "gq_b", "g2bb", "g2b",
           "gst0", "gst1",
           "ustage0", "ustage1", "luT0", "luT1", "wqb", "wnat0", "wnat1",
           "ubuf", "ibuf", "xbuf0", "xbuf1", "xqT")

    wdT = b.sb("wdT", [P, NCF * D], BF16)
    wnat2 = [b.sb(f"wnat2{i}", [P, IF], F32) for i in range(2)]
    ubuf2 = b.sb("ubuf2", [P, IF], F32)
    ibuf2 = b.sb("ibuf2", [P, IF], I32)
    wqb2 = b.sb("wqb2", [P, IF], BF16)
    wdqn = b.sb("wdqn", [P, 8 * IF], BF16)
    wd5 = {"wfree": [0, 0], "evac": sS.n}
    o_total = sOc.n
    p5d_e = [(sV, sV.n), (sS, sS.n), (sT, sT.n), (sOc, sOc.n), (sG2c, sG2c.n)]
    for ht in range(8):
        i2w = ht % 2
        leng, lsem = ("y", sLD) if i2w == 0 else ("g", sGD)
        E(leng, lambda e, ht=ht, i2w=i2w: e.dma_start(
            out=wnat2[i2w][:, :], in_=dwd[ht * P:(ht + 1) * P, :]),
          waits=[(sV, wd5["wfree"][i2w])] + (p5d_e if ht < 2 else []), inc16=lsem)
        wld = lsem.n
        r3 = wnat2[i2w][:, :].rearrange("p (b k) -> p b k", k=32)
        E("v", lambda v, r3=r3: v.tensor_reduce(
            out=sdq[:, 0:96], in_=r3, axis=AX.X, op=ALU.max,
            apply_absolute_value=True), waits=[(lsem, wld), (sS, sS.n)], inc=(sV, 1))
        E("v", lambda v: v.drain())
        E("v", lambda v: v.tensor_scalar(
            out=sdq[:, 0:96], in0=sdq[:, 0:96], scalar1=1.0 / 7.0, scalar2=1e-12,
            op0=ALU.mult, op1=ALU.max))
        E("v", lambda v: v.drain())
        E("v", lambda v: v.reciprocal(out=sdq[:, 96:192], in_=sdq[:, 0:96]))
        E("v", lambda v: v.drain(), inc=(sV, 1))
        E("v", lambda v, r3=r3: v.tensor_tensor(
            out=ubuf2[:, :].rearrange("p (b k) -> p b k", k=32),
            in0=r3, in1=sdq[:, 96:192].unsqueeze(2).to_broadcast((P, 96, 32)),
            op=ALU.mult), inc=(sV, 1))
        vr = sV.n
        E("v", lambda v: v.tensor_copy(out=ibuf2[:, :], in_=ubuf2[:, :]))
        E("v", lambda v: v.tensor_copy(out=ubuf2[:, :], in_=ibuf2[:, :]))
        E("v", lambda v, ht=ht: v.tensor_tensor(
            out=wdqn[:, ht * IF:(ht + 1) * IF].rearrange("p (b k) -> p b k", k=32),
            in0=ubuf2[:, :].rearrange("p (b k) -> p b k", k=32),
            in1=sdq[:, 0:96].unsqueeze(2).to_broadcast((P, 96, 32)),
            op=ALU.mult), inc=(sV, 1))
        wd5["wfree"][i2w] = sV.n
    E("v", lambda v: v.drain(), inc=(sV, 1))
    wdq_v = sV.n
    for ioc in range(NCF):
        for ht in range(8):
            E("t", lambda t, ioc=ioc, ht=ht: t.transpose(
                out=ps_w5[:, ht * P:(ht + 1) * P],
                in_=wdqn[:, ht * IF + ioc * P: ht * IF + (ioc + 1) * P],
                identity=ident_b[:, :]),
              waits=[(sV, wdq_v), (sS, wd5["evac"])] if ht == 0 else [],
              inc=(sT, 1) if ht == 7 else None)
        tW = sT.n
        E("s", lambda s, ioc=ioc: s.activation(
            out=wdT[:, ioc * D:(ioc + 1) * D], in_=ps_w5[:, :], func=AF.Copy,
            bias=0.0, scale=1.0), waits=[(sT, tW)], inc=(sS, 1))
        wd5["evac"] = sS.n
    wdT_done = sS.n
    b.free("wdqn", "wqb2", "ibuf2", "ubuf2", "wnat20", "wnat21")

    so_rdy = scale_setup(cc1o, ar5, 8, 9, [])
    act_total = sACT.n
    oqb = b.sb("oqb", [P, NCF * 512], BF16)
    ostr = [b.sb(f"ostr{i}", [P, 512], BF16) for i in range(2)]
    oi5 = b.sb("oi5", [P, 512], I32)
    osc = b.sb("osc", [P, 512], F32)
    x1s = [b.sb(f"x1s{i}", [P, D], F32) for i in range(2)]
    apS = [b.sb(f"apS{i}", [P, D], F32) for i in range(2)]
    tst = b.sb("tst", [P, D], F32)
    outst = [b.sb(f"outst{i}", [P, D], F32) for i in range(2)]
    p5e = {"ofree": [0, 0], "xfree": [0, 0], "afree": [0, 0], "outfree": [0, 0],
           "mfree": [0, 0], "mmdone": 0}
    for tg in range(4):
        for ioc in range(NCF):
            i2 = ioc % 2
            leng, lsem = ("y", sLD) if i2 == 0 else ("g", sGD)
            E(leng, lambda e, ioc=ioc, tg=tg, i2=i2: e.dma_start(
                out=ostr[i2][:, :], in_=dos[ioc, :, tg * 512:(tg + 1) * 512]),
              waits=[(sOc, o_total), (sV, p5e["ofree"][i2])] +
                    ([(sS, wdT_done)] if tg == 0 and ioc < 2 else []),
              inc16=lsem)
            old = lsem.n
            E("v", lambda v, i2=i2: v.tensor_scalar(
                out=osc[:, :], in0=ostr[i2][:, :], scalar1=scl[:, 9:10], scalar2=0.0,
                op0=ALU.mult, op1=ALU.bypass),
              waits=[(lsem, old), (sV, so_rdy)], inc=(sV, 1))
            p5e["ofree"][i2] = sV.n
            E("v", lambda v: v.tensor_copy(out=oi5[:, :], in_=osc[:, :]))
            E("v", lambda v: v.tensor_copy(out=osc[:, :], in_=oi5[:, :]),
              inc=(sV, 1))
            E("v", lambda v, ioc=ioc: v.tensor_scalar(
                out=oqb[:, ioc * 512:(ioc + 1) * 512], in0=osc[:, :],
                scalar1=scl[:, 8:9], scalar2=0.0, op0=ALU.mult, op1=ALU.bypass),
              waits=[(sT, p5e["mmdone"])] if ioc == 0 else [], inc=(sV, 1))
        E("v", lambda v: v.drain(), inc=(sV, 1))
        oq_v = sV.n
        if upto == "p5e" and False:
            for i in range(8):
                E("g", lambda g, i=i: g.dma_start(
                    out=dout[i * P:(i + 1) * P, 0:512], in_=oqb[:, i * 512:(i + 1) * 512]),
                  waits=[(sV, oq_v)], inc16=sOUT)
            E("y", lambda y: y.dma_start(out=dout[8 * P:9 * P, 0:12], in_=scl[:, :]),
              waits=[(sV, oq_v)], inc16=sOUT)
            E("g", lambda g: g.dma_start(out=dout[9 * P:10 * P, :], in_=wdT[:, 0:1024]),
              waits=[(sV, oq_v)], inc16=sOUT)
        for ts4 in range(4):
            tile = tg * 4 + ts4
            i2 = ts4 % 2
            E("y", lambda y, tile=tile, i2=i2: y.dma_start(
                out=x1s[i2][:, :], in_=dx1[tile * P:(tile + 1) * P, :]),
              waits=[(sACT, act_total), (sV, p5e["xfree"][i2])] +
                    ([(sS, wdT_done)] if tile < 2 else []), inc16=sLD)
            rld = sLD.n
            E("g", lambda g, tile=tile, i2=i2: g.dma_start(
                out=apS[i2][:, :], in_=dattnP[tile * P:(tile + 1) * P, :]),
              waits=[(sACT, act_total), (sV, p5e["afree"][i2])] +
                    ([(sS, wdT_done)] if tile < 2 else []), inc16=sGD)
            rldg = sGD.n
            tops = []
            for half in range(2):
                for ioc in range(NCF):
                    E("t", lambda t, ioc=ioc, ts4=ts4, half=half: t.matmul(
                        out=ps_m5[half][:, :],
                        lhsT=oqb[:, ioc * 512 + ts4 * P: ioc * 512 + (ts4 + 1) * P],
                        rhs=wdT[:, ioc * D + half * 512: ioc * D + (half + 1) * 512],
                        start=(ioc == 0), stop=(ioc == NCF - 1)),
                      waits=[(sV, oq_v), (sS, wdT_done), (sV, p5e["mfree"][half])]
                            if ioc == 0 else [],
                      inc=(sT, 1) if ioc == NCF - 1 else None)
                tops.append(sT.n)
            E("v", lambda v, i2=i2: v.tensor_tensor(
                out=tst[:, :], in0=x1s[i2][:, :], in1=apS[i2][:, :], op=ALU.add),
              waits=[(sLD, rld), (sGD, rldg), (sV, sV.n)], inc=(sV, 1))
            tv = sV.n
            p5e["xfree"][i2] = tv
            p5e["afree"][i2] = tv
            oi = ts4 % 2
            E("v", lambda v, oi=oi: v.tensor_tensor(
                out=outst[oi][:, 0:512], in0=ps_m5[0][:, :], in1=tst[:, 0:512],
                op=ALU.add),
              waits=[(sT, tops[0]), (sOUT, p5e["outfree"][oi])])
            E("v", lambda v, oi=oi: v.tensor_tensor(
                out=outst[oi][:, 512:1024], in0=ps_m5[1][:, :], in1=tst[:, 512:1024],
                op=ALU.add), waits=[(sT, tops[1])], inc=(sV, 1))
            vout = sV.n
            p5e["mfree"][0] = vout
            p5e["mfree"][1] = vout
            E("y", lambda y, tile=tile, oi=oi: y.dma_start(
                out=dout[tile * P:(tile + 1) * P, :], in_=outst[oi][:, :]),
              waits=[(sV, vout)], inc16=sOUT)
            p5e["outfree"][oi] = sOUT.n
        p5e["mmdone"] = sT.n
        if upto == "p5e" and tg == 2:
            fw2 = [(sV, sV.n), (sS, sS.n), (sT, sT.n), (sOUT, sOUT.n)]
            E("y", lambda y: y.dma_start(out=dout[12 * P:13 * P, :], in_=dx1[8 * P:9 * P, :]),
              waits=fw2, inc16=sOUT)
            E("y", lambda y: y.dma_start(out=dout[13 * P:14 * P, :], in_=dattnP[8 * P:9 * P, :]),
              waits=fw2, inc16=sOUT)
            break
    E("y", lambda y: y.nop(), waits=[(sOUT, sOUT.n)])
    _finish(nc, b)
    return nc



_CACHE = {}


def kernel(**inputs):
    if "nc" not in _CACHE:
        _CACHE["nc"] = build_nc()
    nc = _CACHE["nc"]
    x = np.ascontiguousarray(inputs["L0_attn_hidden_states"], dtype=np.float32)
    w = {
        "kw": np.ascontiguousarray(inputs["L0_attn_k_weight"], np.float32),
        "qw": np.ascontiguousarray(inputs["L0_attn_q_weight"], np.float32),
        "lnw": np.ascontiguousarray(inputs["L0_attn_ln_weight"], np.float32),
        "lnb": np.ascontiguousarray(inputs["L0_attn_ln_bias"], np.float32),
        "bias1": np.ascontiguousarray(inputs["L0_attn_bias"], np.float32),
        "wrT": np.ascontiguousarray(np.asarray(inputs["L0_ffn_W_r"]).T, np.float32),
        "wkT": np.ascontiguousarray(np.asarray(inputs["L0_ffn_W_k"]).T, np.float32),
        "wvT": np.ascontiguousarray(np.asarray(inputs["L0_ffn_W_v"]).T, np.float32),
        "mur": np.ascontiguousarray(inputs["L0_ffn_mu_r"], np.float32),
        "muk": np.ascontiguousarray(inputs["L0_ffn_mu_k"], np.float32),
        "l1qT": np.ascontiguousarray(np.asarray(inputs["L1_attn_Wq"]).T, np.float32),
        "l1kT": np.ascontiguousarray(np.asarray(inputs["L1_attn_Wk"]).T, np.float32),
        "l1vT": np.ascontiguousarray(np.asarray(inputs["L1_attn_Wv"]).T, np.float32),
        "l1oT": np.ascontiguousarray(np.asarray(inputs["L1_attn_Wo"]).T, np.float32),
        "bo": np.ascontiguousarray(inputs["L1_attn_bo"], np.float32),
        "wg": np.ascontiguousarray(inputs["L1_ffn_w_gate"], np.float32),
        "wu": np.ascontiguousarray(inputs["L1_ffn_w_up"], np.float32),
        "wd": np.ascontiguousarray(inputs["L1_ffn_w_down"], np.float32),
    }
    in_maps = [dict(w, xin=np.ascontiguousarray(x[c])) for c in range(8)]
    res = run_bass_kernel_spmd(nc, in_maps, core_ids=list(range(8)))
    return np.stack([res.results[c]["out"] for c in range(8)], axis=0)
